# version 03
# baseline (speedup 1.0000x reference)
import numpy as np
from contextlib import ExitStack
import concourse.bass as bass
import concourse.mybir as mybir

F32 = mybir.dt.float32
BF16 = mybir.dt.bfloat16
AF = mybir.ActivationFunctionType
ALU = mybir.AluOpType

D = 1024
S = 4096
BLK = 256
NBLK = S // BLK
MYB = 8
MYT = MYB * BLK
DCH = 8
HCH = 32
TCH = MYT // BLK
M0 = 12.0
M1 = 26.0
NEG = -1.0e30

PAT = [[0, 3, 4, 7, 8, 11, 12, 15], [1, 2, 5, 6, 9, 10, 13, 14]]
C1 = [1, 1, 3, 3, 5, 5, 7, 7]
C0 = [0, 2, 2, 4, 4, 6, 6, 8]
L0 = [max(i, C1[i]) for i in range(8)]
L1 = [max(C0[i], i) for i in range(8)]
RING = 2

SEM_NAMES = ["ld", "wl", "castv", "casts", "sq", "stm", "rowe", "rowms",
             "rowmv", "xnr", "hmm", "hp", "ymm", "xev", "out", "wl2",
             "qmm", "qev", "stg", "stg2", "cc", "kv0", "kv1", "smm", "pxp", "avm",
             "fin", "ymt", "yv"]


def visits_of(band):
    vs = [("own", band)]
    vs += [("r0", j) for j in range(L0[band])]
    vs += [("r1", j) for j in range(L1[band])]
    return vs


class E:

    def __init__(self, nc, engname, sems):
        self.nc = nc
        self.engname = engname
        self.engines = {"vector": nc.vector, "scalar": nc.scalar,
                        "tensor": nc.tensor, "gpsimd": nc.gpsimd, "sync": nc.sync}
        self.semh = sems
        self.cnt = {k: 0 for k in sems}

    def _fin(self, ins, inc, inck):
        if inc is not None:
            if ins is not None:
                ins.then_inc(self.semh[inc], inck)
            self.cnt[inc] += inck
        return ins

    def emit(self, engine, fn, inc=None, inck=1):
        ins = fn(self.engines[engine]) if self.engname == engine else None
        return self._fin(ins, inc, inck)

    def dma(self, engine, out, in_, inc=None):
        ins = None
        if self.engname == engine:
            ins = self.engines[engine].dma_start(out=out, in_=in_)
        return self._fin(ins, inc, 16 if inc else 0)

    def wait(self, engine, sem, val=None):
        if val is None:
            val = self.cnt[sem]
        if val <= 0:
            return
        if self.engname == engine:
            self.engines[engine].wait_ge(self.semh[sem], val)

    def val(self, sem):
        return self.cnt[sem]


def build_program(upto=2):
    QH = HCH // 4
    nc = bass.Bass("TRN2", target_bir_lowering=False, debug=False)

    def din(name, shape, dtype=F32):
        return nc.dram_tensor(name, shape, dtype, kind="ExternalInput").ap()

    x_in = din("x_in", [DCH, 128, MYT])
    w1_in = din("w1", [D, 4 * D])
    w2_in = din("w2", [4 * D, D])
    b1r_in = din("b1r", [1, 4 * D])
    b2_in = din("b2", [128, DCH])
    wq_in = din("wq", [D, D])
    wk_in = din("wk", [D, D])
    wv_in = din("wv", [D, D])
    wo_in = din("wo", [D, D])
    maskd_in = din("maskd", [128, 2, BLK])
    mv0_in = din("mv0", [128, 2, BLK])
    mv1_in = din("mv1", [128, 2, BLK])
    eye_in = din("eye", [128, 128])
    wt_in = din("wt", [D, D])
    ut_in = din("ut", [D, 128])
    vs_in = din("vs", [128, D])
    gw_in = din("gw", [D, 8])
    g16_in = din("g16", [8, 128])
    wqkvt_in = din("wqkvt", [D, 3 * D])
    wot_in = din("wot", [D, D])
    cosi_in = din("cosi", [128, MYT])
    sini_in = din("sini", [128, MYT])
    prot_in = din("prot", [128, 128])
    y_out = nc.dram_tensor("y_out", [DCH, 128, MYT], F32, kind="ExternalOutput").ap()

    KVW = DCH * BLK + 2 * 1040
    kvin = nc.dram_tensor("kvin", [MYB, 128, KVW], BF16)
    kvg = nc.dram_tensor("kvg", [2, MYB, 128, KVW], BF16)
    onorm_d = nc.dram_tensor("onorm_d", [8, 8, 128, BLK], BF16)
    w1c_d = nc.dram_tensor("w1c", [4, DCH, 128, 1024], BF16)
    w2c_d = nc.dram_tensor("w2c", [4, QH, 128, 1024], BF16)
    b1c_d = nc.dram_tensor("b1c", [4, 1, 1024], BF16)
    wtc_d = nc.dram_tensor("wtc", [DCH, 128, 1024], BF16)
    smallc_d = nc.dram_tensor("smallc", [4, 128, 1024], BF16)

    BASE = 16512
    off = [BASE]
    def at(name, shape, dtype, offset=None):
        if offset is None:
            offset = off[0]
        per_part = int(np.prod(shape[1:])) * mybir.dt.size(dtype)
        h = nc.alloc_sbuf_tensor_at(name, shape, dtype, offset=offset)
        off[0] = max(off[0], offset + ((per_part + 63) // 64) * 64)
        return h

    x_sb = at("x_sb", [128, DCH, MYT], F32)
    xn_sb = at("xn_sb", [128, DCH, MYT], BF16)
    stat_sb = at("stat_sb", [128, 4, BLK], F32)
    const_sb = at("const_sb", [128, 16], F32)
    onesrep_sb = at("onesrep_sb", [128, 128], F32)
    onesbf_sb = at("onesbf_sb", [1, BLK], BF16)
    onescol_sb = at("onescol_sb", [128, 2], BF16)
    onesrow_sb = at("onesrow_sb", [1, 128], F32)
    b2h_sb = at("b2h_sb", [128, DCH], F32)
    b2stg_sb = at("b2stg_sb", [128, DCH], F32)
    PERS_END = off[0]

    FREG = PERS_END
    w1q_sb = at("w1q_sb", [128, 2, DCH, QH, 128], BF16, FREG)
    w2q_sb = at("w2q_sb", [128, 2, QH, DCH, 128], BF16, FREG + 32768)
    h_sb = at("h_sb", [128, 2, QH, BLK], BF16, FREG + 65536)
    scr_sb = at("scr_sb", [128, 1024], F32, FREG + 73728)
    b1q_sb = at("b1q_sb", [1, 2, QH * 128], BF16, FREG + 77824)
    b1qstg_sb = at("b1qstg_sb", [1, 2, QH * 128], F32, FREG + 81920)
    xsq_sb = at("xsq_sb", [128, 1024], F32, FREG + 90112)
    FFN_END = FREG + 94208

    wt_sb = at("wt_sb", [128, DCH, DCH, 128], BF16, FREG)
    ut_sb = at("ut_sb", [128, DCH, 128], BF16, FREG + 16384)
    vs_sb = at("vs_sb", [128, DCH, 128], BF16, FREG + 18432)
    gw_sb = at("gw_sb", [128, DCH, 8], BF16, FREG + 20480)
    g16_sb = at("g16_sb", [8, 128], BF16, FREG + 20608)
    gates_sb = at("gates_sb", [128, MYT], BF16, FREG + 20864)
    gte_sb = at("gte_sb", [128, 2, 8], F32, FREG + 24960)
    gtr_sb = at("gtr_sb", [128, 4], F32, FREG + 25024)
    gtn_sb = at("gtn_sb", [128, 2, 8], F32, FREG + 25088)
    gb_sb = at("gb_sb", [128, 2, 512], BF16, FREG + 25152)
    tg_sb = at("tg_sb", [128, 2, 512], BF16, FREG + 27200)
    g16stg_sb = at("g16stg_sb", [8, 128], F32, FREG + 29248)
    eyef_sb = at("eyef_sb", [128, 128], F32, FREG + 29760)

    q_sb = at("q_sb", [128, DCH, MYT], BF16, FREG)
    wqkv_sb = at("wqkv_sb", [128, 2, DCH, DCH, 128], BF16, FREG + 32768)
    kring_sb = at("kring_sb", [128, RING, DCH, BLK], BF16, FREG + 65536)
    vring_sb = at("vring_sb", [128, RING, 2, 1024], BF16, FREG + 73728)
    pp_sb = at("pp_sb", [128, 2, 512], BF16, FREG + 82048)
    maskd_sb = at("maskd_sb", [128, 2, BLK], BF16, FREG + 84096)
    mv0_sb = at("mv0_sb", [128, 2, BLK], BF16, FREG + 85120)
    mv1_sb = at("mv1_sb", [128, 2, BLK], BF16, FREG + 86144)
    eye_sb = at("eye_sb", [128, 128], BF16, FREG + 87168)
    wo_sb = at("wo_sb", [128, 4, DCH, 128], BF16, FREG + 87424)
    lrow_sb = at("lrow_sb", [128, BLK], F32, FREG + 95616)
    rl_sb = at("rl_sb", [128, BLK], F32, FREG + 96640)
    rlb_sb = at("rlb_sb", [128, BLK], F32, FREG + 97664)
    onorm_sb = at("onorm_sb", [128, 4, BLK], BF16, FREG + 98688)
    scrA_sb = at("scrA_sb", [128, 1024], F32, FREG + 100736)
    ATT_END = FREG + 104832

    XNOFF = BASE + 65536
    pp1_sb = at("pp1_sb", [128, 2, 1024], BF16, XNOFF)
    kslr_sb = at("kslr_sb", [128, 2, BLK], BF16, XNOFF + 4096)
    vslr_sb = at("vslr_sb", [128, 2, 2, 130], BF16, XNOFF + 5120)
    cosi_sb = at("cosi_sb", [128, MYT], BF16, FREG + 73728)
    sini_sb = at("sini_sb", [128, MYT], BF16, FREG + 77824)
    vstg1_sb = at("vstg1_sb", [128, 2, 16, 65], BF16, FREG + 87424)
    prot_sb = at("prot_sb", [128, 128], BF16, FREG + 104832)
    ropet_sb = at("ropet_sb", [128, 512], BF16, FREG + 105088)
    wof_sb = at("wof_sb", [128, DCH, DCH, 128], BF16, FREG)
    onld_sb = at("onld_sb", [128, DCH, BLK], BF16, FREG + 32768)
    A1_END = FREG + 106112
    assert max(FFN_END, ATT_END, A1_END) + 0 <= 229344, (FFN_END, ATT_END, A1_END)

    est = ExitStack()
    psA = est.enter_context(nc.psum_tensor("psA", [128, 2048], F32))
    psB = est.enter_context(nc.psum_tensor("psB", [128, 1024], F32))
    psC = est.enter_context(nc.psum_tensor("psC", [128, 1024], F32))
    sems = {n: est.enter_context(nc.semaphore(n)) for n in SEM_NAMES}


    def emit_ln_chunk(em, t, xnr_base):
        V, SC, T = "vector", "scalar", "tensor"
        sl = slice(t * BLK, (t + 1) * BLK)
        em.wait(T, "rowe", em.val("rowe"))
        em.wait(T, "hp", em.val("hp"))
        em.wait(T, "pxp", em.val("pxp"))
        em.wait(T, "xnr", xnr_base)
        em.wait(SC, "xnr", xnr_base)
        em.wait(SC, "ymt", em.val("ymt"))
        for c in range(DCH):
            em.emit(T, lambda e, c=c, sl=sl: e.matmul(
                psB[:, 0:256], onesrep_sb[:], x_sb[:, c, sl],
                start=(c == 0), stop=(c == DCH - 1)),
                inc="stm" if c == DCH - 1 else None)
        for half in range(2):
            c0 = half * 4
            em.wait(SC, "stm", em.val("stm"))
            em.emit(SC, lambda e, c0=c0, sl=sl: e.activation(
                xsq_sb[:].rearrange("p (c m) -> p c m", c=4),
                x_sb[:, c0:c0 + 4, sl], AF.Square), inc="sq")
            em.wait(T, "sq")
            for j in range(4):
                em.emit(T, lambda e, j=j, half=half: e.matmul(
                    psB[:, 256:512], onesrep_sb[:], xsq_sb[:, j * 256:(j + 1) * 256],
                    start=(half == 0 and j == 0), stop=(half == 1 and j == 3)),
                    inc="stm" if j == 3 else None)
        em.wait(V, "stm")
        em.emit(V, lambda e: e.tensor_copy(stat_sb[:, 0, :], psB[:, 0:256]),
                inc="rowe")
        em.emit(V, lambda e: e.tensor_copy(stat_sb[:, 1, :], psB[:, 256:512]),
                inc="rowe")
        em.wait(SC, "rowe")
        em.emit(SC, lambda e: e.activation(stat_sb[:, 2, :], stat_sb[:, 0, :],
                                           AF.Square), inc="rowms")
        em.wait(V, "rowms")
        em.emit(V, lambda e: e.tensor_tensor(stat_sb[:, 1, :], stat_sb[:, 1, :],
                                             stat_sb[:, 2, :], op=ALU.subtract),
                inc="rowmv")
        em.wait(SC, "rowmv")
        em.emit(SC, lambda e: e.activation(stat_sb[:, 2, :], stat_sb[:, 1, :],
                                           AF.Sqrt, bias=const_sb[:, 1:2]),
                inc="rowms")
        em.wait(V, "rowms")
        em.emit(V, lambda e: e.reciprocal(stat_sb[:, 1, :], stat_sb[:, 2, :]),
                inc="rowmv")
        for c in range(DCH):
            em.emit(V, lambda e, c=c, sl=sl: e.tensor_tensor(
                xn_sb[:, c, sl], x_sb[:, c, sl], stat_sb[:, 0, :],
                op=ALU.subtract))
        em.emit(V, lambda e: e.drain())
        for c in range(DCH):
            em.emit(V, lambda e, c=c, sl=sl: e.tensor_tensor(
                xn_sb[:, c, sl], xn_sb[:, c, sl], stat_sb[:, 1, :],
                op=ALU.mult), inc="xnr" if c == DCH - 1 else None)
        em.emit(V, lambda e: e.drain())


    def emit_ffn0(em, first):
        V, SC, T, G, SY = "vector", "scalar", "tensor", "gpsimd", "sync"
        xnr_base = em.val("xnr")
        xnr_at = []
        for t in range(TCH):
            emit_ln_chunk(em, t, xnr_base)
            xnr_at.append(em.val("xnr"))
        XNR_ALL = em.val("xnr")
        for sem in ("smm", "avm", "ymt", "pxp", "qmm", "yv"):
            em.wait(SY, sem, em.val(sem))
        for q in range(4):
            sg = q % 2
            em.wait(SY, "xev", em.val("xev") - 2 * TCH)
            if first:
                qsem = "stg" if q % 2 == 0 else "stg2"
                em.wait(V, qsem, em.val(qsem))
                em.wait(SC, qsem, em.val(qsem))
                for cc in range(DCH):
                    for sh in range(2):
                        wsem = "wl" if sh == 0 else "wl2"
                        em.wait(SY, "castv", em.val("castv") -
                                (0 if (cc == 0 and sh == 0) else 1))
                        em.dma(SY, scr_sb[:, sh * 512:(sh + 1) * 512],
                               w1_in[cc * 128:(cc + 1) * 128,
                                     q * 1024 + sh * 512:
                                     q * 1024 + (sh + 1) * 512], inc=wsem)
                        em.wait(V, wsem)
                        em.emit(V, lambda e, sg=sg, cc=cc, sh=sh: e.tensor_copy(
                            w1q_sb[:, sg, cc]
                            .rearrange("p h m -> p (h m)")[:, sh * 512:
                                                           (sh + 1) * 512],
                            scr_sb[:, sh * 512:(sh + 1) * 512]), inc="castv")
                    em.wait(G, "castv")
                    em.dma(G, w1c_d[q, cc],
                           w1q_sb[:, sg, cc].rearrange("p h m -> p (h m)"),
                           inc=qsem)
                for hh in range(QH):
                    for sh in range(2):
                        wsem = "wl" if sh == 0 else "wl2"
                        em.wait(SY, "castv", em.val("castv"))
                        em.wait(SY, "casts", em.val("casts") -
                                (0 if (hh == 0 and sh == 0) else 1))
                        em.dma(SY, scr_sb[:, sh * 512:(sh + 1) * 512],
                               w2_in[(q * QH + hh) * 128:(q * QH + hh + 1) * 128,
                                     sh * 512:(sh + 1) * 512], inc=wsem)
                        em.wait(SC, wsem)
                        em.emit(SC, lambda e, sg=sg, hh=hh, sh=sh: e.mul(
                            w2q_sb[:, sg, hh]
                            .rearrange("p d m -> p (d m)")[:, sh * 512:
                                                           (sh + 1) * 512],
                            scr_sb[:, sh * 512:(sh + 1) * 512], 0.5),
                            inc="casts")
                    em.wait(G, "casts")
                    em.dma(G, w2c_d[q, hh],
                           w2q_sb[:, sg, hh].rearrange("p d m -> p (d m)"),
                           inc=qsem)
                em.wait(SY, "castv", em.val("castv"))
                em.wait(SY, "casts", em.val("casts"))
                em.dma(SY, b1qstg_sb[:, sg, :],
                       b1r_in[:, q * QH * 128:(q + 1) * QH * 128], inc="wl")
                em.wait(V, "wl")
                em.emit(V, lambda e, sg=sg: e.tensor_copy(
                    b1q_sb[:, sg, :], b1qstg_sb[:, sg, :]), inc="castv")
                em.wait(G, "castv")
                em.dma(G, b1c_d[q], b1q_sb[:, sg, :].rearrange("p m -> p m"),
                       inc=qsem)
            else:
                em.wait(SY, "stg", em.val("stg"))
                em.wait(SY, "stg2", em.val("stg2"))
                em.wait(SY, "wl", em.val("wl"))
                em.dma(SY, w1q_sb[:, sg].rearrange("p c h m -> p c (h m)"),
                       w1c_d[q].rearrange("c p m -> p c m"), inc="wl")
                em.dma(SY, w2q_sb[:, sg].rearrange("p h d m -> p h (d m)"),
                       w2c_d[q].rearrange("h p m -> p h m"), inc="wl")
                em.dma(SY, b1q_sb[:, sg, :], b1c_d[q], inc="wl")
                em.wait(T, "wl", em.val("wl"))
            em.wait(T, "castv")
            em.wait(T, "casts")
            if q == 0:
                em.wait(T, "rowe")
            for t in range(TCH):
                k = q * TCH + t
                hs = k % 2
                sl = slice(t * BLK, (t + 1) * BLK)
                if q == 0:
                    em.wait(T, "xnr", xnr_at[t])
                em.wait(T, "hp", em.val("hp"))
                for grp in range(2):
                    pg = psB if grp == 0 else psC
                    for j in range(4):
                        hh = grp * 4 + j
                        pa = pg[:, j * 256:(j + 1) * 256]
                        em.emit(T, lambda e, sg=sg, hh=hh, pa=pa: e.matmul(
                            pa, b1q_sb[:, sg, hh * 128:(hh + 1) * 128],
                            onesbf_sb[:], start=True, stop=False))
                        for c in range(DCH):
                            em.emit(T, lambda e, sg=sg, hh=hh, c=c, sl=sl, pa=pa:
                                    e.matmul(pa, w1q_sb[:, sg, c, hh, :],
                                             xn_sb[:, c, sl],
                                             start=False, stop=(c == DCH - 1)),
                                    inc="hmm" if c == DCH - 1 else None)
                em.wait(SC, "hmm")
                em.wait(SC, "ymm", em.val("ymm") - 1)
                em.emit(SC, lambda e, hs=hs: e.activation(
                    h_sb[:, hs, 0:4].rearrange("p h m -> p (h m)"), psB[:],
                    AF.Gelu))
                em.emit(SC, lambda e, hs=hs: e.activation(
                    h_sb[:, hs, 4:8].rearrange("p h m -> p (h m)"), psC[:],
                    AF.Gelu), inc="hp")
                em.wait(T, "hp")
                for half in range(2):
                    em.wait(T, "xev", em.val("xev") - 1)
                    for c in range(half * 4, half * 4 + 4):
                        for hh in range(QH):
                            em.emit(T, lambda e, sg=sg, c=c, hh=hh, hs=hs:
                                    e.matmul(
                                        psA[:, c * 256:(c + 1) * 256],
                                        w2q_sb[:, sg, hh, c, :],
                                        h_sb[:, hs, hh, :],
                                        start=(hh == 0), stop=(hh == QH - 1)),
                                    inc="ymm" if (c % 4 == 3 and hh == QH - 1)
                                    else None)
                    em.wait(V, "ymm")
                    em.emit(V, lambda e, sl=sl, half=half: e.tensor_tensor(
                        x_sb[:, half * 4:half * 4 + 4, sl],
                        x_sb[:, half * 4:half * 4 + 4, sl],
                        psA[:, half * 1024:(half + 1) * 1024]
                        .rearrange("p (c m) -> p c m", c=4), op=ALU.add),
                        inc="xev")
        em.wait(V, "xev")
        em.wait(V, "casts", 1)
        for t in range(TCH):
            sl = slice(t * BLK, (t + 1) * BLK)
            for c in range(DCH):
                em.emit(V, lambda e, c=c, sl=sl: e.tensor_scalar_add(
                    x_sb[:, c, sl], x_sb[:, c, sl], b2h_sb[:, c:c + 1]),
                    inc="xnr")
        em.emit(V, lambda e: e.drain())


    def emit_attn0(em):
        xnr_base0 = None
        V, SC, T, G, SY = "vector", "scalar", "tensor", "gpsimd", "sync"
        xnr_base = em.val("xnr")
        for t in range(TCH):
            emit_ln_chunk(em, t, xnr_base)
        XNR_ALL = em.val("xnr")

        em.wait(SY, "xnr", XNR_ALL)
        em.wait(SY, "yv", em.val("yv"))
        for sem in ("ymm", "hmm", "hp", "xev"):
            em.wait(SY, sem, em.val(sem))
        em.dma(SY, scrA_sb[:, 0:512],
               maskd_in.rearrange("p a b -> p (a b)"), inc="wl")
        em.dma(SY, scrA_sb[:, 512:640], eye_in, inc="wl")
        em.wait(V, "wl")
        em.wait(V, "stg", em.val("stg"))
        em.wait(SC, "stg", em.val("stg"))
        em.wait(V, "stg2", em.val("stg2"))
        em.wait(SC, "stg2", em.val("stg2"))
        em.emit(V, lambda e: e.tensor_copy(
            maskd_sb[:].rearrange("p a b -> p (a b)"), scrA_sb[:, 0:512]),
            inc="castv")
        em.emit(V, lambda e: e.tensor_copy(eye_sb[:], scrA_sb[:, 512:640]),
                inc="castv")
        em.wait(SY, "castv", em.val("castv"))
        em.dma(SY, scrA_sb[:, 0:512],
               mv0_in.rearrange("p a b -> p (a b)"), inc="wl")
        em.dma(SY, scrA_sb[:, 512:1024],
               mv1_in.rearrange("p a b -> p (a b)"), inc="wl")
        em.wait(V, "wl")
        em.emit(V, lambda e: e.tensor_copy(
            mv0_sb[:].rearrange("p a b -> p (a b)"), scrA_sb[:, 0:512]),
            inc="castv")
        em.emit(V, lambda e: e.tensor_copy(
            mv1_sb[:].rearrange("p a b -> p (a b)"), scrA_sb[:, 512:1024]),
            inc="castv")
        em.emit(V, lambda e: e.memset(const_sb[:, 2:3], -M0), inc="castv")
        em.emit(V, lambda e: e.memset(const_sb[:, 3:4], 1.0 / 32.0), inc="castv")

        em.wait(T, "xnr", XNR_ALL)
        em.wait(T, "rowe")
        for m in range(3):
            w_in = [wq_in, wk_in, wv_in][m]
            ws = m % 2
            em.wait(SY, "qmm", em.val("qmm") - 32)
            for cin in range(DCH):
                em.wait(SY, "castv", em.val("castv"))
                em.dma(SY, scrA_sb[:], w_in[cin * 128:(cin + 1) * 128, :],
                       inc="wl")
                em.wait(V, "wl")
                em.emit(V, lambda e, ws=ws, cin=cin: e.tensor_copy(
                    wqkv_sb[:, ws, cin].rearrange("p a b -> p (a b)"),
                    scrA_sb[:]), inc="castv")
            em.wait(T, "castv")
            if m < 2:
                for tc in range(4):
                    ts = slice(tc * 512, (tc + 1) * 512)
                    if m == 1:
                        em.wait(V, "stg", em.val("stg"))
                    for c2 in range(DCH):
                        kq = em.val("qmm")
                        pa = psA[:, (kq % 4) * 512:(kq % 4 + 1) * 512]
                        em.wait(T, "qev", em.val("qev") - 3)
                        for cin in range(DCH):
                            em.emit(T, lambda e, ws=ws, c2=c2, cin=cin, ts=ts,
                                    pa=pa: e.matmul(
                                        pa, wqkv_sb[:, ws, cin, c2, :],
                                        xn_sb[:, cin, ts],
                                        start=(cin == 0), stop=(cin == DCH - 1)),
                                    inc="qmm" if cin == DCH - 1 else None)
                        em.wait(V, "qmm")
                        if m == 0:
                            em.emit(V, lambda e, c2=c2, ts=ts, pa=pa:
                                    e.tensor_copy(q_sb[:, c2, ts], pa),
                                    inc="qev")
                        else:
                            for jj in range(2):
                                em.emit(V, lambda e, c2=c2, jj=jj, pa=pa:
                                        e.tensor_copy(
                                            kring_sb[:, jj, c2, :],
                                            pa[:, jj * 256:(jj + 1) * 256]),
                                        inc="qev" if jj == 1 else None)
                    if m == 1:
                        em.wait(G, "qev")
                        for jj in range(2):
                            em.dma(G, kvin[2 * tc + jj][:, 0:2048],
                                   kring_sb[:, jj].rearrange("p a b -> p (a b)"),
                                   inc="stg")
            else:
                for sv in range(TCH):
                    vslot = sv % 2
                    em.wait(V, "stg", em.val("stg"))
                    for u in range(2):
                        for dvh in range(2):
                            pb = psB if dvh == 0 else psC
                            tok0 = sv * 256 + u * 128
                            em.wait(T, "qev", em.val("qev"))
                            for cin in range(DCH):
                                em.emit(T, lambda e, ws=ws, cin=cin, tok0=tok0,
                                        pb=pb, dvh=dvh: e.matmul(
                                            pb[:, 0:512],
                                            xn_sb[:, cin, tok0:tok0 + 128],
                                            wqkv_sb[:, ws, cin,
                                                    dvh * 4:(dvh + 1) * 4, :]
                                            .rearrange("p a b -> p (a b)"),
                                            start=(cin == 0),
                                            stop=(cin == DCH - 1)),
                                        inc="qmm" if cin == DCH - 1 else None)
                        em.wait(V, "qmm")
                        em.emit(V, lambda e, vslot=vslot, u=u: e.tensor_copy(
                            vring_sb[:, vslot, u, 0:512], psB[:, 0:512]))
                        em.emit(V, lambda e, vslot=vslot, u=u: e.tensor_copy(
                            vring_sb[:, vslot, u, 512:1024], psC[:, 0:512]),
                            inc="qev")
                    em.wait(G, "qev")
                    em.dma(G, kvin[sv][:, 2048:4096],
                           vring_sb[:, vslot].rearrange("p a b -> p (a b)"),
                           inc="stg")
        em.emit(V, lambda e: e.memset(pp_sb[:, 0, 0:32], 0.0), inc="fin")
        em.wait(G, "fin")
        for sv in range(MYB):
            em.dma(G, kvin[sv][:, 4096:KVW], pp_sb[:, 0, 0:32], inc="stg")
        em.wait(G, "stg")
        em.emit(G, lambda e: e.collective_compute(
            "AllGather", ALU.bypass,
            replica_groups=[[0, 1], [2, 3], [4, 5], [6, 7]],
            ins=[kvin.ap().opt()],
            outs=[kvg.ap().opt()]), inc="cc", inck=1)

        em.wait(SY, "cc", 1)
        em.wait(T, "qev", em.val("qev"))
        em.wait(T, "cc", 1)
        vno = 0
        for p in range(2):
            for cin in range(4):
                em.wait(SY, "casts", em.val("casts"))
                if p == 1:
                    em.wait(SY, "ymt", em.val("ymt"))
                em.dma(SY, scrA_sb[:],
                       wo_in[(4 * p + cin) * 128:(4 * p + cin + 1) * 128, :],
                       inc="wl")
                em.wait(SC, "wl")
                em.wait(SC, "stm", em.val("stm"))
                em.emit(SC, lambda e, cin=cin: e.copy(
                    wo_sb[:, cin].rearrange("p a b -> p (a b)"), scrA_sb[:]),
                    inc="casts")
            em.wait(T, "casts")
            for band in range(8):
                qs = slice(band * BLK, (band + 1) * BLK)
                vs = visits_of(band)
                for vi, (reg, j) in enumerate(vs):
                    r = vno % RING
                    sb_ = vno % 2
                    kvs = "kv0" if r == 0 else "kv1"
                    em.wait(SY, "avm", em.val("avm") - (RING - 1))
                    em.wait(SY, "pxp", em.val("pxp") - (RING - 1))
                    em.wait(SY, kvs, em.val(kvs))
                    slot = {"own": kvin[j], "r0": kvg[0, j],
                            "r1": kvg[1, j]}[reg]
                    em.dma(SY, kring_sb[:, r].rearrange("p a b -> p (a b)"),
                           slot[:, 0:2048], inc=kvs)
                    em.dma(SY, vring_sb[:, r].rearrange("p a b -> p (a b)"),
                           slot[:, 2048:4096], inc=kvs)
                    em.wait(T, kvs)
                    em.wait(T, "pxp", em.val("pxp") - 1)
                    mask = None
                    if reg == "own":
                        mask = maskd_sb
                    elif reg == "r0" and j == band:
                        mask = mv0_sb
                    elif reg == "r1" and j == band:
                        mask = mv1_sb
                    for ksub in range(2):
                        pt = psB[:, sb_ * 512 + ksub * 256:
                                 sb_ * 512 + (ksub + 1) * 256]
                        if mask is not None:
                            em.emit(T, lambda e, mask=mask, ksub=ksub, pt=pt:
                                    e.matmul(pt, eye_sb[:], mask[:, ksub, :],
                                             start=True, stop=False))
                        for cin in range(DCH):
                            first = (mask is None and cin == 0)
                            em.emit(T, lambda e, r=r, cin=cin, ksub=ksub, qs=qs,
                                    pt=pt, first=first:
                                    e.matmul(pt,
                                             kring_sb[:, r, cin,
                                                      ksub * 128:(ksub + 1) * 128],
                                             q_sb[:, cin, qs],
                                             start=first, stop=(cin == DCH - 1)),
                                    inc="smm" if (ksub == 1 and cin == DCH - 1)
                                    else None)
                    em.wait(SC, "smm")
                    em.wait(SC, "avm", em.val("avm") - 1)
                    em.emit(SC, lambda e, sb_=sb_: e.activation(
                        pp_sb[:, sb_, :], psB[:, sb_ * 512:(sb_ + 1) * 512],
                        AF.Exp, bias=const_sb[:, 2:3], scale=const_sb[:, 3:4]),
                        inc="pxp")
                    em.wait(T, "pxp")
                    if vi == 0:
                        em.wait(T, "fin", em.val("fin"))
                    for ksub in range(2):
                        for cdv in range(4):
                            h0 = 8 * p + 2 * cdv
                            em.emit(T, lambda e, r=r, ksub=ksub, cdv=cdv, h0=h0,
                                    sb_=sb_: e.matmul(
                                        psA[:, cdv * 512:cdv * 512 + 256],
                                        vring_sb[:, r, ksub,
                                                 (h0 // 2) * 128:(h0 // 2) * 128 + 128],
                                        pp_sb[:, sb_, ksub * 256:(ksub + 1) * 256],
                                        start=(vi == 0 and ksub == 0),
                                        stop=(vi == len(vs) - 1 and ksub == 1)))
                        em.emit(T, lambda e, r=r, ksub=ksub, sb_=sb_: e.matmul(
                            psC[0:1, 0:256], onescol_sb[:, 0:1],
                            pp_sb[:, sb_, ksub * 256:(ksub + 1) * 256],
                            start=(vi == 0 and ksub == 0),
                            stop=(vi == len(vs) - 1 and ksub == 1)),
                            inc="avm" if ksub == 1 else None)
                    vno += 1
                em.wait(V, "avm")
                em.emit(V, lambda e: e.tensor_copy(lrow_sb[0:1, :],
                                                   psC[0:1, 0:256]))
                em.emit(V, lambda e: e.drain())
                em.emit(V, lambda e: e.reciprocal(rl_sb[0:1, :], lrow_sb[0:1, :]),
                        inc="fin")
                em.wait(T, "fin")
                em.wait(T, "yv", em.val("yv"))
                em.emit(T, lambda e: e.matmul(psC[:, 768:1024], onesrow_sb[:],
                                              rl_sb[0:1, :], start=True, stop=True),
                        inc="ymt")
                em.wait(V, "ymt")
                em.emit(V, lambda e: e.tensor_copy(rlb_sb[:], psC[:, 768:1024]))
                em.emit(V, lambda e: e.drain())
                for cdv in range(4):
                    em.emit(V, lambda e, cdv=cdv: e.tensor_tensor(
                        onorm_sb[:, cdv, :], psA[:, cdv * 512:cdv * 512 + 256],
                        rlb_sb[:], op=ALU.mult),
                        inc="fin" if cdv == 3 else None)
                em.emit(V, lambda e: e.drain())
                em.wait(T, "fin")
                for c2 in range(DCH):
                    ys = psC[:, 512:768]
                    em.wait(T, "yv", em.val("yv"))
                    for cdv in range(4):
                        em.emit(T, lambda e, c2=c2, cdv=cdv, ys=ys: e.matmul(
                            ys, wo_sb[:, cdv, c2, :], onorm_sb[:, cdv, :],
                            start=(cdv == 0), stop=(cdv == 3)),
                            inc="ymt" if cdv == 3 else None)
                    em.wait(V, "ymt")
                    em.emit(V, lambda e, c2=c2, qs=qs, ys=ys: e.tensor_tensor(
                        x_sb[:, c2, qs], x_sb[:, c2, qs], ys, op=ALU.add),
                        inc="yv")
                em.emit(V, lambda e: e.drain())


    def emit_ffn1(em, first):
        V, SC, T, G, SY = "vector", "scalar", "tensor", "gpsimd", "sync"
        xnr_base = em.val("xnr")
        xnr_at = []
        for t in range(TCH):
            emit_ln_chunk(em, t, xnr_base)
            xnr_at.append(em.val("xnr"))
        XNR_ALL = em.val("xnr")
        for sem in ("smm", "avm", "ymt", "pxp", "qmm", "yv", "ymm", "hmm",
                    "hp", "xev", "fin", "qev"):
            em.wait(SY, sem, em.val(sem))
        em.wait(V, "stg", em.val("stg"))
        em.wait(SC, "stg", em.val("stg"))
        em.wait(V, "stg2", em.val("stg2"))
        em.wait(SC, "stg2", em.val("stg2"))
        if first:
            for cin in range(DCH):
                for sh in range(2):
                    wsem = "wl" if sh == 0 else "wl2"
                    em.wait(SY, "casts", em.val("casts") -
                            (0 if (cin == 0 and sh == 0) else 1))
                    em.dma(SY, scr_sb[:, sh * 512:(sh + 1) * 512],
                           wt_in[cin * 128:(cin + 1) * 128,
                                 sh * 512:(sh + 1) * 512], inc=wsem)
                    em.wait(SC, wsem)
                    em.emit(SC, lambda e, cin=cin, sh=sh: e.mul(
                        wt_sb[:, cin].rearrange("p a b -> p (a b)")
                        [:, sh * 512:(sh + 1) * 512],
                        scr_sb[:, sh * 512:(sh + 1) * 512], 0.5), inc="casts")
                em.wait(G, "casts")
                em.dma(G, wtc_d[cin],
                       wt_sb[:, cin].rearrange("p a b -> p (a b)"), inc="stg")
        else:
            em.wait(SY, "stg", em.val("stg"))
            em.wait(SY, "wl", em.val("wl"))
            em.dma(SY, wt_sb[:].rearrange("p c a b -> p c (a b)"),
                   wtc_d.rearrange("c p m -> p c m"), inc="wl")
            em.wait(T, "wl", em.val("wl"))
            em.wait(SC, "wl", em.val("wl"))
        em.wait(SY, "casts", em.val("casts"))
        em.wait(SY, "wl", em.val("wl"))
        em.dma(SY, scr_sb[:].rearrange("p (c m) -> p c m", c=DCH),
               ut_in.rearrange("(c p) m -> p c m", p=128), inc="wl")
        em.dma(SY, g16stg_sb[:], g16_in, inc="wl")
        em.dma(SY, eyef_sb[:], eye_in, inc="wl")
        em.wait(V, "wl")
        em.emit(V, lambda e: e.tensor_copy(
            ut_sb[:].rearrange("p c m -> p (c m)"), scr_sb[:]), inc="castv")
        em.emit(V, lambda e: e.tensor_copy(g16_sb[:], g16stg_sb[:]), inc="castv")
        em.wait(SY, "castv", em.val("castv"))
        em.dma(SY, scr_sb[:], vs_in, inc="wl")
        em.wait(SC, "wl")
        em.emit(SC, lambda e: e.mul(
            vs_sb[:].rearrange("p c m -> p (c m)"), scr_sb[:], 1.0 / 32.0),
            inc="casts")
        em.wait(SY, "casts", em.val("casts"))
        em.dma(SY, scr_sb[:, 0:64].rearrange("p (c m) -> p c m", c=DCH),
               gw_in.rearrange("(c p) m -> p c m", p=128), inc="wl")
        em.wait(V, "wl")
        em.emit(V, lambda e: e.tensor_copy(
            gw_sb[:].rearrange("p c m -> p (c m)"), scr_sb[:, 0:64]),
            inc="castv")
        em.emit(V, lambda e: e.memset(const_sb[:, 4:5], -8.0), inc="castv")
        em.wait(T, "wl", em.val("wl"))
        em.wait(T, "castv")
        em.wait(T, "casts")
        em.wait(T, "rowe")

        for u in range(MYT // 128):
            tok0 = u * 128
            ub = u % 2
            em.wait(T, "xnr", xnr_at[u // 2])
            em.wait(T, "pxp", em.val("pxp"))
            for cin in range(DCH):
                em.emit(T, lambda e, cin=cin, tok0=tok0: e.matmul(
                    psC[:, 0:8], xn_sb[:, cin, tok0:tok0 + 128],
                    gw_sb[:, cin, :], start=(cin == 0), stop=(cin == DCH - 1)),
                    inc="qmm" if cin == DCH - 1 else None)
            em.wait(SC, "qmm")
            em.wait(SC, "fin", em.val("fin"))
            em.emit(SC, lambda e, ub=ub: e.activation(
                gte_sb[:, ub, :], psC[:, 0:8], AF.Exp,
                bias=const_sb[:, 4:5]), inc="pxp")
            em.wait(V, "pxp")
            em.emit(V, lambda e, ub=ub: e.tensor_reduce(
                gtr_sb[:, ub:ub + 1], gte_sb[:, ub, :],
                axis=mybir.AxisListType.X, op=ALU.add))
            em.emit(V, lambda e: e.drain())
            em.emit(V, lambda e, ub=ub: e.reciprocal(
                gtr_sb[:, 2 + ub:3 + ub], gtr_sb[:, ub:ub + 1]))
            em.emit(V, lambda e: e.drain())
            em.emit(V, lambda e, ub=ub: e.tensor_scalar(
                gtn_sb[:, ub, :], gte_sb[:, ub, :], gtr_sb[:, 2 + ub:3 + ub],
                None, op0=ALU.mult), inc="fin")
            em.wait(T, "fin")
            em.wait(T, "qev", em.val("qev"))
            em.emit(T, lambda e, ub=ub: e.transpose(
                psC[0:8, 512:640], gtn_sb[:, ub, :], eyef_sb[:]), inc="ymt")
            em.wait(V, "ymt")
            em.emit(V, lambda e, tok0=tok0: e.tensor_copy(
                gates_sb[0:8, tok0:tok0 + 128], psC[0:8, 512:640]), inc="qev")

        for j in range(4):
            js = slice(j * 512, (j + 1) * 512)
            jb = j % 2
            em.wait(T, "qev", em.val("qev"))
            em.emit(T, lambda e, js=js: e.matmul(
                psB[:, 0:512], g16_sb[:], gates_sb[0:8, js],
                start=True, stop=True), inc="smm")
            for cin in range(DCH):
                em.emit(T, lambda e, cin=cin, js=js: e.matmul(
                    psB[:, 512:1024], ut_sb[:, cin, :], xn_sb[:, cin, js],
                    start=(cin == 0), stop=(cin == DCH - 1)),
                    inc="hmm" if cin == DCH - 1 else None)
            em.wait(V, "smm")
            em.emit(V, lambda e, jb=jb: e.tensor_copy(gb_sb[:, jb, :],
                                                      psB[:, 0:512]))
            em.emit(V, lambda e: e.drain())
            em.wait(V, "hmm")
            em.wait(V, "ymm", em.val("ymm") - 4)
            em.emit(V, lambda e, jb=jb: e.tensor_tensor(
                tg_sb[:, jb, :], psB[:, 512:1024], gb_sb[:, jb, :],
                op=ALU.mult), inc="fin")
            for u2 in range(2):
                t = j * 2 + u2
                sl = slice(t * BLK, (t + 1) * BLK)
                em.wait(T, "fin")
                for half in range(2):
                    em.wait(T, "xev", em.val("xev") - 1)
                    for c2 in range(half * 4, half * 4 + 4):
                        em.emit(T, lambda e, c2=c2, sl=sl: e.matmul(
                            psA[:, c2 * 256:(c2 + 1) * 256],
                            wt_sb[:, 0, c2, :], xn_sb[:, 0, sl],
                            start=True, stop=False))
                        for cin in range(1, DCH):
                            em.emit(T, lambda e, c2=c2, cin=cin, sl=sl: e.matmul(
                                psA[:, c2 * 256:(c2 + 1) * 256],
                                wt_sb[:, cin, c2, :], xn_sb[:, cin, sl],
                                start=False, stop=False))
                        em.emit(T, lambda e, c2=c2, jb=jb, u2=u2: e.matmul(
                            psA[:, c2 * 256:(c2 + 1) * 256],
                            vs_sb[:, c2, :],
                            tg_sb[:, jb, u2 * 256:(u2 + 1) * 256],
                            start=False, stop=True),
                            inc="ymm" if c2 % 4 == 3 else None)
                    em.wait(V, "ymm")
                    em.emit(V, lambda e, sl=sl, half=half: e.tensor_tensor(
                        x_sb[:, half * 4:half * 4 + 4, sl],
                        x_sb[:, half * 4:half * 4 + 4, sl],
                        psA[:, half * 1024:(half + 1) * 1024]
                        .rearrange("p (c m) -> p c m", c=4), op=ALU.add),
                        inc="xev")
        em.emit(V, lambda e: e.drain())


    def emit_attn1(em):
        V, SC, T, G, SY = "vector", "scalar", "tensor", "gpsimd", "sync"
        xnr_base = em.val("xnr")
        xnr_at = []
        for t in range(TCH):
            emit_ln_chunk(em, t, xnr_base)
            xnr_at.append(em.val("xnr"))
        XNR_ALL = em.val("xnr")

        em.wait(SY, "xnr", XNR_ALL)
        em.wait(SY, "yv", em.val("yv"))
        for sem in ("ymm", "hmm", "hp", "xev", "smm", "avm", "ymt", "pxp",
                    "qmm", "fin", "qev"):
            em.wait(SY, sem, em.val(sem))
        em.wait(V, "stg", em.val("stg"))
        em.wait(SC, "stg", em.val("stg"))
        em.wait(V, "stg2", em.val("stg2"))
        em.wait(SC, "stg2", em.val("stg2"))
        em.dma(SY, scrA_sb[:, 0:512],
               maskd_in.rearrange("p a b -> p (a b)"), inc="wl")
        em.dma(SY, scrA_sb[:, 512:640], eye_in, inc="wl")
        em.wait(V, "wl")
        em.emit(V, lambda e: e.tensor_copy(
            maskd_sb[:].rearrange("p a b -> p (a b)"), scrA_sb[:, 0:512]),
            inc="castv")
        em.emit(V, lambda e: e.tensor_copy(eye_sb[:], scrA_sb[:, 512:640]),
                inc="castv")
        em.wait(SY, "castv", em.val("castv"))
        em.dma(SY, scrA_sb[:, 0:512],
               mv0_in.rearrange("p a b -> p (a b)"), inc="wl")
        em.dma(SY, scrA_sb[:, 512:1024],
               mv1_in.rearrange("p a b -> p (a b)"), inc="wl")
        em.wait(V, "wl")
        em.emit(V, lambda e: e.tensor_copy(
            mv0_sb[:].rearrange("p a b -> p (a b)"), scrA_sb[:, 0:512]),
            inc="castv")
        em.emit(V, lambda e: e.tensor_copy(
            mv1_sb[:].rearrange("p a b -> p (a b)"), scrA_sb[:, 512:1024]),
            inc="castv")
        for piece in range(2):
            em.wait(SY, "castv", em.val("castv"))
            em.dma(SY, scrA_sb[:], cosi_in[:, piece * 1024:(piece + 1) * 1024],
                   inc="wl")
            em.wait(V, "wl")
            em.emit(V, lambda e, piece=piece: e.tensor_copy(
                cosi_sb[:, piece * 1024:(piece + 1) * 1024], scrA_sb[:]),
                inc="castv")
            em.wait(SY, "castv", em.val("castv"))
            em.dma(SY, scrA_sb[:], sini_in[:, piece * 1024:(piece + 1) * 1024],
                   inc="wl")
            em.wait(V, "wl")
            em.emit(V, lambda e, piece=piece: e.tensor_copy(
                sini_sb[:, piece * 1024:(piece + 1) * 1024], scrA_sb[:]),
                inc="castv")
        em.wait(SY, "castv", em.val("castv"))
        em.dma(SY, scrA_sb[:, 0:128], prot_in, inc="wl")
        em.wait(V, "wl")
        em.emit(V, lambda e: e.tensor_copy(prot_sb[:], scrA_sb[:, 0:128]),
                inc="castv")
        em.emit(V, lambda e: e.memset(const_sb[:, 5:6], -M1), inc="castv")
        em.emit(V, lambda e: e.memset(const_sb[:, 6:7], 0.125), inc="castv")
        em.emit(V, lambda e: e.memset(vstg1_sb[:, :, :, 64:65], 1.0), inc="castv")

        em.wait(T, "castv")
        em.wait(T, "rowe")
        for m in range(3):
            ws = m % 2
            em.wait(SY, "qmm", em.val("qmm") - 32)
            for cin in range(DCH):
                em.wait(SY, "castv", em.val("castv"))
                em.dma(SY, scrA_sb[:],
                       wqkvt_in[cin * 128:(cin + 1) * 128,
                                m * D:(m + 1) * D], inc="wl")
                em.wait(V, "wl")
                em.emit(V, lambda e, ws=ws, cin=cin: e.tensor_copy(
                    wqkv_sb[:, ws, cin].rearrange("p a b -> p (a b)"),
                    scrA_sb[:]), inc="castv")
            em.wait(T, "castv")
            if m < 2:
                for tc in range(4):
                    ts = slice(tc * 512, (tc + 1) * 512)
                    em.wait(T, "xnr", xnr_at[2 * tc + 1])
                    if m == 1:
                        em.wait(V, "stg", em.val("stg"))
                    for c2 in range(DCH):
                        kq = em.val("qmm")
                        pa = psA[:, (kq % 2) * 512:(kq % 2) * 512 + 512]
                        em.wait(T, "qev", em.val("qev") - 3)
                        for cin in range(DCH):
                            em.emit(T, lambda e, ws=ws, c2=c2, cin=cin, ts=ts,
                                    pa=pa: e.matmul(
                                        pa, wqkv_sb[:, ws, cin, c2, :],
                                        xn_sb[:, cin, ts],
                                        start=(cin == 0), stop=(cin == DCH - 1)),
                                    inc="qmm" if cin == DCH - 1 else None)
                        dst = (lambda e, c2=c2, ts=ts: q_sb[:, c2, ts]) if m == 0                             else None
                        em.wait(V, "qmm")
                        em.emit(V, lambda e, pa=pa: e.tensor_copy(
                            ropet_sb[:], pa), inc="qev")
                        em.wait(T, "qev")
                        em.wait(T, "fin", em.val("fin") - 1)
                        em.emit(T, lambda e, kq=kq: e.matmul(
                            psB[:, (kq % 2) * 512:(kq % 2) * 512 + 512],
                            prot_sb[:], ropet_sb[:], start=True, stop=True),
                            inc="smm")
                        em.wait(V, "smm")
                        em.emit(V, lambda e, ts=ts: e.tensor_tensor(
                            ropet_sb[:], ropet_sb[:], cosi_sb[:, ts],
                            op=ALU.mult))
                        em.emit(V, lambda e: e.drain())
                        if m == 0:
                            tgt = lambda e, c2=c2, ts=ts: q_sb[:, c2, ts]
                        else:
                            tgt = lambda e, c2=c2: kring_sb[:, 0, c2, :]                                 .rearrange("p b -> p b") if False else None
                        if m == 0:
                            em.emit(V, lambda e, c2=c2, ts=ts, kq=kq:
                                    e.scalar_tensor_tensor(
                                        q_sb[:, c2, ts],
                                        psB[:, (kq % 2) * 512:(kq % 2) * 512 + 512],
                                        1.0, sini_sb[:, ts],
                                        op0=ALU.mult, op1=ALU.mult))
                            em.emit(V, lambda e: e.drain())
                            em.emit(V, lambda e, c2=c2, ts=ts: e.tensor_tensor(
                                q_sb[:, c2, ts], q_sb[:, c2, ts], ropet_sb[:],
                                op=ALU.add), inc="fin")
                        else:
                            for jj in range(2):
                                em.emit(V, lambda e, c2=c2, ts=ts, kq=kq, jj=jj:
                                        e.scalar_tensor_tensor(
                                            kring_sb[:, jj, c2, :],
                                            psB[:, (kq % 2) * 512 + jj * 256:
                                                (kq % 2) * 512 + (jj + 1) * 256],
                                            1.0,
                                            sini_sb[:, ts.start + jj * 256:
                                                    ts.start + (jj + 1) * 256],
                                            op0=ALU.mult, op1=ALU.mult))
                            em.emit(V, lambda e: e.drain())
                            for jj in range(2):
                                em.emit(V, lambda e, c2=c2, jj=jj: e.tensor_tensor(
                                    kring_sb[:, jj, c2, :], kring_sb[:, jj, c2, :],
                                    ropet_sb[:, jj * 256:(jj + 1) * 256],
                                    op=ALU.add), inc="fin" if jj == 1 else None)
                        em.emit(V, lambda e: e.drain())
                    if m == 1:
                        em.wait(G, "fin", em.val("fin"))
                        for jj in range(2):
                            em.dma(G, kvin[2 * tc + jj][:, 0:2048],
                                   kring_sb[:, jj].rearrange("p a b -> p (a b)"),
                                   inc="stg")
            else:
                for sv in range(TCH):
                    em.wait(T, "xnr", xnr_at[sv])
                    em.wait(V, "stg", em.val("stg"))
                    for u in range(2):
                        for dvh in range(2):
                            pb = psB if dvh == 0 else psC
                            tok0 = sv * 256 + u * 128
                            em.wait(T, "qev", em.val("qev"))
                            for cin in range(DCH):
                                em.emit(T, lambda e, ws=ws, cin=cin, tok0=tok0,
                                        pb=pb, dvh=dvh: e.matmul(
                                            pb[:, 0:512],
                                            xn_sb[:, cin, tok0:tok0 + 128],
                                            wqkv_sb[:, ws, cin,
                                                    dvh * 4:(dvh + 1) * 4, :]
                                            .rearrange("p a b -> p (a b)"),
                                            start=(cin == 0),
                                            stop=(cin == DCH - 1)),
                                        inc="qmm" if cin == DCH - 1 else None)
                        em.wait(V, "qmm")
                        em.emit(V, lambda e, u=u: e.tensor_copy(
                            vstg1_sb[:, u, 0:8, 0:64],
                            psB[:, 0:512].rearrange("p (a b) -> p a b", a=8)))
                        em.emit(V, lambda e, u=u: e.tensor_copy(
                            vstg1_sb[:, u, 8:16, 0:64],
                            psC[:, 0:512].rearrange("p (a b) -> p a b", a=8)),
                            inc="qev")
                    em.wait(G, "qev")
                    em.dma(G, kvin[sv][:, 2048:KVW],
                           vstg1_sb[:].rearrange("p a b c -> p (a b c)"),
                           inc="stg")
        em.wait(G, "stg")
        em.emit(G, lambda e: e.collective_compute(
            "AllGather", ALU.bypass,
            replica_groups=[[0, 1], [2, 3], [4, 5], [6, 7]],
            ins=[kvin.ap().opt()],
            outs=[kvg.ap().opt()]), inc="cc", inck=1)

        em.wait(SY, "cc", em.val("cc"))
        em.wait(T, "qev", em.val("qev"))
        em.wait(T, "cc", em.val("cc"))
        vno = 0
        for pr in range(8):
            for band in range(8):
                qs = slice(band * BLK, (band + 1) * BLK)
                vs = visits_of(band)
                for vi, (reg, j) in enumerate(vs):
                    r = vno % 2
                    kvs = "kv0" if r == 0 else "kv1"
                    em.wait(SY, "avm", em.val("avm") - 1)
                    em.wait(SY, "pxp", em.val("pxp") - 1)
                    em.wait(SY, kvs, em.val(kvs))
                    slot = {"own": kvin[j], "r0": kvg[0, j],
                            "r1": kvg[1, j]}[reg]
                    em.dma(SY, kslr_sb[:, r, :],
                           slot[:, pr * 256:(pr + 1) * 256], inc=kvs)
                    vv = slot[:, 2048:KVW].rearrange("p (a b) -> p a b", a=2)
                    em.dma(SY, vslr_sb[:, r],
                           vv[:, :, pr * 130:(pr + 1) * 130], inc=kvs)
                    em.wait(T, kvs)
                    em.wait(T, "pxp", em.val("pxp") - 1)
                    mask = None
                    if reg == "own":
                        mask = maskd_sb
                    elif reg == "r0" and j == band:
                        mask = mv0_sb
                    elif reg == "r1" and j == band:
                        mask = mv1_sb
                    sbk = psB if (vno % 2 == 0) else psC
                    for hh in range(2):
                        for ksub in range(2):
                            pt = sbk[:, hh * 512 + ksub * 256:
                                     hh * 512 + (ksub + 1) * 256]
                            if mask is not None:
                                em.emit(T, lambda e, mask=mask, ksub=ksub, pt=pt:
                                        e.matmul(pt, eye_sb[:], mask[:, ksub, :],
                                                 start=True, stop=False))
                            em.emit(T, lambda e, r=r, hh=hh, ksub=ksub, qs=qs,
                                    pr=pr, pt=pt, first=(mask is None):
                                    e.matmul(
                                        pt,
                                        kslr_sb[0 + 64 * hh:64 + 64 * hh, r,
                                                ksub * 128:(ksub + 1) * 128],
                                        q_sb[0 + 64 * hh:64 + 64 * hh, pr, qs],
                                        start=first, stop=True,
                                        tile_position=(64 * hh, 0)),
                                    inc="smm" if (hh == 1 and ksub == 1)
                                    else None)
                    em.wait(SC, "smm")
                    em.wait(SC, "avm", em.val("avm") - 1)
                    em.emit(SC, lambda e, r=r, sbk=sbk: e.activation(
                        pp1_sb[:, r, :], sbk[:],
                        AF.Exp, bias=const_sb[:, 5:6], scale=const_sb[:, 6:7]),
                        inc="pxp")
                    em.wait(T, "pxp")
                    if vi == 0:
                        em.wait(T, "fin", em.val("fin"))
                    for hh in range(2):
                        ob = psA[:, hh * 512: hh * 512 + 256]
                        for ksub in range(2):
                            em.emit(T, lambda e, r=r, hh=hh, ksub=ksub, ob=ob:
                                    e.matmul(
                                        ob[0:65, :],
                                        vslr_sb[:, r, ksub,
                                                hh * 65:(hh + 1) * 65],
                                        pp1_sb[:, r, hh * 512 + ksub * 256:
                                               hh * 512 + (ksub + 1) * 256],
                                        start=(vi == 0 and ksub == 0),
                                        stop=(vi == len(vs) - 1 and ksub == 1)),
                                    inc="avm" if (hh == 1 and ksub == 1)
                                    else None)
                    vno += 1
                em.wait(V, "avm")
                em.wait(V, "stg", em.val("stg"))
                em.emit(V, lambda e: e.tensor_copy(onorm_sb[64:65, 2, :],
                                                   psA[64:65, 0:256]))
                em.emit(V, lambda e: e.tensor_copy(onorm_sb[64:65, 3, :],
                                                   psA[64:65, 512:768]),
                        inc="fin")
                em.wait(G, "fin")
                em.dma(G, scrA_sb[0:1, 0:512],
                       onorm_sb[64:65, 2:4, :].rearrange("p a b -> p (a b)"),
                       inc="stg")
                em.wait(V, "stg", em.val("stg"))
                em.emit(V, lambda e: e.reciprocal(scrA_sb[0:1, 512:768],
                                                  scrA_sb[0:1, 0:256]))
                em.emit(V, lambda e: e.reciprocal(scrA_sb[0:1, 768:1024],
                                                  scrA_sb[0:1, 256:512]),
                        inc="fin")
                em.wait(T, "fin")
                em.emit(T, lambda e: e.matmul(
                    psA[0:64, 1024:1280],
                    onesrow_sb[0:1, 0:64], scrA_sb[0:1, 512:768],
                    start=True, stop=True))
                em.emit(T, lambda e: e.matmul(
                    psA[0:64, 1536:1792],
                    onesrow_sb[0:1, 0:64], scrA_sb[0:1, 768:1024],
                    start=True, stop=True), inc="ymt")
                em.wait(V, "ymt")
                em.emit(V, lambda e: e.tensor_copy(scrA_sb[0:64, 0:256],
                                                   psA[0:64, 1024:1280]))
                em.emit(V, lambda e: e.tensor_copy(scrA_sb[0:64, 256:512],
                                                   psA[0:64, 1536:1792]))
                em.emit(V, lambda e: e.drain())
                em.emit(V, lambda e: e.tensor_tensor(
                    onorm_sb[0:64, 0, :], psA[0:64, 0:256],
                    scrA_sb[0:64, 0:256], op=ALU.mult))
                em.emit(V, lambda e: e.tensor_tensor(
                    onorm_sb[0:64, 1, :], psA[0:64, 512:768],
                    scrA_sb[0:64, 256:512], op=ALU.mult), inc="fin")
                em.emit(V, lambda e: e.drain())
                em.wait(G, "fin")
                em.dma(G, onorm_d[pr, band, 0:64], onorm_sb[0:64, 0, :],
                       inc="stg")
                em.dma(G, onorm_d[pr, band, 64:128], onorm_sb[0:64, 1, :],
                       inc="stg")

        em.wait(SY, "smm", em.val("smm"))
        em.wait(SY, "stg", em.val("stg"))
        for cin in range(DCH):
            em.wait(SY, "casts", em.val("casts"))
            em.dma(SY, scrA_sb[:], wot_in[cin * 128:(cin + 1) * 128, :], inc="wl")
            em.wait(SC, "wl")
            em.emit(SC, lambda e, cin=cin: e.copy(
                wof_sb[:, cin].rearrange("p a b -> p (a b)"), scrA_sb[:]),
                inc="casts")
        em.wait(T, "casts")
        for band in range(8):
            qs = slice(band * BLK, (band + 1) * BLK)
            em.wait(SY, "ymm", em.val("ymm"))
            em.dma(SY, onld_sb[:],
                   onorm_d[0:8, band].rearrange("a p b -> p a b"), inc="kv0")
            em.wait(T, "kv0")
            em.wait(T, "xev", em.val("xev"))
            for c2 in range(DCH):
                em.emit(T, lambda e, c2=c2: e.matmul(
                    psA[:, c2 * 256:(c2 + 1) * 256],
                    wof_sb[:, 0, c2, :], onld_sb[:, 0, :],
                    start=True, stop=False))
                for cin in range(1, DCH):
                    em.emit(T, lambda e, c2=c2, cin=cin: e.matmul(
                        psA[:, c2 * 256:(c2 + 1) * 256],
                        wof_sb[:, cin, c2, :], onld_sb[:, cin, :],
                        start=False, stop=(cin == DCH - 1)),
                        inc="ymm" if (c2 == DCH - 1 and cin == DCH - 1) else None)
            em.wait(V, "ymm")
            em.emit(V, lambda e, qs=qs: e.tensor_tensor(
                x_sb[:, :, qs], x_sb[:, :, qs],
                psA[:].rearrange("p (c m) -> p c m", c=DCH), op=ALU.add),
                inc="xev")
        em.emit(V, lambda e: e.drain())


    def body(em: E):
        V, SC, T, G, SY = "vector", "scalar", "tensor", "gpsimd", "sync"
        for c in range(DCH):
            em.dma(SY, x_sb[:, c, :], x_in[c], inc="ld")
        em.dma(SY, b2stg_sb[:], b2_in, inc="ld")
        em.emit(V, lambda e: e.memset(const_sb[:, 1:2], 1e-5), inc="castv")
        em.emit(V, lambda e: e.memset(onesrep_sb[:], 1.0 / D), inc="castv")
        em.emit(V, lambda e: e.memset(onesbf_sb[:], 1.0), inc="castv")
        em.emit(V, lambda e: e.memset(onesrow_sb[:], 1.0), inc="castv")
        em.emit(V, lambda e: e.memset(onescol_sb[:], 1.0), inc="castv")
        em.wait(SC, "ld", em.val("ld"))
        em.emit(SC, lambda e: e.mul(b2h_sb[:], b2stg_sb[:], 0.5), inc="casts")
        em.wait(T, "ld", em.val("ld"))
        em.wait(T, "castv", 5)
        em.wait(SC, "castv", 5)
        em.wait(V, "ld", em.val("ld"))

        emit_ffn0(em, True)
        if upto >= 2:
            emit_attn0(em)
        if upto >= 3:
            emit_ffn0(em, False)
        if upto >= 4:
            emit_ffn1(em, True)
        if upto >= 5:
            emit_attn1(em)
        if upto >= 6:
            emit_ffn1(em, False)

        em.wait(SY, "xnr")
        em.wait(SY, "yv", em.val("yv"))
        em.wait(SY, "xev", em.val("xev"))
        for c in range(DCH):
            em.dma(SY, y_out[c], x_sb[:, c, :], inc="out")
        em.wait(SY, "out")

    with nc.Block() as block:
        @block.vector
        def _(eng):
            body(E(nc, "vector", sems))

        @block.scalar
        def _(eng):
            body(E(nc, "scalar", sems))

        @block.tensor
        def _(eng):
            body(E(nc, "tensor", sems))

        @block.gpsimd
        def _(eng):
            body(E(nc, "gpsimd", sems))

        @block.sync
        def _(eng):
            body(E(nc, "sync", sems))

    est.close()
    return nc



def core_idx(core):
    pr = core % 2
    return np.concatenate([np.arange(blk * BLK, (blk + 1) * BLK)
                           for blk in PAT[pr]])


def make_masks():
    j = np.arange(BLK)[None, :]
    pp = np.arange(128)[:, None]
    d0 = np.where(pp <= j, 0.0, NEG).astype(np.float32)
    d1 = np.where(pp + 128 <= j, 0.0, NEG).astype(np.float32)
    maskd = np.ascontiguousarray(np.stack([d0, d1], axis=1))
    full = np.full((128, 2, BLK), NEG, np.float32)
    zero = np.zeros((128, 2, BLK), np.float32)
    return maskd, full, zero


def stage_inputs(d):
    x = np.asarray(d["L0_attn_x"], np.float32)
    w1 = np.ascontiguousarray(np.asarray(d["L0_ffn_w1"], np.float32))
    w2 = np.ascontiguousarray(np.asarray(d["L0_ffn_w2"], np.float32))
    b1 = np.asarray(d["L0_ffn_b1"], np.float32)
    b2 = np.asarray(d["L0_ffn_b2"], np.float32)
    maskd, full, zero = make_masks()
    prot = np.zeros((128, 128), np.float32)
    for blk0 in (0, 64):
        for i in range(32):
            prot[blk0 + 2 * i + 1, blk0 + 2 * i] = -1.0
            prot[blk0 + 2 * i, blk0 + 2 * i + 1] = 1.0
    maps = []
    for core in range(8):
        b = core // 2
        pr = core % 2
        idx = core_idx(core)
        inv = 10000.0 ** (-np.arange(0, 64, 2, dtype=np.float64) / 64.0)
        ang = idx[None, :].astype(np.float64) * inv[:, None]
        cos64 = np.repeat(np.cos(ang), 2, axis=0)
        sin64 = np.repeat(np.sin(ang), 2, axis=0)
        rope_c = np.ascontiguousarray(
            np.concatenate([cos64, cos64], axis=0).astype(np.float32))
        rope_s = np.ascontiguousarray(
            np.concatenate([sin64, sin64], axis=0).astype(np.float32))
        xs = np.ascontiguousarray(x[b][idx].T).reshape(DCH, 128, MYT)
        maps.append({
            "x_in": xs,
            "w1": w1,
            "w2": w2,
            "b1r": np.ascontiguousarray(b1.reshape(1, 4 * D)),
            "b2": np.ascontiguousarray(b2.reshape(DCH, 128).T),
            "wq": np.ascontiguousarray(np.asarray(d["L0_attn_Wq"], np.float32)),
            "wk": np.ascontiguousarray(np.asarray(d["L0_attn_Wk"], np.float32)),
            "wv": np.ascontiguousarray(np.asarray(d["L0_attn_Wv"], np.float32)),
            "wo": np.ascontiguousarray(np.asarray(d["L0_attn_Wo"], np.float32)),
            "maskd": maskd,
            "mv0": full if pr == 0 else zero,
            "mv1": zero if pr == 0 else full,
            "eye": np.eye(128, dtype=np.float32),
            "wt": np.ascontiguousarray(np.asarray(d["L1_ffn_weight"], np.float32).T),
            "ut": np.ascontiguousarray(
                np.asarray(d["L1_ffn_lora_U"], np.float32).reshape(128, D).T),
            "vs": np.ascontiguousarray(
                np.asarray(d["L1_ffn_lora_V"], np.float32).transpose(0, 2, 1)
                .reshape(128, D)),
            "gw": np.ascontiguousarray(np.asarray(d["L1_ffn_gate_w"], np.float32).T),
            "g16": np.ascontiguousarray(
                np.repeat(np.eye(8, dtype=np.float32), 16, axis=1)),
            "wqkvt": np.ascontiguousarray(
                np.asarray(d["L1_attn_W_qkv"], np.float32).T),
            "wot": np.ascontiguousarray(np.asarray(d["L1_attn_W_o"], np.float32).T),
            "cosi": rope_c, "sini": rope_s,
            "prot": prot,
        })
    return maps


def gather_output(results):
    out = np.zeros((4, S, D), np.float32)
    for core in range(8):
        b = core // 2
        idx = core_idx(core)
        y = results[core]["y_out"].reshape(D, MYT).T
        out[b][idx] = y
    return out



from concourse.bass_utils import run_bass_kernel_spmd

_NC_CACHE = []


def kernel(**inputs):
    maps = stage_inputs(inputs)
    if not _NC_CACHE:
        _NC_CACHE.append(build_program(6))
    nc = _NC_CACHE[0]
    res = run_bass_kernel_spmd(nc, maps, core_ids=list(range(8)))
    return gather_output(res.results)


# version 04
# speedup vs baseline: 1.0584x; 1.0584x over previous
import numpy as np
from contextlib import ExitStack
import concourse.bass as bass
import concourse.mybir as mybir

F32 = mybir.dt.float32
BF16 = mybir.dt.bfloat16
AF = mybir.ActivationFunctionType
ALU = mybir.AluOpType

D = 1024
S = 4096
BLK = 256
NBLK = S // BLK
MYB = 8
MYT = MYB * BLK
DCH = 8
HCH = 32
TCH = MYT // BLK
M0 = 12.0
M1 = 26.0
NEG = -1.0e30

PAT = [[0, 3, 4, 7, 8, 11, 12, 15], [1, 2, 5, 6, 9, 10, 13, 14]]
C1 = [1, 1, 3, 3, 5, 5, 7, 7]
C0 = [0, 2, 2, 4, 4, 6, 6, 8]
L0 = [max(i, C1[i]) for i in range(8)]
L1 = [max(C0[i], i) for i in range(8)]
RING = 2

SEM_NAMES = ["ld", "wl", "castv", "casts", "sq", "stm", "rowe", "rowms",
             "rowmv", "xnr", "hmm", "hp", "ymm", "xev", "out", "wl2",
             "qmm", "qev", "stg", "stg2", "cc", "kv0", "kv1", "kvl", "smm", "pxp", "avm",
             "fin", "ymt", "yv"]


def visits_of(band):
    vs = [("own", band)]
    vs += [("r0", j) for j in range(L0[band])]
    vs += [("r1", j) for j in range(L1[band])]
    return vs


class E:

    def __init__(self, nc, engname, sems):
        self.nc = nc
        self.engname = engname
        self.engines = {"vector": nc.vector, "scalar": nc.scalar,
                        "tensor": nc.tensor, "gpsimd": nc.gpsimd, "sync": nc.sync}
        self.semh = sems
        self.cnt = {k: 0 for k in sems}

    def _fin(self, ins, inc, inck):
        if inc is not None:
            if ins is not None:
                ins.then_inc(self.semh[inc], inck)
            self.cnt[inc] += inck
        return ins

    def emit(self, engine, fn, inc=None, inck=1):
        ins = fn(self.engines[engine]) if self.engname == engine else None
        return self._fin(ins, inc, inck)

    def dma(self, engine, out, in_, inc=None):
        ins = None
        if self.engname == engine:
            ins = self.engines[engine].dma_start(out=out, in_=in_)
        return self._fin(ins, inc, 16 if inc else 0)

    def wait(self, engine, sem, val=None):
        if val is None:
            val = self.cnt[sem]
        if val <= 0:
            return
        if self.engname == engine:
            self.engines[engine].wait_ge(self.semh[sem], val)

    def val(self, sem):
        return self.cnt[sem]


def build_program(upto=2):
    QH = HCH // 4
    nc = bass.Bass("TRN2", target_bir_lowering=False, debug=False)

    def din(name, shape, dtype=F32):
        return nc.dram_tensor(name, shape, dtype, kind="ExternalInput").ap()

    x_in = din("x_in", [DCH, 128, MYT])
    w1_in = din("w1", [D, 4 * D])
    w2_in = din("w2", [4 * D, D])
    b1r_in = din("b1r", [1, 4 * D])
    b2_in = din("b2", [128, DCH])
    wq_in = din("wq", [D, D])
    wk_in = din("wk", [D, D])
    wv_in = din("wv", [D, D])
    wo_in = din("wo", [D, D])
    maskd_in = din("maskd", [128, 2, BLK])
    mv0_in = din("mv0", [128, 2, BLK])
    mv1_in = din("mv1", [128, 2, BLK])
    eye_in = din("eye", [128, 128])
    wt_in = din("wt", [D, D])
    ut_in = din("ut", [D, 128])
    vs_in = din("vs", [128, D])
    gw_in = din("gw", [D, 8])
    g16_in = din("g16", [8, 128])
    wqkvt_in = din("wqkvt", [D, 3 * D])
    wot_in = din("wot", [D, D])
    cosi_in = din("cosi", [128, MYT])
    sini_in = din("sini", [128, MYT])
    prot_in = din("prot", [128, 128])
    bc16_in = din("bc16", [16, 8, 128])
    y_out = nc.dram_tensor("y_out", [DCH, 128, MYT], F32, kind="ExternalOutput").ap()

    KVW = DCH * BLK + 2 * 1040
    kvin = nc.dram_tensor("kvin", [MYB, 128, KVW], BF16)
    kvg = nc.dram_tensor("kvg", [2, MYB, 128, KVW], BF16)
    onorm_d = nc.dram_tensor("onorm_d", [8, 8, 2, 65, BLK], BF16)
    w1c_d = nc.dram_tensor("w1c", [4, DCH, 128, 1024], BF16)
    w2c_d = nc.dram_tensor("w2c", [4, QH, 128, 1024], BF16)
    b1c_d = nc.dram_tensor("b1c", [4, 1, 1024], BF16)
    wtc_d = nc.dram_tensor("wtc", [DCH, 128, 1024], BF16)
    smallc_d = nc.dram_tensor("smallc", [4, 128, 1024], BF16)

    BASE = 16512
    off = [BASE]
    def at(name, shape, dtype, offset=None):
        if offset is None:
            offset = off[0]
        per_part = int(np.prod(shape[1:])) * mybir.dt.size(dtype)
        h = nc.alloc_sbuf_tensor_at(name, shape, dtype, offset=offset)
        off[0] = max(off[0], offset + ((per_part + 63) // 64) * 64)
        return h

    x_sb = at("x_sb", [128, DCH, MYT], F32)
    xn_sb = at("xn_sb", [128, DCH, MYT], BF16)
    stat_sb = at("stat_sb", [128, 4, BLK], F32)
    const_sb = at("const_sb", [128, 16], F32)
    onesrep_sb = at("onesrep_sb", [128, 128], F32)
    onesbf_sb = at("onesbf_sb", [1, BLK], BF16)
    onescol_sb = at("onescol_sb", [128, 2], BF16)
    onesrow_sb = at("onesrow_sb", [1, 128], F32)
    b2h_sb = at("b2h_sb", [128, DCH], F32)
    b2stg_sb = at("b2stg_sb", [128, DCH], F32)
    PERS_END = off[0]

    FREG = PERS_END
    w1q_sb = at("w1q_sb", [128, 2, DCH, QH, 128], BF16, FREG)
    w2q_sb = at("w2q_sb", [128, 2, QH, DCH, 128], BF16, FREG + 32768)
    h_sb = at("h_sb", [128, 2, QH, BLK], BF16, FREG + 65536)
    scr_sb = at("scr_sb", [128, 1024], F32, FREG + 73728)
    b1q_sb = at("b1q_sb", [1, 2, QH * 128], BF16, FREG + 77824)
    b1qstg_sb = at("b1qstg_sb", [1, 2, QH * 128], F32, FREG + 81920)
    xsq_sb = at("xsq_sb", [128, 1024], F32, FREG + 90112)
    FFN_END = FREG + 94208

    wt_sb = at("wt_sb", [128, DCH, DCH, 128], BF16, FREG)
    ut_sb = at("ut_sb", [128, DCH, 128], BF16, FREG + 16384)
    vs_sb = at("vs_sb", [128, DCH, 128], BF16, FREG + 18432)
    gw_sb = at("gw_sb", [128, DCH, 8], BF16, FREG + 20480)
    g16_sb = at("g16_sb", [8, 128], BF16, FREG + 20608)
    gates_sb = at("gates_sb", [128, MYT], BF16, FREG + 20864)
    gte_sb = at("gte_sb", [128, 2, 8], F32, FREG + 24960)
    gtr_sb = at("gtr_sb", [128, 4], F32, FREG + 25024)
    gtn_sb = at("gtn_sb", [128, 2, 8], F32, FREG + 25088)
    gb_sb = at("gb_sb", [128, 2, 512], BF16, FREG + 25152)
    tg_sb = at("tg_sb", [128, 2, 512], BF16, FREG + 27200)
    g16stg_sb = at("g16stg_sb", [8, 128], F32, FREG + 29248)
    eyef_sb = at("eyef_sb", [128, 128], F32, FREG + 29760)

    q_sb = at("q_sb", [128, DCH, MYT], BF16, FREG)
    wqkv_sb = at("wqkv_sb", [128, 2, DCH, DCH, 128], BF16, FREG + 32768)
    kring_sb = at("kring_sb", [128, RING, DCH, BLK], BF16, FREG + 65536)
    vring_sb = at("vring_sb", [128, RING, 2, 1024], BF16, FREG + 73728)
    pp_sb = at("pp_sb", [128, 2, 512], BF16, FREG + 82048)
    maskd_sb = at("maskd_sb", [128, 2, BLK], BF16, FREG + 84096)
    mv0_sb = at("mv0_sb", [128, 2, BLK], BF16, FREG + 85120)
    mv1_sb = at("mv1_sb", [128, 2, BLK], BF16, FREG + 86144)
    eye_sb = at("eye_sb", [128, 128], BF16, FREG + 87168)
    wo_sb = at("wo_sb", [128, 4, DCH, 128], BF16, FREG + 87424)
    lrow_sb = at("lrow_sb", [128, BLK], F32, FREG + 95616)
    rl_sb = at("rl_sb", [128, BLK], F32, FREG + 96640)
    rlb_sb = at("rlb_sb", [128, BLK], F32, FREG + 97664)
    onorm_sb = at("onorm_sb", [128, 4, BLK], BF16, FREG + 98688)
    scrA_sb = at("scrA_sb", [128, 1024], F32, FREG + 100736)
    ATT_END = FREG + 104832

    XNOFF = BASE + 65536
    kband_sb = at("kband_sb", [128, 2, 16, BLK], BF16, XNOFF)
    vband_sb = at("vband_sb", [128, 2, 16, 2, 130], BF16, XNOFF + 16384)
    pp1_sb = at("pp1_sb", [128, 2, 1024], BF16, FREG + 100736)
    bc16_sb = at("bc16_sb", [16, DCH, 128], F32, FREG + 65536)
    lld_sb = at("lld_sb", [16, 2, BLK], F32, FREG + 91648)
    rlbb_sb = at("rlbb_sb", [128, BLK], F32, FREG + 93696)
    onf_sb = at("onf_sb", [128, DCH, BLK], BF16, FREG + 16384)
    cosi_sb = at("cosi_sb", [128, MYT], BF16, FREG + 73728)
    sini_sb = at("sini_sb", [128, MYT], BF16, FREG + 77824)
    vstg1_sb = at("vstg1_sb", [128, 2, 16, 65], BF16, FREG + 87424)
    prot_sb = at("prot_sb", [128, 128], BF16, FREG + 104832)
    ropet_sb = at("ropet_sb", [128, 512], BF16, FREG + 105088)
    wof_sb = at("wof_sb", [128, DCH, DCH, 128], BF16, FREG)
    onld_sb = at("onld_sb", [128, DCH, BLK], BF16, FREG + 32768)
    A1_END = FREG + 106112
    assert max(FFN_END, ATT_END, A1_END) + 0 <= 229344, (FFN_END, ATT_END, A1_END)

    est = ExitStack()
    psA = est.enter_context(nc.psum_tensor("psA", [128, 2048], F32))
    psB = est.enter_context(nc.psum_tensor("psB", [128, 1024], F32))
    psC = est.enter_context(nc.psum_tensor("psC", [128, 1024], F32))
    sems = {n: est.enter_context(nc.semaphore(n)) for n in SEM_NAMES}


    def emit_ln_chunk(em, t, xnr_base):
        V, SC, T = "vector", "scalar", "tensor"
        sl = slice(t * BLK, (t + 1) * BLK)
        em.wait(T, "rowe", em.val("rowe"))
        em.wait(T, "hp", em.val("hp"))
        em.wait(T, "pxp", em.val("pxp"))
        em.wait(T, "xnr", xnr_base)
        em.wait(SC, "xnr", xnr_base)
        em.wait(SC, "ymt", em.val("ymt"))
        for c in range(DCH):
            em.emit(T, lambda e, c=c, sl=sl: e.matmul(
                psB[:, 0:256], onesrep_sb[:], x_sb[:, c, sl],
                start=(c == 0), stop=(c == DCH - 1)),
                inc="stm" if c == DCH - 1 else None)
        for half in range(2):
            c0 = half * 4
            em.wait(SC, "stm", em.val("stm"))
            em.emit(SC, lambda e, c0=c0, sl=sl: e.activation(
                xsq_sb[:].rearrange("p (c m) -> p c m", c=4),
                x_sb[:, c0:c0 + 4, sl], AF.Square), inc="sq")
            em.wait(T, "sq")
            for j in range(4):
                em.emit(T, lambda e, j=j, half=half: e.matmul(
                    psB[:, 256:512], onesrep_sb[:], xsq_sb[:, j * 256:(j + 1) * 256],
                    start=(half == 0 and j == 0), stop=(half == 1 and j == 3)),
                    inc="stm" if j == 3 else None)
        em.wait(V, "stm")
        em.emit(V, lambda e: e.tensor_copy(stat_sb[:, 0, :], psB[:, 0:256]),
                inc="rowe")
        em.emit(V, lambda e: e.tensor_copy(stat_sb[:, 1, :], psB[:, 256:512]),
                inc="rowe")
        em.wait(SC, "rowe")
        em.emit(SC, lambda e: e.activation(stat_sb[:, 2, :], stat_sb[:, 0, :],
                                           AF.Square), inc="rowms")
        em.wait(V, "rowms")
        em.emit(V, lambda e: e.tensor_tensor(stat_sb[:, 1, :], stat_sb[:, 1, :],
                                             stat_sb[:, 2, :], op=ALU.subtract),
                inc="rowmv")
        em.wait(SC, "rowmv")
        em.emit(SC, lambda e: e.activation(stat_sb[:, 2, :], stat_sb[:, 1, :],
                                           AF.Sqrt, bias=const_sb[:, 1:2]),
                inc="rowms")
        em.wait(V, "rowms")
        em.emit(V, lambda e: e.reciprocal(stat_sb[:, 1, :], stat_sb[:, 2, :]),
                inc="rowmv")
        for c in range(DCH):
            em.emit(V, lambda e, c=c, sl=sl: e.tensor_tensor(
                xn_sb[:, c, sl], x_sb[:, c, sl], stat_sb[:, 0, :],
                op=ALU.subtract))
        em.emit(V, lambda e: e.drain())
        for c in range(DCH):
            em.emit(V, lambda e, c=c, sl=sl: e.tensor_tensor(
                xn_sb[:, c, sl], xn_sb[:, c, sl], stat_sb[:, 1, :],
                op=ALU.mult), inc="xnr" if c == DCH - 1 else None)
        em.emit(V, lambda e: e.drain())


    def emit_ffn0(em, first):
        V, SC, T, G, SY = "vector", "scalar", "tensor", "gpsimd", "sync"
        xnr_base = em.val("xnr")
        xnr_at = []
        for t in range(TCH):
            emit_ln_chunk(em, t, xnr_base)
            xnr_at.append(em.val("xnr"))
        XNR_ALL = em.val("xnr")
        for sem in ("smm", "avm", "ymt", "pxp", "qmm", "yv"):
            em.wait(SY, sem, em.val(sem))
        for q in range(4):
            sg = q % 2
            em.wait(SY, "xev", em.val("xev") - 2 * TCH)
            if first:
                qsem = "stg" if q % 2 == 0 else "stg2"
                em.wait(V, qsem, em.val(qsem))
                em.wait(SC, qsem, em.val(qsem))
                for cc in range(DCH):
                    for sh in range(2):
                        wsem = "wl" if sh == 0 else "wl2"
                        em.wait(SY, "castv", em.val("castv") -
                                (0 if (cc == 0 and sh == 0) else 1))
                        em.dma(SY, scr_sb[:, sh * 512:(sh + 1) * 512],
                               w1_in[cc * 128:(cc + 1) * 128,
                                     q * 1024 + sh * 512:
                                     q * 1024 + (sh + 1) * 512], inc=wsem)
                        em.wait(V, wsem)
                        em.emit(V, lambda e, sg=sg, cc=cc, sh=sh: e.tensor_copy(
                            w1q_sb[:, sg, cc]
                            .rearrange("p h m -> p (h m)")[:, sh * 512:
                                                           (sh + 1) * 512],
                            scr_sb[:, sh * 512:(sh + 1) * 512]), inc="castv")
                    em.wait(G, "castv")
                    em.dma(G, w1c_d[q, cc],
                           w1q_sb[:, sg, cc].rearrange("p h m -> p (h m)"),
                           inc=qsem)
                for hh in range(QH):
                    for sh in range(2):
                        wsem = "wl" if sh == 0 else "wl2"
                        em.wait(SY, "castv", em.val("castv"))
                        em.wait(SY, "casts", em.val("casts") -
                                (0 if (hh == 0 and sh == 0) else 1))
                        em.dma(SY, scr_sb[:, sh * 512:(sh + 1) * 512],
                               w2_in[(q * QH + hh) * 128:(q * QH + hh + 1) * 128,
                                     sh * 512:(sh + 1) * 512], inc=wsem)
                        em.wait(SC, wsem)
                        em.emit(SC, lambda e, sg=sg, hh=hh, sh=sh: e.mul(
                            w2q_sb[:, sg, hh]
                            .rearrange("p d m -> p (d m)")[:, sh * 512:
                                                           (sh + 1) * 512],
                            scr_sb[:, sh * 512:(sh + 1) * 512], 0.5),
                            inc="casts")
                    em.wait(G, "casts")
                    em.dma(G, w2c_d[q, hh],
                           w2q_sb[:, sg, hh].rearrange("p d m -> p (d m)"),
                           inc=qsem)
                em.wait(SY, "castv", em.val("castv"))
                em.wait(SY, "casts", em.val("casts"))
                em.dma(SY, b1qstg_sb[:, sg, :],
                       b1r_in[:, q * QH * 128:(q + 1) * QH * 128], inc="wl")
                em.wait(V, "wl")
                em.emit(V, lambda e, sg=sg: e.tensor_copy(
                    b1q_sb[:, sg, :], b1qstg_sb[:, sg, :]), inc="castv")
                em.wait(G, "castv")
                em.dma(G, b1c_d[q], b1q_sb[:, sg, :].rearrange("p m -> p m"),
                       inc=qsem)
            else:
                em.wait(SY, "stg", em.val("stg"))
                em.wait(SY, "stg2", em.val("stg2"))
                em.wait(SY, "wl", em.val("wl"))
                em.dma(SY, w1q_sb[:, sg].rearrange("p c h m -> p c (h m)"),
                       w1c_d[q].rearrange("c p m -> p c m"), inc="wl")
                em.dma(SY, w2q_sb[:, sg].rearrange("p h d m -> p h (d m)"),
                       w2c_d[q].rearrange("h p m -> p h m"), inc="wl")
                em.dma(SY, b1q_sb[:, sg, :], b1c_d[q], inc="wl")
                em.wait(T, "wl", em.val("wl"))
            em.wait(T, "castv")
            em.wait(T, "casts")
            if q == 0:
                em.wait(T, "rowe")
            for t in range(TCH):
                k = q * TCH + t
                hs = k % 2
                sl = slice(t * BLK, (t + 1) * BLK)
                if q == 0:
                    em.wait(T, "xnr", xnr_at[t])
                em.wait(T, "hp", em.val("hp"))
                for grp in range(2):
                    pg = psB if grp == 0 else psC
                    for j in range(4):
                        hh = grp * 4 + j
                        pa = pg[:, j * 256:(j + 1) * 256]
                        em.emit(T, lambda e, sg=sg, hh=hh, pa=pa: e.matmul(
                            pa, b1q_sb[:, sg, hh * 128:(hh + 1) * 128],
                            onesbf_sb[:], start=True, stop=False))
                        for c in range(DCH):
                            em.emit(T, lambda e, sg=sg, hh=hh, c=c, sl=sl, pa=pa:
                                    e.matmul(pa, w1q_sb[:, sg, c, hh, :],
                                             xn_sb[:, c, sl],
                                             start=False, stop=(c == DCH - 1)),
                                    inc="hmm" if c == DCH - 1 else None)
                em.wait(SC, "hmm")
                em.wait(SC, "ymm", em.val("ymm") - 1)
                em.emit(SC, lambda e, hs=hs: e.activation(
                    h_sb[:, hs, 0:4].rearrange("p h m -> p (h m)"), psB[:],
                    AF.Gelu))
                em.emit(SC, lambda e, hs=hs: e.activation(
                    h_sb[:, hs, 4:8].rearrange("p h m -> p (h m)"), psC[:],
                    AF.Gelu), inc="hp")
                em.wait(T, "hp")
                for half in range(2):
                    em.wait(T, "xev", em.val("xev") - 1)
                    for c in range(half * 4, half * 4 + 4):
                        for hh in range(QH):
                            em.emit(T, lambda e, sg=sg, c=c, hh=hh, hs=hs:
                                    e.matmul(
                                        psA[:, c * 256:(c + 1) * 256],
                                        w2q_sb[:, sg, hh, c, :],
                                        h_sb[:, hs, hh, :],
                                        start=(hh == 0), stop=(hh == QH - 1)),
                                    inc="ymm" if (c % 4 == 3 and hh == QH - 1)
                                    else None)
                    em.wait(V, "ymm")
                    em.emit(V, lambda e, sl=sl, half=half: e.tensor_tensor(
                        x_sb[:, half * 4:half * 4 + 4, sl],
                        x_sb[:, half * 4:half * 4 + 4, sl],
                        psA[:, half * 1024:(half + 1) * 1024]
                        .rearrange("p (c m) -> p c m", c=4), op=ALU.add),
                        inc="xev")
        em.wait(V, "xev")
        em.wait(V, "casts", 1)
        for t in range(TCH):
            sl = slice(t * BLK, (t + 1) * BLK)
            for c in range(DCH):
                em.emit(V, lambda e, c=c, sl=sl: e.tensor_scalar_add(
                    x_sb[:, c, sl], x_sb[:, c, sl], b2h_sb[:, c:c + 1]),
                    inc="xnr")
        em.emit(V, lambda e: e.drain())


    def emit_attn0(em):
        xnr_base0 = None
        V, SC, T, G, SY = "vector", "scalar", "tensor", "gpsimd", "sync"
        xnr_base = em.val("xnr")
        for t in range(TCH):
            emit_ln_chunk(em, t, xnr_base)
        XNR_ALL = em.val("xnr")

        em.wait(SY, "xnr", XNR_ALL)
        em.wait(SY, "yv", em.val("yv"))
        for sem in ("ymm", "hmm", "hp", "xev"):
            em.wait(SY, sem, em.val(sem))
        em.dma(SY, scrA_sb[:, 0:512],
               maskd_in.rearrange("p a b -> p (a b)"), inc="wl")
        em.dma(SY, scrA_sb[:, 512:640], eye_in, inc="wl")
        em.wait(V, "wl")
        em.wait(V, "stg", em.val("stg"))
        em.wait(SC, "stg", em.val("stg"))
        em.wait(V, "stg2", em.val("stg2"))
        em.wait(SC, "stg2", em.val("stg2"))
        em.emit(V, lambda e: e.tensor_copy(
            maskd_sb[:].rearrange("p a b -> p (a b)"), scrA_sb[:, 0:512]),
            inc="castv")
        em.emit(V, lambda e: e.tensor_copy(eye_sb[:], scrA_sb[:, 512:640]),
                inc="castv")
        em.wait(SY, "castv", em.val("castv"))
        em.dma(SY, scrA_sb[:, 0:512],
               mv0_in.rearrange("p a b -> p (a b)"), inc="wl")
        em.dma(SY, scrA_sb[:, 512:1024],
               mv1_in.rearrange("p a b -> p (a b)"), inc="wl")
        em.wait(V, "wl")
        em.emit(V, lambda e: e.tensor_copy(
            mv0_sb[:].rearrange("p a b -> p (a b)"), scrA_sb[:, 0:512]),
            inc="castv")
        em.emit(V, lambda e: e.tensor_copy(
            mv1_sb[:].rearrange("p a b -> p (a b)"), scrA_sb[:, 512:1024]),
            inc="castv")
        em.emit(V, lambda e: e.memset(const_sb[:, 2:3], -M0), inc="castv")
        em.emit(V, lambda e: e.memset(const_sb[:, 3:4], 1.0 / 32.0), inc="castv")

        em.wait(T, "xnr", XNR_ALL)
        em.wait(T, "rowe")
        for m in range(3):
            w_in = [wq_in, wk_in, wv_in][m]
            ws = m % 2
            em.wait(SY, "qmm", em.val("qmm") - 32)
            for cin in range(DCH):
                em.wait(SY, "castv", em.val("castv"))
                em.dma(SY, scrA_sb[:], w_in[cin * 128:(cin + 1) * 128, :],
                       inc="wl")
                em.wait(V, "wl")
                em.emit(V, lambda e, ws=ws, cin=cin: e.tensor_copy(
                    wqkv_sb[:, ws, cin].rearrange("p a b -> p (a b)"),
                    scrA_sb[:]), inc="castv")
            em.wait(T, "castv")
            if m < 2:
                for tc in range(4):
                    ts = slice(tc * 512, (tc + 1) * 512)
                    if m == 1:
                        em.wait(V, "stg", em.val("stg"))
                    for c2 in range(DCH):
                        kq = em.val("qmm")
                        pa = psA[:, (kq % 4) * 512:(kq % 4 + 1) * 512]
                        em.wait(T, "qev", em.val("qev") - 3)
                        for cin in range(DCH):
                            em.emit(T, lambda e, ws=ws, c2=c2, cin=cin, ts=ts,
                                    pa=pa: e.matmul(
                                        pa, wqkv_sb[:, ws, cin, c2, :],
                                        xn_sb[:, cin, ts],
                                        start=(cin == 0), stop=(cin == DCH - 1)),
                                    inc="qmm" if cin == DCH - 1 else None)
                        em.wait(V, "qmm")
                        if m == 0:
                            em.emit(V, lambda e, c2=c2, ts=ts, pa=pa:
                                    e.tensor_copy(q_sb[:, c2, ts], pa),
                                    inc="qev")
                        else:
                            for jj in range(2):
                                em.emit(V, lambda e, c2=c2, jj=jj, pa=pa:
                                        e.tensor_copy(
                                            kring_sb[:, jj, c2, :],
                                            pa[:, jj * 256:(jj + 1) * 256]),
                                        inc="qev" if jj == 1 else None)
                    if m == 1:
                        em.wait(G, "qev")
                        for jj in range(2):
                            em.dma(G, kvin[2 * tc + jj][:, 0:2048],
                                   kring_sb[:, jj].rearrange("p a b -> p (a b)"),
                                   inc="stg")
            else:
                for sv in range(TCH):
                    vslot = sv % 2
                    em.wait(V, "stg", em.val("stg"))
                    for u in range(2):
                        for dvh in range(2):
                            pb = psB if dvh == 0 else psC
                            tok0 = sv * 256 + u * 128
                            em.wait(T, "qev", em.val("qev"))
                            for cin in range(DCH):
                                em.emit(T, lambda e, ws=ws, cin=cin, tok0=tok0,
                                        pb=pb, dvh=dvh: e.matmul(
                                            pb[:, 0:512],
                                            xn_sb[:, cin, tok0:tok0 + 128],
                                            wqkv_sb[:, ws, cin,
                                                    dvh * 4:(dvh + 1) * 4, :]
                                            .rearrange("p a b -> p (a b)"),
                                            start=(cin == 0),
                                            stop=(cin == DCH - 1)),
                                        inc="qmm" if cin == DCH - 1 else None)
                        em.wait(V, "qmm")
                        em.emit(V, lambda e, vslot=vslot, u=u: e.tensor_copy(
                            vring_sb[:, vslot, u, 0:512], psB[:, 0:512]))
                        em.emit(V, lambda e, vslot=vslot, u=u: e.tensor_copy(
                            vring_sb[:, vslot, u, 512:1024], psC[:, 0:512]),
                            inc="qev")
                    em.wait(G, "qev")
                    em.dma(G, kvin[sv][:, 2048:4096],
                           vring_sb[:, vslot].rearrange("p a b -> p (a b)"),
                           inc="stg")
        em.emit(V, lambda e: e.memset(pp_sb[:, 0, 0:32], 0.0), inc="fin")
        em.wait(G, "fin")
        for sv in range(MYB):
            em.dma(G, kvin[sv][:, 4096:KVW], pp_sb[:, 0, 0:32], inc="stg")
        em.wait(G, "stg")
        em.emit(G, lambda e: e.collective_compute(
            "AllGather", ALU.bypass,
            replica_groups=[[0, 1], [2, 3], [4, 5], [6, 7]],
            ins=[kvin.ap().opt()],
            outs=[kvg.ap().opt()]), inc="cc", inck=1)

        em.wait(SY, "cc", 1)
        em.wait(T, "qev", em.val("qev"))
        em.wait(T, "cc", 1)
        vno = 0
        for p in range(2):
            for cin in range(4):
                em.wait(SY, "casts", em.val("casts"))
                if p == 1:
                    em.wait(SY, "ymt", em.val("ymt"))
                em.dma(SY, scrA_sb[:],
                       wo_in[(4 * p + cin) * 128:(4 * p + cin + 1) * 128, :],
                       inc="wl")
                em.wait(SC, "wl")
                em.wait(SC, "stm", em.val("stm"))
                em.emit(SC, lambda e, cin=cin: e.copy(
                    wo_sb[:, cin].rearrange("p a b -> p (a b)"), scrA_sb[:]),
                    inc="casts")
            em.wait(T, "casts")
            for band in range(8):
                qs = slice(band * BLK, (band + 1) * BLK)
                vs = visits_of(band)
                for vi, (reg, j) in enumerate(vs):
                    r = vno % RING
                    sb_ = vno % 2
                    kvs = "kv0" if r == 0 else "kv1"
                    em.wait(SY, "avm", em.val("avm") - (RING - 1))
                    em.wait(SY, "pxp", em.val("pxp") - (RING - 1))
                    em.wait(SY, kvs, em.val(kvs))
                    slot = {"own": kvin[j], "r0": kvg[0, j],
                            "r1": kvg[1, j]}[reg]
                    em.dma(SY, kring_sb[:, r].rearrange("p a b -> p (a b)"),
                           slot[:, 0:2048], inc=kvs)
                    em.dma(SY, vring_sb[:, r].rearrange("p a b -> p (a b)"),
                           slot[:, 2048:4096], inc=kvs)
                    em.wait(T, kvs)
                    em.wait(T, "pxp", em.val("pxp") - 1)
                    mask = None
                    if reg == "own":
                        mask = maskd_sb
                    elif reg == "r0" and j == band:
                        mask = mv0_sb
                    elif reg == "r1" and j == band:
                        mask = mv1_sb
                    for ksub in range(2):
                        pt = psB[:, sb_ * 512 + ksub * 256:
                                 sb_ * 512 + (ksub + 1) * 256]
                        if mask is not None:
                            em.emit(T, lambda e, mask=mask, ksub=ksub, pt=pt:
                                    e.matmul(pt, eye_sb[:], mask[:, ksub, :],
                                             start=True, stop=False))
                        for cin in range(DCH):
                            first = (mask is None and cin == 0)
                            em.emit(T, lambda e, r=r, cin=cin, ksub=ksub, qs=qs,
                                    pt=pt, first=first:
                                    e.matmul(pt,
                                             kring_sb[:, r, cin,
                                                      ksub * 128:(ksub + 1) * 128],
                                             q_sb[:, cin, qs],
                                             start=first, stop=(cin == DCH - 1)),
                                    inc="smm" if (ksub == 1 and cin == DCH - 1)
                                    else None)
                    em.wait(SC, "smm")
                    em.wait(SC, "avm", em.val("avm") - 1)
                    em.emit(SC, lambda e, sb_=sb_: e.activation(
                        pp_sb[:, sb_, :], psB[:, sb_ * 512:(sb_ + 1) * 512],
                        AF.Exp, bias=const_sb[:, 2:3], scale=const_sb[:, 3:4]),
                        inc="pxp")
                    em.wait(T, "pxp")
                    if vi == 0:
                        em.wait(T, "fin", em.val("fin"))
                    for ksub in range(2):
                        for cdv in range(4):
                            h0 = 8 * p + 2 * cdv
                            em.emit(T, lambda e, r=r, ksub=ksub, cdv=cdv, h0=h0,
                                    sb_=sb_: e.matmul(
                                        psA[:, cdv * 512:cdv * 512 + 256],
                                        vring_sb[:, r, ksub,
                                                 (h0 // 2) * 128:(h0 // 2) * 128 + 128],
                                        pp_sb[:, sb_, ksub * 256:(ksub + 1) * 256],
                                        start=(vi == 0 and ksub == 0),
                                        stop=(vi == len(vs) - 1 and ksub == 1)))
                        em.emit(T, lambda e, r=r, ksub=ksub, sb_=sb_: e.matmul(
                            psC[0:1, 0:256], onescol_sb[:, 0:1],
                            pp_sb[:, sb_, ksub * 256:(ksub + 1) * 256],
                            start=(vi == 0 and ksub == 0),
                            stop=(vi == len(vs) - 1 and ksub == 1)),
                            inc="avm" if ksub == 1 else None)
                    vno += 1
                em.wait(V, "avm")
                em.emit(V, lambda e: e.tensor_copy(lrow_sb[0:1, :],
                                                   psC[0:1, 0:256]))
                em.emit(V, lambda e: e.drain())
                em.emit(V, lambda e: e.reciprocal(rl_sb[0:1, :], lrow_sb[0:1, :]),
                        inc="fin")
                em.wait(T, "fin")
                em.wait(T, "yv", em.val("yv"))
                em.emit(T, lambda e: e.matmul(psC[:, 768:1024], onesrow_sb[:],
                                              rl_sb[0:1, :], start=True, stop=True),
                        inc="ymt")
                em.wait(V, "ymt")
                em.emit(V, lambda e: e.tensor_copy(rlb_sb[:], psC[:, 768:1024]))
                em.emit(V, lambda e: e.drain())
                for cdv in range(4):
                    em.emit(V, lambda e, cdv=cdv: e.tensor_tensor(
                        onorm_sb[:, cdv, :], psA[:, cdv * 512:cdv * 512 + 256],
                        rlb_sb[:], op=ALU.mult),
                        inc="fin" if cdv == 3 else None)
                em.emit(V, lambda e: e.drain())
                em.wait(T, "fin")
                for c2 in range(DCH):
                    ys = psC[:, 512:768]
                    em.wait(T, "yv", em.val("yv"))
                    for cdv in range(4):
                        em.emit(T, lambda e, c2=c2, cdv=cdv, ys=ys: e.matmul(
                            ys, wo_sb[:, cdv, c2, :], onorm_sb[:, cdv, :],
                            start=(cdv == 0), stop=(cdv == 3)),
                            inc="ymt" if cdv == 3 else None)
                    em.wait(V, "ymt")
                    em.emit(V, lambda e, c2=c2, qs=qs, ys=ys: e.tensor_tensor(
                        x_sb[:, c2, qs], x_sb[:, c2, qs], ys, op=ALU.add),
                        inc="yv")
                em.emit(V, lambda e: e.drain())


    def emit_ffn1(em, first):
        V, SC, T, G, SY = "vector", "scalar", "tensor", "gpsimd", "sync"
        xnr_base = em.val("xnr")
        xnr_at = []
        for t in range(TCH):
            emit_ln_chunk(em, t, xnr_base)
            xnr_at.append(em.val("xnr"))
        XNR_ALL = em.val("xnr")
        for sem in ("smm", "avm", "ymt", "pxp", "qmm", "yv", "ymm", "hmm",
                    "hp", "xev", "fin", "qev"):
            em.wait(SY, sem, em.val(sem))
        em.wait(V, "stg", em.val("stg"))
        em.wait(SC, "stg", em.val("stg"))
        em.wait(V, "stg2", em.val("stg2"))
        em.wait(SC, "stg2", em.val("stg2"))
        if first:
            for cin in range(DCH):
                for sh in range(2):
                    wsem = "wl" if sh == 0 else "wl2"
                    em.wait(SY, "casts", em.val("casts") -
                            (0 if (cin == 0 and sh == 0) else 1))
                    em.dma(SY, scr_sb[:, sh * 512:(sh + 1) * 512],
                           wt_in[cin * 128:(cin + 1) * 128,
                                 sh * 512:(sh + 1) * 512], inc=wsem)
                    em.wait(SC, wsem)
                    em.emit(SC, lambda e, cin=cin, sh=sh: e.mul(
                        wt_sb[:, cin].rearrange("p a b -> p (a b)")
                        [:, sh * 512:(sh + 1) * 512],
                        scr_sb[:, sh * 512:(sh + 1) * 512], 0.5), inc="casts")
                em.wait(G, "casts")
                em.dma(G, wtc_d[cin],
                       wt_sb[:, cin].rearrange("p a b -> p (a b)"), inc="stg")
        else:
            em.wait(SY, "stg", em.val("stg"))
            em.wait(SY, "wl", em.val("wl"))
            em.dma(SY, wt_sb[:].rearrange("p c a b -> p c (a b)"),
                   wtc_d.rearrange("c p m -> p c m"), inc="wl")
            em.wait(T, "wl", em.val("wl"))
            em.wait(SC, "wl", em.val("wl"))
        em.wait(SY, "casts", em.val("casts"))
        em.wait(SY, "wl", em.val("wl"))
        em.dma(SY, scr_sb[:].rearrange("p (c m) -> p c m", c=DCH),
               ut_in.rearrange("(c p) m -> p c m", p=128), inc="wl")
        em.dma(SY, g16stg_sb[:], g16_in, inc="wl")
        em.dma(SY, eyef_sb[:], eye_in, inc="wl")
        em.wait(V, "wl")
        em.emit(V, lambda e: e.tensor_copy(
            ut_sb[:].rearrange("p c m -> p (c m)"), scr_sb[:]), inc="castv")
        em.emit(V, lambda e: e.tensor_copy(g16_sb[:], g16stg_sb[:]), inc="castv")
        em.wait(SY, "castv", em.val("castv"))
        em.dma(SY, scr_sb[:], vs_in, inc="wl")
        em.wait(SC, "wl")
        em.emit(SC, lambda e: e.mul(
            vs_sb[:].rearrange("p c m -> p (c m)"), scr_sb[:], 1.0 / 32.0),
            inc="casts")
        em.wait(SY, "casts", em.val("casts"))
        em.dma(SY, scr_sb[:, 0:64].rearrange("p (c m) -> p c m", c=DCH),
               gw_in.rearrange("(c p) m -> p c m", p=128), inc="wl")
        em.wait(V, "wl")
        em.emit(V, lambda e: e.tensor_copy(
            gw_sb[:].rearrange("p c m -> p (c m)"), scr_sb[:, 0:64]),
            inc="castv")
        em.emit(V, lambda e: e.memset(const_sb[:, 4:5], -8.0), inc="castv")
        em.wait(T, "wl", em.val("wl"))
        em.wait(T, "castv")
        em.wait(T, "casts")
        em.wait(T, "rowe")

        for u in range(MYT // 128):
            tok0 = u * 128
            ub = u % 2
            em.wait(T, "xnr", xnr_at[u // 2])
            em.wait(T, "pxp", em.val("pxp"))
            for cin in range(DCH):
                em.emit(T, lambda e, cin=cin, tok0=tok0: e.matmul(
                    psC[:, 0:8], xn_sb[:, cin, tok0:tok0 + 128],
                    gw_sb[:, cin, :], start=(cin == 0), stop=(cin == DCH - 1)),
                    inc="qmm" if cin == DCH - 1 else None)
            em.wait(SC, "qmm")
            em.wait(SC, "fin", em.val("fin"))
            em.emit(SC, lambda e, ub=ub: e.activation(
                gte_sb[:, ub, :], psC[:, 0:8], AF.Exp,
                bias=const_sb[:, 4:5]), inc="pxp")
            em.wait(V, "pxp")
            em.emit(V, lambda e, ub=ub: e.tensor_reduce(
                gtr_sb[:, ub:ub + 1], gte_sb[:, ub, :],
                axis=mybir.AxisListType.X, op=ALU.add))
            em.emit(V, lambda e: e.drain())
            em.emit(V, lambda e, ub=ub: e.reciprocal(
                gtr_sb[:, 2 + ub:3 + ub], gtr_sb[:, ub:ub + 1]))
            em.emit(V, lambda e: e.drain())
            em.emit(V, lambda e, ub=ub: e.tensor_scalar(
                gtn_sb[:, ub, :], gte_sb[:, ub, :], gtr_sb[:, 2 + ub:3 + ub],
                None, op0=ALU.mult), inc="fin")
            em.wait(T, "fin")
            em.wait(T, "qev", em.val("qev"))
            em.emit(T, lambda e, ub=ub: e.transpose(
                psC[0:8, 512:640], gtn_sb[:, ub, :], eyef_sb[:]), inc="ymt")
            em.wait(V, "ymt")
            em.emit(V, lambda e, tok0=tok0: e.tensor_copy(
                gates_sb[0:8, tok0:tok0 + 128], psC[0:8, 512:640]), inc="qev")

        for j in range(4):
            js = slice(j * 512, (j + 1) * 512)
            jb = j % 2
            em.wait(T, "qev", em.val("qev"))
            em.emit(T, lambda e, js=js: e.matmul(
                psB[:, 0:512], g16_sb[:], gates_sb[0:8, js],
                start=True, stop=True), inc="smm")
            for cin in range(DCH):
                em.emit(T, lambda e, cin=cin, js=js: e.matmul(
                    psB[:, 512:1024], ut_sb[:, cin, :], xn_sb[:, cin, js],
                    start=(cin == 0), stop=(cin == DCH - 1)),
                    inc="hmm" if cin == DCH - 1 else None)
            em.wait(V, "smm")
            em.emit(V, lambda e, jb=jb: e.tensor_copy(gb_sb[:, jb, :],
                                                      psB[:, 0:512]))
            em.emit(V, lambda e: e.drain())
            em.wait(V, "hmm")
            em.wait(V, "ymm", em.val("ymm") - 4)
            em.emit(V, lambda e, jb=jb: e.tensor_tensor(
                tg_sb[:, jb, :], psB[:, 512:1024], gb_sb[:, jb, :],
                op=ALU.mult), inc="fin")
            for u2 in range(2):
                t = j * 2 + u2
                sl = slice(t * BLK, (t + 1) * BLK)
                em.wait(T, "fin")
                for half in range(2):
                    em.wait(T, "xev", em.val("xev") - 1)
                    for c2 in range(half * 4, half * 4 + 4):
                        em.emit(T, lambda e, c2=c2, sl=sl: e.matmul(
                            psA[:, c2 * 256:(c2 + 1) * 256],
                            wt_sb[:, 0, c2, :], xn_sb[:, 0, sl],
                            start=True, stop=False))
                        for cin in range(1, DCH):
                            em.emit(T, lambda e, c2=c2, cin=cin, sl=sl: e.matmul(
                                psA[:, c2 * 256:(c2 + 1) * 256],
                                wt_sb[:, cin, c2, :], xn_sb[:, cin, sl],
                                start=False, stop=False))
                        em.emit(T, lambda e, c2=c2, jb=jb, u2=u2: e.matmul(
                            psA[:, c2 * 256:(c2 + 1) * 256],
                            vs_sb[:, c2, :],
                            tg_sb[:, jb, u2 * 256:(u2 + 1) * 256],
                            start=False, stop=True),
                            inc="ymm" if c2 % 4 == 3 else None)
                    em.wait(V, "ymm")
                    em.emit(V, lambda e, sl=sl, half=half: e.tensor_tensor(
                        x_sb[:, half * 4:half * 4 + 4, sl],
                        x_sb[:, half * 4:half * 4 + 4, sl],
                        psA[:, half * 1024:(half + 1) * 1024]
                        .rearrange("p (c m) -> p c m", c=4), op=ALU.add),
                        inc="xev")
        em.emit(V, lambda e: e.drain())


    def emit_attn1(em):
        V, SC, T, G, SY = "vector", "scalar", "tensor", "gpsimd", "sync"
        xnr_base = em.val("xnr")
        xnr_at = []
        for t in range(TCH):
            emit_ln_chunk(em, t, xnr_base)
            xnr_at.append(em.val("xnr"))
        XNR_ALL = em.val("xnr")

        em.wait(SY, "xnr", XNR_ALL)
        em.wait(SY, "yv", em.val("yv"))
        for sem in ("ymm", "hmm", "hp", "xev", "smm", "avm", "ymt", "pxp",
                    "qmm", "fin", "qev"):
            em.wait(SY, sem, em.val(sem))
        em.wait(V, "stg", em.val("stg"))
        em.wait(SC, "stg", em.val("stg"))
        em.wait(V, "stg2", em.val("stg2"))
        em.wait(SC, "stg2", em.val("stg2"))
        em.dma(SY, scrA_sb[:, 0:512],
               maskd_in.rearrange("p a b -> p (a b)"), inc="wl")
        em.dma(SY, scrA_sb[:, 512:640], eye_in, inc="wl")
        em.wait(V, "wl")
        em.emit(V, lambda e: e.tensor_copy(
            maskd_sb[:].rearrange("p a b -> p (a b)"), scrA_sb[:, 0:512]),
            inc="castv")
        em.emit(V, lambda e: e.tensor_copy(eye_sb[:], scrA_sb[:, 512:640]),
                inc="castv")
        em.wait(SY, "castv", em.val("castv"))
        em.dma(SY, scrA_sb[:, 0:512],
               mv0_in.rearrange("p a b -> p (a b)"), inc="wl")
        em.dma(SY, scrA_sb[:, 512:1024],
               mv1_in.rearrange("p a b -> p (a b)"), inc="wl")
        em.wait(V, "wl")
        em.emit(V, lambda e: e.tensor_copy(
            mv0_sb[:].rearrange("p a b -> p (a b)"), scrA_sb[:, 0:512]),
            inc="castv")
        em.emit(V, lambda e: e.tensor_copy(
            mv1_sb[:].rearrange("p a b -> p (a b)"), scrA_sb[:, 512:1024]),
            inc="castv")
        for piece in range(2):
            em.wait(SY, "castv", em.val("castv"))
            em.dma(SY, scrA_sb[:], cosi_in[:, piece * 1024:(piece + 1) * 1024],
                   inc="wl")
            em.wait(V, "wl")
            em.emit(V, lambda e, piece=piece: e.tensor_copy(
                cosi_sb[:, piece * 1024:(piece + 1) * 1024], scrA_sb[:]),
                inc="castv")
            em.wait(SY, "castv", em.val("castv"))
            em.dma(SY, scrA_sb[:], sini_in[:, piece * 1024:(piece + 1) * 1024],
                   inc="wl")
            em.wait(V, "wl")
            em.emit(V, lambda e, piece=piece: e.tensor_copy(
                sini_sb[:, piece * 1024:(piece + 1) * 1024], scrA_sb[:]),
                inc="castv")
        em.wait(SY, "castv", em.val("castv"))
        em.dma(SY, scrA_sb[:, 0:128], prot_in, inc="wl")
        em.wait(V, "wl")
        em.emit(V, lambda e: e.tensor_copy(prot_sb[:], scrA_sb[:, 0:128]),
                inc="castv")
        em.emit(V, lambda e: e.memset(const_sb[:, 5:6], -M1), inc="castv")
        em.emit(V, lambda e: e.memset(const_sb[:, 6:7], 0.125), inc="castv")
        em.emit(V, lambda e: e.memset(vstg1_sb[:, :, :, 64:65], 1.0), inc="castv")

        em.wait(T, "castv")
        em.wait(T, "rowe")
        for m in range(3):
            ws = m % 2
            em.wait(SY, "qmm", em.val("qmm") - 32)
            for cin in range(DCH):
                em.wait(SY, "castv", em.val("castv"))
                em.dma(SY, scrA_sb[:],
                       wqkvt_in[cin * 128:(cin + 1) * 128,
                                m * D:(m + 1) * D], inc="wl")
                em.wait(V, "wl")
                em.emit(V, lambda e, ws=ws, cin=cin: e.tensor_copy(
                    wqkv_sb[:, ws, cin].rearrange("p a b -> p (a b)"),
                    scrA_sb[:]), inc="castv")
            em.wait(T, "castv")
            if m < 2:
                for tc in range(4):
                    ts = slice(tc * 512, (tc + 1) * 512)
                    em.wait(T, "xnr", xnr_at[2 * tc + 1])
                    if m == 1:
                        em.wait(V, "stg", em.val("stg"))
                    for c2 in range(DCH):
                        kq = em.val("qmm")
                        pa = psA[:, (kq % 2) * 512:(kq % 2) * 512 + 512]
                        em.wait(T, "qev", em.val("qev") - 3)
                        for cin in range(DCH):
                            em.emit(T, lambda e, ws=ws, c2=c2, cin=cin, ts=ts,
                                    pa=pa: e.matmul(
                                        pa, wqkv_sb[:, ws, cin, c2, :],
                                        xn_sb[:, cin, ts],
                                        start=(cin == 0), stop=(cin == DCH - 1)),
                                    inc="qmm" if cin == DCH - 1 else None)
                        dst = (lambda e, c2=c2, ts=ts: q_sb[:, c2, ts]) if m == 0                             else None
                        em.wait(V, "qmm")
                        em.emit(V, lambda e, pa=pa: e.tensor_copy(
                            ropet_sb[:], pa), inc="qev")
                        em.wait(T, "qev")
                        em.wait(T, "fin", em.val("fin") - 1)
                        em.emit(T, lambda e, kq=kq: e.matmul(
                            psB[:, (kq % 2) * 512:(kq % 2) * 512 + 512],
                            prot_sb[:], ropet_sb[:], start=True, stop=True),
                            inc="smm")
                        em.wait(V, "smm")
                        em.emit(V, lambda e, ts=ts: e.tensor_tensor(
                            ropet_sb[:], ropet_sb[:], cosi_sb[:, ts],
                            op=ALU.mult))
                        em.emit(V, lambda e: e.drain())
                        if m == 0:
                            tgt = lambda e, c2=c2, ts=ts: q_sb[:, c2, ts]
                        else:
                            tgt = lambda e, c2=c2: kring_sb[:, 0, c2, :]                                 .rearrange("p b -> p b") if False else None
                        if m == 0:
                            em.emit(V, lambda e, c2=c2, ts=ts, kq=kq:
                                    e.scalar_tensor_tensor(
                                        q_sb[:, c2, ts],
                                        psB[:, (kq % 2) * 512:(kq % 2) * 512 + 512],
                                        1.0, sini_sb[:, ts],
                                        op0=ALU.mult, op1=ALU.mult))
                            em.emit(V, lambda e: e.drain())
                            em.emit(V, lambda e, c2=c2, ts=ts: e.tensor_tensor(
                                q_sb[:, c2, ts], q_sb[:, c2, ts], ropet_sb[:],
                                op=ALU.add), inc="fin")
                        else:
                            for jj in range(2):
                                em.emit(V, lambda e, c2=c2, ts=ts, kq=kq, jj=jj:
                                        e.scalar_tensor_tensor(
                                            kring_sb[:, jj, c2, :],
                                            psB[:, (kq % 2) * 512 + jj * 256:
                                                (kq % 2) * 512 + (jj + 1) * 256],
                                            1.0,
                                            sini_sb[:, ts.start + jj * 256:
                                                    ts.start + (jj + 1) * 256],
                                            op0=ALU.mult, op1=ALU.mult))
                            em.emit(V, lambda e: e.drain())
                            for jj in range(2):
                                em.emit(V, lambda e, c2=c2, jj=jj: e.tensor_tensor(
                                    kring_sb[:, jj, c2, :], kring_sb[:, jj, c2, :],
                                    ropet_sb[:, jj * 256:(jj + 1) * 256],
                                    op=ALU.add), inc="fin" if jj == 1 else None)
                        em.emit(V, lambda e: e.drain())
                    if m == 1:
                        em.wait(G, "fin", em.val("fin"))
                        for jj in range(2):
                            em.dma(G, kvin[2 * tc + jj][:, 0:2048],
                                   kring_sb[:, jj].rearrange("p a b -> p (a b)"),
                                   inc="stg")
            else:
                for sv in range(TCH):
                    em.wait(T, "xnr", xnr_at[sv])
                    em.wait(V, "stg", em.val("stg"))
                    for u in range(2):
                        for dvh in range(2):
                            pb = psB if dvh == 0 else psC
                            tok0 = sv * 256 + u * 128
                            em.wait(T, "qev", em.val("qev"))
                            for cin in range(DCH):
                                em.emit(T, lambda e, ws=ws, cin=cin, tok0=tok0,
                                        pb=pb, dvh=dvh: e.matmul(
                                            pb[:, 0:512],
                                            xn_sb[:, cin, tok0:tok0 + 128],
                                            wqkv_sb[:, ws, cin,
                                                    dvh * 4:(dvh + 1) * 4, :]
                                            .rearrange("p a b -> p (a b)"),
                                            start=(cin == 0),
                                            stop=(cin == DCH - 1)),
                                        inc="qmm" if cin == DCH - 1 else None)
                        em.wait(V, "qmm")
                        em.emit(V, lambda e, u=u: e.tensor_copy(
                            vstg1_sb[:, u, 0:8, 0:64],
                            psB[:, 0:512].rearrange("p (a b) -> p a b", a=8)))
                        em.emit(V, lambda e, u=u: e.tensor_copy(
                            vstg1_sb[:, u, 8:16, 0:64],
                            psC[:, 0:512].rearrange("p (a b) -> p a b", a=8)),
                            inc="qev")
                    em.wait(G, "qev")
                    em.dma(G, kvin[sv][:, 2048:KVW],
                           vstg1_sb[:].rearrange("p a b c -> p (a b c)"),
                           inc="stg")
        em.wait(G, "stg")
        em.emit(G, lambda e: e.collective_compute(
            "AllGather", ALU.bypass,
            replica_groups=[[0, 1], [2, 3], [4, 5], [6, 7]],
            ins=[kvin.ap().opt()],
            outs=[kvg.ap().opt()]), inc="cc", inck=1)

        em.wait(SY, "cc", em.val("cc"))
        em.wait(T, "qev", em.val("qev"))
        em.wait(T, "cc", em.val("cc"))
        vno = 0
        for pr in range(8):
            for band in range(8):
                qs = slice(band * BLK, (band + 1) * BLK)
                vs = visits_of(band)
                bb = band % 2
                nv = len(vs)
                kvs = "kv0" if bb == 0 else "kv1"
                guard = em.val("avm") - (len(visits_of(band - 1))
                                         if band > 0 else 0)
                em.wait(SY, "avm", guard)
                em.wait(SY, kvs, em.val(kvs))
                segs = [("own", band, 0, 1), ("r0", 0, 1, L0[band]),
                        ("r1", 0, 1 + L0[band], L1[band])]
                for reg, j0, at_, cnt in segs:
                    if cnt == 0:
                        continue
                    if reg == "own":
                        em.dma(SY, kband_sb[:, bb, at_],
                               kvin[j0][:, pr * 256:(pr + 1) * 256], inc=kvs)
                        em.dma(SY, vband_sb[:, bb, at_],
                               kvin[j0][:, 2048:KVW].rearrange(
                                   "p (a b) -> p a b", a=2)[:, :,
                                   pr * 130:(pr + 1) * 130], inc=kvs)
                    else:
                        rr = 0 if reg == "r0" else 1
                        em.dma(SY, kband_sb[:, bb, at_:at_ + cnt],
                               kvg[rr, j0:j0 + cnt][:, :,
                                                    pr * 256:(pr + 1) * 256]
                               .rearrange("s p b -> p s b"), inc=kvs)
                        for cc2 in range(cnt):
                            em.dma(SY, vband_sb[:, bb, at_ + cc2],
                                   kvg[rr, j0 + cc2][:, 2048:KVW]
                                   .rearrange("p (a b) -> p a b", a=2)
                                   [:, :, pr * 130:(pr + 1) * 130], inc=kvs)
                em.wait(T, kvs)
                for vi, (reg, j) in enumerate(vs):
                    r = vno % 2
                    em.wait(T, "pxp", em.val("pxp") - 1)
                    mask = None
                    if reg == "own":
                        mask = maskd_sb
                    elif reg == "r0" and j == band:
                        mask = mv0_sb
                    elif reg == "r1" and j == band:
                        mask = mv1_sb
                    vsl = vi
                    sbk = psB if (vno % 2 == 0) else psC
                    for hh in range(2):
                        for ksub in range(2):
                            pt = sbk[:, hh * 512 + ksub * 256:
                                     hh * 512 + (ksub + 1) * 256]
                            if mask is not None:
                                em.emit(T, lambda e, mask=mask, ksub=ksub, pt=pt:
                                        e.matmul(pt, eye_sb[:], mask[:, ksub, :],
                                                 start=True, stop=False))
                            em.emit(T, lambda e, bb=bb, vsl=vsl, hh=hh,
                                    ksub=ksub, qs=qs, pr=pr, pt=pt,
                                    first=(mask is None):
                                    e.matmul(
                                        pt,
                                        kband_sb[0 + 64 * hh:64 + 64 * hh, bb,
                                                 vsl,
                                                 ksub * 128:(ksub + 1) * 128],
                                        q_sb[0 + 64 * hh:64 + 64 * hh, pr, qs],
                                        start=first, stop=True,
                                        tile_position=(64 * hh, 0)),
                                    inc="smm" if (hh == 1 and ksub == 1)
                                    else None)
                    em.wait(SC, "smm")
                    em.wait(SC, "avm", em.val("avm") - 1)
                    em.emit(SC, lambda e, r=r, sbk=sbk: e.activation(
                        pp1_sb[:, r, :], sbk[:],
                        AF.Exp, bias=const_sb[:, 5:6], scale=const_sb[:, 6:7]),
                        inc="pxp")
                    em.wait(T, "pxp")
                    if vi == 0:
                        em.wait(T, "fin", em.val("fin"))
                    for hh in range(2):
                        ob = psA[:, hh * 512: hh * 512 + 256]
                        for ksub in range(2):
                            em.emit(T, lambda e, bb=bb, vsl=vsl, hh=hh,
                                    ksub=ksub, r=r, ob=ob: e.matmul(
                                        ob[0:65, :],
                                        vband_sb[:, bb, vsl, ksub,
                                                 hh * 65:(hh + 1) * 65],
                                        pp1_sb[:, r, hh * 512 + ksub * 256:
                                               hh * 512 + (ksub + 1) * 256],
                                        start=(vi == 0 and ksub == 0),
                                        stop=(vi == len(vs) - 1 and ksub == 1)),
                                    inc="avm" if (hh == 1 and ksub == 1)
                                    else None)
                    vno += 1
                em.wait(V, "avm")
                em.wait(V, "stg", em.val("stg"))
                em.emit(V, lambda e: e.tensor_copy(onorm_sb[0:65, 0, :],
                                                   psA[0:65, 0:256]))
                em.emit(V, lambda e: e.tensor_copy(onorm_sb[0:65, 1, :],
                                                   psA[0:65, 512:768]),
                        inc="fin")
                em.wait(G, "fin")
                em.dma(G, onorm_d[pr, band, 0], onorm_sb[0:65, 0, :],
                       inc="stg")
                em.dma(G, onorm_d[pr, band, 1], onorm_sb[0:65, 1, :],
                       inc="stg")

        em.wait(SY, "smm", em.val("smm"))
        em.wait(SY, "avm", em.val("avm"))
        em.wait(SY, "pxp", em.val("pxp"))
        em.wait(SY, "stg", em.val("stg"))
        em.dma(SY, bc16_sb[:], bc16_in, inc="wl")
        for cin in range(DCH):
            em.wait(SY, "casts", em.val("casts"))
            em.dma(SY, scrA_sb[:], wot_in[cin * 128:(cin + 1) * 128, :], inc="wl")
            em.wait(SC, "wl")
            em.emit(SC, lambda e, cin=cin: e.copy(
                wof_sb[:, cin].rearrange("p a b -> p (a b)"), scrA_sb[:]),
                inc="casts")
        em.wait(G, "stg", em.val("stg"))
        em.wait(T, "wl", em.val("wl"))
        em.wait(T, "casts")
        for band in range(8):
            qs = slice(band * BLK, (band + 1) * BLK)
            em.wait(SY, "ymm", em.val("ymm"))
            em.wait(SY, "kv0", em.val("kv0"))
            for cin in range(DCH):
                for h2 in range(2):
                    em.dma(SY, onld_sb[64 * h2:64 * h2 + 64, cin, :],
                           onorm_d[cin, band, h2][0:64, :], inc="kv0")
            em.wait(G, "kvl", em.val("kvl"))
            em.wait(G, "fin", em.val("fin"))
            for s8 in range(8):
                em.dma(G, lld_sb[2 * s8:2 * s8 + 2, 0, :],
                       onorm_d[s8, band][:, 64, :], inc="kvl")
            em.wait(V, "kv0")
            em.wait(V, "kvl")
            em.emit(V, lambda e: e.reciprocal(lld_sb[:, 1, :], lld_sb[:, 0, :]),
                    inc="fin")
            em.emit(V, lambda e: e.drain())
            for cin in range(DCH):
                em.wait(T, "fin")
                em.emit(T, lambda e, cin=cin: e.matmul(
                    psB[:, 0:256], bc16_sb[:, cin, :],
                    lld_sb[:, 1, :], start=True, stop=True), inc="ymt")
                em.wait(V, "ymt")
                em.emit(V, lambda e: e.tensor_copy(rlbb_sb[:], psB[:, 0:256]))
                em.emit(V, lambda e: e.drain())
                em.emit(V, lambda e, cin=cin: e.tensor_tensor(
                    onf_sb[:, cin, :], onld_sb[:, cin, :], rlbb_sb[:],
                    op=ALU.mult), inc="fin")
                em.emit(V, lambda e: e.drain())
            em.wait(T, "fin")
            em.wait(T, "xev", em.val("xev"))
            for c2 in range(DCH):
                for cin in range(DCH):
                    em.emit(T, lambda e, c2=c2, cin=cin: e.matmul(
                        psA[:, c2 * 256:(c2 + 1) * 256],
                        wof_sb[:, cin, c2, :], onf_sb[:, cin, :],
                        start=(cin == 0), stop=(cin == DCH - 1)),
                        inc="ymm" if (c2 == DCH - 1 and cin == DCH - 1)
                        else None)
            em.wait(V, "ymm")
            em.emit(V, lambda e, qs=qs: e.tensor_tensor(
                x_sb[:, :, qs], x_sb[:, :, qs],
                psA[:].rearrange("p (c m) -> p c m", c=DCH), op=ALU.add),
                inc="xev")
        em.emit(V, lambda e: e.drain())


    def body(em: E):
        V, SC, T, G, SY = "vector", "scalar", "tensor", "gpsimd", "sync"
        for c in range(DCH):
            em.dma(SY, x_sb[:, c, :], x_in[c], inc="ld")
        em.dma(SY, b2stg_sb[:], b2_in, inc="ld")
        em.emit(V, lambda e: e.memset(const_sb[:, 1:2], 1e-5), inc="castv")
        em.emit(V, lambda e: e.memset(onesrep_sb[:], 1.0 / D), inc="castv")
        em.emit(V, lambda e: e.memset(onesbf_sb[:], 1.0), inc="castv")
        em.emit(V, lambda e: e.memset(onesrow_sb[:], 1.0), inc="castv")
        em.emit(V, lambda e: e.memset(onescol_sb[:], 1.0), inc="castv")
        em.wait(SC, "ld", em.val("ld"))
        em.emit(SC, lambda e: e.mul(b2h_sb[:], b2stg_sb[:], 0.5), inc="casts")
        em.wait(T, "ld", em.val("ld"))
        em.wait(T, "castv", 5)
        em.wait(SC, "castv", 5)
        em.wait(V, "ld", em.val("ld"))

        emit_ffn0(em, True)
        if upto >= 2:
            emit_attn0(em)
        if upto >= 3:
            emit_ffn0(em, False)
        if upto >= 4:
            emit_ffn1(em, True)
        if upto >= 5:
            emit_attn1(em)
        if upto >= 6:
            emit_ffn1(em, False)

        em.wait(SY, "xnr")
        em.wait(SY, "yv", em.val("yv"))
        em.wait(SY, "xev", em.val("xev"))
        for c in range(DCH):
            em.dma(SY, y_out[c], x_sb[:, c, :], inc="out")
        em.wait(SY, "out")

    with nc.Block() as block:
        @block.vector
        def _(eng):
            body(E(nc, "vector", sems))

        @block.scalar
        def _(eng):
            body(E(nc, "scalar", sems))

        @block.tensor
        def _(eng):
            body(E(nc, "tensor", sems))

        @block.gpsimd
        def _(eng):
            body(E(nc, "gpsimd", sems))

        @block.sync
        def _(eng):
            body(E(nc, "sync", sems))

    est.close()
    return nc



def core_idx(core):
    pr = core % 2
    return np.concatenate([np.arange(blk * BLK, (blk + 1) * BLK)
                           for blk in PAT[pr]])


def make_masks():
    j = np.arange(BLK)[None, :]
    pp = np.arange(128)[:, None]
    d0 = np.where(pp <= j, 0.0, NEG).astype(np.float32)
    d1 = np.where(pp + 128 <= j, 0.0, NEG).astype(np.float32)
    maskd = np.ascontiguousarray(np.stack([d0, d1], axis=1))
    full = np.full((128, 2, BLK), NEG, np.float32)
    zero = np.zeros((128, 2, BLK), np.float32)
    return maskd, full, zero


def stage_inputs(d):
    x = np.asarray(d["L0_attn_x"], np.float32)
    w1 = np.ascontiguousarray(np.asarray(d["L0_ffn_w1"], np.float32))
    w2 = np.ascontiguousarray(np.asarray(d["L0_ffn_w2"], np.float32))
    b1 = np.asarray(d["L0_ffn_b1"], np.float32)
    b2 = np.asarray(d["L0_ffn_b2"], np.float32)
    maskd, full, zero = make_masks()
    prot = np.zeros((128, 128), np.float32)
    for blk0 in (0, 64):
        for i in range(32):
            prot[blk0 + 2 * i + 1, blk0 + 2 * i] = -1.0
            prot[blk0 + 2 * i, blk0 + 2 * i + 1] = 1.0
    bc16 = np.zeros((16, 8, 128), np.float32)
    for cin8 in range(8):
        bc16[2 * cin8, cin8, 0:64] = 1.0
        bc16[2 * cin8 + 1, cin8, 64:128] = 1.0
    maps = []
    for core in range(8):
        b = core // 2
        pr = core % 2
        idx = core_idx(core)
        inv = 10000.0 ** (-np.arange(0, 64, 2, dtype=np.float64) / 64.0)
        ang = idx[None, :].astype(np.float64) * inv[:, None]
        cos64 = np.repeat(np.cos(ang), 2, axis=0)
        sin64 = np.repeat(np.sin(ang), 2, axis=0)
        rope_c = np.ascontiguousarray(
            np.concatenate([cos64, cos64], axis=0).astype(np.float32))
        rope_s = np.ascontiguousarray(
            np.concatenate([sin64, sin64], axis=0).astype(np.float32))
        xs = np.ascontiguousarray(x[b][idx].T).reshape(DCH, 128, MYT)
        maps.append({
            "x_in": xs,
            "w1": w1,
            "w2": w2,
            "b1r": np.ascontiguousarray(b1.reshape(1, 4 * D)),
            "b2": np.ascontiguousarray(b2.reshape(DCH, 128).T),
            "wq": np.ascontiguousarray(np.asarray(d["L0_attn_Wq"], np.float32)),
            "wk": np.ascontiguousarray(np.asarray(d["L0_attn_Wk"], np.float32)),
            "wv": np.ascontiguousarray(np.asarray(d["L0_attn_Wv"], np.float32)),
            "wo": np.ascontiguousarray(np.asarray(d["L0_attn_Wo"], np.float32)),
            "maskd": maskd,
            "mv0": full if pr == 0 else zero,
            "mv1": zero if pr == 0 else full,
            "eye": np.eye(128, dtype=np.float32),
            "wt": np.ascontiguousarray(np.asarray(d["L1_ffn_weight"], np.float32).T),
            "ut": np.ascontiguousarray(
                np.asarray(d["L1_ffn_lora_U"], np.float32).reshape(128, D).T),
            "vs": np.ascontiguousarray(
                np.asarray(d["L1_ffn_lora_V"], np.float32).transpose(0, 2, 1)
                .reshape(128, D)),
            "gw": np.ascontiguousarray(np.asarray(d["L1_ffn_gate_w"], np.float32).T),
            "g16": np.ascontiguousarray(
                np.repeat(np.eye(8, dtype=np.float32), 16, axis=1)),
            "wqkvt": np.ascontiguousarray(
                np.asarray(d["L1_attn_W_qkv"], np.float32).T),
            "wot": np.ascontiguousarray(np.asarray(d["L1_attn_W_o"], np.float32).T),
            "cosi": rope_c, "sini": rope_s,
            "prot": prot,
            "bc16": bc16,
        })
    return maps


def gather_output(results):
    out = np.zeros((4, S, D), np.float32)
    for core in range(8):
        b = core // 2
        idx = core_idx(core)
        y = results[core]["y_out"].reshape(D, MYT).T
        out[b][idx] = y
    return out


from concourse.bass_utils import run_bass_kernel_spmd

_NC_CACHE = []


def kernel(**inputs):
    maps = stage_inputs(inputs)
    if not _NC_CACHE:
        _NC_CACHE.append(build_program(6))
    nc = _NC_CACHE[0]
    res = run_bass_kernel_spmd(nc, maps, core_ids=list(range(8)))
    return gather_output(res.results)


# version 05
# speedup vs baseline: 1.2317x; 1.1637x over previous
import numpy as np
from contextlib import ExitStack
import concourse.bass as bass
import concourse.mybir as mybir

F32 = mybir.dt.float32
BF16 = mybir.dt.bfloat16
AF = mybir.ActivationFunctionType
ALU = mybir.AluOpType

D = 1024
S = 4096
BLK = 256
NBLK = S // BLK
MYB = 8
MYT = MYB * BLK
DCH = 8
HCH = 32
TCH = MYT // BLK
M0 = 12.0
M1 = 26.0
NEG = -1.0e30

PAT = [[0, 3, 4, 7, 8, 11, 12, 15], [1, 2, 5, 6, 9, 10, 13, 14]]
C1 = [1, 1, 3, 3, 5, 5, 7, 7]
C0 = [0, 2, 2, 4, 4, 6, 6, 8]
L0 = [max(i, C1[i]) for i in range(8)]
L1 = [max(C0[i], i) for i in range(8)]
RING = 2

SEM_NAMES = ["ld", "wl", "castv", "casts", "sq", "stm", "rowe", "rowms",
             "rowmv", "xnr", "hmm", "hp", "ymm", "xev", "out", "wl2",
             "qmm", "qev", "stg", "stg2", "cc", "kv0", "kv1", "kvl", "smm", "pxp", "avm",
             "fin", "ymt", "yv"]


def visits_of(band):
    vs = [("own", band)]
    vs += [("r0", j) for j in range(L0[band])]
    vs += [("r1", j) for j in range(L1[band])]
    return vs


class E:

    def __init__(self, nc, engname, sems):
        self.nc = nc
        self.engname = engname
        self.engines = {"vector": nc.vector, "scalar": nc.scalar,
                        "tensor": nc.tensor, "gpsimd": nc.gpsimd, "sync": nc.sync}
        self.semh = sems
        self.cnt = {k: 0 for k in sems}

    def _fin(self, ins, inc, inck):
        if inc is not None:
            if ins is not None:
                ins.then_inc(self.semh[inc], inck)
            self.cnt[inc] += inck
        return ins

    def emit(self, engine, fn, inc=None, inck=1):
        ins = fn(self.engines[engine]) if self.engname == engine else None
        return self._fin(ins, inc, inck)

    def dma(self, engine, out, in_, inc=None):
        ins = None
        if self.engname == engine:
            ins = self.engines[engine].dma_start(out=out, in_=in_)
        return self._fin(ins, inc, 16 if inc else 0)

    def wait(self, engine, sem, val=None):
        if val is None:
            val = self.cnt[sem]
        if val <= 0:
            return
        if self.engname == engine:
            self.engines[engine].wait_ge(self.semh[sem], val)

    def val(self, sem):
        return self.cnt[sem]


def build_program(upto=2):
    QH = HCH // 4
    nc = bass.Bass("TRN2", target_bir_lowering=False, debug=False)

    def din(name, shape, dtype=F32):
        return nc.dram_tensor(name, shape, dtype, kind="ExternalInput").ap()

    x_in = din("x_in", [DCH, 128, MYT])
    w1_in = din("w1", [D, 4 * D])
    w2_in = din("w2", [4 * D, D])
    b1r_in = din("b1r", [1, 4 * D])
    b2_in = din("b2", [128, DCH])
    wq_in = din("wq", [D, D])
    wk_in = din("wk", [D, D])
    wv_in = din("wv", [D, D])
    wo_in = din("wo", [D, D])
    maskd_in = din("maskd", [128, 2, BLK])
    mv0_in = din("mv0", [128, 2, BLK])
    mv1_in = din("mv1", [128, 2, BLK])
    eye_in = din("eye", [128, 128])
    wt_in = din("wt", [D, D])
    ut_in = din("ut", [D, 128])
    vs_in = din("vs", [128, D])
    gw_in = din("gw", [D, 8])
    g16_in = din("g16", [8, 128])
    wqkvt_in = din("wqkvt", [D, 3 * D])
    wot_in = din("wot", [D, D])
    cosi_in = din("cosi", [128, MYT])
    sini_in = din("sini", [128, MYT])
    prot_in = din("prot", [128, 128])
    bc16_in = din("bc16", [16, 8, 128])
    y_out = nc.dram_tensor("y_out", [DCH, 128, MYT], F32, kind="ExternalOutput").ap()

    KVW = DCH * BLK + 2 * 1040
    kvin = nc.dram_tensor("kvin", [MYB, 128, KVW], BF16)
    kvg = nc.dram_tensor("kvg", [2, MYB, 128, KVW], BF16)
    onorm_d = nc.dram_tensor("onorm_d", [8, 8, 2, 65, BLK], BF16)
    w1c_d = nc.dram_tensor("w1c", [4, DCH, 128, 1024], BF16)
    w2c_d = nc.dram_tensor("w2c", [4, QH, 128, 1024], BF16)
    b1c_d = nc.dram_tensor("b1c", [4, 1, 1024], BF16)
    wtc_d = nc.dram_tensor("wtc", [DCH, 128, 1024], BF16)
    smallc_d = nc.dram_tensor("smallc", [4, 128, 1024], BF16)

    BASE = 16512
    off = [BASE]
    def at(name, shape, dtype, offset=None):
        if offset is None:
            offset = off[0]
        per_part = int(np.prod(shape[1:])) * mybir.dt.size(dtype)
        h = nc.alloc_sbuf_tensor_at(name, shape, dtype, offset=offset)
        off[0] = max(off[0], offset + ((per_part + 63) // 64) * 64)
        return h

    x_sb = at("x_sb", [128, DCH, MYT], F32)
    xn_sb = at("xn_sb", [128, DCH, MYT], BF16)
    stat_sb = at("stat_sb", [128, 4, BLK], F32)
    const_sb = at("const_sb", [128, 16], F32)
    onesrep_sb = at("onesrep_sb", [128, 128], F32)
    onesbf_sb = at("onesbf_sb", [1, BLK], BF16)
    onescol_sb = at("onescol_sb", [128, 2], BF16)
    onesrow_sb = at("onesrow_sb", [1, 128], F32)
    b2h_sb = at("b2h_sb", [128, DCH], F32)
    b2stg_sb = at("b2stg_sb", [128, DCH], F32)
    PERS_END = off[0]

    FREG = PERS_END
    w1q_sb = at("w1q_sb", [128, 2, DCH, QH, 128], BF16, FREG)
    w2q_sb = at("w2q_sb", [128, 2, QH, DCH, 128], BF16, FREG + 32768)
    h_sb = at("h_sb", [128, 2, QH, BLK], BF16, FREG + 65536)
    scr_sb = at("scr_sb", [128, 1024], F32, FREG + 73728)
    b1q_sb = at("b1q_sb", [1, 2, QH * 128], BF16, FREG + 77824)
    b1qstg_sb = at("b1qstg_sb", [1, 2, QH * 128], F32, FREG + 81920)
    xsq_sb = at("xsq_sb", [128, 1024], F32, FREG + 90112)
    FFN_END = FREG + 94208

    wt_sb = at("wt_sb", [128, DCH, DCH, 128], BF16, FREG)
    ut_sb = at("ut_sb", [128, DCH, 128], BF16, FREG + 16384)
    vs_sb = at("vs_sb", [128, DCH, 128], BF16, FREG + 18432)
    gw_sb = at("gw_sb", [128, DCH, 8], BF16, FREG + 20480)
    g16_sb = at("g16_sb", [8, 128], BF16, FREG + 20608)
    gates_sb = at("gates_sb", [128, MYT], BF16, FREG + 20864)
    gte_sb = at("gte_sb", [128, 2, 8], F32, FREG + 24960)
    gtr_sb = at("gtr_sb", [128, 4], F32, FREG + 25024)
    gtn_sb = at("gtn_sb", [128, 2, 8], F32, FREG + 25088)
    gb_sb = at("gb_sb", [128, 2, 512], BF16, FREG + 25152)
    tg_sb = at("tg_sb", [128, 2, 512], BF16, FREG + 27200)
    g16stg_sb = at("g16stg_sb", [8, 128], F32, FREG + 29248)
    eyef_sb = at("eyef_sb", [128, 128], F32, FREG + 29760)

    q_sb = at("q_sb", [128, DCH, MYT], BF16, FREG)
    wqkv_sb = at("wqkv_sb", [128, 2, DCH, DCH, 128], BF16, FREG + 32768)
    kring_sb = at("kring_sb", [128, RING, DCH, BLK], BF16, FREG + 65536)
    vring_sb = at("vring_sb", [128, RING, 2, 1024], BF16, FREG + 73728)
    pp_sb = at("pp_sb", [128, 2, 512], BF16, FREG + 82048)
    maskd_sb = at("maskd_sb", [128, 2, BLK], BF16, FREG + 84096)
    mv0_sb = at("mv0_sb", [128, 2, BLK], BF16, FREG + 85120)
    mv1_sb = at("mv1_sb", [128, 2, BLK], BF16, FREG + 86144)
    eye_sb = at("eye_sb", [128, 128], BF16, FREG + 87168)
    wo_sb = at("wo_sb", [128, 4, DCH, 128], BF16, FREG + 87424)
    lrow_sb = at("lrow_sb", [128, BLK], F32, FREG + 95616)
    rl_sb = at("rl_sb", [128, BLK], F32, FREG + 96640)
    rlb_sb = at("rlb_sb", [128, BLK], F32, FREG + 97664)
    onorm_sb = at("onorm_sb", [128, 4, BLK], BF16, FREG + 98688)
    scrA_sb = at("scrA_sb", [128, 1024], F32, FREG + 100736)
    ATT_END = FREG + 104832

    XNOFF = BASE + 65536
    kband_sb = at("kband_sb", [128, 2, 16, BLK], BF16, XNOFF)
    vband_sb = at("vband_sb", [128, 2, 16, 2, 130], BF16, XNOFF + 16384)
    pp1_sb = at("pp1_sb", [128, 2, 1024], BF16, FREG + 100736)
    bc16_sb = at("bc16_sb", [16, DCH, 128], F32, FREG + 65536)
    lld_sb = at("lld_sb", [16, 2, BLK], F32, FREG + 91648)
    rlbb_sb = at("rlbb_sb", [128, BLK], F32, FREG + 93696)
    onf_sb = at("onf_sb", [128, DCH, BLK], BF16, FREG + 16384)
    cosi_sb = at("cosi_sb", [128, MYT], BF16, FREG + 73728)
    sini_sb = at("sini_sb", [128, MYT], BF16, FREG + 77824)
    vstg1_sb = at("vstg1_sb", [128, 2, 16, 65], BF16, FREG + 87424)
    prot_sb = at("prot_sb", [128, 128], BF16, FREG + 104832)
    ropet_sb = at("ropet_sb", [128, 512], BF16, FREG + 105088)
    wof_sb = at("wof_sb", [128, DCH, DCH, 128], BF16, FREG)
    onld_sb = at("onld_sb", [128, DCH, BLK], BF16, FREG + 32768)
    A1_END = FREG + 106112
    assert max(FFN_END, ATT_END, A1_END) + 0 <= 229344, (FFN_END, ATT_END, A1_END)

    est = ExitStack()
    psA = est.enter_context(nc.psum_tensor("psA", [128, 2048], F32))
    psB = est.enter_context(nc.psum_tensor("psB", [128, 1024], F32))
    psC = est.enter_context(nc.psum_tensor("psC", [128, 1024], F32))
    sems = {n: est.enter_context(nc.semaphore(n)) for n in SEM_NAMES}


    def emit_ln_chunk(em, t, xnr_base):
        V, SC, T = "vector", "scalar", "tensor"
        sl = slice(t * BLK, (t + 1) * BLK)
        em.wait(T, "rowe", em.val("rowe"))
        em.wait(T, "hp", em.val("hp"))
        em.wait(T, "pxp", em.val("pxp"))
        em.wait(T, "xnr", xnr_base)
        em.wait(SC, "xnr", xnr_base)
        em.wait(SC, "ymt", em.val("ymt"))
        for c in range(DCH):
            em.emit(T, lambda e, c=c, sl=sl: e.matmul(
                psB[:, 0:256], onesrep_sb[:], x_sb[:, c, sl],
                start=(c == 0), stop=(c == DCH - 1)),
                inc="stm" if c == DCH - 1 else None)
        for half in range(2):
            c0 = half * 4
            em.wait(SC, "stm", em.val("stm"))
            em.emit(SC, lambda e, c0=c0, sl=sl: e.activation(
                xsq_sb[:].rearrange("p (c m) -> p c m", c=4),
                x_sb[:, c0:c0 + 4, sl], AF.Square), inc="sq")
            em.wait(T, "sq")
            for j in range(4):
                em.emit(T, lambda e, j=j, half=half: e.matmul(
                    psB[:, 256:512], onesrep_sb[:], xsq_sb[:, j * 256:(j + 1) * 256],
                    start=(half == 0 and j == 0), stop=(half == 1 and j == 3)),
                    inc="stm" if j == 3 else None)
        em.wait(V, "stm")
        em.emit(V, lambda e: e.tensor_copy(stat_sb[:, 0, :], psB[:, 0:256]),
                inc="rowe")
        em.emit(V, lambda e: e.tensor_copy(stat_sb[:, 1, :], psB[:, 256:512]),
                inc="rowe")
        em.wait(SC, "rowe")
        em.emit(SC, lambda e: e.activation(stat_sb[:, 2, :], stat_sb[:, 0, :],
                                           AF.Square), inc="rowms")
        em.wait(V, "rowms")
        em.emit(V, lambda e: e.tensor_tensor(stat_sb[:, 1, :], stat_sb[:, 1, :],
                                             stat_sb[:, 2, :], op=ALU.subtract),
                inc="rowmv")
        em.wait(SC, "rowmv")
        em.emit(SC, lambda e: e.activation(stat_sb[:, 2, :], stat_sb[:, 1, :],
                                           AF.Sqrt, bias=const_sb[:, 1:2]),
                inc="rowms")
        em.wait(V, "rowms")
        em.emit(V, lambda e: e.reciprocal(stat_sb[:, 1, :], stat_sb[:, 2, :]),
                inc="rowmv")
        for c in range(DCH):
            em.emit(V, lambda e, c=c, sl=sl: e.tensor_tensor(
                xn_sb[:, c, sl], x_sb[:, c, sl], stat_sb[:, 0, :],
                op=ALU.subtract))
        em.emit(V, lambda e: e.drain())
        for c in range(DCH):
            em.emit(V, lambda e, c=c, sl=sl: e.tensor_tensor(
                xn_sb[:, c, sl], xn_sb[:, c, sl], stat_sb[:, 1, :],
                op=ALU.mult), inc="xnr" if c == DCH - 1 else None)
        em.emit(V, lambda e: e.drain())


    def emit_ffn0(em, first):
        V, SC, T, G, SY = "vector", "scalar", "tensor", "gpsimd", "sync"
        xnr_base = em.val("xnr")
        xnr_at = []
        for t in range(TCH):
            emit_ln_chunk(em, t, xnr_base)
            xnr_at.append(em.val("xnr"))
        XNR_ALL = em.val("xnr")
        for sem in ("smm", "avm", "ymt", "pxp", "qmm", "yv"):
            em.wait(SY, sem, em.val(sem))
        for q in range(4):
            sg = q % 2
            em.wait(SY, "xev", em.val("xev") - 2 * TCH)
            if first:
                qsem = "stg" if q % 2 == 0 else "stg2"
                em.wait(V, qsem, em.val(qsem))
                em.wait(SC, qsem, em.val(qsem))
                for cc in range(DCH):
                    for sh in range(2):
                        wsem = "wl" if sh == 0 else "wl2"
                        em.wait(SY, "castv", em.val("castv") -
                                (0 if (cc == 0 and sh == 0) else 1))
                        em.dma(SY, scr_sb[:, sh * 512:(sh + 1) * 512],
                               w1_in[cc * 128:(cc + 1) * 128,
                                     q * 1024 + sh * 512:
                                     q * 1024 + (sh + 1) * 512], inc=wsem)
                        em.wait(V, wsem)
                        em.emit(V, lambda e, sg=sg, cc=cc, sh=sh: e.tensor_copy(
                            w1q_sb[:, sg, cc]
                            .rearrange("p h m -> p (h m)")[:, sh * 512:
                                                           (sh + 1) * 512],
                            scr_sb[:, sh * 512:(sh + 1) * 512]), inc="castv")
                    em.wait(G, "castv")
                    em.dma(G, w1c_d[q, cc],
                           w1q_sb[:, sg, cc].rearrange("p h m -> p (h m)"),
                           inc=qsem)
                for hh in range(QH):
                    for sh in range(2):
                        wsem = "wl" if sh == 0 else "wl2"
                        em.wait(SY, "castv", em.val("castv"))
                        em.wait(SY, "casts", em.val("casts") -
                                (0 if (hh == 0 and sh == 0) else 1))
                        em.dma(SY, scr_sb[:, sh * 512:(sh + 1) * 512],
                               w2_in[(q * QH + hh) * 128:(q * QH + hh + 1) * 128,
                                     sh * 512:(sh + 1) * 512], inc=wsem)
                        em.wait(SC, wsem)
                        em.emit(SC, lambda e, sg=sg, hh=hh, sh=sh: e.mul(
                            w2q_sb[:, sg, hh]
                            .rearrange("p d m -> p (d m)")[:, sh * 512:
                                                           (sh + 1) * 512],
                            scr_sb[:, sh * 512:(sh + 1) * 512], 0.5),
                            inc="casts")
                    em.wait(G, "casts")
                    em.dma(G, w2c_d[q, hh],
                           w2q_sb[:, sg, hh].rearrange("p d m -> p (d m)"),
                           inc=qsem)
                em.wait(SY, "castv", em.val("castv"))
                em.wait(SY, "casts", em.val("casts"))
                em.dma(SY, b1qstg_sb[:, sg, :],
                       b1r_in[:, q * QH * 128:(q + 1) * QH * 128], inc="wl")
                em.wait(V, "wl")
                em.emit(V, lambda e, sg=sg: e.tensor_copy(
                    b1q_sb[:, sg, :], b1qstg_sb[:, sg, :]), inc="castv")
                em.wait(G, "castv")
                em.dma(G, b1c_d[q], b1q_sb[:, sg, :].rearrange("p m -> p m"),
                       inc=qsem)
            else:
                em.wait(SY, "stg", em.val("stg"))
                em.wait(SY, "stg2", em.val("stg2"))
                em.wait(SY, "wl", em.val("wl"))
                em.dma(SY, w1q_sb[:, sg].rearrange("p c h m -> p c (h m)"),
                       w1c_d[q].rearrange("c p m -> p c m"), inc="wl")
                em.dma(SY, w2q_sb[:, sg].rearrange("p h d m -> p h (d m)"),
                       w2c_d[q].rearrange("h p m -> p h m"), inc="wl")
                em.dma(SY, b1q_sb[:, sg, :], b1c_d[q], inc="wl")
                em.wait(T, "wl", em.val("wl"))
            em.wait(T, "castv")
            em.wait(T, "casts")
            if q == 0:
                em.wait(T, "rowe")
            for t in range(TCH):
                k = q * TCH + t
                hs = k % 2
                sl = slice(t * BLK, (t + 1) * BLK)
                if q == 0:
                    em.wait(T, "xnr", xnr_at[t])
                em.wait(T, "hp", em.val("hp"))
                for grp in range(2):
                    pg = psB if grp == 0 else psC
                    for j in range(4):
                        hh = grp * 4 + j
                        pa = pg[:, j * 256:(j + 1) * 256]
                        em.emit(T, lambda e, sg=sg, hh=hh, pa=pa: e.matmul(
                            pa, b1q_sb[:, sg, hh * 128:(hh + 1) * 128],
                            onesbf_sb[:], start=True, stop=False))
                        for c in range(DCH):
                            em.emit(T, lambda e, sg=sg, hh=hh, c=c, sl=sl, pa=pa:
                                    e.matmul(pa, w1q_sb[:, sg, c, hh, :],
                                             xn_sb[:, c, sl],
                                             start=False, stop=(c == DCH - 1)),
                                    inc="hmm" if c == DCH - 1 else None)
                em.wait(SC, "hmm")
                em.wait(SC, "ymm", em.val("ymm") - 1)
                em.emit(SC, lambda e, hs=hs: e.activation(
                    h_sb[:, hs, 0:4].rearrange("p h m -> p (h m)"), psB[:],
                    AF.Gelu))
                em.emit(SC, lambda e, hs=hs: e.activation(
                    h_sb[:, hs, 4:8].rearrange("p h m -> p (h m)"), psC[:],
                    AF.Gelu), inc="hp")
                em.wait(T, "hp")
                for half in range(2):
                    em.wait(T, "xev", em.val("xev") - 1)
                    for c in range(half * 4, half * 4 + 4):
                        for hh in range(QH):
                            em.emit(T, lambda e, sg=sg, c=c, hh=hh, hs=hs:
                                    e.matmul(
                                        psA[:, c * 256:(c + 1) * 256],
                                        w2q_sb[:, sg, hh, c, :],
                                        h_sb[:, hs, hh, :],
                                        start=(hh == 0), stop=(hh == QH - 1)),
                                    inc="ymm" if (c % 4 == 3 and hh == QH - 1)
                                    else None)
                    em.wait(V, "ymm")
                    em.emit(V, lambda e, sl=sl, half=half: e.tensor_tensor(
                        x_sb[:, half * 4:half * 4 + 4, sl],
                        x_sb[:, half * 4:half * 4 + 4, sl],
                        psA[:, half * 1024:(half + 1) * 1024]
                        .rearrange("p (c m) -> p c m", c=4), op=ALU.add),
                        inc="xev")
        em.wait(V, "xev")
        em.wait(V, "casts", 1)
        for t in range(TCH):
            sl = slice(t * BLK, (t + 1) * BLK)
            for c in range(DCH):
                em.emit(V, lambda e, c=c, sl=sl: e.tensor_scalar_add(
                    x_sb[:, c, sl], x_sb[:, c, sl], b2h_sb[:, c:c + 1]),
                    inc="xnr")
        em.emit(V, lambda e: e.drain())


    def emit_attn0(em):
        xnr_base0 = None
        V, SC, T, G, SY = "vector", "scalar", "tensor", "gpsimd", "sync"
        xnr_base = em.val("xnr")
        for t in range(TCH):
            emit_ln_chunk(em, t, xnr_base)
        XNR_ALL = em.val("xnr")

        em.wait(SY, "xnr", XNR_ALL)
        em.wait(SY, "yv", em.val("yv"))
        for sem in ("ymm", "hmm", "hp", "xev"):
            em.wait(SY, sem, em.val(sem))
        em.dma(SY, scrA_sb[:, 0:512],
               maskd_in.rearrange("p a b -> p (a b)"), inc="wl")
        em.dma(SY, scrA_sb[:, 512:640], eye_in, inc="wl")
        em.wait(V, "wl")
        em.wait(V, "stg", em.val("stg"))
        em.wait(SC, "stg", em.val("stg"))
        em.wait(V, "stg2", em.val("stg2"))
        em.wait(SC, "stg2", em.val("stg2"))
        em.emit(V, lambda e: e.tensor_copy(
            maskd_sb[:].rearrange("p a b -> p (a b)"), scrA_sb[:, 0:512]),
            inc="castv")
        em.emit(V, lambda e: e.tensor_copy(eye_sb[:], scrA_sb[:, 512:640]),
                inc="castv")
        em.wait(SY, "castv", em.val("castv"))
        em.dma(SY, scrA_sb[:, 0:512],
               mv0_in.rearrange("p a b -> p (a b)"), inc="wl")
        em.dma(SY, scrA_sb[:, 512:1024],
               mv1_in.rearrange("p a b -> p (a b)"), inc="wl")
        em.wait(V, "wl")
        em.emit(V, lambda e: e.tensor_copy(
            mv0_sb[:].rearrange("p a b -> p (a b)"), scrA_sb[:, 0:512]),
            inc="castv")
        em.emit(V, lambda e: e.tensor_copy(
            mv1_sb[:].rearrange("p a b -> p (a b)"), scrA_sb[:, 512:1024]),
            inc="castv")
        em.emit(V, lambda e: e.memset(const_sb[:, 2:3], -M0), inc="castv")
        em.emit(V, lambda e: e.memset(const_sb[:, 3:4], 1.0 / 32.0), inc="castv")

        em.wait(T, "xnr", XNR_ALL)
        em.wait(T, "rowe")
        for m in range(3):
            w_in = [wq_in, wk_in, wv_in][m]
            ws = m % 2
            em.wait(SY, "qmm", em.val("qmm") - 32)
            for cin in range(DCH):
                em.wait(SY, "castv", em.val("castv"))
                em.dma(SY, scrA_sb[:], w_in[cin * 128:(cin + 1) * 128, :],
                       inc="wl")
                em.wait(V, "wl")
                em.emit(V, lambda e, ws=ws, cin=cin: e.tensor_copy(
                    wqkv_sb[:, ws, cin].rearrange("p a b -> p (a b)"),
                    scrA_sb[:]), inc="castv")
            em.wait(T, "castv")
            if m < 2:
                for tc in range(4):
                    ts = slice(tc * 512, (tc + 1) * 512)
                    if m == 1:
                        em.wait(V, "stg", em.val("stg"))
                    for c2 in range(DCH):
                        kq = em.val("qmm")
                        pa = psA[:, (kq % 4) * 512:(kq % 4 + 1) * 512]
                        em.wait(T, "qev", em.val("qev") - 3)
                        for cin in range(DCH):
                            em.emit(T, lambda e, ws=ws, c2=c2, cin=cin, ts=ts,
                                    pa=pa: e.matmul(
                                        pa, wqkv_sb[:, ws, cin, c2, :],
                                        xn_sb[:, cin, ts],
                                        start=(cin == 0), stop=(cin == DCH - 1)),
                                    inc="qmm" if cin == DCH - 1 else None)
                        em.wait(V, "qmm")
                        if m == 0:
                            em.emit(V, lambda e, c2=c2, ts=ts, pa=pa:
                                    e.tensor_copy(q_sb[:, c2, ts], pa),
                                    inc="qev")
                        else:
                            for jj in range(2):
                                em.emit(V, lambda e, c2=c2, jj=jj, pa=pa:
                                        e.tensor_copy(
                                            kring_sb[:, jj, c2, :],
                                            pa[:, jj * 256:(jj + 1) * 256]),
                                        inc="qev" if jj == 1 else None)
                    if m == 1:
                        em.wait(G, "qev")
                        for jj in range(2):
                            em.dma(G, kvin[2 * tc + jj][:, 0:2048],
                                   kring_sb[:, jj].rearrange("p a b -> p (a b)"),
                                   inc="stg")
            else:
                for sv in range(TCH):
                    vslot = sv % 2
                    em.wait(V, "stg", em.val("stg"))
                    for u in range(2):
                        for dvh in range(2):
                            pb = psB if dvh == 0 else psC
                            tok0 = sv * 256 + u * 128
                            em.wait(T, "qev", em.val("qev"))
                            for cin in range(DCH):
                                em.emit(T, lambda e, ws=ws, cin=cin, tok0=tok0,
                                        pb=pb, dvh=dvh: e.matmul(
                                            pb[:, 0:512],
                                            xn_sb[:, cin, tok0:tok0 + 128],
                                            wqkv_sb[:, ws, cin,
                                                    dvh * 4:(dvh + 1) * 4, :]
                                            .rearrange("p a b -> p (a b)"),
                                            start=(cin == 0),
                                            stop=(cin == DCH - 1)),
                                        inc="qmm" if cin == DCH - 1 else None)
                        em.wait(V, "qmm")
                        em.emit(V, lambda e, vslot=vslot, u=u: e.tensor_copy(
                            vring_sb[:, vslot, u, 0:512], psB[:, 0:512]))
                        em.emit(V, lambda e, vslot=vslot, u=u: e.tensor_copy(
                            vring_sb[:, vslot, u, 512:1024], psC[:, 0:512]),
                            inc="qev")
                    em.wait(G, "qev")
                    em.dma(G, kvin[sv][:, 2048:4096],
                           vring_sb[:, vslot].rearrange("p a b -> p (a b)"),
                           inc="stg")
        em.emit(V, lambda e: e.memset(pp_sb[:, 0, 0:32], 0.0), inc="fin")
        em.wait(G, "fin")
        for sv in range(MYB):
            em.dma(G, kvin[sv][:, 4096:KVW], pp_sb[:, 0, 0:32], inc="stg")
        em.wait(G, "stg")
        em.emit(G, lambda e: e.collective_compute(
            "AllGather", ALU.bypass,
            replica_groups=[[0, 1], [2, 3], [4, 5], [6, 7]],
            ins=[kvin.ap().opt()],
            outs=[kvg.ap().opt()]), inc="cc", inck=1)

        em.wait(SY, "cc", 1)
        em.wait(T, "qev", em.val("qev"))
        em.wait(T, "cc", 1)
        vno = 0
        for p in range(2):
            for cin in range(4):
                em.wait(SY, "casts", em.val("casts"))
                if p == 1:
                    em.wait(SY, "ymt", em.val("ymt"))
                em.dma(SY, scrA_sb[:],
                       wo_in[(4 * p + cin) * 128:(4 * p + cin + 1) * 128, :],
                       inc="wl")
                em.wait(SC, "wl")
                em.wait(SC, "stm", em.val("stm"))
                em.emit(SC, lambda e, cin=cin: e.copy(
                    wo_sb[:, cin].rearrange("p a b -> p (a b)"), scrA_sb[:]),
                    inc="casts")
            em.wait(T, "casts")
            for band in range(8):
                qs = slice(band * BLK, (band + 1) * BLK)
                vs = visits_of(band)
                for vi, (reg, j) in enumerate(vs):
                    r = vno % RING
                    sb_ = vno % 2
                    kvs = "kv0" if r == 0 else "kv1"
                    em.wait(SY, "avm", em.val("avm") - (RING - 1))
                    em.wait(SY, "pxp", em.val("pxp") - (RING - 1))
                    em.wait(SY, kvs, em.val(kvs))
                    slot = {"own": kvin[j], "r0": kvg[0, j],
                            "r1": kvg[1, j]}[reg]
                    em.dma(SY, kring_sb[:, r].rearrange("p a b -> p (a b)"),
                           slot[:, 0:2048], inc=kvs)
                    em.dma(SY, vring_sb[:, r].rearrange("p a b -> p (a b)"),
                           slot[:, 2048:4096], inc=kvs)
                    em.wait(T, kvs)
                    em.wait(T, "pxp", em.val("pxp") - 1)
                    mask = None
                    if reg == "own":
                        mask = maskd_sb
                    elif reg == "r0" and j == band:
                        mask = mv0_sb
                    elif reg == "r1" and j == band:
                        mask = mv1_sb
                    for ksub in range(2):
                        pt = psB[:, sb_ * 512 + ksub * 256:
                                 sb_ * 512 + (ksub + 1) * 256]
                        if mask is not None:
                            em.emit(T, lambda e, mask=mask, ksub=ksub, pt=pt:
                                    e.matmul(pt, eye_sb[:], mask[:, ksub, :],
                                             start=True, stop=False))
                        for cin in range(DCH):
                            first = (mask is None and cin == 0)
                            em.emit(T, lambda e, r=r, cin=cin, ksub=ksub, qs=qs,
                                    pt=pt, first=first:
                                    e.matmul(pt,
                                             kring_sb[:, r, cin,
                                                      ksub * 128:(ksub + 1) * 128],
                                             q_sb[:, cin, qs],
                                             start=first, stop=(cin == DCH - 1)),
                                    inc="smm" if (ksub == 1 and cin == DCH - 1)
                                    else None)
                    em.wait(SC, "smm")
                    em.wait(SC, "avm", em.val("avm") - 1)
                    em.emit(SC, lambda e, sb_=sb_: e.activation(
                        pp_sb[:, sb_, :], psB[:, sb_ * 512:(sb_ + 1) * 512],
                        AF.Exp, bias=const_sb[:, 2:3], scale=const_sb[:, 3:4]),
                        inc="pxp")
                    em.wait(T, "pxp")
                    if vi == 0:
                        em.wait(T, "fin", em.val("fin"))
                    for ksub in range(2):
                        for cdv in range(4):
                            h0 = 8 * p + 2 * cdv
                            em.emit(T, lambda e, r=r, ksub=ksub, cdv=cdv, h0=h0,
                                    sb_=sb_: e.matmul(
                                        psA[:, cdv * 512:cdv * 512 + 256],
                                        vring_sb[:, r, ksub,
                                                 (h0 // 2) * 128:(h0 // 2) * 128 + 128],
                                        pp_sb[:, sb_, ksub * 256:(ksub + 1) * 256],
                                        start=(vi == 0 and ksub == 0),
                                        stop=(vi == len(vs) - 1 and ksub == 1)))
                        em.emit(T, lambda e, r=r, ksub=ksub, sb_=sb_: e.matmul(
                            psC[0:1, 0:256], onescol_sb[:, 0:1],
                            pp_sb[:, sb_, ksub * 256:(ksub + 1) * 256],
                            start=(vi == 0 and ksub == 0),
                            stop=(vi == len(vs) - 1 and ksub == 1)),
                            inc="avm" if ksub == 1 else None)
                    vno += 1
                em.wait(V, "avm")
                em.emit(V, lambda e: e.tensor_copy(lrow_sb[0:1, :],
                                                   psC[0:1, 0:256]))
                em.emit(V, lambda e: e.drain())
                em.emit(V, lambda e: e.reciprocal(rl_sb[0:1, :], lrow_sb[0:1, :]),
                        inc="fin")
                em.wait(T, "fin")
                em.wait(T, "yv", em.val("yv"))
                em.emit(T, lambda e: e.matmul(psC[:, 768:1024], onesrow_sb[:],
                                              rl_sb[0:1, :], start=True, stop=True),
                        inc="ymt")
                em.wait(V, "ymt")
                em.emit(V, lambda e: e.tensor_copy(rlb_sb[:], psC[:, 768:1024]))
                em.emit(V, lambda e: e.drain())
                for cdv in range(4):
                    em.emit(V, lambda e, cdv=cdv: e.tensor_tensor(
                        onorm_sb[:, cdv, :], psA[:, cdv * 512:cdv * 512 + 256],
                        rlb_sb[:], op=ALU.mult),
                        inc="fin" if cdv == 3 else None)
                em.emit(V, lambda e: e.drain())
                em.wait(T, "fin")
                for c2 in range(DCH):
                    ys = psC[:, 512:768]
                    em.wait(T, "yv", em.val("yv"))
                    for cdv in range(4):
                        em.emit(T, lambda e, c2=c2, cdv=cdv, ys=ys: e.matmul(
                            ys, wo_sb[:, cdv, c2, :], onorm_sb[:, cdv, :],
                            start=(cdv == 0), stop=(cdv == 3)),
                            inc="ymt" if cdv == 3 else None)
                    em.wait(V, "ymt")
                    em.emit(V, lambda e, c2=c2, qs=qs, ys=ys: e.tensor_tensor(
                        x_sb[:, c2, qs], x_sb[:, c2, qs], ys, op=ALU.add),
                        inc="yv")
                em.emit(V, lambda e: e.drain())


    def emit_ffn1(em, first):
        V, SC, T, G, SY = "vector", "scalar", "tensor", "gpsimd", "sync"
        xnr_base = em.val("xnr")
        xnr_at = []
        for t in range(TCH):
            emit_ln_chunk(em, t, xnr_base)
            xnr_at.append(em.val("xnr"))
        XNR_ALL = em.val("xnr")
        for sem in ("smm", "avm", "ymt", "pxp", "qmm", "yv", "ymm", "hmm",
                    "hp", "xev", "fin", "qev"):
            em.wait(SY, sem, em.val(sem))
        em.wait(V, "stg", em.val("stg"))
        em.wait(SC, "stg", em.val("stg"))
        em.wait(V, "stg2", em.val("stg2"))
        em.wait(SC, "stg2", em.val("stg2"))
        if first:
            for cin in range(DCH):
                for sh in range(2):
                    wsem = "wl" if sh == 0 else "wl2"
                    em.wait(SY, "casts", em.val("casts") -
                            (0 if (cin == 0 and sh == 0) else 1))
                    em.dma(SY, scr_sb[:, sh * 512:(sh + 1) * 512],
                           wt_in[cin * 128:(cin + 1) * 128,
                                 sh * 512:(sh + 1) * 512], inc=wsem)
                    em.wait(SC, wsem)
                    em.emit(SC, lambda e, cin=cin, sh=sh: e.mul(
                        wt_sb[:, cin].rearrange("p a b -> p (a b)")
                        [:, sh * 512:(sh + 1) * 512],
                        scr_sb[:, sh * 512:(sh + 1) * 512], 0.5), inc="casts")
                em.wait(G, "casts")
                em.dma(G, wtc_d[cin],
                       wt_sb[:, cin].rearrange("p a b -> p (a b)"), inc="stg")
        else:
            em.wait(SY, "stg", em.val("stg"))
            em.wait(SY, "wl", em.val("wl"))
            em.dma(SY, wt_sb[:].rearrange("p c a b -> p c (a b)"),
                   wtc_d.rearrange("c p m -> p c m"), inc="wl")
            em.wait(T, "wl", em.val("wl"))
            em.wait(SC, "wl", em.val("wl"))
        em.wait(SY, "casts", em.val("casts"))
        em.wait(SY, "wl", em.val("wl"))
        em.dma(SY, scr_sb[:].rearrange("p (c m) -> p c m", c=DCH),
               ut_in.rearrange("(c p) m -> p c m", p=128), inc="wl")
        em.dma(SY, g16stg_sb[:], g16_in, inc="wl")
        em.dma(SY, eyef_sb[:], eye_in, inc="wl")
        em.wait(V, "wl")
        em.emit(V, lambda e: e.tensor_copy(
            ut_sb[:].rearrange("p c m -> p (c m)"), scr_sb[:]), inc="castv")
        em.emit(V, lambda e: e.tensor_copy(g16_sb[:], g16stg_sb[:]), inc="castv")
        em.wait(SY, "castv", em.val("castv"))
        em.dma(SY, scr_sb[:], vs_in, inc="wl")
        em.wait(SC, "wl")
        em.emit(SC, lambda e: e.mul(
            vs_sb[:].rearrange("p c m -> p (c m)"), scr_sb[:], 1.0 / 32.0),
            inc="casts")
        em.wait(SY, "casts", em.val("casts"))
        em.dma(SY, scr_sb[:, 0:64].rearrange("p (c m) -> p c m", c=DCH),
               gw_in.rearrange("(c p) m -> p c m", p=128), inc="wl")
        em.wait(V, "wl")
        em.emit(V, lambda e: e.tensor_copy(
            gw_sb[:].rearrange("p c m -> p (c m)"), scr_sb[:, 0:64]),
            inc="castv")
        em.emit(V, lambda e: e.memset(const_sb[:, 4:5], -8.0), inc="castv")
        em.wait(T, "wl", em.val("wl"))
        em.wait(T, "castv")
        em.wait(T, "casts")
        em.wait(T, "rowe")

        for u in range(MYT // 128):
            tok0 = u * 128
            ub = u % 2
            em.wait(T, "xnr", xnr_at[u // 2])
            em.wait(T, "pxp", em.val("pxp"))
            for cin in range(DCH):
                em.emit(T, lambda e, cin=cin, tok0=tok0: e.matmul(
                    psC[:, 0:8], xn_sb[:, cin, tok0:tok0 + 128],
                    gw_sb[:, cin, :], start=(cin == 0), stop=(cin == DCH - 1)),
                    inc="qmm" if cin == DCH - 1 else None)
            em.wait(SC, "qmm")
            em.wait(SC, "fin", em.val("fin"))
            em.emit(SC, lambda e, ub=ub: e.activation(
                gte_sb[:, ub, :], psC[:, 0:8], AF.Exp,
                bias=const_sb[:, 4:5]), inc="pxp")
            em.wait(V, "pxp")
            em.emit(V, lambda e, ub=ub: e.tensor_reduce(
                gtr_sb[:, ub:ub + 1], gte_sb[:, ub, :],
                axis=mybir.AxisListType.X, op=ALU.add))
            em.emit(V, lambda e: e.drain())
            em.emit(V, lambda e, ub=ub: e.reciprocal(
                gtr_sb[:, 2 + ub:3 + ub], gtr_sb[:, ub:ub + 1]))
            em.emit(V, lambda e: e.drain())
            em.emit(V, lambda e, ub=ub: e.tensor_scalar(
                gtn_sb[:, ub, :], gte_sb[:, ub, :], gtr_sb[:, 2 + ub:3 + ub],
                None, op0=ALU.mult), inc="fin")
            em.wait(T, "fin")
            em.wait(T, "qev", em.val("qev"))
            em.emit(T, lambda e, ub=ub: e.transpose(
                psC[0:8, 512:640], gtn_sb[:, ub, :], eyef_sb[:]), inc="ymt")
            em.wait(V, "ymt")
            em.emit(V, lambda e, tok0=tok0: e.tensor_copy(
                gates_sb[0:8, tok0:tok0 + 128], psC[0:8, 512:640]), inc="qev")

        for j in range(4):
            js = slice(j * 512, (j + 1) * 512)
            jb = j % 2
            em.wait(T, "qev", em.val("qev"))
            em.emit(T, lambda e, js=js: e.matmul(
                psB[:, 0:512], g16_sb[:], gates_sb[0:8, js],
                start=True, stop=True), inc="smm")
            for cin in range(DCH):
                em.emit(T, lambda e, cin=cin, js=js: e.matmul(
                    psB[:, 512:1024], ut_sb[:, cin, :], xn_sb[:, cin, js],
                    start=(cin == 0), stop=(cin == DCH - 1)),
                    inc="hmm" if cin == DCH - 1 else None)
            em.wait(V, "smm")
            em.emit(V, lambda e, jb=jb: e.tensor_copy(gb_sb[:, jb, :],
                                                      psB[:, 0:512]))
            em.emit(V, lambda e: e.drain())
            em.wait(V, "hmm")
            em.wait(V, "ymm", em.val("ymm") - 4)
            em.emit(V, lambda e, jb=jb: e.tensor_tensor(
                tg_sb[:, jb, :], psB[:, 512:1024], gb_sb[:, jb, :],
                op=ALU.mult), inc="fin")
            for u2 in range(2):
                t = j * 2 + u2
                sl = slice(t * BLK, (t + 1) * BLK)
                em.wait(T, "fin")
                for half in range(2):
                    em.wait(T, "xev", em.val("xev") - 1)
                    for c2 in range(half * 4, half * 4 + 4):
                        em.emit(T, lambda e, c2=c2, sl=sl: e.matmul(
                            psA[:, c2 * 256:(c2 + 1) * 256],
                            wt_sb[:, 0, c2, :], xn_sb[:, 0, sl],
                            start=True, stop=False))
                        for cin in range(1, DCH):
                            em.emit(T, lambda e, c2=c2, cin=cin, sl=sl: e.matmul(
                                psA[:, c2 * 256:(c2 + 1) * 256],
                                wt_sb[:, cin, c2, :], xn_sb[:, cin, sl],
                                start=False, stop=False))
                        em.emit(T, lambda e, c2=c2, jb=jb, u2=u2: e.matmul(
                            psA[:, c2 * 256:(c2 + 1) * 256],
                            vs_sb[:, c2, :],
                            tg_sb[:, jb, u2 * 256:(u2 + 1) * 256],
                            start=False, stop=True),
                            inc="ymm" if c2 % 4 == 3 else None)
                    em.wait(V, "ymm")
                    em.emit(V, lambda e, sl=sl, half=half: e.tensor_tensor(
                        x_sb[:, half * 4:half * 4 + 4, sl],
                        x_sb[:, half * 4:half * 4 + 4, sl],
                        psA[:, half * 1024:(half + 1) * 1024]
                        .rearrange("p (c m) -> p c m", c=4), op=ALU.add),
                        inc="xev")
        em.emit(V, lambda e: e.drain())


    def emit_attn1(em):
        V, SC, T, G, SY = "vector", "scalar", "tensor", "gpsimd", "sync"
        xnr_base = em.val("xnr")
        xnr_at = []
        for t in range(TCH):
            emit_ln_chunk(em, t, xnr_base)
            xnr_at.append(em.val("xnr"))
        XNR_ALL = em.val("xnr")

        em.wait(SY, "xnr", XNR_ALL)
        em.wait(SY, "yv", em.val("yv"))
        for sem in ("ymm", "hmm", "hp", "xev", "smm", "avm", "ymt", "pxp",
                    "qmm", "fin", "qev"):
            em.wait(SY, sem, em.val(sem))
        em.wait(V, "stg", em.val("stg"))
        em.wait(SC, "stg", em.val("stg"))
        em.wait(V, "stg2", em.val("stg2"))
        em.wait(SC, "stg2", em.val("stg2"))
        em.dma(SY, scrA_sb[:, 0:512],
               maskd_in.rearrange("p a b -> p (a b)"), inc="wl")
        em.dma(SY, scrA_sb[:, 512:640], eye_in, inc="wl")
        em.wait(V, "wl")
        em.emit(V, lambda e: e.tensor_copy(
            maskd_sb[:].rearrange("p a b -> p (a b)"), scrA_sb[:, 0:512]),
            inc="castv")
        em.emit(V, lambda e: e.tensor_copy(eye_sb[:], scrA_sb[:, 512:640]),
                inc="castv")
        em.wait(SY, "castv", em.val("castv"))
        em.dma(SY, scrA_sb[:, 0:512],
               mv0_in.rearrange("p a b -> p (a b)"), inc="wl")
        em.dma(SY, scrA_sb[:, 512:1024],
               mv1_in.rearrange("p a b -> p (a b)"), inc="wl")
        em.wait(V, "wl")
        em.emit(V, lambda e: e.tensor_copy(
            mv0_sb[:].rearrange("p a b -> p (a b)"), scrA_sb[:, 0:512]),
            inc="castv")
        em.emit(V, lambda e: e.tensor_copy(
            mv1_sb[:].rearrange("p a b -> p (a b)"), scrA_sb[:, 512:1024]),
            inc="castv")
        for piece in range(2):
            em.wait(SY, "castv", em.val("castv"))
            em.dma(SY, scrA_sb[:], cosi_in[:, piece * 1024:(piece + 1) * 1024],
                   inc="wl")
            em.wait(V, "wl")
            em.emit(V, lambda e, piece=piece: e.tensor_copy(
                cosi_sb[:, piece * 1024:(piece + 1) * 1024], scrA_sb[:]),
                inc="castv")
            em.wait(SY, "castv", em.val("castv"))
            em.dma(SY, scrA_sb[:], sini_in[:, piece * 1024:(piece + 1) * 1024],
                   inc="wl")
            em.wait(V, "wl")
            em.emit(V, lambda e, piece=piece: e.tensor_copy(
                sini_sb[:, piece * 1024:(piece + 1) * 1024], scrA_sb[:]),
                inc="castv")
        em.wait(SY, "castv", em.val("castv"))
        em.dma(SY, scrA_sb[:, 0:128], prot_in, inc="wl")
        em.wait(V, "wl")
        em.emit(V, lambda e: e.tensor_copy(prot_sb[:], scrA_sb[:, 0:128]),
                inc="castv")
        em.emit(V, lambda e: e.memset(const_sb[:, 5:6], -M1), inc="castv")
        em.emit(V, lambda e: e.memset(const_sb[:, 6:7], 0.125), inc="castv")
        em.emit(V, lambda e: e.memset(vstg1_sb[:, :, :, 64:65], 1.0), inc="castv")

        em.wait(T, "castv")
        em.wait(T, "rowe")
        for m in range(3):
            ws = m % 2
            em.wait(SY, "qmm", em.val("qmm") - 32)
            for cin in range(DCH):
                em.wait(SY, "castv", em.val("castv"))
                em.dma(SY, scrA_sb[:],
                       wqkvt_in[cin * 128:(cin + 1) * 128,
                                m * D:(m + 1) * D], inc="wl")
                em.wait(V, "wl")
                em.emit(V, lambda e, ws=ws, cin=cin: e.tensor_copy(
                    wqkv_sb[:, ws, cin].rearrange("p a b -> p (a b)"),
                    scrA_sb[:]), inc="castv")
            em.wait(T, "castv")
            if m < 2:
                for tc in range(4):
                    ts = slice(tc * 512, (tc + 1) * 512)
                    em.wait(T, "xnr", xnr_at[2 * tc + 1])
                    if m == 1:
                        em.wait(V, "stg", em.val("stg"))
                    for c2 in range(DCH):
                        kq = em.val("qmm")
                        pa = psA[:, (kq % 2) * 512:(kq % 2) * 512 + 512]
                        em.wait(T, "qev", em.val("qev") - 3)
                        for cin in range(DCH):
                            em.emit(T, lambda e, ws=ws, c2=c2, cin=cin, ts=ts,
                                    pa=pa: e.matmul(
                                        pa, wqkv_sb[:, ws, cin, c2, :],
                                        xn_sb[:, cin, ts],
                                        start=(cin == 0), stop=(cin == DCH - 1)),
                                    inc="qmm" if cin == DCH - 1 else None)
                        dst = (lambda e, c2=c2, ts=ts: q_sb[:, c2, ts]) if m == 0                             else None
                        em.wait(V, "qmm")
                        em.emit(V, lambda e, pa=pa: e.tensor_copy(
                            ropet_sb[:], pa), inc="qev")
                        em.wait(T, "qev")
                        em.wait(T, "fin", em.val("fin") - 1)
                        em.emit(T, lambda e, kq=kq: e.matmul(
                            psB[:, (kq % 2) * 512:(kq % 2) * 512 + 512],
                            prot_sb[:], ropet_sb[:], start=True, stop=True),
                            inc="smm")
                        em.wait(V, "smm")
                        em.emit(V, lambda e, ts=ts: e.tensor_tensor(
                            ropet_sb[:], ropet_sb[:], cosi_sb[:, ts],
                            op=ALU.mult))
                        em.emit(V, lambda e: e.drain())
                        if m == 0:
                            tgt = lambda e, c2=c2, ts=ts: q_sb[:, c2, ts]
                        else:
                            tgt = lambda e, c2=c2: kring_sb[:, 0, c2, :]                                 .rearrange("p b -> p b") if False else None
                        if m == 0:
                            em.emit(V, lambda e, c2=c2, ts=ts, kq=kq:
                                    e.scalar_tensor_tensor(
                                        q_sb[:, c2, ts],
                                        psB[:, (kq % 2) * 512:(kq % 2) * 512 + 512],
                                        1.0, sini_sb[:, ts],
                                        op0=ALU.mult, op1=ALU.mult))
                            em.emit(V, lambda e: e.drain())
                            em.emit(V, lambda e, c2=c2, ts=ts: e.tensor_tensor(
                                q_sb[:, c2, ts], q_sb[:, c2, ts], ropet_sb[:],
                                op=ALU.add), inc="fin")
                        else:
                            for jj in range(2):
                                em.emit(V, lambda e, c2=c2, ts=ts, kq=kq, jj=jj:
                                        e.scalar_tensor_tensor(
                                            kring_sb[:, jj, c2, :],
                                            psB[:, (kq % 2) * 512 + jj * 256:
                                                (kq % 2) * 512 + (jj + 1) * 256],
                                            1.0,
                                            sini_sb[:, ts.start + jj * 256:
                                                    ts.start + (jj + 1) * 256],
                                            op0=ALU.mult, op1=ALU.mult))
                            em.emit(V, lambda e: e.drain())
                            for jj in range(2):
                                em.emit(V, lambda e, c2=c2, jj=jj: e.tensor_tensor(
                                    kring_sb[:, jj, c2, :], kring_sb[:, jj, c2, :],
                                    ropet_sb[:, jj * 256:(jj + 1) * 256],
                                    op=ALU.add), inc="fin" if jj == 1 else None)
                        em.emit(V, lambda e: e.drain())
                    if m == 1:
                        em.wait(G, "fin", em.val("fin"))
                        for jj in range(2):
                            em.dma(G, kvin[2 * tc + jj][:, 0:2048],
                                   kring_sb[:, jj].rearrange("p a b -> p (a b)"),
                                   inc="stg")
            else:
                for sv in range(TCH):
                    em.wait(T, "xnr", xnr_at[sv])
                    em.wait(V, "stg", em.val("stg"))
                    for u in range(2):
                        for dvh in range(2):
                            pb = psB if dvh == 0 else psC
                            tok0 = sv * 256 + u * 128
                            em.wait(T, "qev", em.val("qev"))
                            for cin in range(DCH):
                                em.emit(T, lambda e, ws=ws, cin=cin, tok0=tok0,
                                        pb=pb, dvh=dvh: e.matmul(
                                            pb[:, 0:512],
                                            xn_sb[:, cin, tok0:tok0 + 128],
                                            wqkv_sb[:, ws, cin,
                                                    dvh * 4:(dvh + 1) * 4, :]
                                            .rearrange("p a b -> p (a b)"),
                                            start=(cin == 0),
                                            stop=(cin == DCH - 1)),
                                        inc="qmm" if cin == DCH - 1 else None)
                        em.wait(V, "qmm")
                        em.emit(V, lambda e, u=u: e.tensor_copy(
                            vstg1_sb[:, u, 0:8, 0:64],
                            psB[:, 0:512].rearrange("p (a b) -> p a b", a=8)))
                        em.emit(V, lambda e, u=u: e.tensor_copy(
                            vstg1_sb[:, u, 8:16, 0:64],
                            psC[:, 0:512].rearrange("p (a b) -> p a b", a=8)),
                            inc="qev")
                    em.wait(G, "qev")
                    em.dma(G, kvin[sv][:, 2048:KVW],
                           vstg1_sb[:].rearrange("p a b c -> p (a b c)"),
                           inc="stg")
        em.wait(G, "stg")
        em.emit(G, lambda e: e.collective_compute(
            "AllGather", ALU.bypass,
            replica_groups=[[0, 1], [2, 3], [4, 5], [6, 7]],
            ins=[kvin.ap().opt()],
            outs=[kvg.ap().opt()]), inc="cc", inck=1)

        em.wait(SY, "cc", em.val("cc"))
        em.wait(T, "qev", em.val("qev"))
        em.wait(T, "cc", em.val("cc"))
        vno = 0
        for pr in range(8):
            for band in range(8):
                qs = slice(band * BLK, (band + 1) * BLK)
                vs = visits_of(band)
                bb = band % 2
                nv = len(vs)
                kvs = "kv0" if bb == 0 else "kv1"
                guard = em.val("avm") - (len(visits_of(band - 1))
                                         if band > 0 else 0)
                em.wait(SY, "avm", guard)
                em.wait(SY, kvs, em.val(kvs))
                segs = [("own", band, 0, 1), ("r0", 0, 1, L0[band]),
                        ("r1", 0, 1 + L0[band], L1[band])]
                for reg, j0, at_, cnt in segs:
                    if cnt == 0:
                        continue
                    if reg == "own":
                        em.dma(SY, kband_sb[:, bb, at_],
                               kvin[j0][:, pr * 256:(pr + 1) * 256], inc=kvs)
                        em.dma(SY, vband_sb[:, bb, at_],
                               kvin[j0][:, 2048:KVW].rearrange(
                                   "p (a b) -> p a b", a=2)[:, :,
                                   pr * 130:(pr + 1) * 130], inc=kvs)
                    else:
                        rr = 0 if reg == "r0" else 1
                        em.dma(SY, kband_sb[:, bb, at_:at_ + cnt],
                               kvg[rr, j0:j0 + cnt][:, :,
                                                    pr * 256:(pr + 1) * 256]
                               .rearrange("s p b -> p s b"), inc=kvs)
                        for cc2 in range(cnt):
                            em.dma(SY, vband_sb[:, bb, at_ + cc2],
                                   kvg[rr, j0 + cc2][:, 2048:KVW]
                                   .rearrange("p (a b) -> p a b", a=2)
                                   [:, :, pr * 130:(pr + 1) * 130], inc=kvs)
                em.wait(T, kvs)
                base = vno

                def emit_av(vi, tail):
                    rr_ = (base + vi) % 2
                    if tail:
                        em.wait(T, "pxp")
                    else:
                        em.wait(T, "pxp", em.val("pxp") - 1)
                    if vi == 0:
                        em.wait(T, "fin", em.val("fin"))
                    for hh in range(2):
                        ob = psA[:, hh * 512: hh * 512 + 256]
                        for ksub in range(2):
                            em.emit(T, lambda e, bb=bb, vsl=vi, hh=hh,
                                    ksub=ksub, r=rr_, ob=ob: e.matmul(
                                        ob[0:65, :],
                                        vband_sb[:, bb, vsl, ksub,
                                                 hh * 65:(hh + 1) * 65],
                                        pp1_sb[:, r, hh * 512 + ksub * 256:
                                               hh * 512 + (ksub + 1) * 256],
                                        start=(vi == 0 and ksub == 0),
                                        stop=(vi == nv - 1 and ksub == 1)),
                                    inc="avm" if (hh == 1 and ksub == 1)
                                    else None)

                for vi, (reg, j) in enumerate(vs):
                    r = vno % 2
                    em.wait(T, "pxp", em.val("pxp") - 1)
                    mask = None
                    if reg == "own":
                        mask = maskd_sb
                    elif reg == "r0" and j == band:
                        mask = mv0_sb
                    elif reg == "r1" and j == band:
                        mask = mv1_sb
                    vsl = vi
                    sbk = psB if (vno % 2 == 0) else psC
                    for hh in range(2):
                        for ksub in range(2):
                            pt = sbk[:, hh * 512 + ksub * 256:
                                     hh * 512 + (ksub + 1) * 256]
                            if mask is not None:
                                em.emit(T, lambda e, mask=mask, ksub=ksub, pt=pt:
                                        e.matmul(pt, eye_sb[:], mask[:, ksub, :],
                                                 start=True, stop=False))
                            em.emit(T, lambda e, bb=bb, vsl=vsl, hh=hh,
                                    ksub=ksub, qs=qs, pr=pr, pt=pt,
                                    first=(mask is None):
                                    e.matmul(
                                        pt,
                                        kband_sb[0 + 64 * hh:64 + 64 * hh, bb,
                                                 vsl,
                                                 ksub * 128:(ksub + 1) * 128],
                                        q_sb[0 + 64 * hh:64 + 64 * hh, pr, qs],
                                        start=first, stop=True,
                                        tile_position=(64 * hh, 0)),
                                    inc="smm" if (hh == 1 and ksub == 1)
                                    else None)
                    em.wait(SC, "smm")
                    em.wait(SC, "avm", em.val("avm"))
                    em.emit(SC, lambda e, r=r, sbk=sbk: e.activation(
                        pp1_sb[:, r, :], sbk[:],
                        AF.Exp, bias=const_sb[:, 5:6], scale=const_sb[:, 6:7]),
                        inc="pxp")
                    if vi > 0:
                        emit_av(vi - 1, False)
                    vno += 1
                emit_av(nv - 1, True)
                em.wait(V, "avm")
                em.wait(V, "stg", em.val("stg"))
                em.emit(V, lambda e: e.tensor_copy(onorm_sb[0:65, 0, :],
                                                   psA[0:65, 0:256]))
                em.emit(V, lambda e: e.tensor_copy(onorm_sb[0:65, 1, :],
                                                   psA[0:65, 512:768]),
                        inc="fin")
                em.wait(G, "fin")
                em.dma(G, onorm_d[pr, band, 0], onorm_sb[0:65, 0, :],
                       inc="stg")
                em.dma(G, onorm_d[pr, band, 1], onorm_sb[0:65, 1, :],
                       inc="stg")

        em.wait(SY, "smm", em.val("smm"))
        em.wait(SY, "avm", em.val("avm"))
        em.wait(SY, "pxp", em.val("pxp"))
        em.wait(SY, "stg", em.val("stg"))
        em.dma(SY, bc16_sb[:], bc16_in, inc="wl")
        for cin in range(DCH):
            em.wait(SY, "casts", em.val("casts"))
            em.dma(SY, scrA_sb[:], wot_in[cin * 128:(cin + 1) * 128, :], inc="wl")
            em.wait(SC, "wl")
            em.emit(SC, lambda e, cin=cin: e.copy(
                wof_sb[:, cin].rearrange("p a b -> p (a b)"), scrA_sb[:]),
                inc="casts")
        em.wait(G, "stg", em.val("stg"))
        em.wait(T, "wl", em.val("wl"))
        em.wait(T, "casts")
        for band in range(8):
            qs = slice(band * BLK, (band + 1) * BLK)
            em.wait(SY, "ymm", em.val("ymm"))
            em.wait(SY, "kv0", em.val("kv0"))
            for cin in range(DCH):
                for h2 in range(2):
                    em.dma(SY, onld_sb[64 * h2:64 * h2 + 64, cin, :],
                           onorm_d[cin, band, h2][0:64, :], inc="kv0")
            em.wait(G, "kvl", em.val("kvl"))
            em.wait(G, "fin", em.val("fin"))
            for s8 in range(8):
                em.dma(G, lld_sb[2 * s8:2 * s8 + 2, 0, :],
                       onorm_d[s8, band][:, 64, :], inc="kvl")
            em.wait(V, "kv0")
            em.wait(V, "kvl")
            em.emit(V, lambda e: e.reciprocal(lld_sb[:, 1, :], lld_sb[:, 0, :]),
                    inc="fin")
            em.emit(V, lambda e: e.drain())
            for cin in range(DCH):
                em.wait(T, "fin")
                em.emit(T, lambda e, cin=cin: e.matmul(
                    psB[:, 0:256], bc16_sb[:, cin, :],
                    lld_sb[:, 1, :], start=True, stop=True), inc="ymt")
                em.wait(V, "ymt")
                em.emit(V, lambda e: e.tensor_copy(rlbb_sb[:], psB[:, 0:256]))
                em.emit(V, lambda e: e.drain())
                em.emit(V, lambda e, cin=cin: e.tensor_tensor(
                    onf_sb[:, cin, :], onld_sb[:, cin, :], rlbb_sb[:],
                    op=ALU.mult), inc="fin")
                em.emit(V, lambda e: e.drain())
            em.wait(T, "fin")
            em.wait(T, "xev", em.val("xev"))
            for c2 in range(DCH):
                for cin in range(DCH):
                    em.emit(T, lambda e, c2=c2, cin=cin: e.matmul(
                        psA[:, c2 * 256:(c2 + 1) * 256],
                        wof_sb[:, cin, c2, :], onf_sb[:, cin, :],
                        start=(cin == 0), stop=(cin == DCH - 1)),
                        inc="ymm" if (c2 == DCH - 1 and cin == DCH - 1)
                        else None)
            em.wait(V, "ymm")
            em.emit(V, lambda e, qs=qs: e.tensor_tensor(
                x_sb[:, :, qs], x_sb[:, :, qs],
                psA[:].rearrange("p (c m) -> p c m", c=DCH), op=ALU.add),
                inc="xev")
        em.emit(V, lambda e: e.drain())


    def body(em: E):
        V, SC, T, G, SY = "vector", "scalar", "tensor", "gpsimd", "sync"
        for c in range(DCH):
            em.dma(SY, x_sb[:, c, :], x_in[c], inc="ld")
        em.dma(SY, b2stg_sb[:], b2_in, inc="ld")
        em.emit(V, lambda e: e.memset(const_sb[:, 1:2], 1e-5), inc="castv")
        em.emit(V, lambda e: e.memset(onesrep_sb[:], 1.0 / D), inc="castv")
        em.emit(V, lambda e: e.memset(onesbf_sb[:], 1.0), inc="castv")
        em.emit(V, lambda e: e.memset(onesrow_sb[:], 1.0), inc="castv")
        em.emit(V, lambda e: e.memset(onescol_sb[:], 1.0), inc="castv")
        em.wait(SC, "ld", em.val("ld"))
        em.emit(SC, lambda e: e.mul(b2h_sb[:], b2stg_sb[:], 0.5), inc="casts")
        em.wait(T, "ld", em.val("ld"))
        em.wait(T, "castv", 5)
        em.wait(SC, "castv", 5)
        em.wait(V, "ld", em.val("ld"))

        emit_ffn0(em, True)
        if upto >= 2:
            emit_attn0(em)
        if upto >= 3:
            emit_ffn0(em, False)
        if upto >= 4:
            emit_ffn1(em, True)
        if upto >= 5:
            emit_attn1(em)
        if upto >= 6:
            emit_ffn1(em, False)

        em.wait(SY, "xnr")
        em.wait(SY, "yv", em.val("yv"))
        em.wait(SY, "xev", em.val("xev"))
        for c in range(DCH):
            em.dma(SY, y_out[c], x_sb[:, c, :], inc="out")
        em.wait(SY, "out")

    with nc.Block() as block:
        @block.vector
        def _(eng):
            body(E(nc, "vector", sems))

        @block.scalar
        def _(eng):
            body(E(nc, "scalar", sems))

        @block.tensor
        def _(eng):
            body(E(nc, "tensor", sems))

        @block.gpsimd
        def _(eng):
            body(E(nc, "gpsimd", sems))

        @block.sync
        def _(eng):
            body(E(nc, "sync", sems))

    est.close()
    return nc



def core_idx(core):
    pr = core % 2
    return np.concatenate([np.arange(blk * BLK, (blk + 1) * BLK)
                           for blk in PAT[pr]])


def make_masks():
    j = np.arange(BLK)[None, :]
    pp = np.arange(128)[:, None]
    d0 = np.where(pp <= j, 0.0, NEG).astype(np.float32)
    d1 = np.where(pp + 128 <= j, 0.0, NEG).astype(np.float32)
    maskd = np.ascontiguousarray(np.stack([d0, d1], axis=1))
    full = np.full((128, 2, BLK), NEG, np.float32)
    zero = np.zeros((128, 2, BLK), np.float32)
    return maskd, full, zero


def stage_inputs(d):
    x = np.asarray(d["L0_attn_x"], np.float32)
    w1 = np.ascontiguousarray(np.asarray(d["L0_ffn_w1"], np.float32))
    w2 = np.ascontiguousarray(np.asarray(d["L0_ffn_w2"], np.float32))
    b1 = np.asarray(d["L0_ffn_b1"], np.float32)
    b2 = np.asarray(d["L0_ffn_b2"], np.float32)
    maskd, full, zero = make_masks()
    prot = np.zeros((128, 128), np.float32)
    for blk0 in (0, 64):
        for i in range(32):
            prot[blk0 + 2 * i + 1, blk0 + 2 * i] = -1.0
            prot[blk0 + 2 * i, blk0 + 2 * i + 1] = 1.0
    bc16 = np.zeros((16, 8, 128), np.float32)
    for cin8 in range(8):
        bc16[2 * cin8, cin8, 0:64] = 1.0
        bc16[2 * cin8 + 1, cin8, 64:128] = 1.0
    maps = []
    for core in range(8):
        b = core // 2
        pr = core % 2
        idx = core_idx(core)
        inv = 10000.0 ** (-np.arange(0, 64, 2, dtype=np.float64) / 64.0)
        ang = idx[None, :].astype(np.float64) * inv[:, None]
        cos64 = np.repeat(np.cos(ang), 2, axis=0)
        sin64 = np.repeat(np.sin(ang), 2, axis=0)
        rope_c = np.ascontiguousarray(
            np.concatenate([cos64, cos64], axis=0).astype(np.float32))
        rope_s = np.ascontiguousarray(
            np.concatenate([sin64, sin64], axis=0).astype(np.float32))
        xs = np.ascontiguousarray(x[b][idx].T).reshape(DCH, 128, MYT)
        maps.append({
            "x_in": xs,
            "w1": w1,
            "w2": w2,
            "b1r": np.ascontiguousarray(b1.reshape(1, 4 * D)),
            "b2": np.ascontiguousarray(b2.reshape(DCH, 128).T),
            "wq": np.ascontiguousarray(np.asarray(d["L0_attn_Wq"], np.float32)),
            "wk": np.ascontiguousarray(np.asarray(d["L0_attn_Wk"], np.float32)),
            "wv": np.ascontiguousarray(np.asarray(d["L0_attn_Wv"], np.float32)),
            "wo": np.ascontiguousarray(np.asarray(d["L0_attn_Wo"], np.float32)),
            "maskd": maskd,
            "mv0": full if pr == 0 else zero,
            "mv1": zero if pr == 0 else full,
            "eye": np.eye(128, dtype=np.float32),
            "wt": np.ascontiguousarray(np.asarray(d["L1_ffn_weight"], np.float32).T),
            "ut": np.ascontiguousarray(
                np.asarray(d["L1_ffn_lora_U"], np.float32).reshape(128, D).T),
            "vs": np.ascontiguousarray(
                np.asarray(d["L1_ffn_lora_V"], np.float32).transpose(0, 2, 1)
                .reshape(128, D)),
            "gw": np.ascontiguousarray(np.asarray(d["L1_ffn_gate_w"], np.float32).T),
            "g16": np.ascontiguousarray(
                np.repeat(np.eye(8, dtype=np.float32), 16, axis=1)),
            "wqkvt": np.ascontiguousarray(
                np.asarray(d["L1_attn_W_qkv"], np.float32).T),
            "wot": np.ascontiguousarray(np.asarray(d["L1_attn_W_o"], np.float32).T),
            "cosi": rope_c, "sini": rope_s,
            "prot": prot,
            "bc16": bc16,
        })
    return maps


def gather_output(results):
    out = np.zeros((4, S, D), np.float32)
    for core in range(8):
        b = core // 2
        idx = core_idx(core)
        y = results[core]["y_out"].reshape(D, MYT).T
        out[b][idx] = y
    return out


from concourse.bass_utils import run_bass_kernel_spmd

_NC_CACHE = []


def kernel(**inputs):
    maps = stage_inputs(inputs)
    if not _NC_CACHE:
        _NC_CACHE.append(build_program(6))
    nc = _NC_CACHE[0]
    res = run_bass_kernel_spmd(nc, maps, core_ids=list(range(8)))
    return gather_output(res.results)


# version 06
# speedup vs baseline: 1.2914x; 1.0485x over previous
import numpy as np
from contextlib import ExitStack
import concourse.bass as bass
import concourse.mybir as mybir

F32 = mybir.dt.float32
BF16 = mybir.dt.bfloat16
AF = mybir.ActivationFunctionType
ALU = mybir.AluOpType

D = 1024
S = 4096
BLK = 256
NBLK = S // BLK
MYB = 8
MYT = MYB * BLK
DCH = 8
HCH = 32
TCH = MYT // BLK
M0 = 12.0
M1 = 26.0
NEG = -1.0e30

PAT = [[0, 3, 4, 7, 8, 11, 12, 15], [1, 2, 5, 6, 9, 10, 13, 14]]
C1 = [1, 1, 3, 3, 5, 5, 7, 7]
C0 = [0, 2, 2, 4, 4, 6, 6, 8]
L0 = [max(i, C1[i]) for i in range(8)]
L1 = [max(C0[i], i) for i in range(8)]
RING = 2

SEM_NAMES = ["ld", "wl", "castv", "casts", "sq", "stm", "rowe", "rowms",
             "rowmv", "xnr", "hmm", "hp", "ymm", "xev", "out", "wl2",
             "qmm", "qev", "stg", "stg2", "cc", "kv0", "kv1", "kvl", "smm", "pxp", "avm",
             "fin", "ymt", "yv"]


def visits_of(band):
    vs = [("own", band)]
    vs += [("r0", j) for j in range(L0[band])]
    vs += [("r1", j) for j in range(L1[band])]
    return vs


class E:

    def __init__(self, nc, engname, sems):
        self.nc = nc
        self.engname = engname
        self.engines = {"vector": nc.vector, "scalar": nc.scalar,
                        "tensor": nc.tensor, "gpsimd": nc.gpsimd, "sync": nc.sync}
        self.semh = sems
        self.cnt = {k: 0 for k in sems}

    def _fin(self, ins, inc, inck):
        if inc is not None:
            if ins is not None:
                ins.then_inc(self.semh[inc], inck)
            self.cnt[inc] += inck
        return ins

    def emit(self, engine, fn, inc=None, inck=1):
        ins = fn(self.engines[engine]) if self.engname == engine else None
        return self._fin(ins, inc, inck)

    def dma(self, engine, out, in_, inc=None):
        ins = None
        if self.engname == engine:
            ins = self.engines[engine].dma_start(out=out, in_=in_)
        return self._fin(ins, inc, 16 if inc else 0)

    def wait(self, engine, sem, val=None):
        if val is None:
            val = self.cnt[sem]
        if val <= 0:
            return
        if self.engname == engine:
            self.engines[engine].wait_ge(self.semh[sem], val)

    def val(self, sem):
        return self.cnt[sem]


def build_program(upto=2):
    QH = HCH // 4
    nc = bass.Bass("TRN2", target_bir_lowering=False, debug=False)

    def din(name, shape, dtype=F32):
        return nc.dram_tensor(name, shape, dtype, kind="ExternalInput").ap()

    x_in = din("x_in", [DCH, 128, MYT])
    w1_in = din("w1", [D, 4 * D])
    w2_in = din("w2", [4 * D, D])
    b1r_in = din("b1r", [1, 4 * D])
    b2_in = din("b2", [128, DCH])
    wq_in = din("wq", [D, D])
    wk_in = din("wk", [D, D])
    wv_in = din("wv", [D, D])
    wo_in = din("wo", [D, D])
    maskd_in = din("maskd", [128, 2, BLK])
    mv0_in = din("mv0", [128, 2, BLK])
    mv1_in = din("mv1", [128, 2, BLK])
    eye_in = din("eye", [128, 128])
    wt_in = din("wt", [D, D])
    ut_in = din("ut", [D, 128])
    vs_in = din("vs", [128, D])
    gw_in = din("gw", [D, 8])
    g16_in = din("g16", [8, 128])
    wqkvt_in = din("wqkvt", [D, 3 * D])
    wot_in = din("wot", [D, D])
    cosi_in = din("cosi", [128, MYT])
    sini_in = din("sini", [128, MYT])
    prot_in = din("prot", [128, 128])
    bc16_in = din("bc16", [16, 8, 128])
    y_out = nc.dram_tensor("y_out", [DCH, 128, MYT], F32, kind="ExternalOutput").ap()

    KVW = DCH * BLK + 2 * 1040
    kvin = nc.dram_tensor("kvin", [MYB, 128, KVW], BF16)
    kvg = nc.dram_tensor("kvg", [2, MYB, 128, KVW], BF16)
    onorm_d = nc.dram_tensor("onorm_d", [8, 8, 2, 65, BLK], BF16)
    w1c_d = nc.dram_tensor("w1c", [4, DCH, 128, 1024], BF16)
    w2c_d = nc.dram_tensor("w2c", [4, QH, 128, 1024], BF16)
    b1c_d = nc.dram_tensor("b1c", [4, 1, 1024], BF16)
    wtc_d = nc.dram_tensor("wtc", [DCH, 128, 1024], BF16)
    smallc_d = nc.dram_tensor("smallc", [4, 128, 1024], BF16)

    BASE = 16512
    off = [BASE]
    def at(name, shape, dtype, offset=None):
        if offset is None:
            offset = off[0]
        per_part = int(np.prod(shape[1:])) * mybir.dt.size(dtype)
        h = nc.alloc_sbuf_tensor_at(name, shape, dtype, offset=offset)
        off[0] = max(off[0], offset + ((per_part + 63) // 64) * 64)
        return h

    x_sb = at("x_sb", [128, DCH, MYT], F32)
    xn_sb = at("xn_sb", [128, DCH, MYT], BF16)
    stat_sb = at("stat_sb", [128, 4, BLK], F32)
    const_sb = at("const_sb", [128, 16], F32)
    onesrep_sb = at("onesrep_sb", [128, 128], F32)
    onesbf_sb = at("onesbf_sb", [1, BLK], BF16)
    onescol_sb = at("onescol_sb", [128, 2], BF16)
    onesrow_sb = at("onesrow_sb", [1, 128], F32)
    b2h_sb = at("b2h_sb", [128, DCH], F32)
    b2stg_sb = at("b2stg_sb", [128, DCH], F32)
    PERS_END = off[0]

    FREG = PERS_END
    w1q_sb = at("w1q_sb", [128, 2, DCH, QH, 128], BF16, FREG)
    w2q_sb = at("w2q_sb", [128, 2, QH, DCH, 128], BF16, FREG + 32768)
    h_sb = at("h_sb", [128, 2, QH, BLK], BF16, FREG + 65536)
    scr_sb = at("scr_sb", [128, 1024], F32, FREG + 73728)
    b1q_sb = at("b1q_sb", [1, 2, QH * 128], BF16, FREG + 77824)
    b1qstg_sb = at("b1qstg_sb", [1, 2, QH * 128], F32, FREG + 81920)
    xsq_sb = at("xsq_sb", [128, 1024], F32, FREG + 90112)
    FFN_END = FREG + 94208

    wt_sb = at("wt_sb", [128, DCH, DCH, 128], BF16, FREG)
    ut_sb = at("ut_sb", [128, DCH, 128], BF16, FREG + 16384)
    vs_sb = at("vs_sb", [128, DCH, 128], BF16, FREG + 18432)
    gw_sb = at("gw_sb", [128, DCH, 8], BF16, FREG + 20480)
    g16_sb = at("g16_sb", [8, 128], BF16, FREG + 20608)
    gates_sb = at("gates_sb", [128, MYT], BF16, FREG + 20864)
    gte_sb = at("gte_sb", [128, 2, 8], F32, FREG + 24960)
    gtr_sb = at("gtr_sb", [128, 4], F32, FREG + 25024)
    gtn_sb = at("gtn_sb", [128, 2, 8], F32, FREG + 25088)
    gb_sb = at("gb_sb", [128, 2, 512], BF16, FREG + 25152)
    tg_sb = at("tg_sb", [128, 2, 512], BF16, FREG + 27200)
    g16stg_sb = at("g16stg_sb", [8, 128], F32, FREG + 29248)
    eyef_sb = at("eyef_sb", [128, 128], F32, FREG + 29760)

    q_sb = at("q_sb", [128, DCH, MYT], BF16, FREG)
    wqkv_sb = at("wqkv_sb", [128, 2, DCH, DCH, 128], BF16, FREG + 32768)
    kring_sb = at("kring_sb", [128, RING, DCH, BLK], BF16, FREG + 65536)
    vring_sb = at("vring_sb", [128, RING, 2, 1024], BF16, FREG + 73728)
    pp_sb = at("pp_sb", [128, 2, 512], BF16, FREG + 82048)
    maskd_sb = at("maskd_sb", [128, 2, BLK], BF16, FREG + 84096)
    mv0_sb = at("mv0_sb", [128, 2, BLK], BF16, FREG + 85120)
    mv1_sb = at("mv1_sb", [128, 2, BLK], BF16, FREG + 86144)
    eye_sb = at("eye_sb", [128, 128], BF16, FREG + 87168)
    wo_sb = at("wo_sb", [128, 4, DCH, 128], BF16, FREG + 87424)
    lrow_sb = at("lrow_sb", [128, BLK], F32, FREG + 95616)
    rl_sb = at("rl_sb", [128, BLK], F32, FREG + 96640)
    rlb_sb = at("rlb_sb", [128, BLK], F32, FREG + 97664)
    onorm_sb = at("onorm_sb", [128, 4, BLK], BF16, FREG + 98688)
    scrA_sb = at("scrA_sb", [128, 1024], F32, FREG + 100736)
    ATT_END = FREG + 104832

    XNOFF = BASE + 65536
    kband_sb = at("kband_sb", [128, 2, 16, BLK], BF16, XNOFF)
    vband_sb = at("vband_sb", [128, 2, 16, 2, 130], BF16, XNOFF + 16384)
    pp1_sb = at("pp1_sb", [128, 2, 1024], BF16, FREG + 100736)
    bc16_sb = at("bc16_sb", [16, DCH, 128], F32, FREG + 65536)
    lld_sb = at("lld_sb", [16, 2, BLK], F32, FREG + 91648)
    rlbb_sb = at("rlbb_sb", [128, BLK], F32, FREG + 93696)
    onf_sb = at("onf_sb", [128, DCH, BLK], BF16, FREG + 16384)
    cosi_sb = at("cosi_sb", [128, MYT], BF16, FREG + 73728)
    sini_sb = at("sini_sb", [128, MYT], BF16, FREG + 77824)
    vstg1_sb = at("vstg1_sb", [128, 2, 16, 65], BF16, FREG + 87424)
    prot_sb = at("prot_sb", [128, 128], BF16, FREG + 104832)
    ropet_sb = at("ropet_sb", [128, 512], BF16, FREG + 105088)
    wof_sb = at("wof_sb", [128, DCH, DCH, 128], BF16, FREG)
    onld_sb = at("onld_sb", [128, DCH, BLK], BF16, FREG + 32768)
    A1_END = FREG + 106112
    assert max(FFN_END, ATT_END, A1_END) + 0 <= 229344, (FFN_END, ATT_END, A1_END)

    est = ExitStack()
    psA = est.enter_context(nc.psum_tensor("psA", [128, 2048], F32))
    psB = est.enter_context(nc.psum_tensor("psB", [128, 1024], F32))
    psC = est.enter_context(nc.psum_tensor("psC", [128, 1024], F32))
    sems = {n: est.enter_context(nc.semaphore(n)) for n in SEM_NAMES}


    def emit_ln_chunk(em, t, xnr_base):
        V, SC, T = "vector", "scalar", "tensor"
        sl = slice(t * BLK, (t + 1) * BLK)
        em.wait(T, "rowe", em.val("rowe"))
        em.wait(T, "hp", em.val("hp"))
        em.wait(T, "pxp", em.val("pxp"))
        em.wait(T, "xnr", xnr_base)
        em.wait(SC, "xnr", xnr_base)
        em.wait(SC, "ymt", em.val("ymt"))
        for c in range(DCH):
            em.emit(T, lambda e, c=c, sl=sl: e.matmul(
                psB[:, 0:256], onesrep_sb[:], x_sb[:, c, sl],
                start=(c == 0), stop=(c == DCH - 1)),
                inc="stm" if c == DCH - 1 else None)
        for half in range(2):
            c0 = half * 4
            em.wait(SC, "stm", em.val("stm"))
            em.emit(SC, lambda e, c0=c0, sl=sl: e.activation(
                xsq_sb[:].rearrange("p (c m) -> p c m", c=4),
                x_sb[:, c0:c0 + 4, sl], AF.Square), inc="sq")
            em.wait(T, "sq")
            for j in range(4):
                em.emit(T, lambda e, j=j, half=half: e.matmul(
                    psB[:, 256:512], onesrep_sb[:], xsq_sb[:, j * 256:(j + 1) * 256],
                    start=(half == 0 and j == 0), stop=(half == 1 and j == 3)),
                    inc="stm" if j == 3 else None)
        em.wait(V, "stm")
        em.emit(V, lambda e: e.tensor_copy(stat_sb[:, 0, :], psB[:, 0:256]),
                inc="rowe")
        em.emit(V, lambda e: e.tensor_copy(stat_sb[:, 1, :], psB[:, 256:512]),
                inc="rowe")
        em.wait(SC, "rowe")
        em.emit(SC, lambda e: e.activation(stat_sb[:, 2, :], stat_sb[:, 0, :],
                                           AF.Square), inc="rowms")
        em.wait(V, "rowms")
        em.emit(V, lambda e: e.tensor_tensor(stat_sb[:, 1, :], stat_sb[:, 1, :],
                                             stat_sb[:, 2, :], op=ALU.subtract),
                inc="rowmv")
        em.wait(SC, "rowmv")
        em.emit(SC, lambda e: e.activation(stat_sb[:, 2, :], stat_sb[:, 1, :],
                                           AF.Sqrt, bias=const_sb[:, 1:2]),
                inc="rowms")
        em.wait(V, "rowms")
        em.emit(V, lambda e: e.reciprocal(stat_sb[:, 1, :], stat_sb[:, 2, :]),
                inc="rowmv")
        for c in range(DCH):
            em.emit(V, lambda e, c=c, sl=sl: e.tensor_tensor(
                xn_sb[:, c, sl], x_sb[:, c, sl], stat_sb[:, 0, :],
                op=ALU.subtract))
        em.emit(V, lambda e: e.drain())
        for c in range(DCH):
            em.emit(V, lambda e, c=c, sl=sl: e.tensor_tensor(
                xn_sb[:, c, sl], xn_sb[:, c, sl], stat_sb[:, 1, :],
                op=ALU.mult), inc="xnr" if c == DCH - 1 else None)
        em.emit(V, lambda e: e.drain())


    def emit_ffn0(em, first):
        V, SC, T, G, SY = "vector", "scalar", "tensor", "gpsimd", "sync"
        xnr_base = em.val("xnr")
        xnr_at = []
        for t in range(TCH):
            emit_ln_chunk(em, t, xnr_base)
            xnr_at.append(em.val("xnr"))
        XNR_ALL = em.val("xnr")
        for sem in ("smm", "avm", "ymt", "pxp", "qmm", "yv"):
            em.wait(SY, sem, em.val(sem))
        for q in range(4):
            sg = q % 2
            em.wait(SY, "xev", em.val("xev") - 2 * TCH)
            if first:
                qsem = "stg" if q % 2 == 0 else "stg2"
                em.wait(V, qsem, em.val(qsem))
                em.wait(SC, qsem, em.val(qsem))
                for cc in range(DCH):
                    for sh in range(2):
                        wsem = "wl" if sh == 0 else "wl2"
                        em.wait(SY, "castv", em.val("castv") -
                                (0 if (cc == 0 and sh == 0) else 1))
                        em.dma(SY, scr_sb[:, sh * 512:(sh + 1) * 512],
                               w1_in[cc * 128:(cc + 1) * 128,
                                     q * 1024 + sh * 512:
                                     q * 1024 + (sh + 1) * 512], inc=wsem)
                        em.wait(V, wsem)
                        em.emit(V, lambda e, sg=sg, cc=cc, sh=sh: e.tensor_copy(
                            w1q_sb[:, sg, cc]
                            .rearrange("p h m -> p (h m)")[:, sh * 512:
                                                           (sh + 1) * 512],
                            scr_sb[:, sh * 512:(sh + 1) * 512]), inc="castv")
                    em.wait(G, "castv")
                    em.dma(G, w1c_d[q, cc],
                           w1q_sb[:, sg, cc].rearrange("p h m -> p (h m)"),
                           inc=qsem)
                for hh in range(QH):
                    for sh in range(2):
                        wsem = "wl" if sh == 0 else "wl2"
                        em.wait(SY, "castv", em.val("castv"))
                        em.wait(SY, "casts", em.val("casts") -
                                (0 if (hh == 0 and sh == 0) else 1))
                        em.dma(SY, scr_sb[:, sh * 512:(sh + 1) * 512],
                               w2_in[(q * QH + hh) * 128:(q * QH + hh + 1) * 128,
                                     sh * 512:(sh + 1) * 512], inc=wsem)
                        em.wait(SC, wsem)
                        em.emit(SC, lambda e, sg=sg, hh=hh, sh=sh: e.mul(
                            w2q_sb[:, sg, hh]
                            .rearrange("p d m -> p (d m)")[:, sh * 512:
                                                           (sh + 1) * 512],
                            scr_sb[:, sh * 512:(sh + 1) * 512], 0.5),
                            inc="casts")
                    em.wait(G, "casts")
                    em.dma(G, w2c_d[q, hh],
                           w2q_sb[:, sg, hh].rearrange("p d m -> p (d m)"),
                           inc=qsem)
                em.wait(SY, "castv", em.val("castv"))
                em.wait(SY, "casts", em.val("casts"))
                em.dma(SY, b1qstg_sb[:, sg, :],
                       b1r_in[:, q * QH * 128:(q + 1) * QH * 128], inc="wl")
                em.wait(V, "wl")
                em.emit(V, lambda e, sg=sg: e.tensor_copy(
                    b1q_sb[:, sg, :], b1qstg_sb[:, sg, :]), inc="castv")
                em.wait(G, "castv")
                em.dma(G, b1c_d[q], b1q_sb[:, sg, :].rearrange("p m -> p m"),
                       inc=qsem)
            else:
                em.wait(SY, "stg", em.val("stg"))
                em.wait(SY, "stg2", em.val("stg2"))
                em.wait(SY, "wl", em.val("wl"))
                em.dma(SY, w1q_sb[:, sg].rearrange("p c h m -> p c (h m)"),
                       w1c_d[q].rearrange("c p m -> p c m"), inc="wl")
                em.dma(SY, w2q_sb[:, sg].rearrange("p h d m -> p h (d m)"),
                       w2c_d[q].rearrange("h p m -> p h m"), inc="wl")
                em.dma(SY, b1q_sb[:, sg, :], b1c_d[q], inc="wl")
                em.wait(T, "wl", em.val("wl"))
            em.wait(T, "castv")
            em.wait(T, "casts")
            if q == 0:
                em.wait(T, "rowe")
            for t in range(TCH):
                k = q * TCH + t
                hs = k % 2
                sl = slice(t * BLK, (t + 1) * BLK)
                if q == 0:
                    em.wait(T, "xnr", xnr_at[t])
                em.wait(T, "hp", em.val("hp"))
                for grp in range(2):
                    pg = psB if grp == 0 else psC
                    for j in range(4):
                        hh = grp * 4 + j
                        pa = pg[:, j * 256:(j + 1) * 256]
                        em.emit(T, lambda e, sg=sg, hh=hh, pa=pa: e.matmul(
                            pa, b1q_sb[:, sg, hh * 128:(hh + 1) * 128],
                            onesbf_sb[:], start=True, stop=False))
                        for c in range(DCH):
                            em.emit(T, lambda e, sg=sg, hh=hh, c=c, sl=sl, pa=pa:
                                    e.matmul(pa, w1q_sb[:, sg, c, hh, :],
                                             xn_sb[:, c, sl],
                                             start=False, stop=(c == DCH - 1)),
                                    inc="hmm" if c == DCH - 1 else None)
                em.wait(SC, "hmm")
                em.wait(SC, "ymm", em.val("ymm") - 1)
                em.emit(SC, lambda e, hs=hs: e.activation(
                    h_sb[:, hs, 0:4].rearrange("p h m -> p (h m)"), psB[:],
                    AF.Gelu))
                em.emit(SC, lambda e, hs=hs: e.activation(
                    h_sb[:, hs, 4:8].rearrange("p h m -> p (h m)"), psC[:],
                    AF.Gelu), inc="hp")
                em.wait(T, "hp")
                for half in range(2):
                    em.wait(T, "xev", em.val("xev") - 1)
                    for c in range(half * 4, half * 4 + 4):
                        for hh in range(QH):
                            em.emit(T, lambda e, sg=sg, c=c, hh=hh, hs=hs:
                                    e.matmul(
                                        psA[:, c * 256:(c + 1) * 256],
                                        w2q_sb[:, sg, hh, c, :],
                                        h_sb[:, hs, hh, :],
                                        start=(hh == 0), stop=(hh == QH - 1)),
                                    inc="ymm" if (c % 4 == 3 and hh == QH - 1)
                                    else None)
                    em.wait(V, "ymm")
                    em.emit(V, lambda e, sl=sl, half=half: e.tensor_tensor(
                        x_sb[:, half * 4:half * 4 + 4, sl],
                        x_sb[:, half * 4:half * 4 + 4, sl],
                        psA[:, half * 1024:(half + 1) * 1024]
                        .rearrange("p (c m) -> p c m", c=4), op=ALU.add),
                        inc="xev")
        em.wait(V, "xev")
        em.wait(V, "casts", 1)
        for t in range(TCH):
            sl = slice(t * BLK, (t + 1) * BLK)
            for c in range(DCH):
                em.emit(V, lambda e, c=c, sl=sl: e.tensor_scalar_add(
                    x_sb[:, c, sl], x_sb[:, c, sl], b2h_sb[:, c:c + 1]),
                    inc="xnr")
        em.emit(V, lambda e: e.drain())


    def emit_attn0(em):
        xnr_base0 = None
        V, SC, T, G, SY = "vector", "scalar", "tensor", "gpsimd", "sync"
        xnr_base = em.val("xnr")
        for t in range(TCH):
            emit_ln_chunk(em, t, xnr_base)
        XNR_ALL = em.val("xnr")

        em.wait(SY, "xnr", XNR_ALL)
        em.wait(SY, "yv", em.val("yv"))
        for sem in ("ymm", "hmm", "hp", "xev"):
            em.wait(SY, sem, em.val(sem))
        em.dma(SY, scrA_sb[:, 0:512],
               maskd_in.rearrange("p a b -> p (a b)"), inc="wl")
        em.dma(SY, scrA_sb[:, 512:640], eye_in, inc="wl")
        em.wait(V, "wl")
        em.wait(V, "stg", em.val("stg"))
        em.wait(SC, "stg", em.val("stg"))
        em.wait(V, "stg2", em.val("stg2"))
        em.wait(SC, "stg2", em.val("stg2"))
        em.emit(V, lambda e: e.tensor_copy(
            maskd_sb[:].rearrange("p a b -> p (a b)"), scrA_sb[:, 0:512]),
            inc="castv")
        em.emit(V, lambda e: e.tensor_copy(eye_sb[:], scrA_sb[:, 512:640]),
                inc="castv")
        em.wait(SY, "castv", em.val("castv"))
        em.dma(SY, scrA_sb[:, 0:512],
               mv0_in.rearrange("p a b -> p (a b)"), inc="wl")
        em.dma(SY, scrA_sb[:, 512:1024],
               mv1_in.rearrange("p a b -> p (a b)"), inc="wl")
        em.wait(V, "wl")
        em.emit(V, lambda e: e.tensor_copy(
            mv0_sb[:].rearrange("p a b -> p (a b)"), scrA_sb[:, 0:512]),
            inc="castv")
        em.emit(V, lambda e: e.tensor_copy(
            mv1_sb[:].rearrange("p a b -> p (a b)"), scrA_sb[:, 512:1024]),
            inc="castv")
        em.emit(V, lambda e: e.memset(const_sb[:, 2:3], -M0), inc="castv")
        em.emit(V, lambda e: e.memset(const_sb[:, 3:4], 1.0 / 32.0), inc="castv")

        em.wait(T, "xnr", XNR_ALL)
        em.wait(T, "rowe")
        for pos, m in enumerate([1, 2, 0]):
            w_in = [wq_in, wk_in, wv_in][m]
            ws = pos % 2
            em.wait(SY, "qmm", em.val("qmm") - 32)
            for cin in range(DCH):
                em.wait(SY, "castv", em.val("castv"))
                em.dma(SY, scrA_sb[:], w_in[cin * 128:(cin + 1) * 128, :],
                       inc="wl")
                em.wait(V, "wl")
                em.emit(V, lambda e, ws=ws, cin=cin: e.tensor_copy(
                    wqkv_sb[:, ws, cin].rearrange("p a b -> p (a b)"),
                    scrA_sb[:]), inc="castv")
            em.wait(T, "castv")
            if m < 2:
                for tc in range(4):
                    ts = slice(tc * 512, (tc + 1) * 512)
                    if m == 1:
                        em.wait(V, "stg", em.val("stg"))
                    for c2 in range(DCH):
                        kq = em.val("qmm")
                        pa = psA[:, (kq % 4) * 512:(kq % 4 + 1) * 512]
                        em.wait(T, "qev", em.val("qev") - 3)
                        for cin in range(DCH):
                            em.emit(T, lambda e, ws=ws, c2=c2, cin=cin, ts=ts,
                                    pa=pa: e.matmul(
                                        pa, wqkv_sb[:, ws, cin, c2, :],
                                        xn_sb[:, cin, ts],
                                        start=(cin == 0), stop=(cin == DCH - 1)),
                                    inc="qmm" if cin == DCH - 1 else None)
                        em.wait(V, "qmm")
                        if m == 0:
                            em.emit(V, lambda e, c2=c2, ts=ts, pa=pa:
                                    e.tensor_copy(q_sb[:, c2, ts], pa),
                                    inc="qev")
                        else:
                            for jj in range(2):
                                em.emit(V, lambda e, c2=c2, jj=jj, pa=pa:
                                        e.tensor_copy(
                                            kring_sb[:, jj, c2, :],
                                            pa[:, jj * 256:(jj + 1) * 256]),
                                        inc="qev" if jj == 1 else None)
                    if m == 1:
                        em.wait(G, "qev")
                        for jj in range(2):
                            em.dma(G, kvin[2 * tc + jj][:, 0:2048],
                                   kring_sb[:, jj].rearrange("p a b -> p (a b)"),
                                   inc="stg")
            else:
                for sv in range(TCH):
                    vslot = sv % 2
                    em.wait(V, "stg", em.val("stg"))
                    for u in range(2):
                        for dvh in range(2):
                            pb = psB if dvh == 0 else psC
                            tok0 = sv * 256 + u * 128
                            em.wait(T, "qev", em.val("qev"))
                            for cin in range(DCH):
                                em.emit(T, lambda e, ws=ws, cin=cin, tok0=tok0,
                                        pb=pb, dvh=dvh: e.matmul(
                                            pb[:, 0:512],
                                            xn_sb[:, cin, tok0:tok0 + 128],
                                            wqkv_sb[:, ws, cin,
                                                    dvh * 4:(dvh + 1) * 4, :]
                                            .rearrange("p a b -> p (a b)"),
                                            start=(cin == 0),
                                            stop=(cin == DCH - 1)),
                                        inc="qmm" if cin == DCH - 1 else None)
                        em.wait(V, "qmm")
                        em.emit(V, lambda e, vslot=vslot, u=u: e.tensor_copy(
                            vring_sb[:, vslot, u, 0:512], psB[:, 0:512]))
                        em.emit(V, lambda e, vslot=vslot, u=u: e.tensor_copy(
                            vring_sb[:, vslot, u, 512:1024], psC[:, 0:512]),
                            inc="qev")
                    em.wait(G, "qev")
                    em.dma(G, kvin[sv][:, 2048:4096],
                           vring_sb[:, vslot].rearrange("p a b -> p (a b)"),
                           inc="stg")
                em.emit(V, lambda e: e.memset(pp_sb[:, 0, 0:32], 0.0),
                        inc="fin")
                em.wait(G, "fin", em.val("fin"))
                for sv in range(MYB):
                    em.dma(G, kvin[sv][:, 4096:KVW], pp_sb[:, 0, 0:32],
                           inc="stg")
                em.wait(G, "stg")
                em.emit(G, lambda e: e.collective_compute(
                    "AllGather", ALU.bypass,
                    replica_groups=[[0, 1], [2, 3], [4, 5], [6, 7]],
                    ins=[kvin.ap().opt()],
                    outs=[kvg.ap().opt()]), inc="cc", inck=1)

        em.wait(SY, "cc", 1)
        em.wait(T, "qev", em.val("qev"))
        em.wait(T, "cc", 1)
        vno = 0
        for p in range(2):
            for cin in range(4):
                em.wait(SY, "casts", em.val("casts"))
                em.wait(SY, "castv", em.val("castv"))
                if p == 1:
                    em.wait(SY, "ymt", em.val("ymt"))
                em.dma(SY, scrA_sb[:],
                       wo_in[(4 * p + cin) * 128:(4 * p + cin + 1) * 128, :],
                       inc="wl")
                em.wait(SC, "wl")
                em.wait(SC, "stm", em.val("stm"))
                em.emit(SC, lambda e, cin=cin: e.copy(
                    wo_sb[:, cin].rearrange("p a b -> p (a b)"), scrA_sb[:]),
                    inc="casts")
            em.wait(T, "casts")
            for band in range(8):
                qs = slice(band * BLK, (band + 1) * BLK)
                vs = visits_of(band)
                for vi, (reg, j) in enumerate(vs):
                    r = vno % RING
                    sb_ = vno % 2
                    kvs = "kv0" if r == 0 else "kv1"
                    em.wait(SY, "avm", em.val("avm") - (RING - 1))
                    em.wait(SY, "pxp", em.val("pxp") - (RING - 1))
                    em.wait(SY, kvs, em.val(kvs))
                    slot = {"own": kvin[j], "r0": kvg[0, j],
                            "r1": kvg[1, j]}[reg]
                    em.dma(SY, kring_sb[:, r].rearrange("p a b -> p (a b)"),
                           slot[:, 0:2048], inc=kvs)
                    em.dma(SY, vring_sb[:, r].rearrange("p a b -> p (a b)"),
                           slot[:, 2048:4096], inc=kvs)
                    em.wait(T, kvs)
                    em.wait(T, "pxp", em.val("pxp") - 1)
                    mask = None
                    if reg == "own":
                        mask = maskd_sb
                    elif reg == "r0" and j == band:
                        mask = mv0_sb
                    elif reg == "r1" and j == band:
                        mask = mv1_sb
                    for ksub in range(2):
                        pt = psB[:, sb_ * 512 + ksub * 256:
                                 sb_ * 512 + (ksub + 1) * 256]
                        if mask is not None:
                            em.emit(T, lambda e, mask=mask, ksub=ksub, pt=pt:
                                    e.matmul(pt, eye_sb[:], mask[:, ksub, :],
                                             start=True, stop=False))
                        for cin in range(DCH):
                            first = (mask is None and cin == 0)
                            em.emit(T, lambda e, r=r, cin=cin, ksub=ksub, qs=qs,
                                    pt=pt, first=first:
                                    e.matmul(pt,
                                             kring_sb[:, r, cin,
                                                      ksub * 128:(ksub + 1) * 128],
                                             q_sb[:, cin, qs],
                                             start=first, stop=(cin == DCH - 1)),
                                    inc="smm" if (ksub == 1 and cin == DCH - 1)
                                    else None)
                    em.wait(SC, "smm")
                    em.wait(SC, "avm", em.val("avm") - 1)
                    em.emit(SC, lambda e, sb_=sb_: e.activation(
                        pp_sb[:, sb_, :], psB[:, sb_ * 512:(sb_ + 1) * 512],
                        AF.Exp, bias=const_sb[:, 2:3], scale=const_sb[:, 3:4]),
                        inc="pxp")
                    em.wait(T, "pxp")
                    if vi == 0:
                        em.wait(T, "fin", em.val("fin"))
                    for ksub in range(2):
                        for cdv in range(4):
                            h0 = 8 * p + 2 * cdv
                            em.emit(T, lambda e, r=r, ksub=ksub, cdv=cdv, h0=h0,
                                    sb_=sb_: e.matmul(
                                        psA[:, cdv * 512:cdv * 512 + 256],
                                        vring_sb[:, r, ksub,
                                                 (h0 // 2) * 128:(h0 // 2) * 128 + 128],
                                        pp_sb[:, sb_, ksub * 256:(ksub + 1) * 256],
                                        start=(vi == 0 and ksub == 0),
                                        stop=(vi == len(vs) - 1 and ksub == 1)))
                        em.emit(T, lambda e, r=r, ksub=ksub, sb_=sb_: e.matmul(
                            psC[0:1, 0:256], onescol_sb[:, 0:1],
                            pp_sb[:, sb_, ksub * 256:(ksub + 1) * 256],
                            start=(vi == 0 and ksub == 0),
                            stop=(vi == len(vs) - 1 and ksub == 1)),
                            inc="avm" if ksub == 1 else None)
                    vno += 1
                em.wait(V, "avm")
                em.emit(V, lambda e: e.tensor_copy(lrow_sb[0:1, :],
                                                   psC[0:1, 0:256]))
                em.emit(V, lambda e: e.drain())
                em.emit(V, lambda e: e.reciprocal(rl_sb[0:1, :], lrow_sb[0:1, :]),
                        inc="fin")
                em.wait(T, "fin")
                em.wait(T, "yv", em.val("yv"))
                em.emit(T, lambda e: e.matmul(psC[:, 768:1024], onesrow_sb[:],
                                              rl_sb[0:1, :], start=True, stop=True),
                        inc="ymt")
                em.wait(V, "ymt")
                em.emit(V, lambda e: e.tensor_copy(rlb_sb[:], psC[:, 768:1024]))
                em.emit(V, lambda e: e.drain())
                for cdv in range(4):
                    em.emit(V, lambda e, cdv=cdv: e.tensor_tensor(
                        onorm_sb[:, cdv, :], psA[:, cdv * 512:cdv * 512 + 256],
                        rlb_sb[:], op=ALU.mult),
                        inc="fin" if cdv == 3 else None)
                em.emit(V, lambda e: e.drain())
                em.wait(T, "fin")
                for c2 in range(DCH):
                    ys = psC[:, 512:768]
                    em.wait(T, "yv", em.val("yv"))
                    for cdv in range(4):
                        em.emit(T, lambda e, c2=c2, cdv=cdv, ys=ys: e.matmul(
                            ys, wo_sb[:, cdv, c2, :], onorm_sb[:, cdv, :],
                            start=(cdv == 0), stop=(cdv == 3)),
                            inc="ymt" if cdv == 3 else None)
                    em.wait(V, "ymt")
                    em.emit(V, lambda e, c2=c2, qs=qs, ys=ys: e.tensor_tensor(
                        x_sb[:, c2, qs], x_sb[:, c2, qs], ys, op=ALU.add),
                        inc="yv")
                em.emit(V, lambda e: e.drain())


    def emit_ffn1(em, first):
        V, SC, T, G, SY = "vector", "scalar", "tensor", "gpsimd", "sync"
        xnr_base = em.val("xnr")
        xnr_at = []
        for t in range(TCH):
            emit_ln_chunk(em, t, xnr_base)
            xnr_at.append(em.val("xnr"))
        XNR_ALL = em.val("xnr")
        for sem in ("smm", "avm", "ymt", "pxp", "qmm", "yv", "ymm", "hmm",
                    "hp", "xev", "fin", "qev"):
            em.wait(SY, sem, em.val(sem))
        em.wait(V, "stg", em.val("stg"))
        em.wait(SC, "stg", em.val("stg"))
        em.wait(V, "stg2", em.val("stg2"))
        em.wait(SC, "stg2", em.val("stg2"))
        if first:
            for cin in range(DCH):
                for sh in range(2):
                    wsem = "wl" if sh == 0 else "wl2"
                    em.wait(SY, "casts", em.val("casts") -
                            (0 if (cin == 0 and sh == 0) else 1))
                    em.dma(SY, scr_sb[:, sh * 512:(sh + 1) * 512],
                           wt_in[cin * 128:(cin + 1) * 128,
                                 sh * 512:(sh + 1) * 512], inc=wsem)
                    em.wait(SC, wsem)
                    em.emit(SC, lambda e, cin=cin, sh=sh: e.mul(
                        wt_sb[:, cin].rearrange("p a b -> p (a b)")
                        [:, sh * 512:(sh + 1) * 512],
                        scr_sb[:, sh * 512:(sh + 1) * 512], 0.5), inc="casts")
                em.wait(G, "casts")
                em.dma(G, wtc_d[cin],
                       wt_sb[:, cin].rearrange("p a b -> p (a b)"), inc="stg")
        else:
            em.wait(SY, "stg", em.val("stg"))
            em.wait(SY, "wl", em.val("wl"))
            em.dma(SY, wt_sb[:].rearrange("p c a b -> p c (a b)"),
                   wtc_d.rearrange("c p m -> p c m"), inc="wl")
            em.wait(T, "wl", em.val("wl"))
            em.wait(SC, "wl", em.val("wl"))
        em.wait(SY, "casts", em.val("casts"))
        em.wait(SY, "wl", em.val("wl"))
        em.dma(SY, scr_sb[:].rearrange("p (c m) -> p c m", c=DCH),
               ut_in.rearrange("(c p) m -> p c m", p=128), inc="wl")
        em.dma(SY, g16stg_sb[:], g16_in, inc="wl")
        em.dma(SY, eyef_sb[:], eye_in, inc="wl")
        em.wait(V, "wl")
        em.emit(V, lambda e: e.tensor_copy(
            ut_sb[:].rearrange("p c m -> p (c m)"), scr_sb[:]), inc="castv")
        em.emit(V, lambda e: e.tensor_copy(g16_sb[:], g16stg_sb[:]), inc="castv")
        em.wait(SY, "castv", em.val("castv"))
        em.dma(SY, scr_sb[:], vs_in, inc="wl")
        em.wait(SC, "wl")
        em.emit(SC, lambda e: e.mul(
            vs_sb[:].rearrange("p c m -> p (c m)"), scr_sb[:], 1.0 / 32.0),
            inc="casts")
        em.wait(SY, "casts", em.val("casts"))
        em.dma(SY, scr_sb[:, 0:64].rearrange("p (c m) -> p c m", c=DCH),
               gw_in.rearrange("(c p) m -> p c m", p=128), inc="wl")
        em.wait(V, "wl")
        em.emit(V, lambda e: e.tensor_copy(
            gw_sb[:].rearrange("p c m -> p (c m)"), scr_sb[:, 0:64]),
            inc="castv")
        em.emit(V, lambda e: e.memset(const_sb[:, 4:5], -8.0), inc="castv")
        em.wait(T, "wl", em.val("wl"))
        em.wait(T, "castv")
        em.wait(T, "casts")
        em.wait(T, "rowe")

        for u in range(MYT // 128):
            tok0 = u * 128
            ub = u % 2
            em.wait(T, "xnr", xnr_at[u // 2])
            em.wait(T, "pxp", em.val("pxp"))
            for cin in range(DCH):
                em.emit(T, lambda e, cin=cin, tok0=tok0: e.matmul(
                    psC[:, 0:8], xn_sb[:, cin, tok0:tok0 + 128],
                    gw_sb[:, cin, :], start=(cin == 0), stop=(cin == DCH - 1)),
                    inc="qmm" if cin == DCH - 1 else None)
            em.wait(SC, "qmm")
            em.wait(SC, "fin", em.val("fin"))
            em.emit(SC, lambda e, ub=ub: e.activation(
                gte_sb[:, ub, :], psC[:, 0:8], AF.Exp,
                bias=const_sb[:, 4:5]), inc="pxp")
            em.wait(V, "pxp")
            em.emit(V, lambda e, ub=ub: e.tensor_reduce(
                gtr_sb[:, ub:ub + 1], gte_sb[:, ub, :],
                axis=mybir.AxisListType.X, op=ALU.add))
            em.emit(V, lambda e: e.drain())
            em.emit(V, lambda e, ub=ub: e.reciprocal(
                gtr_sb[:, 2 + ub:3 + ub], gtr_sb[:, ub:ub + 1]))
            em.emit(V, lambda e: e.drain())
            em.emit(V, lambda e, ub=ub: e.tensor_scalar(
                gtn_sb[:, ub, :], gte_sb[:, ub, :], gtr_sb[:, 2 + ub:3 + ub],
                None, op0=ALU.mult), inc="fin")
            em.wait(T, "fin")
            em.wait(T, "qev", em.val("qev"))
            em.emit(T, lambda e, ub=ub: e.transpose(
                psC[0:8, 512:640], gtn_sb[:, ub, :], eyef_sb[:]), inc="ymt")
            em.wait(V, "ymt")
            em.emit(V, lambda e, tok0=tok0: e.tensor_copy(
                gates_sb[0:8, tok0:tok0 + 128], psC[0:8, 512:640]), inc="qev")

        for j in range(4):
            js = slice(j * 512, (j + 1) * 512)
            jb = j % 2
            em.wait(T, "qev", em.val("qev"))
            em.emit(T, lambda e, js=js: e.matmul(
                psB[:, 0:512], g16_sb[:], gates_sb[0:8, js],
                start=True, stop=True), inc="smm")
            for cin in range(DCH):
                em.emit(T, lambda e, cin=cin, js=js: e.matmul(
                    psB[:, 512:1024], ut_sb[:, cin, :], xn_sb[:, cin, js],
                    start=(cin == 0), stop=(cin == DCH - 1)),
                    inc="hmm" if cin == DCH - 1 else None)
            em.wait(V, "smm")
            em.emit(V, lambda e, jb=jb: e.tensor_copy(gb_sb[:, jb, :],
                                                      psB[:, 0:512]))
            em.emit(V, lambda e: e.drain())
            em.wait(V, "hmm")
            em.wait(V, "ymm", em.val("ymm") - 4)
            em.emit(V, lambda e, jb=jb: e.tensor_tensor(
                tg_sb[:, jb, :], psB[:, 512:1024], gb_sb[:, jb, :],
                op=ALU.mult), inc="fin")
            for u2 in range(2):
                t = j * 2 + u2
                sl = slice(t * BLK, (t + 1) * BLK)
                em.wait(T, "fin")
                for half in range(2):
                    em.wait(T, "xev", em.val("xev") - 1)
                    for c2 in range(half * 4, half * 4 + 4):
                        em.emit(T, lambda e, c2=c2, sl=sl: e.matmul(
                            psA[:, c2 * 256:(c2 + 1) * 256],
                            wt_sb[:, 0, c2, :], xn_sb[:, 0, sl],
                            start=True, stop=False))
                        for cin in range(1, DCH):
                            em.emit(T, lambda e, c2=c2, cin=cin, sl=sl: e.matmul(
                                psA[:, c2 * 256:(c2 + 1) * 256],
                                wt_sb[:, cin, c2, :], xn_sb[:, cin, sl],
                                start=False, stop=False))
                        em.emit(T, lambda e, c2=c2, jb=jb, u2=u2: e.matmul(
                            psA[:, c2 * 256:(c2 + 1) * 256],
                            vs_sb[:, c2, :],
                            tg_sb[:, jb, u2 * 256:(u2 + 1) * 256],
                            start=False, stop=True),
                            inc="ymm" if c2 % 4 == 3 else None)
                    em.wait(V, "ymm")
                    em.emit(V, lambda e, sl=sl, half=half: e.tensor_tensor(
                        x_sb[:, half * 4:half * 4 + 4, sl],
                        x_sb[:, half * 4:half * 4 + 4, sl],
                        psA[:, half * 1024:(half + 1) * 1024]
                        .rearrange("p (c m) -> p c m", c=4), op=ALU.add),
                        inc="xev")
        em.emit(V, lambda e: e.drain())


    def emit_attn1(em):
        V, SC, T, G, SY = "vector", "scalar", "tensor", "gpsimd", "sync"
        xnr_base = em.val("xnr")
        xnr_at = []
        for t in range(TCH):
            emit_ln_chunk(em, t, xnr_base)
            xnr_at.append(em.val("xnr"))
        XNR_ALL = em.val("xnr")

        em.wait(SY, "xnr", XNR_ALL)
        em.wait(SY, "yv", em.val("yv"))
        for sem in ("ymm", "hmm", "hp", "xev", "smm", "avm", "ymt", "pxp",
                    "qmm", "fin", "qev"):
            em.wait(SY, sem, em.val(sem))
        em.wait(V, "stg", em.val("stg"))
        em.wait(SC, "stg", em.val("stg"))
        em.wait(V, "stg2", em.val("stg2"))
        em.wait(SC, "stg2", em.val("stg2"))
        em.dma(SY, scrA_sb[:, 0:512],
               maskd_in.rearrange("p a b -> p (a b)"), inc="wl")
        em.dma(SY, scrA_sb[:, 512:640], eye_in, inc="wl")
        em.wait(V, "wl")
        em.emit(V, lambda e: e.tensor_copy(
            maskd_sb[:].rearrange("p a b -> p (a b)"), scrA_sb[:, 0:512]),
            inc="castv")
        em.emit(V, lambda e: e.tensor_copy(eye_sb[:], scrA_sb[:, 512:640]),
                inc="castv")
        em.wait(SY, "castv", em.val("castv"))
        em.dma(SY, scrA_sb[:, 0:512],
               mv0_in.rearrange("p a b -> p (a b)"), inc="wl")
        em.dma(SY, scrA_sb[:, 512:1024],
               mv1_in.rearrange("p a b -> p (a b)"), inc="wl")
        em.wait(V, "wl")
        em.emit(V, lambda e: e.tensor_copy(
            mv0_sb[:].rearrange("p a b -> p (a b)"), scrA_sb[:, 0:512]),
            inc="castv")
        em.emit(V, lambda e: e.tensor_copy(
            mv1_sb[:].rearrange("p a b -> p (a b)"), scrA_sb[:, 512:1024]),
            inc="castv")
        for piece in range(2):
            em.wait(SY, "castv", em.val("castv"))
            em.dma(SY, scrA_sb[:], cosi_in[:, piece * 1024:(piece + 1) * 1024],
                   inc="wl")
            em.wait(V, "wl")
            em.emit(V, lambda e, piece=piece: e.tensor_copy(
                cosi_sb[:, piece * 1024:(piece + 1) * 1024], scrA_sb[:]),
                inc="castv")
            em.wait(SY, "castv", em.val("castv"))
            em.dma(SY, scrA_sb[:], sini_in[:, piece * 1024:(piece + 1) * 1024],
                   inc="wl")
            em.wait(V, "wl")
            em.emit(V, lambda e, piece=piece: e.tensor_copy(
                sini_sb[:, piece * 1024:(piece + 1) * 1024], scrA_sb[:]),
                inc="castv")
        em.wait(SY, "castv", em.val("castv"))
        em.dma(SY, scrA_sb[:, 0:128], prot_in, inc="wl")
        em.wait(V, "wl")
        em.emit(V, lambda e: e.tensor_copy(prot_sb[:], scrA_sb[:, 0:128]),
                inc="castv")
        em.emit(V, lambda e: e.memset(const_sb[:, 5:6], -M1), inc="castv")
        em.emit(V, lambda e: e.memset(const_sb[:, 6:7], 0.125), inc="castv")
        em.emit(V, lambda e: e.memset(vstg1_sb[:, :, :, 64:65], 1.0), inc="castv")

        em.wait(T, "castv")
        em.wait(T, "rowe")
        for pos, m in enumerate([1, 2, 0]):
            ws = pos % 2
            em.wait(SY, "qmm", em.val("qmm") - 32)
            for cin in range(DCH):
                em.wait(SY, "castv", em.val("castv"))
                em.dma(SY, scrA_sb[:],
                       wqkvt_in[cin * 128:(cin + 1) * 128,
                                m * D:(m + 1) * D], inc="wl")
                em.wait(V, "wl")
                em.emit(V, lambda e, ws=ws, cin=cin: e.tensor_copy(
                    wqkv_sb[:, ws, cin].rearrange("p a b -> p (a b)"),
                    scrA_sb[:]), inc="castv")
            em.wait(T, "castv")
            if m < 2:
                for tc in range(4):
                    ts = slice(tc * 512, (tc + 1) * 512)
                    em.wait(T, "xnr", xnr_at[2 * tc + 1])
                    if m == 1:
                        em.wait(V, "stg", em.val("stg"))
                    for c2 in range(DCH):
                        kq = em.val("qmm")
                        pa = psA[:, (kq % 2) * 512:(kq % 2) * 512 + 512]
                        em.wait(T, "qev", em.val("qev") - 3)
                        for cin in range(DCH):
                            em.emit(T, lambda e, ws=ws, c2=c2, cin=cin, ts=ts,
                                    pa=pa: e.matmul(
                                        pa, wqkv_sb[:, ws, cin, c2, :],
                                        xn_sb[:, cin, ts],
                                        start=(cin == 0), stop=(cin == DCH - 1)),
                                    inc="qmm" if cin == DCH - 1 else None)
                        dst = (lambda e, c2=c2, ts=ts: q_sb[:, c2, ts]) if m == 0                             else None
                        em.wait(V, "qmm")
                        em.emit(V, lambda e, pa=pa: e.tensor_copy(
                            ropet_sb[:], pa), inc="qev")
                        em.wait(T, "qev")
                        em.wait(T, "fin", em.val("fin") - 1)
                        em.emit(T, lambda e, kq=kq: e.matmul(
                            psB[:, (kq % 2) * 512:(kq % 2) * 512 + 512],
                            prot_sb[:], ropet_sb[:], start=True, stop=True),
                            inc="smm")
                        em.wait(V, "smm")
                        em.emit(V, lambda e, ts=ts: e.tensor_tensor(
                            ropet_sb[:], ropet_sb[:], cosi_sb[:, ts],
                            op=ALU.mult))
                        em.emit(V, lambda e: e.drain())
                        if m == 0:
                            tgt = lambda e, c2=c2, ts=ts: q_sb[:, c2, ts]
                        else:
                            tgt = lambda e, c2=c2: kring_sb[:, 0, c2, :]                                 .rearrange("p b -> p b") if False else None
                        if m == 0:
                            em.emit(V, lambda e, c2=c2, ts=ts, kq=kq:
                                    e.scalar_tensor_tensor(
                                        q_sb[:, c2, ts],
                                        psB[:, (kq % 2) * 512:(kq % 2) * 512 + 512],
                                        1.0, sini_sb[:, ts],
                                        op0=ALU.mult, op1=ALU.mult))
                            em.emit(V, lambda e: e.drain())
                            em.emit(V, lambda e, c2=c2, ts=ts: e.tensor_tensor(
                                q_sb[:, c2, ts], q_sb[:, c2, ts], ropet_sb[:],
                                op=ALU.add), inc="fin")
                        else:
                            for jj in range(2):
                                em.emit(V, lambda e, c2=c2, ts=ts, kq=kq, jj=jj:
                                        e.scalar_tensor_tensor(
                                            kring_sb[:, jj, c2, :],
                                            psB[:, (kq % 2) * 512 + jj * 256:
                                                (kq % 2) * 512 + (jj + 1) * 256],
                                            1.0,
                                            sini_sb[:, ts.start + jj * 256:
                                                    ts.start + (jj + 1) * 256],
                                            op0=ALU.mult, op1=ALU.mult))
                            em.emit(V, lambda e: e.drain())
                            for jj in range(2):
                                em.emit(V, lambda e, c2=c2, jj=jj: e.tensor_tensor(
                                    kring_sb[:, jj, c2, :], kring_sb[:, jj, c2, :],
                                    ropet_sb[:, jj * 256:(jj + 1) * 256],
                                    op=ALU.add), inc="fin" if jj == 1 else None)
                        em.emit(V, lambda e: e.drain())
                    if m == 1:
                        em.wait(G, "fin", em.val("fin"))
                        for jj in range(2):
                            em.dma(G, kvin[2 * tc + jj][:, 0:2048],
                                   kring_sb[:, jj].rearrange("p a b -> p (a b)"),
                                   inc="stg")
            else:
                for sv in range(TCH):
                    em.wait(T, "xnr", xnr_at[sv])
                    em.wait(V, "stg", em.val("stg"))
                    for u in range(2):
                        for dvh in range(2):
                            pb = psB if dvh == 0 else psC
                            tok0 = sv * 256 + u * 128
                            em.wait(T, "qev", em.val("qev"))
                            for cin in range(DCH):
                                em.emit(T, lambda e, ws=ws, cin=cin, tok0=tok0,
                                        pb=pb, dvh=dvh: e.matmul(
                                            pb[:, 0:512],
                                            xn_sb[:, cin, tok0:tok0 + 128],
                                            wqkv_sb[:, ws, cin,
                                                    dvh * 4:(dvh + 1) * 4, :]
                                            .rearrange("p a b -> p (a b)"),
                                            start=(cin == 0),
                                            stop=(cin == DCH - 1)),
                                        inc="qmm" if cin == DCH - 1 else None)
                        em.wait(V, "qmm")
                        em.emit(V, lambda e, u=u: e.tensor_copy(
                            vstg1_sb[:, u, 0:8, 0:64],
                            psB[:, 0:512].rearrange("p (a b) -> p a b", a=8)))
                        em.emit(V, lambda e, u=u: e.tensor_copy(
                            vstg1_sb[:, u, 8:16, 0:64],
                            psC[:, 0:512].rearrange("p (a b) -> p a b", a=8)),
                            inc="qev")
                    em.wait(G, "qev")
                    em.dma(G, kvin[sv][:, 2048:KVW],
                           vstg1_sb[:].rearrange("p a b c -> p (a b c)"),
                           inc="stg")
                em.wait(G, "stg")
                em.emit(G, lambda e: e.collective_compute(
                    "AllGather", ALU.bypass,
                    replica_groups=[[0, 1], [2, 3], [4, 5], [6, 7]],
                    ins=[kvin.ap().opt()],
                    outs=[kvg.ap().opt()]), inc="cc", inck=1)

        em.wait(SY, "cc", em.val("cc"))
        em.wait(SY, "qmm", em.val("qmm"))
        em.wait(T, "qev", em.val("qev"))
        em.wait(T, "fin", em.val("fin"))
        em.wait(T, "cc", em.val("cc"))
        vno = 0
        for pr in range(8):
            for band in range(8):
                qs = slice(band * BLK, (band + 1) * BLK)
                vs = visits_of(band)
                bb = band % 2
                nv = len(vs)
                kvs = "kv0" if bb == 0 else "kv1"
                guard = em.val("avm") - (len(visits_of(band - 1))
                                         if band > 0 else 0)
                em.wait(SY, "avm", guard)
                em.wait(SY, kvs, em.val(kvs))
                segs = [("own", band, 0, 1), ("r0", 0, 1, L0[band]),
                        ("r1", 0, 1 + L0[band], L1[band])]
                for reg, j0, at_, cnt in segs:
                    if cnt == 0:
                        continue
                    if reg == "own":
                        em.dma(SY, kband_sb[:, bb, at_],
                               kvin[j0][:, pr * 256:(pr + 1) * 256], inc=kvs)
                        em.dma(SY, vband_sb[:, bb, at_],
                               kvin[j0][:, 2048:KVW].rearrange(
                                   "p (a b) -> p a b", a=2)[:, :,
                                   pr * 130:(pr + 1) * 130], inc=kvs)
                    else:
                        rr = 0 if reg == "r0" else 1
                        em.dma(SY, kband_sb[:, bb, at_:at_ + cnt],
                               kvg[rr, j0:j0 + cnt][:, :,
                                                    pr * 256:(pr + 1) * 256]
                               .rearrange("s p b -> p s b"), inc=kvs)
                        for cc2 in range(cnt):
                            em.dma(SY, vband_sb[:, bb, at_ + cc2],
                                   kvg[rr, j0 + cc2][:, 2048:KVW]
                                   .rearrange("p (a b) -> p a b", a=2)
                                   [:, :, pr * 130:(pr + 1) * 130], inc=kvs)
                em.wait(T, kvs)
                base = vno

                def emit_av(vi, tail):
                    rr_ = (base + vi) % 2
                    if tail:
                        em.wait(T, "pxp")
                    else:
                        em.wait(T, "pxp", em.val("pxp") - 1)
                    if vi == 0:
                        em.wait(T, "fin", em.val("fin"))
                    for hh in range(2):
                        ob = psA[:, hh * 512: hh * 512 + 256]
                        for ksub in range(2):
                            em.emit(T, lambda e, bb=bb, vsl=vi, hh=hh,
                                    ksub=ksub, r=rr_, ob=ob: e.matmul(
                                        ob[0:65, :],
                                        vband_sb[:, bb, vsl, ksub,
                                                 hh * 65:(hh + 1) * 65],
                                        pp1_sb[:, r, hh * 512 + ksub * 256:
                                               hh * 512 + (ksub + 1) * 256],
                                        start=(vi == 0 and ksub == 0),
                                        stop=(vi == nv - 1 and ksub == 1)),
                                    inc="avm" if (hh == 1 and ksub == 1)
                                    else None)

                for vi, (reg, j) in enumerate(vs):
                    r = vno % 2
                    em.wait(T, "pxp", em.val("pxp") - 1)
                    mask = None
                    if reg == "own":
                        mask = maskd_sb
                    elif reg == "r0" and j == band:
                        mask = mv0_sb
                    elif reg == "r1" and j == band:
                        mask = mv1_sb
                    vsl = vi
                    sbk = psB if (vno % 2 == 0) else psC
                    for hh in range(2):
                        for ksub in range(2):
                            pt = sbk[:, hh * 512 + ksub * 256:
                                     hh * 512 + (ksub + 1) * 256]
                            if mask is not None:
                                em.emit(T, lambda e, mask=mask, ksub=ksub, pt=pt:
                                        e.matmul(pt, eye_sb[:], mask[:, ksub, :],
                                                 start=True, stop=False))
                            em.emit(T, lambda e, bb=bb, vsl=vsl, hh=hh,
                                    ksub=ksub, qs=qs, pr=pr, pt=pt,
                                    first=(mask is None):
                                    e.matmul(
                                        pt,
                                        kband_sb[0 + 64 * hh:64 + 64 * hh, bb,
                                                 vsl,
                                                 ksub * 128:(ksub + 1) * 128],
                                        q_sb[0 + 64 * hh:64 + 64 * hh, pr, qs],
                                        start=first, stop=True,
                                        tile_position=(64 * hh, 0)),
                                    inc="smm" if (hh == 1 and ksub == 1)
                                    else None)
                    em.wait(SC, "smm")
                    em.wait(SC, "avm", em.val("avm"))
                    em.emit(SC, lambda e, r=r, sbk=sbk: e.activation(
                        pp1_sb[:, r, :], sbk[:],
                        AF.Exp, bias=const_sb[:, 5:6], scale=const_sb[:, 6:7]),
                        inc="pxp")
                    if vi > 0:
                        emit_av(vi - 1, False)
                    vno += 1
                emit_av(nv - 1, True)
                em.wait(V, "avm")
                em.wait(V, "stg", em.val("stg"))
                em.emit(V, lambda e: e.tensor_copy(onorm_sb[0:65, 0, :],
                                                   psA[0:65, 0:256]))
                em.emit(V, lambda e: e.tensor_copy(onorm_sb[0:65, 1, :],
                                                   psA[0:65, 512:768]),
                        inc="fin")
                em.wait(G, "fin")
                em.dma(G, onorm_d[pr, band, 0], onorm_sb[0:65, 0, :],
                       inc="stg")
                em.dma(G, onorm_d[pr, band, 1], onorm_sb[0:65, 1, :],
                       inc="stg")

        em.wait(SY, "smm", em.val("smm"))
        em.wait(SY, "avm", em.val("avm"))
        em.wait(SY, "pxp", em.val("pxp"))
        em.wait(SY, "stg", em.val("stg"))
        em.dma(SY, bc16_sb[:], bc16_in, inc="wl")
        for cin in range(DCH):
            em.wait(SY, "casts", em.val("casts"))
            em.dma(SY, scrA_sb[:], wot_in[cin * 128:(cin + 1) * 128, :], inc="wl")
            em.wait(SC, "wl")
            em.emit(SC, lambda e, cin=cin: e.copy(
                wof_sb[:, cin].rearrange("p a b -> p (a b)"), scrA_sb[:]),
                inc="casts")
        em.wait(G, "stg", em.val("stg"))
        em.wait(T, "wl", em.val("wl"))
        em.wait(T, "casts")
        for band in range(8):
            qs = slice(band * BLK, (band + 1) * BLK)
            em.wait(SY, "ymm", em.val("ymm"))
            em.wait(SY, "kv0", em.val("kv0"))
            for cin in range(DCH):
                for h2 in range(2):
                    em.dma(SY, onld_sb[64 * h2:64 * h2 + 64, cin, :],
                           onorm_d[cin, band, h2][0:64, :], inc="kv0")
            em.wait(G, "kvl", em.val("kvl"))
            em.wait(G, "fin", em.val("fin"))
            for s8 in range(8):
                em.dma(G, lld_sb[2 * s8:2 * s8 + 2, 0, :],
                       onorm_d[s8, band][:, 64, :], inc="kvl")
            em.wait(V, "kv0")
            em.wait(V, "kvl")
            em.emit(V, lambda e: e.reciprocal(lld_sb[:, 1, :], lld_sb[:, 0, :]),
                    inc="fin")
            em.emit(V, lambda e: e.drain())
            for cin in range(DCH):
                em.wait(T, "fin")
                em.emit(T, lambda e, cin=cin: e.matmul(
                    psB[:, 0:256], bc16_sb[:, cin, :],
                    lld_sb[:, 1, :], start=True, stop=True), inc="ymt")
                em.wait(V, "ymt")
                em.emit(V, lambda e: e.tensor_copy(rlbb_sb[:], psB[:, 0:256]))
                em.emit(V, lambda e: e.drain())
                em.emit(V, lambda e, cin=cin: e.tensor_tensor(
                    onf_sb[:, cin, :], onld_sb[:, cin, :], rlbb_sb[:],
                    op=ALU.mult), inc="fin")
                em.emit(V, lambda e: e.drain())
            em.wait(T, "fin")
            em.wait(T, "xev", em.val("xev"))
            for c2 in range(DCH):
                for cin in range(DCH):
                    em.emit(T, lambda e, c2=c2, cin=cin: e.matmul(
                        psA[:, c2 * 256:(c2 + 1) * 256],
                        wof_sb[:, cin, c2, :], onf_sb[:, cin, :],
                        start=(cin == 0), stop=(cin == DCH - 1)),
                        inc="ymm" if (c2 == DCH - 1 and cin == DCH - 1)
                        else None)
            em.wait(V, "ymm")
            em.emit(V, lambda e, qs=qs: e.tensor_tensor(
                x_sb[:, :, qs], x_sb[:, :, qs],
                psA[:].rearrange("p (c m) -> p c m", c=DCH), op=ALU.add),
                inc="xev")
        em.emit(V, lambda e: e.drain())


    def body(em: E):
        V, SC, T, G, SY = "vector", "scalar", "tensor", "gpsimd", "sync"
        for c in range(DCH):
            em.dma(SY, x_sb[:, c, :], x_in[c], inc="ld")
        em.dma(SY, b2stg_sb[:], b2_in, inc="ld")
        em.emit(V, lambda e: e.memset(const_sb[:, 1:2], 1e-5), inc="castv")
        em.emit(V, lambda e: e.memset(onesrep_sb[:], 1.0 / D), inc="castv")
        em.emit(V, lambda e: e.memset(onesbf_sb[:], 1.0), inc="castv")
        em.emit(V, lambda e: e.memset(onesrow_sb[:], 1.0), inc="castv")
        em.emit(V, lambda e: e.memset(onescol_sb[:], 1.0), inc="castv")
        em.wait(SC, "ld", em.val("ld"))
        em.emit(SC, lambda e: e.mul(b2h_sb[:], b2stg_sb[:], 0.5), inc="casts")
        em.wait(T, "ld", em.val("ld"))
        em.wait(T, "castv", 5)
        em.wait(SC, "castv", 5)
        em.wait(V, "ld", em.val("ld"))

        emit_ffn0(em, True)
        if upto >= 2:
            emit_attn0(em)
        if upto >= 3:
            emit_ffn0(em, False)
        if upto >= 4:
            emit_ffn1(em, True)
        if upto >= 5:
            emit_attn1(em)
        if upto >= 6:
            emit_ffn1(em, False)

        em.wait(SY, "xnr")
        em.wait(SY, "yv", em.val("yv"))
        em.wait(SY, "xev", em.val("xev"))
        for c in range(DCH):
            em.dma(SY, y_out[c], x_sb[:, c, :], inc="out")
        em.wait(SY, "out")

    with nc.Block() as block:
        @block.vector
        def _(eng):
            body(E(nc, "vector", sems))

        @block.scalar
        def _(eng):
            body(E(nc, "scalar", sems))

        @block.tensor
        def _(eng):
            body(E(nc, "tensor", sems))

        @block.gpsimd
        def _(eng):
            body(E(nc, "gpsimd", sems))

        @block.sync
        def _(eng):
            body(E(nc, "sync", sems))

    est.close()
    return nc



def core_idx(core):
    pr = core % 2
    return np.concatenate([np.arange(blk * BLK, (blk + 1) * BLK)
                           for blk in PAT[pr]])


def make_masks():
    j = np.arange(BLK)[None, :]
    pp = np.arange(128)[:, None]
    d0 = np.where(pp <= j, 0.0, NEG).astype(np.float32)
    d1 = np.where(pp + 128 <= j, 0.0, NEG).astype(np.float32)
    maskd = np.ascontiguousarray(np.stack([d0, d1], axis=1))
    full = np.full((128, 2, BLK), NEG, np.float32)
    zero = np.zeros((128, 2, BLK), np.float32)
    return maskd, full, zero


def stage_inputs(d):
    x = np.asarray(d["L0_attn_x"], np.float32)
    w1 = np.ascontiguousarray(np.asarray(d["L0_ffn_w1"], np.float32))
    w2 = np.ascontiguousarray(np.asarray(d["L0_ffn_w2"], np.float32))
    b1 = np.asarray(d["L0_ffn_b1"], np.float32)
    b2 = np.asarray(d["L0_ffn_b2"], np.float32)
    maskd, full, zero = make_masks()
    prot = np.zeros((128, 128), np.float32)
    for blk0 in (0, 64):
        for i in range(32):
            prot[blk0 + 2 * i + 1, blk0 + 2 * i] = -1.0
            prot[blk0 + 2 * i, blk0 + 2 * i + 1] = 1.0
    bc16 = np.zeros((16, 8, 128), np.float32)
    for cin8 in range(8):
        bc16[2 * cin8, cin8, 0:64] = 1.0
        bc16[2 * cin8 + 1, cin8, 64:128] = 1.0
    maps = []
    for core in range(8):
        b = core // 2
        pr = core % 2
        idx = core_idx(core)
        inv = 10000.0 ** (-np.arange(0, 64, 2, dtype=np.float64) / 64.0)
        ang = idx[None, :].astype(np.float64) * inv[:, None]
        cos64 = np.repeat(np.cos(ang), 2, axis=0)
        sin64 = np.repeat(np.sin(ang), 2, axis=0)
        rope_c = np.ascontiguousarray(
            np.concatenate([cos64, cos64], axis=0).astype(np.float32))
        rope_s = np.ascontiguousarray(
            np.concatenate([sin64, sin64], axis=0).astype(np.float32))
        xs = np.ascontiguousarray(x[b][idx].T).reshape(DCH, 128, MYT)
        maps.append({
            "x_in": xs,
            "w1": w1,
            "w2": w2,
            "b1r": np.ascontiguousarray(b1.reshape(1, 4 * D)),
            "b2": np.ascontiguousarray(b2.reshape(DCH, 128).T),
            "wq": np.ascontiguousarray(np.asarray(d["L0_attn_Wq"], np.float32)),
            "wk": np.ascontiguousarray(np.asarray(d["L0_attn_Wk"], np.float32)),
            "wv": np.ascontiguousarray(np.asarray(d["L0_attn_Wv"], np.float32)),
            "wo": np.ascontiguousarray(np.asarray(d["L0_attn_Wo"], np.float32)),
            "maskd": maskd,
            "mv0": full if pr == 0 else zero,
            "mv1": zero if pr == 0 else full,
            "eye": np.eye(128, dtype=np.float32),
            "wt": np.ascontiguousarray(np.asarray(d["L1_ffn_weight"], np.float32).T),
            "ut": np.ascontiguousarray(
                np.asarray(d["L1_ffn_lora_U"], np.float32).reshape(128, D).T),
            "vs": np.ascontiguousarray(
                np.asarray(d["L1_ffn_lora_V"], np.float32).transpose(0, 2, 1)
                .reshape(128, D)),
            "gw": np.ascontiguousarray(np.asarray(d["L1_ffn_gate_w"], np.float32).T),
            "g16": np.ascontiguousarray(
                np.repeat(np.eye(8, dtype=np.float32), 16, axis=1)),
            "wqkvt": np.ascontiguousarray(
                np.asarray(d["L1_attn_W_qkv"], np.float32).T),
            "wot": np.ascontiguousarray(np.asarray(d["L1_attn_W_o"], np.float32).T),
            "cosi": rope_c, "sini": rope_s,
            "prot": prot,
            "bc16": bc16,
        })
    return maps


def gather_output(results):
    out = np.zeros((4, S, D), np.float32)
    for core in range(8):
        b = core // 2
        idx = core_idx(core)
        y = results[core]["y_out"].reshape(D, MYT).T
        out[b][idx] = y
    return out


from concourse.bass_utils import run_bass_kernel_spmd

_NC_CACHE = []


def kernel(**inputs):
    maps = stage_inputs(inputs)
    if not _NC_CACHE:
        _NC_CACHE.append(build_program(6))
    nc = _NC_CACHE[0]
    res = run_bass_kernel_spmd(nc, maps, core_ids=list(range(8)))
    return gather_output(res.results)


# version 07
# speedup vs baseline: 1.3594x; 1.0527x over previous
import numpy as np
from contextlib import ExitStack
import concourse.bass as bass
import concourse.mybir as mybir

F32 = mybir.dt.float32
BF16 = mybir.dt.bfloat16
AF = mybir.ActivationFunctionType
ALU = mybir.AluOpType

D = 1024
S = 4096
BLK = 256
NBLK = S // BLK
MYB = 8
MYT = MYB * BLK
DCH = 8
HCH = 32
TCH = MYT // BLK
M0 = 12.0
M1 = 26.0
NEG = -1.0e30

PAT = [[0, 3, 4, 7, 8, 11, 12, 15], [1, 2, 5, 6, 9, 10, 13, 14]]
C1 = [1, 1, 3, 3, 5, 5, 7, 7]
C0 = [0, 2, 2, 4, 4, 6, 6, 8]
L0 = [max(i, C1[i]) for i in range(8)]
L1 = [max(C0[i], i) for i in range(8)]
RING = 2

SEM_NAMES = ["ld", "wl", "castv", "casts", "sq", "stm", "rowe", "rowms",
             "rowmv", "xnr", "hmm", "hp", "ymm", "xev", "out", "wl2",
             "qmm", "qev", "stg", "stg2", "cc", "kv0", "kv1", "kvl", "gwl", "smm", "pxp", "avm",
             "fin", "ymt", "yv"]


def visits_of(band):
    vs = [("own", band)]
    vs += [("r0", j) for j in range(L0[band])]
    vs += [("r1", j) for j in range(L1[band])]
    return vs


class E:

    def __init__(self, nc, engname, sems):
        self.nc = nc
        self.engname = engname
        self.engines = {"vector": nc.vector, "scalar": nc.scalar,
                        "tensor": nc.tensor, "gpsimd": nc.gpsimd, "sync": nc.sync}
        self.semh = sems
        self.cnt = {k: 0 for k in sems}

    def _fin(self, ins, inc, inck):
        if inc is not None:
            if ins is not None:
                ins.then_inc(self.semh[inc], inck)
            self.cnt[inc] += inck
        return ins

    def emit(self, engine, fn, inc=None, inck=1):
        ins = fn(self.engines[engine]) if self.engname == engine else None
        return self._fin(ins, inc, inck)

    def dma(self, engine, out, in_, inc=None):
        ins = None
        if self.engname == engine:
            ins = self.engines[engine].dma_start(out=out, in_=in_)
        return self._fin(ins, inc, 16 if inc else 0)

    def wait(self, engine, sem, val=None):
        if val is None:
            val = self.cnt[sem]
        if val <= 0:
            return
        if self.engname == engine:
            self.engines[engine].wait_ge(self.semh[sem], val)

    def val(self, sem):
        return self.cnt[sem]


def build_program(upto=2):
    QH = HCH // 4
    nc = bass.Bass("TRN2", target_bir_lowering=False, debug=False)

    def din(name, shape, dtype=F32):
        return nc.dram_tensor(name, shape, dtype, kind="ExternalInput").ap()

    x_in = din("x_in", [DCH, 128, MYT])
    w1_in = din("w1", [D, 4 * D])
    w2_in = din("w2", [4 * D, D])
    b1r_in = din("b1r", [1, 4 * D])
    b2_in = din("b2", [128, DCH])
    wq_in = din("wq", [D, D])
    wk_in = din("wk", [D, D])
    wv_in = din("wv", [D, D])
    wo_in = din("wo", [D, D])
    maskd_in = din("maskd", [128, 2, BLK])
    mv0_in = din("mv0", [128, 2, BLK])
    mv1_in = din("mv1", [128, 2, BLK])
    eye_in = din("eye", [128, 128])
    wt_in = din("wt", [D, D])
    ut_in = din("ut", [D, 128])
    vs_in = din("vs", [128, D])
    gw_in = din("gw", [D, 8])
    g16_in = din("g16", [8, 128])
    wqkvt_in = din("wqkvt", [D, 3 * D])
    wot_in = din("wot", [D, D])
    cosi_in = din("cosi", [128, MYT])
    sini_in = din("sini", [128, MYT])
    prot_in = din("prot", [128, 128])
    bc16_in = din("bc16", [16, 8, 128])
    y_out = nc.dram_tensor("y_out", [DCH, 128, MYT], F32, kind="ExternalOutput").ap()

    KVW = DCH * BLK + 2 * 1040
    kvin = nc.dram_tensor("kvin", [MYB, 128, KVW], BF16)
    kvg = nc.dram_tensor("kvg", [2, MYB, 128, KVW], BF16)
    onorm_d = nc.dram_tensor("onorm_d", [8, 8, 2, 65, BLK], BF16)
    w1c_d = nc.dram_tensor("w1c", [4, DCH, 128, 1024], BF16)
    w2c_d = nc.dram_tensor("w2c", [4, QH, 128, 1024], BF16)
    b1c_d = nc.dram_tensor("b1c", [4, 1, 1024], BF16)
    wtc_d = nc.dram_tensor("wtc", [DCH, 128, 1024], BF16)
    smallc_d = nc.dram_tensor("smallc", [4, 128, 1024], BF16)

    BASE = 16512
    off = [BASE]
    def at(name, shape, dtype, offset=None):
        if offset is None:
            offset = off[0]
        per_part = int(np.prod(shape[1:])) * mybir.dt.size(dtype)
        h = nc.alloc_sbuf_tensor_at(name, shape, dtype, offset=offset)
        off[0] = max(off[0], offset + ((per_part + 63) // 64) * 64)
        return h

    x_sb = at("x_sb", [128, DCH, MYT], F32)
    xn_sb = at("xn_sb", [128, DCH, MYT], BF16)
    stat_sb = at("stat_sb", [128, 4, BLK], F32)
    const_sb = at("const_sb", [128, 16], F32)
    onesrep_sb = at("onesrep_sb", [128, 128], F32)
    onesbf_sb = at("onesbf_sb", [1, BLK], BF16)
    onescol_sb = at("onescol_sb", [128, 2], BF16)
    onesrow_sb = at("onesrow_sb", [1, 128], F32)
    b2h_sb = at("b2h_sb", [128, DCH], F32)
    b2stg_sb = at("b2stg_sb", [128, DCH], F32)
    PERS_END = off[0]

    FREG = PERS_END
    w1q_sb = at("w1q_sb", [128, 2, DCH, QH, 128], BF16, FREG)
    w2q_sb = at("w2q_sb", [128, 2, QH, DCH, 128], BF16, FREG + 32768)
    h_sb = at("h_sb", [128, 2, QH, BLK], BF16, FREG + 65536)
    scr_sb = at("scr_sb", [128, 1024], F32, FREG + 73728)
    b1q_sb = at("b1q_sb", [1, 2, QH * 128], BF16, FREG + 77824)
    b1qstg_sb = at("b1qstg_sb", [1, 2, QH * 128], F32, FREG + 81920)
    xsq_sb = at("xsq_sb", [128, 1024], F32, FREG + 90112)
    FFN_END = FREG + 94208

    wt_sb = at("wt_sb", [128, DCH, DCH, 128], BF16, FREG)
    ut_sb = at("ut_sb", [128, DCH, 128], BF16, FREG + 16384)
    vs_sb = at("vs_sb", [128, DCH, 128], BF16, FREG + 18432)
    gw_sb = at("gw_sb", [128, DCH, 8], BF16, FREG + 20480)
    g16_sb = at("g16_sb", [8, 128], BF16, FREG + 20608)
    gates_sb = at("gates_sb", [128, MYT], BF16, FREG + 20864)
    gte_sb = at("gte_sb", [128, 2, 8], F32, FREG + 24960)
    gtr_sb = at("gtr_sb", [128, 4], F32, FREG + 25024)
    gtn_sb = at("gtn_sb", [128, 2, 8], F32, FREG + 25088)
    gb_sb = at("gb_sb", [128, 2, 512], BF16, FREG + 25152)
    tg_sb = at("tg_sb", [128, 2, 512], BF16, FREG + 27200)
    g16stg_sb = at("g16stg_sb", [8, 128], F32, FREG + 29248)
    eyef_sb = at("eyef_sb", [128, 128], F32, FREG + 29760)

    q_sb = at("q_sb", [128, DCH, MYT], BF16, FREG)
    wqkv_sb = at("wqkv_sb", [128, 2, DCH, DCH, 128], BF16, FREG + 32768)
    kring_sb = at("kring_sb", [128, RING, DCH, BLK], BF16, FREG + 65536)
    vring_sb = at("vring_sb", [128, RING, 2, 1024], BF16, FREG + 73728)
    pp_sb = at("pp_sb", [128, 2, 512], BF16, FREG + 82048)
    maskd_sb = at("maskd_sb", [128, 2, BLK], BF16, FREG + 84096)
    mv0_sb = at("mv0_sb", [128, 2, BLK], BF16, FREG + 85120)
    mv1_sb = at("mv1_sb", [128, 2, BLK], BF16, FREG + 86144)
    eye_sb = at("eye_sb", [128, 128], BF16, FREG + 87168)
    wo_sb = at("wo_sb", [128, 4, DCH, 128], BF16, FREG + 87424)
    lrow_sb = at("lrow_sb", [128, BLK], F32, FREG + 95616)
    rl_sb = at("rl_sb", [128, BLK], F32, FREG + 96640)
    rlb_sb = at("rlb_sb", [128, BLK], F32, FREG + 97664)
    onorm_sb = at("onorm_sb", [128, 4, BLK], BF16, FREG + 98688)
    scrA_sb = at("scrA_sb", [128, 1024], F32, FREG + 100736)
    ATT_END = FREG + 104832

    XNOFF = BASE + 65536
    kband_sb = at("kband_sb", [128, 2, 16, BLK], BF16, XNOFF)
    vband_sb = at("vband_sb", [128, 2, 16, 2, 130], BF16, XNOFF + 16384)
    pp1_sb = at("pp1_sb", [128, 2, 1024], BF16, FREG + 100736)
    bc16_sb = at("bc16_sb", [16, DCH, 128], F32, FREG + 65536)
    lld_sb = at("lld_sb", [16, 2, BLK], F32, FREG + 91648)
    rlbb_sb = at("rlbb_sb", [128, BLK], F32, FREG + 93696)
    onf_sb = at("onf_sb", [128, DCH, BLK], BF16, FREG + 16384)
    cosi_sb = at("cosi_sb", [128, MYT], BF16, FREG + 73728)
    sini_sb = at("sini_sb", [128, MYT], BF16, FREG + 77824)
    vstg1_sb = at("vstg1_sb", [128, 2, 16, 65], BF16, FREG + 87424)
    prot_sb = at("prot_sb", [128, 128], BF16, FREG + 104832)
    ropet_sb = at("ropet_sb", [128, 512], BF16, FREG + 105088)
    wof_sb = at("wof_sb", [128, DCH, DCH, 128], BF16, FREG)
    onld_sb = at("onld_sb", [128, DCH, BLK], BF16, FREG + 32768)
    A1_END = FREG + 106112
    assert max(FFN_END, ATT_END, A1_END) + 0 <= 229344, (FFN_END, ATT_END, A1_END)

    est = ExitStack()
    psA = est.enter_context(nc.psum_tensor("psA", [128, 2048], F32))
    psB = est.enter_context(nc.psum_tensor("psB", [128, 1024], F32))
    psC = est.enter_context(nc.psum_tensor("psC", [128, 1024], F32))
    sems = {n: est.enter_context(nc.semaphore(n)) for n in SEM_NAMES}


    def emit_ln_chunk(em, t, xnr_base):
        V, SC, T = "vector", "scalar", "tensor"
        sl = slice(t * BLK, (t + 1) * BLK)
        em.wait(T, "rowe", em.val("rowe"))
        em.wait(T, "hp", em.val("hp"))
        em.wait(T, "pxp", em.val("pxp"))
        em.wait(T, "xnr", xnr_base)
        em.wait(SC, "xnr", xnr_base)
        em.wait(SC, "ymt", em.val("ymt"))
        for c in range(DCH):
            em.emit(T, lambda e, c=c, sl=sl: e.matmul(
                psB[:, 0:256], onesrep_sb[:], x_sb[:, c, sl],
                start=(c == 0), stop=(c == DCH - 1)),
                inc="stm" if c == DCH - 1 else None)
        for half in range(2):
            c0 = half * 4
            em.wait(SC, "stm", em.val("stm"))
            em.emit(SC, lambda e, c0=c0, sl=sl: e.activation(
                xsq_sb[:].rearrange("p (c m) -> p c m", c=4),
                x_sb[:, c0:c0 + 4, sl], AF.Square), inc="sq")
            em.wait(T, "sq")
            for j in range(4):
                em.emit(T, lambda e, j=j, half=half: e.matmul(
                    psB[:, 256:512], onesrep_sb[:], xsq_sb[:, j * 256:(j + 1) * 256],
                    start=(half == 0 and j == 0), stop=(half == 1 and j == 3)),
                    inc="stm" if j == 3 else None)
        em.wait(V, "stm")
        em.emit(V, lambda e: e.tensor_copy(stat_sb[:, 0, :], psB[:, 0:256]),
                inc="rowe")
        em.emit(V, lambda e: e.tensor_copy(stat_sb[:, 1, :], psB[:, 256:512]),
                inc="rowe")
        em.wait(SC, "rowe")
        em.emit(SC, lambda e: e.activation(stat_sb[:, 2, :], stat_sb[:, 0, :],
                                           AF.Square), inc="rowms")
        em.wait(V, "rowms")
        em.emit(V, lambda e: e.tensor_tensor(stat_sb[:, 1, :], stat_sb[:, 1, :],
                                             stat_sb[:, 2, :], op=ALU.subtract),
                inc="rowmv")
        em.wait(SC, "rowmv")
        em.emit(SC, lambda e: e.activation(stat_sb[:, 2, :], stat_sb[:, 1, :],
                                           AF.Sqrt, bias=const_sb[:, 1:2]),
                inc="rowms")
        em.wait(V, "rowms")
        em.emit(V, lambda e: e.reciprocal(stat_sb[:, 1, :], stat_sb[:, 2, :]),
                inc="rowmv")
        for c in range(DCH):
            em.emit(V, lambda e, c=c, sl=sl: e.tensor_tensor(
                xn_sb[:, c, sl], x_sb[:, c, sl], stat_sb[:, 0, :],
                op=ALU.subtract))
        em.emit(V, lambda e: e.drain())
        for c in range(DCH):
            em.emit(V, lambda e, c=c, sl=sl: e.tensor_tensor(
                xn_sb[:, c, sl], xn_sb[:, c, sl], stat_sb[:, 1, :],
                op=ALU.mult), inc="xnr" if c == DCH - 1 else None)
        em.emit(V, lambda e: e.drain())


    def emit_ffn0(em, first):
        V, SC, T, G, SY = "vector", "scalar", "tensor", "gpsimd", "sync"
        xnr_base = em.val("xnr")
        xnr_at = []
        for t in range(TCH):
            emit_ln_chunk(em, t, xnr_base)
            xnr_at.append(em.val("xnr"))
        XNR_ALL = em.val("xnr")
        for sem in ("smm", "avm", "ymt", "pxp", "qmm", "yv"):
            em.wait(SY, sem, em.val(sem))
        for q in range(4):
            sg = q % 2
            em.wait(SY, "xev", em.val("xev") - 2 * TCH)
            if first:
                qsem = "stg" if q % 2 == 0 else "stg2"
                em.wait(G, qsem, em.val(qsem))
                em.wait(G, "ymm", max(0, em.val("ymm") - 16))
                em.dma(G, w1q_sb[:, sg].rearrange("p c h m -> p c (h m)"),
                       w1_in.rearrange("(c p) m -> p c m", p=128)
                       [:, :, q * 1024:(q + 1) * 1024], inc="gwl")
                em.dma(G, w2q_sb[:, sg].rearrange("p h d m -> p h (d m)"),
                       w2_in.rearrange("(a p) m -> p a m", p=128)
                       [:, q * QH:(q + 1) * QH, :], inc="gwl")
                em.dma(G, b1q_sb[:, sg, :],
                       b1r_in[:, q * QH * 128:(q + 1) * QH * 128], inc="gwl")
                em.wait(G, "gwl", em.val("gwl"))
                em.dma(G, w1c_d[q].rearrange("c p m -> p c m"),
                       w1q_sb[:, sg].rearrange("p c h m -> p c (h m)"),
                       inc=qsem)
                em.dma(G, w2c_d[q].rearrange("h p m -> p h m"),
                       w2q_sb[:, sg].rearrange("p h d m -> p h (d m)"),
                       inc=qsem)
                em.dma(G, b1c_d[q], b1q_sb[:, sg, :].rearrange("p m -> p m"),
                       inc=qsem)
            else:
                em.wait(SY, "stg", em.val("stg"))
                em.wait(SY, "stg2", em.val("stg2"))
                em.wait(SY, "wl", em.val("wl"))
                em.dma(SY, w1q_sb[:, sg].rearrange("p c h m -> p c (h m)"),
                       w1c_d[q].rearrange("c p m -> p c m"), inc="wl")
                em.dma(SY, w2q_sb[:, sg].rearrange("p h d m -> p h (d m)"),
                       w2c_d[q].rearrange("h p m -> p h m"), inc="wl")
                em.dma(SY, b1q_sb[:, sg, :], b1c_d[q], inc="wl")
                em.wait(T, "wl", em.val("wl"))
            em.wait(T, "castv")
            em.wait(T, "casts")
            em.wait(T, "gwl", em.val("gwl"))
            if q == 0:
                em.wait(T, "rowe")
            for t in range(TCH):
                k = q * TCH + t
                hs = k % 2
                sl = slice(t * BLK, (t + 1) * BLK)
                if q == 0:
                    em.wait(T, "xnr", xnr_at[t])
                em.wait(T, "hp", em.val("hp"))
                for grp in range(2):
                    pg = psB if grp == 0 else psC
                    for j in range(4):
                        hh = grp * 4 + j
                        pa = pg[:, j * 256:(j + 1) * 256]
                        em.emit(T, lambda e, sg=sg, hh=hh, pa=pa: e.matmul(
                            pa, b1q_sb[:, sg, hh * 128:(hh + 1) * 128],
                            onesbf_sb[:], start=True, stop=False))
                        for c in range(DCH):
                            em.emit(T, lambda e, sg=sg, hh=hh, c=c, sl=sl, pa=pa:
                                    e.matmul(pa, w1q_sb[:, sg, c, hh, :],
                                             xn_sb[:, c, sl],
                                             start=False, stop=(c == DCH - 1)),
                                    inc="hmm" if c == DCH - 1 else None)
                em.wait(SC, "hmm")
                em.wait(SC, "ymm", em.val("ymm") - 1)
                em.emit(SC, lambda e, hs=hs: e.activation(
                    h_sb[:, hs, 0:4].rearrange("p h m -> p (h m)"), psB[:],
                    AF.Gelu))
                em.emit(SC, lambda e, hs=hs: e.activation(
                    h_sb[:, hs, 4:8].rearrange("p h m -> p (h m)"), psC[:],
                    AF.Gelu), inc="hp")
                em.wait(T, "hp")
                for half in range(2):
                    em.wait(T, "xev", em.val("xev") - 1)
                    for c in range(half * 4, half * 4 + 4):
                        for hh in range(QH):
                            em.emit(T, lambda e, sg=sg, c=c, hh=hh, hs=hs:
                                    e.matmul(
                                        psA[:, c * 256:(c + 1) * 256],
                                        w2q_sb[:, sg, hh, c, :],
                                        h_sb[:, hs, hh, :],
                                        start=(hh == 0), stop=(hh == QH - 1)),
                                    inc="ymm" if (c % 4 == 3 and hh == QH - 1)
                                    else None)
                    em.wait(V, "ymm")
                    em.emit(V, lambda e, sl=sl, half=half:
                            e.scalar_tensor_tensor(
                                x_sb[:, half * 4:half * 4 + 4, sl],
                                psA[:, half * 1024:(half + 1) * 1024]
                                .rearrange("p (c m) -> p c m", c=4),
                                0.5, x_sb[:, half * 4:half * 4 + 4, sl],
                                op0=ALU.mult, op1=ALU.add),
                        inc="xev")
        em.wait(V, "xev")
        em.wait(V, "casts", 1)
        for t in range(TCH):
            sl = slice(t * BLK, (t + 1) * BLK)
            for c in range(DCH):
                em.emit(V, lambda e, c=c, sl=sl: e.tensor_scalar_add(
                    x_sb[:, c, sl], x_sb[:, c, sl], b2h_sb[:, c:c + 1]),
                    inc="xnr")
        em.emit(V, lambda e: e.drain())


    def emit_attn0(em):
        xnr_base0 = None
        V, SC, T, G, SY = "vector", "scalar", "tensor", "gpsimd", "sync"
        xnr_base = em.val("xnr")
        for t in range(TCH):
            emit_ln_chunk(em, t, xnr_base)
        XNR_ALL = em.val("xnr")

        em.wait(SY, "xnr", XNR_ALL)
        em.wait(SY, "yv", em.val("yv"))
        for sem in ("ymm", "hmm", "hp", "xev"):
            em.wait(SY, sem, em.val(sem))
        em.dma(SY, scrA_sb[:, 0:512],
               maskd_in.rearrange("p a b -> p (a b)"), inc="wl")
        em.dma(SY, scrA_sb[:, 512:640], eye_in, inc="wl")
        em.wait(V, "wl")
        em.wait(V, "stg", em.val("stg"))
        em.wait(SC, "stg", em.val("stg"))
        em.wait(V, "stg2", em.val("stg2"))
        em.wait(SC, "stg2", em.val("stg2"))
        em.emit(V, lambda e: e.tensor_copy(
            maskd_sb[:].rearrange("p a b -> p (a b)"), scrA_sb[:, 0:512]),
            inc="castv")
        em.emit(V, lambda e: e.tensor_copy(eye_sb[:], scrA_sb[:, 512:640]),
                inc="castv")
        em.wait(SY, "castv", em.val("castv"))
        em.dma(SY, scrA_sb[:, 0:512],
               mv0_in.rearrange("p a b -> p (a b)"), inc="wl")
        em.dma(SY, scrA_sb[:, 512:1024],
               mv1_in.rearrange("p a b -> p (a b)"), inc="wl")
        em.wait(V, "wl")
        em.emit(V, lambda e: e.tensor_copy(
            mv0_sb[:].rearrange("p a b -> p (a b)"), scrA_sb[:, 0:512]),
            inc="castv")
        em.emit(V, lambda e: e.tensor_copy(
            mv1_sb[:].rearrange("p a b -> p (a b)"), scrA_sb[:, 512:1024]),
            inc="castv")
        em.emit(V, lambda e: e.memset(const_sb[:, 2:3], -M0), inc="castv")
        em.emit(V, lambda e: e.memset(const_sb[:, 3:4], 1.0 / 32.0), inc="castv")

        em.wait(T, "xnr", XNR_ALL)
        em.wait(T, "rowe")
        for pos, m in enumerate([1, 2, 0]):
            w_in = [wq_in, wk_in, wv_in][m]
            ws = pos % 2
            em.wait(SY, "qmm", em.val("qmm") - 32)
            for cin in range(DCH):
                em.wait(SY, "castv", em.val("castv"))
                em.dma(SY, scrA_sb[:], w_in[cin * 128:(cin + 1) * 128, :],
                       inc="wl")
                em.wait(V, "wl")
                em.emit(V, lambda e, ws=ws, cin=cin: e.tensor_copy(
                    wqkv_sb[:, ws, cin].rearrange("p a b -> p (a b)"),
                    scrA_sb[:]), inc="castv")
            em.wait(T, "castv")
            if m < 2:
                for tc in range(4):
                    ts = slice(tc * 512, (tc + 1) * 512)
                    if m == 1:
                        em.wait(V, "stg", em.val("stg"))
                    for c2 in range(DCH):
                        kq = em.val("qmm")
                        pa = psA[:, (kq % 4) * 512:(kq % 4 + 1) * 512]
                        em.wait(T, "qev", em.val("qev") - 3)
                        for cin in range(DCH):
                            em.emit(T, lambda e, ws=ws, c2=c2, cin=cin, ts=ts,
                                    pa=pa: e.matmul(
                                        pa, wqkv_sb[:, ws, cin, c2, :],
                                        xn_sb[:, cin, ts],
                                        start=(cin == 0), stop=(cin == DCH - 1)),
                                    inc="qmm" if cin == DCH - 1 else None)
                        em.wait(V, "qmm")
                        if m == 0:
                            em.emit(V, lambda e, c2=c2, ts=ts, pa=pa:
                                    e.tensor_copy(q_sb[:, c2, ts], pa),
                                    inc="qev")
                        else:
                            for jj in range(2):
                                em.emit(V, lambda e, c2=c2, jj=jj, pa=pa:
                                        e.tensor_copy(
                                            kring_sb[:, jj, c2, :],
                                            pa[:, jj * 256:(jj + 1) * 256]),
                                        inc="qev" if jj == 1 else None)
                    if m == 1:
                        em.wait(G, "qev")
                        for jj in range(2):
                            em.dma(G, kvin[2 * tc + jj][:, 0:2048],
                                   kring_sb[:, jj].rearrange("p a b -> p (a b)"),
                                   inc="stg")
            else:
                for sv in range(TCH):
                    vslot = sv % 2
                    em.wait(V, "stg", em.val("stg"))
                    for u in range(2):
                        for dvh in range(2):
                            pb = psB if dvh == 0 else psC
                            tok0 = sv * 256 + u * 128
                            em.wait(T, "qev", em.val("qev"))
                            for cin in range(DCH):
                                em.emit(T, lambda e, ws=ws, cin=cin, tok0=tok0,
                                        pb=pb, dvh=dvh: e.matmul(
                                            pb[:, 0:512],
                                            xn_sb[:, cin, tok0:tok0 + 128],
                                            wqkv_sb[:, ws, cin,
                                                    dvh * 4:(dvh + 1) * 4, :]
                                            .rearrange("p a b -> p (a b)"),
                                            start=(cin == 0),
                                            stop=(cin == DCH - 1)),
                                        inc="qmm" if cin == DCH - 1 else None)
                        em.wait(V, "qmm")
                        em.emit(V, lambda e, vslot=vslot, u=u: e.tensor_copy(
                            vring_sb[:, vslot, u, 0:512], psB[:, 0:512]))
                        em.emit(V, lambda e, vslot=vslot, u=u: e.tensor_copy(
                            vring_sb[:, vslot, u, 512:1024], psC[:, 0:512]),
                            inc="qev")
                    em.wait(G, "qev")
                    em.dma(G, kvin[sv][:, 2048:4096],
                           vring_sb[:, vslot].rearrange("p a b -> p (a b)"),
                           inc="stg")
                em.emit(V, lambda e: e.memset(pp_sb[:, 0, 0:32], 0.0),
                        inc="fin")
                em.wait(G, "fin", em.val("fin"))
                for sv in range(MYB):
                    em.dma(G, kvin[sv][:, 4096:KVW], pp_sb[:, 0, 0:32],
                           inc="stg")
                em.wait(G, "stg")
                em.emit(G, lambda e: e.collective_compute(
                    "AllGather", ALU.bypass,
                    replica_groups=[[0, 1], [2, 3], [4, 5], [6, 7]],
                    ins=[kvin.ap().opt()],
                    outs=[kvg.ap().opt()]), inc="cc", inck=1)

        em.wait(SY, "cc", 1)
        em.wait(T, "qev", em.val("qev"))
        em.wait(T, "cc", 1)
        vno = 0
        for p in range(2):
            for cin in range(4):
                em.wait(SY, "casts", em.val("casts"))
                em.wait(SY, "castv", em.val("castv"))
                if p == 1:
                    em.wait(SY, "ymt", em.val("ymt"))
                em.dma(SY, scrA_sb[:],
                       wo_in[(4 * p + cin) * 128:(4 * p + cin + 1) * 128, :],
                       inc="wl")
                em.wait(SC, "wl")
                em.wait(SC, "stm", em.val("stm"))
                em.emit(SC, lambda e, cin=cin: e.copy(
                    wo_sb[:, cin].rearrange("p a b -> p (a b)"), scrA_sb[:]),
                    inc="casts")
            em.wait(T, "casts")
            for band in range(8):
                qs = slice(band * BLK, (band + 1) * BLK)
                vs = visits_of(band)
                for vi, (reg, j) in enumerate(vs):
                    r = vno % RING
                    sb_ = vno % 2
                    kvs = "kv0" if r == 0 else "kv1"
                    em.wait(SY, "avm", em.val("avm") - (RING - 1))
                    em.wait(SY, "pxp", em.val("pxp") - (RING - 1))
                    em.wait(SY, kvs, em.val(kvs))
                    slot = {"own": kvin[j], "r0": kvg[0, j],
                            "r1": kvg[1, j]}[reg]
                    em.dma(SY, kring_sb[:, r].rearrange("p a b -> p (a b)"),
                           slot[:, 0:2048], inc=kvs)
                    em.dma(SY, vring_sb[:, r].rearrange("p a b -> p (a b)"),
                           slot[:, 2048:4096], inc=kvs)
                    em.wait(T, kvs)
                    em.wait(T, "pxp", em.val("pxp") - 1)
                    mask = None
                    if reg == "own":
                        mask = maskd_sb
                    elif reg == "r0" and j == band:
                        mask = mv0_sb
                    elif reg == "r1" and j == band:
                        mask = mv1_sb
                    for ksub in range(2):
                        pt = psB[:, sb_ * 512 + ksub * 256:
                                 sb_ * 512 + (ksub + 1) * 256]
                        if mask is not None:
                            em.emit(T, lambda e, mask=mask, ksub=ksub, pt=pt:
                                    e.matmul(pt, eye_sb[:], mask[:, ksub, :],
                                             start=True, stop=False))
                        for cin in range(DCH):
                            first = (mask is None and cin == 0)
                            em.emit(T, lambda e, r=r, cin=cin, ksub=ksub, qs=qs,
                                    pt=pt, first=first:
                                    e.matmul(pt,
                                             kring_sb[:, r, cin,
                                                      ksub * 128:(ksub + 1) * 128],
                                             q_sb[:, cin, qs],
                                             start=first, stop=(cin == DCH - 1)),
                                    inc="smm" if (ksub == 1 and cin == DCH - 1)
                                    else None)
                    em.wait(SC, "smm")
                    em.wait(SC, "avm", em.val("avm") - 1)
                    em.emit(SC, lambda e, sb_=sb_: e.activation(
                        pp_sb[:, sb_, :], psB[:, sb_ * 512:(sb_ + 1) * 512],
                        AF.Exp, bias=const_sb[:, 2:3], scale=const_sb[:, 3:4]),
                        inc="pxp")
                    em.wait(T, "pxp")
                    if vi == 0:
                        em.wait(T, "fin", em.val("fin"))
                    for ksub in range(2):
                        for cdv in range(4):
                            h0 = 8 * p + 2 * cdv
                            em.emit(T, lambda e, r=r, ksub=ksub, cdv=cdv, h0=h0,
                                    sb_=sb_: e.matmul(
                                        psA[:, cdv * 512:cdv * 512 + 256],
                                        vring_sb[:, r, ksub,
                                                 (h0 // 2) * 128:(h0 // 2) * 128 + 128],
                                        pp_sb[:, sb_, ksub * 256:(ksub + 1) * 256],
                                        start=(vi == 0 and ksub == 0),
                                        stop=(vi == len(vs) - 1 and ksub == 1)))
                        em.emit(T, lambda e, r=r, ksub=ksub, sb_=sb_: e.matmul(
                            psC[0:1, 0:256], onescol_sb[:, 0:1],
                            pp_sb[:, sb_, ksub * 256:(ksub + 1) * 256],
                            start=(vi == 0 and ksub == 0),
                            stop=(vi == len(vs) - 1 and ksub == 1)),
                            inc="avm" if ksub == 1 else None)
                    vno += 1
                em.wait(V, "avm")
                em.emit(V, lambda e: e.tensor_copy(lrow_sb[0:1, :],
                                                   psC[0:1, 0:256]))
                em.emit(V, lambda e: e.drain())
                em.emit(V, lambda e: e.reciprocal(rl_sb[0:1, :], lrow_sb[0:1, :]),
                        inc="fin")
                em.wait(T, "fin")
                em.wait(T, "yv", em.val("yv"))
                em.emit(T, lambda e: e.matmul(psC[:, 768:1024], onesrow_sb[:],
                                              rl_sb[0:1, :], start=True, stop=True),
                        inc="ymt")
                em.wait(V, "ymt")
                em.emit(V, lambda e: e.tensor_copy(rlb_sb[:], psC[:, 768:1024]))
                em.emit(V, lambda e: e.drain())
                for cdv in range(4):
                    em.emit(V, lambda e, cdv=cdv: e.tensor_tensor(
                        onorm_sb[:, cdv, :], psA[:, cdv * 512:cdv * 512 + 256],
                        rlb_sb[:], op=ALU.mult),
                        inc="fin" if cdv == 3 else None)
                em.emit(V, lambda e: e.drain())
                em.wait(T, "fin")
                for c2 in range(DCH):
                    ys = psC[:, 512:768]
                    em.wait(T, "yv", em.val("yv"))
                    for cdv in range(4):
                        em.emit(T, lambda e, c2=c2, cdv=cdv, ys=ys: e.matmul(
                            ys, wo_sb[:, cdv, c2, :], onorm_sb[:, cdv, :],
                            start=(cdv == 0), stop=(cdv == 3)),
                            inc="ymt" if cdv == 3 else None)
                    em.wait(V, "ymt")
                    em.emit(V, lambda e, c2=c2, qs=qs, ys=ys: e.tensor_tensor(
                        x_sb[:, c2, qs], x_sb[:, c2, qs], ys, op=ALU.add),
                        inc="yv")
                em.emit(V, lambda e: e.drain())


    def emit_ffn1(em, first):
        V, SC, T, G, SY = "vector", "scalar", "tensor", "gpsimd", "sync"
        xnr_base = em.val("xnr")
        xnr_at = []
        for t in range(TCH):
            emit_ln_chunk(em, t, xnr_base)
            xnr_at.append(em.val("xnr"))
        XNR_ALL = em.val("xnr")
        for sem in ("smm", "avm", "ymt", "pxp", "qmm", "yv", "ymm", "hmm",
                    "hp", "xev", "fin", "qev"):
            em.wait(SY, sem, em.val(sem))
        em.wait(V, "stg", em.val("stg"))
        em.wait(SC, "stg", em.val("stg"))
        em.wait(V, "stg2", em.val("stg2"))
        em.wait(SC, "stg2", em.val("stg2"))
        if first:
            for cin in range(DCH):
                for sh in range(2):
                    wsem = "wl" if sh == 0 else "wl2"
                    em.wait(SY, "casts", em.val("casts") -
                            (0 if (cin == 0 and sh == 0) else 1))
                    em.dma(SY, scr_sb[:, sh * 512:(sh + 1) * 512],
                           wt_in[cin * 128:(cin + 1) * 128,
                                 sh * 512:(sh + 1) * 512], inc=wsem)
                    em.wait(SC, wsem)
                    em.emit(SC, lambda e, cin=cin, sh=sh: e.mul(
                        wt_sb[:, cin].rearrange("p a b -> p (a b)")
                        [:, sh * 512:(sh + 1) * 512],
                        scr_sb[:, sh * 512:(sh + 1) * 512], 0.5), inc="casts")
                em.wait(G, "casts")
                em.dma(G, wtc_d[cin],
                       wt_sb[:, cin].rearrange("p a b -> p (a b)"), inc="stg")
        else:
            em.wait(SY, "stg", em.val("stg"))
            em.wait(SY, "wl", em.val("wl"))
            em.dma(SY, wt_sb[:].rearrange("p c a b -> p c (a b)"),
                   wtc_d.rearrange("c p m -> p c m"), inc="wl")
            em.wait(T, "wl", em.val("wl"))
            em.wait(SC, "wl", em.val("wl"))
        em.wait(SY, "casts", em.val("casts"))
        em.wait(SY, "wl", em.val("wl"))
        em.dma(SY, scr_sb[:].rearrange("p (c m) -> p c m", c=DCH),
               ut_in.rearrange("(c p) m -> p c m", p=128), inc="wl")
        em.dma(SY, g16stg_sb[:], g16_in, inc="wl")
        em.dma(SY, eyef_sb[:], eye_in, inc="wl")
        em.wait(V, "wl")
        em.emit(V, lambda e: e.tensor_copy(
            ut_sb[:].rearrange("p c m -> p (c m)"), scr_sb[:]), inc="castv")
        em.emit(V, lambda e: e.tensor_copy(g16_sb[:], g16stg_sb[:]), inc="castv")
        em.wait(SY, "castv", em.val("castv"))
        em.dma(SY, scr_sb[:], vs_in, inc="wl")
        em.wait(SC, "wl")
        em.emit(SC, lambda e: e.mul(
            vs_sb[:].rearrange("p c m -> p (c m)"), scr_sb[:], 1.0 / 32.0),
            inc="casts")
        em.wait(SY, "casts", em.val("casts"))
        em.dma(SY, scr_sb[:, 0:64].rearrange("p (c m) -> p c m", c=DCH),
               gw_in.rearrange("(c p) m -> p c m", p=128), inc="wl")
        em.wait(V, "wl")
        em.emit(V, lambda e: e.tensor_copy(
            gw_sb[:].rearrange("p c m -> p (c m)"), scr_sb[:, 0:64]),
            inc="castv")
        em.emit(V, lambda e: e.memset(const_sb[:, 4:5], -8.0), inc="castv")
        em.wait(T, "wl", em.val("wl"))
        em.wait(T, "castv")
        em.wait(T, "casts")
        em.wait(T, "rowe")

        for u in range(MYT // 128):
            tok0 = u * 128
            ub = u % 2
            em.wait(T, "xnr", xnr_at[u // 2])
            em.wait(T, "pxp", em.val("pxp"))
            for cin in range(DCH):
                em.emit(T, lambda e, cin=cin, tok0=tok0: e.matmul(
                    psC[:, 0:8], xn_sb[:, cin, tok0:tok0 + 128],
                    gw_sb[:, cin, :], start=(cin == 0), stop=(cin == DCH - 1)),
                    inc="qmm" if cin == DCH - 1 else None)
            em.wait(SC, "qmm")
            em.wait(SC, "fin", em.val("fin"))
            em.emit(SC, lambda e, ub=ub: e.activation(
                gte_sb[:, ub, :], psC[:, 0:8], AF.Exp,
                bias=const_sb[:, 4:5]), inc="pxp")
            em.wait(V, "pxp")
            em.emit(V, lambda e, ub=ub: e.tensor_reduce(
                gtr_sb[:, ub:ub + 1], gte_sb[:, ub, :],
                axis=mybir.AxisListType.X, op=ALU.add))
            em.emit(V, lambda e: e.drain())
            em.emit(V, lambda e, ub=ub: e.reciprocal(
                gtr_sb[:, 2 + ub:3 + ub], gtr_sb[:, ub:ub + 1]))
            em.emit(V, lambda e: e.drain())
            em.emit(V, lambda e, ub=ub: e.tensor_scalar(
                gtn_sb[:, ub, :], gte_sb[:, ub, :], gtr_sb[:, 2 + ub:3 + ub],
                None, op0=ALU.mult), inc="fin")
            em.wait(T, "fin")
            em.wait(T, "qev", em.val("qev"))
            em.emit(T, lambda e, ub=ub: e.transpose(
                psC[0:8, 512:640], gtn_sb[:, ub, :], eyef_sb[:]), inc="ymt")
            em.wait(V, "ymt")
            em.emit(V, lambda e, tok0=tok0: e.tensor_copy(
                gates_sb[0:8, tok0:tok0 + 128], psC[0:8, 512:640]), inc="qev")

        for j in range(4):
            js = slice(j * 512, (j + 1) * 512)
            jb = j % 2
            em.wait(T, "qev", em.val("qev"))
            em.emit(T, lambda e, js=js: e.matmul(
                psB[:, 0:512], g16_sb[:], gates_sb[0:8, js],
                start=True, stop=True), inc="smm")
            for cin in range(DCH):
                em.emit(T, lambda e, cin=cin, js=js: e.matmul(
                    psB[:, 512:1024], ut_sb[:, cin, :], xn_sb[:, cin, js],
                    start=(cin == 0), stop=(cin == DCH - 1)),
                    inc="hmm" if cin == DCH - 1 else None)
            em.wait(V, "smm")
            em.emit(V, lambda e, jb=jb: e.tensor_copy(gb_sb[:, jb, :],
                                                      psB[:, 0:512]))
            em.emit(V, lambda e: e.drain())
            em.wait(V, "hmm")
            em.wait(V, "ymm", em.val("ymm") - 4)
            em.emit(V, lambda e, jb=jb: e.tensor_tensor(
                tg_sb[:, jb, :], psB[:, 512:1024], gb_sb[:, jb, :],
                op=ALU.mult), inc="fin")
            for u2 in range(2):
                t = j * 2 + u2
                sl = slice(t * BLK, (t + 1) * BLK)
                em.wait(T, "fin")
                for half in range(2):
                    em.wait(T, "xev", em.val("xev") - 1)
                    for c2 in range(half * 4, half * 4 + 4):
                        em.emit(T, lambda e, c2=c2, sl=sl: e.matmul(
                            psA[:, c2 * 256:(c2 + 1) * 256],
                            wt_sb[:, 0, c2, :], xn_sb[:, 0, sl],
                            start=True, stop=False))
                        for cin in range(1, DCH):
                            em.emit(T, lambda e, c2=c2, cin=cin, sl=sl: e.matmul(
                                psA[:, c2 * 256:(c2 + 1) * 256],
                                wt_sb[:, cin, c2, :], xn_sb[:, cin, sl],
                                start=False, stop=False))
                        em.emit(T, lambda e, c2=c2, jb=jb, u2=u2: e.matmul(
                            psA[:, c2 * 256:(c2 + 1) * 256],
                            vs_sb[:, c2, :],
                            tg_sb[:, jb, u2 * 256:(u2 + 1) * 256],
                            start=False, stop=True),
                            inc="ymm" if c2 % 4 == 3 else None)
                    em.wait(V, "ymm")
                    em.emit(V, lambda e, sl=sl, half=half: e.tensor_tensor(
                        x_sb[:, half * 4:half * 4 + 4, sl],
                        x_sb[:, half * 4:half * 4 + 4, sl],
                        psA[:, half * 1024:(half + 1) * 1024]
                        .rearrange("p (c m) -> p c m", c=4), op=ALU.add),
                        inc="xev")
        em.emit(V, lambda e: e.drain())


    def emit_attn1(em):
        V, SC, T, G, SY = "vector", "scalar", "tensor", "gpsimd", "sync"
        xnr_base = em.val("xnr")
        xnr_at = []
        for t in range(TCH):
            emit_ln_chunk(em, t, xnr_base)
            xnr_at.append(em.val("xnr"))
        XNR_ALL = em.val("xnr")

        em.wait(SY, "xnr", XNR_ALL)
        em.wait(SY, "yv", em.val("yv"))
        for sem in ("ymm", "hmm", "hp", "xev", "smm", "avm", "ymt", "pxp",
                    "qmm", "fin", "qev"):
            em.wait(SY, sem, em.val(sem))
        em.wait(V, "stg", em.val("stg"))
        em.wait(SC, "stg", em.val("stg"))
        em.wait(V, "stg2", em.val("stg2"))
        em.wait(SC, "stg2", em.val("stg2"))
        em.dma(SY, scrA_sb[:, 0:512],
               maskd_in.rearrange("p a b -> p (a b)"), inc="wl")
        em.dma(SY, scrA_sb[:, 512:640], eye_in, inc="wl")
        em.wait(V, "wl")
        em.emit(V, lambda e: e.tensor_copy(
            maskd_sb[:].rearrange("p a b -> p (a b)"), scrA_sb[:, 0:512]),
            inc="castv")
        em.emit(V, lambda e: e.tensor_copy(eye_sb[:], scrA_sb[:, 512:640]),
                inc="castv")
        em.wait(SY, "castv", em.val("castv"))
        em.dma(SY, scrA_sb[:, 0:512],
               mv0_in.rearrange("p a b -> p (a b)"), inc="wl")
        em.dma(SY, scrA_sb[:, 512:1024],
               mv1_in.rearrange("p a b -> p (a b)"), inc="wl")
        em.wait(V, "wl")
        em.emit(V, lambda e: e.tensor_copy(
            mv0_sb[:].rearrange("p a b -> p (a b)"), scrA_sb[:, 0:512]),
            inc="castv")
        em.emit(V, lambda e: e.tensor_copy(
            mv1_sb[:].rearrange("p a b -> p (a b)"), scrA_sb[:, 512:1024]),
            inc="castv")
        for piece in range(2):
            em.wait(SY, "castv", em.val("castv"))
            em.dma(SY, scrA_sb[:], cosi_in[:, piece * 1024:(piece + 1) * 1024],
                   inc="wl")
            em.wait(V, "wl")
            em.emit(V, lambda e, piece=piece: e.tensor_copy(
                cosi_sb[:, piece * 1024:(piece + 1) * 1024], scrA_sb[:]),
                inc="castv")
            em.wait(SY, "castv", em.val("castv"))
            em.dma(SY, scrA_sb[:], sini_in[:, piece * 1024:(piece + 1) * 1024],
                   inc="wl")
            em.wait(V, "wl")
            em.emit(V, lambda e, piece=piece: e.tensor_copy(
                sini_sb[:, piece * 1024:(piece + 1) * 1024], scrA_sb[:]),
                inc="castv")
        em.wait(SY, "castv", em.val("castv"))
        em.dma(SY, scrA_sb[:, 0:128], prot_in, inc="wl")
        em.wait(V, "wl")
        em.emit(V, lambda e: e.tensor_copy(prot_sb[:], scrA_sb[:, 0:128]),
                inc="castv")
        em.emit(V, lambda e: e.memset(const_sb[:, 5:6], -M1), inc="castv")
        em.emit(V, lambda e: e.memset(const_sb[:, 6:7], 0.125), inc="castv")
        em.emit(V, lambda e: e.memset(vstg1_sb[:, :, :, 64:65], 1.0), inc="castv")

        em.wait(T, "castv")
        em.wait(T, "rowe")
        for pos, m in enumerate([1, 2, 0]):
            ws = pos % 2
            em.wait(SY, "qmm", em.val("qmm") - 32)
            for cin in range(DCH):
                em.wait(SY, "castv", em.val("castv"))
                em.dma(SY, scrA_sb[:],
                       wqkvt_in[cin * 128:(cin + 1) * 128,
                                m * D:(m + 1) * D], inc="wl")
                em.wait(V, "wl")
                em.emit(V, lambda e, ws=ws, cin=cin: e.tensor_copy(
                    wqkv_sb[:, ws, cin].rearrange("p a b -> p (a b)"),
                    scrA_sb[:]), inc="castv")
            em.wait(T, "castv")
            if m < 2:
                for tc in range(4):
                    ts = slice(tc * 512, (tc + 1) * 512)
                    em.wait(T, "xnr", xnr_at[2 * tc + 1])
                    if m == 1:
                        em.wait(V, "stg", em.val("stg"))
                    for c2 in range(DCH):
                        kq = em.val("qmm")
                        pa = psA[:, (kq % 2) * 512:(kq % 2) * 512 + 512]
                        em.wait(T, "qev", em.val("qev") - 3)
                        for cin in range(DCH):
                            em.emit(T, lambda e, ws=ws, c2=c2, cin=cin, ts=ts,
                                    pa=pa: e.matmul(
                                        pa, wqkv_sb[:, ws, cin, c2, :],
                                        xn_sb[:, cin, ts],
                                        start=(cin == 0), stop=(cin == DCH - 1)),
                                    inc="qmm" if cin == DCH - 1 else None)
                        dst = (lambda e, c2=c2, ts=ts: q_sb[:, c2, ts]) if m == 0                             else None
                        em.wait(V, "qmm")
                        em.emit(V, lambda e, pa=pa: e.tensor_copy(
                            ropet_sb[:], pa), inc="qev")
                        em.wait(T, "qev")
                        em.wait(T, "fin", em.val("fin") - 1)
                        em.emit(T, lambda e, kq=kq: e.matmul(
                            psB[:, (kq % 2) * 512:(kq % 2) * 512 + 512],
                            prot_sb[:], ropet_sb[:], start=True, stop=True),
                            inc="smm")
                        em.wait(V, "smm")
                        em.emit(V, lambda e, ts=ts: e.tensor_tensor(
                            ropet_sb[:], ropet_sb[:], cosi_sb[:, ts],
                            op=ALU.mult))
                        em.emit(V, lambda e: e.drain())
                        if m == 0:
                            tgt = lambda e, c2=c2, ts=ts: q_sb[:, c2, ts]
                        else:
                            tgt = lambda e, c2=c2: kring_sb[:, 0, c2, :]                                 .rearrange("p b -> p b") if False else None
                        if m == 0:
                            em.emit(V, lambda e, c2=c2, ts=ts, kq=kq:
                                    e.scalar_tensor_tensor(
                                        q_sb[:, c2, ts],
                                        psB[:, (kq % 2) * 512:(kq % 2) * 512 + 512],
                                        1.0, sini_sb[:, ts],
                                        op0=ALU.mult, op1=ALU.mult))
                            em.emit(V, lambda e: e.drain())
                            em.emit(V, lambda e, c2=c2, ts=ts: e.tensor_tensor(
                                q_sb[:, c2, ts], q_sb[:, c2, ts], ropet_sb[:],
                                op=ALU.add), inc="fin")
                        else:
                            for jj in range(2):
                                em.emit(V, lambda e, c2=c2, ts=ts, kq=kq, jj=jj:
                                        e.scalar_tensor_tensor(
                                            kring_sb[:, jj, c2, :],
                                            psB[:, (kq % 2) * 512 + jj * 256:
                                                (kq % 2) * 512 + (jj + 1) * 256],
                                            1.0,
                                            sini_sb[:, ts.start + jj * 256:
                                                    ts.start + (jj + 1) * 256],
                                            op0=ALU.mult, op1=ALU.mult))
                            em.emit(V, lambda e: e.drain())
                            for jj in range(2):
                                em.emit(V, lambda e, c2=c2, jj=jj: e.tensor_tensor(
                                    kring_sb[:, jj, c2, :], kring_sb[:, jj, c2, :],
                                    ropet_sb[:, jj * 256:(jj + 1) * 256],
                                    op=ALU.add), inc="fin" if jj == 1 else None)
                        em.emit(V, lambda e: e.drain())
                    if m == 1:
                        em.wait(G, "fin", em.val("fin"))
                        for jj in range(2):
                            em.dma(G, kvin[2 * tc + jj][:, 0:2048],
                                   kring_sb[:, jj].rearrange("p a b -> p (a b)"),
                                   inc="stg")
            else:
                for sv in range(TCH):
                    em.wait(T, "xnr", xnr_at[sv])
                    em.wait(V, "stg", em.val("stg"))
                    for u in range(2):
                        for dvh in range(2):
                            pb = psB if dvh == 0 else psC
                            tok0 = sv * 256 + u * 128
                            em.wait(T, "qev", em.val("qev"))
                            for cin in range(DCH):
                                em.emit(T, lambda e, ws=ws, cin=cin, tok0=tok0,
                                        pb=pb, dvh=dvh: e.matmul(
                                            pb[:, 0:512],
                                            xn_sb[:, cin, tok0:tok0 + 128],
                                            wqkv_sb[:, ws, cin,
                                                    dvh * 4:(dvh + 1) * 4, :]
                                            .rearrange("p a b -> p (a b)"),
                                            start=(cin == 0),
                                            stop=(cin == DCH - 1)),
                                        inc="qmm" if cin == DCH - 1 else None)
                        em.wait(V, "qmm")
                        em.emit(V, lambda e, u=u: e.tensor_copy(
                            vstg1_sb[:, u, 0:8, 0:64],
                            psB[:, 0:512].rearrange("p (a b) -> p a b", a=8)))
                        em.emit(V, lambda e, u=u: e.tensor_copy(
                            vstg1_sb[:, u, 8:16, 0:64],
                            psC[:, 0:512].rearrange("p (a b) -> p a b", a=8)),
                            inc="qev")
                    em.wait(G, "qev")
                    em.dma(G, kvin[sv][:, 2048:KVW],
                           vstg1_sb[:].rearrange("p a b c -> p (a b c)"),
                           inc="stg")
                em.wait(G, "stg")
                em.emit(G, lambda e: e.collective_compute(
                    "AllGather", ALU.bypass,
                    replica_groups=[[0, 1], [2, 3], [4, 5], [6, 7]],
                    ins=[kvin.ap().opt()],
                    outs=[kvg.ap().opt()]), inc="cc", inck=1)

        em.wait(SY, "cc", em.val("cc"))
        em.wait(SY, "qmm", em.val("qmm"))
        em.wait(T, "qev", em.val("qev"))
        em.wait(T, "fin", em.val("fin"))
        em.wait(T, "cc", em.val("cc"))
        vno = 0
        for pr in range(8):
            for band in range(8):
                qs = slice(band * BLK, (band + 1) * BLK)
                vs = visits_of(band)
                bb = band % 2
                nv = len(vs)
                kvs = "kv0" if bb == 0 else "kv1"
                guard = em.val("avm") - (len(visits_of(band - 1))
                                         if band > 0 else 0)
                em.wait(SY, "avm", guard)
                em.wait(SY, kvs, em.val(kvs))
                segs = [("own", band, 0, 1), ("r0", 0, 1, L0[band]),
                        ("r1", 0, 1 + L0[band], L1[band])]
                for reg, j0, at_, cnt in segs:
                    if cnt == 0:
                        continue
                    if reg == "own":
                        em.dma(SY, kband_sb[:, bb, at_],
                               kvin[j0][:, pr * 256:(pr + 1) * 256], inc=kvs)
                        em.dma(SY, vband_sb[:, bb, at_],
                               kvin[j0][:, 2048:KVW].rearrange(
                                   "p (a b) -> p a b", a=2)[:, :,
                                   pr * 130:(pr + 1) * 130], inc=kvs)
                    else:
                        rr = 0 if reg == "r0" else 1
                        em.dma(SY, kband_sb[:, bb, at_:at_ + cnt],
                               kvg[rr, j0:j0 + cnt][:, :,
                                                    pr * 256:(pr + 1) * 256]
                               .rearrange("s p b -> p s b"), inc=kvs)
                        for cc2 in range(cnt):
                            em.dma(SY, vband_sb[:, bb, at_ + cc2],
                                   kvg[rr, j0 + cc2][:, 2048:KVW]
                                   .rearrange("p (a b) -> p a b", a=2)
                                   [:, :, pr * 130:(pr + 1) * 130], inc=kvs)
                em.wait(T, kvs)
                base = vno

                def emit_av(vi, tail):
                    rr_ = (base + vi) % 2
                    if tail:
                        em.wait(T, "pxp")
                    else:
                        em.wait(T, "pxp", em.val("pxp") - 1)
                    if vi == 0:
                        em.wait(T, "fin", em.val("fin"))
                    for hh in range(2):
                        ob = psA[:, hh * 512: hh * 512 + 256]
                        for ksub in range(2):
                            em.emit(T, lambda e, bb=bb, vsl=vi, hh=hh,
                                    ksub=ksub, r=rr_, ob=ob: e.matmul(
                                        ob[0:65, :],
                                        vband_sb[:, bb, vsl, ksub,
                                                 hh * 65:(hh + 1) * 65],
                                        pp1_sb[:, r, hh * 512 + ksub * 256:
                                               hh * 512 + (ksub + 1) * 256],
                                        start=(vi == 0 and ksub == 0),
                                        stop=(vi == nv - 1 and ksub == 1)),
                                    inc="avm" if (hh == 1 and ksub == 1)
                                    else None)

                for vi, (reg, j) in enumerate(vs):
                    r = vno % 2
                    em.wait(T, "pxp", em.val("pxp") - 1)
                    mask = None
                    if reg == "own":
                        mask = maskd_sb
                    elif reg == "r0" and j == band:
                        mask = mv0_sb
                    elif reg == "r1" and j == band:
                        mask = mv1_sb
                    vsl = vi
                    sbk = psB if (vno % 2 == 0) else psC
                    for hh in range(2):
                        for ksub in range(2):
                            pt = sbk[:, hh * 512 + ksub * 256:
                                     hh * 512 + (ksub + 1) * 256]
                            if mask is not None:
                                em.emit(T, lambda e, mask=mask, ksub=ksub, pt=pt:
                                        e.matmul(pt, eye_sb[:], mask[:, ksub, :],
                                                 start=True, stop=False))
                            em.emit(T, lambda e, bb=bb, vsl=vsl, hh=hh,
                                    ksub=ksub, qs=qs, pr=pr, pt=pt,
                                    first=(mask is None):
                                    e.matmul(
                                        pt,
                                        kband_sb[0 + 64 * hh:64 + 64 * hh, bb,
                                                 vsl,
                                                 ksub * 128:(ksub + 1) * 128],
                                        q_sb[0 + 64 * hh:64 + 64 * hh, pr, qs],
                                        start=first, stop=True,
                                        tile_position=(64 * hh, 0)),
                                    inc="smm" if (hh == 1 and ksub == 1)
                                    else None)
                    em.wait(SC, "smm")
                    em.wait(SC, "avm", em.val("avm"))
                    em.emit(SC, lambda e, r=r, sbk=sbk: e.activation(
                        pp1_sb[:, r, :], sbk[:],
                        AF.Exp, bias=const_sb[:, 5:6], scale=const_sb[:, 6:7]),
                        inc="pxp")
                    if vi > 0:
                        emit_av(vi - 1, False)
                    vno += 1
                emit_av(nv - 1, True)
                em.wait(V, "avm")
                em.wait(V, "stg", em.val("stg"))
                em.emit(V, lambda e: e.tensor_copy(onorm_sb[0:65, 0, :],
                                                   psA[0:65, 0:256]))
                em.emit(V, lambda e: e.tensor_copy(onorm_sb[0:65, 1, :],
                                                   psA[0:65, 512:768]),
                        inc="fin")
                em.wait(G, "fin")
                em.dma(G, onorm_d[pr, band, 0], onorm_sb[0:65, 0, :],
                       inc="stg")
                em.dma(G, onorm_d[pr, band, 1], onorm_sb[0:65, 1, :],
                       inc="stg")

        em.wait(SY, "smm", em.val("smm"))
        em.wait(SY, "avm", em.val("avm"))
        em.wait(SY, "pxp", em.val("pxp"))
        em.wait(SY, "stg", em.val("stg"))
        em.dma(SY, bc16_sb[:], bc16_in, inc="wl")
        for cin in range(DCH):
            em.wait(SY, "casts", em.val("casts"))
            em.dma(SY, scrA_sb[:], wot_in[cin * 128:(cin + 1) * 128, :], inc="wl")
            em.wait(SC, "wl")
            em.emit(SC, lambda e, cin=cin: e.copy(
                wof_sb[:, cin].rearrange("p a b -> p (a b)"), scrA_sb[:]),
                inc="casts")
        em.wait(G, "stg", em.val("stg"))
        em.wait(T, "wl", em.val("wl"))
        em.wait(T, "casts")
        for band in range(8):
            qs = slice(band * BLK, (band + 1) * BLK)
            em.wait(SY, "ymm", em.val("ymm"))
            em.wait(SY, "kv0", em.val("kv0"))
            for cin in range(DCH):
                for h2 in range(2):
                    em.dma(SY, onld_sb[64 * h2:64 * h2 + 64, cin, :],
                           onorm_d[cin, band, h2][0:64, :], inc="kv0")
            em.wait(G, "kvl", em.val("kvl"))
            em.wait(G, "fin", em.val("fin"))
            for s8 in range(8):
                em.dma(G, lld_sb[2 * s8:2 * s8 + 2, 0, :],
                       onorm_d[s8, band][:, 64, :], inc="kvl")
            em.wait(V, "kv0")
            em.wait(V, "kvl")
            em.emit(V, lambda e: e.reciprocal(lld_sb[:, 1, :], lld_sb[:, 0, :]),
                    inc="fin")
            em.emit(V, lambda e: e.drain())
            for cin in range(DCH):
                em.wait(T, "fin")
                em.emit(T, lambda e, cin=cin: e.matmul(
                    psB[:, 0:256], bc16_sb[:, cin, :],
                    lld_sb[:, 1, :], start=True, stop=True), inc="ymt")
                em.wait(V, "ymt")
                em.emit(V, lambda e: e.tensor_copy(rlbb_sb[:], psB[:, 0:256]))
                em.emit(V, lambda e: e.drain())
                em.emit(V, lambda e, cin=cin: e.tensor_tensor(
                    onf_sb[:, cin, :], onld_sb[:, cin, :], rlbb_sb[:],
                    op=ALU.mult), inc="fin")
                em.emit(V, lambda e: e.drain())
            em.wait(T, "fin")
            em.wait(T, "xev", em.val("xev"))
            for c2 in range(DCH):
                for cin in range(DCH):
                    em.emit(T, lambda e, c2=c2, cin=cin: e.matmul(
                        psA[:, c2 * 256:(c2 + 1) * 256],
                        wof_sb[:, cin, c2, :], onf_sb[:, cin, :],
                        start=(cin == 0), stop=(cin == DCH - 1)),
                        inc="ymm" if (c2 == DCH - 1 and cin == DCH - 1)
                        else None)
            em.wait(V, "ymm")
            em.emit(V, lambda e, qs=qs: e.tensor_tensor(
                x_sb[:, :, qs], x_sb[:, :, qs],
                psA[:].rearrange("p (c m) -> p c m", c=DCH), op=ALU.add),
                inc="xev")
        em.emit(V, lambda e: e.drain())


    def body(em: E):
        V, SC, T, G, SY = "vector", "scalar", "tensor", "gpsimd", "sync"
        for c in range(DCH):
            em.dma(SY, x_sb[:, c, :], x_in[c], inc="ld")
        em.dma(SY, b2stg_sb[:], b2_in, inc="ld")
        em.emit(V, lambda e: e.memset(const_sb[:, 1:2], 1e-5), inc="castv")
        em.emit(V, lambda e: e.memset(onesrep_sb[:], 1.0 / D), inc="castv")
        em.emit(V, lambda e: e.memset(onesbf_sb[:], 1.0), inc="castv")
        em.emit(V, lambda e: e.memset(onesrow_sb[:], 1.0), inc="castv")
        em.emit(V, lambda e: e.memset(onescol_sb[:], 1.0), inc="castv")
        em.wait(SC, "ld", em.val("ld"))
        em.emit(SC, lambda e: e.mul(b2h_sb[:], b2stg_sb[:], 0.5), inc="casts")
        em.wait(T, "ld", em.val("ld"))
        em.wait(T, "castv", 5)
        em.wait(SC, "castv", 5)
        em.wait(V, "ld", em.val("ld"))

        emit_ffn0(em, True)
        if upto >= 2:
            emit_attn0(em)
        if upto >= 3:
            emit_ffn0(em, False)
        if upto >= 4:
            emit_ffn1(em, True)
        if upto >= 5:
            emit_attn1(em)
        if upto >= 6:
            emit_ffn1(em, False)

        em.wait(SY, "xnr")
        em.wait(SY, "yv", em.val("yv"))
        em.wait(SY, "xev", em.val("xev"))
        for c in range(DCH):
            em.dma(SY, y_out[c], x_sb[:, c, :], inc="out")
        em.wait(SY, "out")

    with nc.Block() as block:
        @block.vector
        def _(eng):
            body(E(nc, "vector", sems))

        @block.scalar
        def _(eng):
            body(E(nc, "scalar", sems))

        @block.tensor
        def _(eng):
            body(E(nc, "tensor", sems))

        @block.gpsimd
        def _(eng):
            body(E(nc, "gpsimd", sems))

        @block.sync
        def _(eng):
            body(E(nc, "sync", sems))

    est.close()
    return nc



def core_idx(core):
    pr = core % 2
    return np.concatenate([np.arange(blk * BLK, (blk + 1) * BLK)
                           for blk in PAT[pr]])


def make_masks():
    j = np.arange(BLK)[None, :]
    pp = np.arange(128)[:, None]
    d0 = np.where(pp <= j, 0.0, NEG).astype(np.float32)
    d1 = np.where(pp + 128 <= j, 0.0, NEG).astype(np.float32)
    maskd = np.ascontiguousarray(np.stack([d0, d1], axis=1))
    full = np.full((128, 2, BLK), NEG, np.float32)
    zero = np.zeros((128, 2, BLK), np.float32)
    return maskd, full, zero


def stage_inputs(d):
    x = np.asarray(d["L0_attn_x"], np.float32)
    w1 = np.ascontiguousarray(np.asarray(d["L0_ffn_w1"], np.float32))
    w2 = np.ascontiguousarray(np.asarray(d["L0_ffn_w2"], np.float32))
    b1 = np.asarray(d["L0_ffn_b1"], np.float32)
    b2 = np.asarray(d["L0_ffn_b2"], np.float32)
    maskd, full, zero = make_masks()
    prot = np.zeros((128, 128), np.float32)
    for blk0 in (0, 64):
        for i in range(32):
            prot[blk0 + 2 * i + 1, blk0 + 2 * i] = -1.0
            prot[blk0 + 2 * i, blk0 + 2 * i + 1] = 1.0
    bc16 = np.zeros((16, 8, 128), np.float32)
    for cin8 in range(8):
        bc16[2 * cin8, cin8, 0:64] = 1.0
        bc16[2 * cin8 + 1, cin8, 64:128] = 1.0
    maps = []
    for core in range(8):
        b = core // 2
        pr = core % 2
        idx = core_idx(core)
        inv = 10000.0 ** (-np.arange(0, 64, 2, dtype=np.float64) / 64.0)
        ang = idx[None, :].astype(np.float64) * inv[:, None]
        cos64 = np.repeat(np.cos(ang), 2, axis=0)
        sin64 = np.repeat(np.sin(ang), 2, axis=0)
        rope_c = np.ascontiguousarray(
            np.concatenate([cos64, cos64], axis=0).astype(np.float32))
        rope_s = np.ascontiguousarray(
            np.concatenate([sin64, sin64], axis=0).astype(np.float32))
        xs = np.ascontiguousarray(x[b][idx].T).reshape(DCH, 128, MYT)
        maps.append({
            "x_in": xs,
            "w1": w1,
            "w2": w2,
            "b1r": np.ascontiguousarray(b1.reshape(1, 4 * D)),
            "b2": np.ascontiguousarray(b2.reshape(DCH, 128).T),
            "wq": np.ascontiguousarray(np.asarray(d["L0_attn_Wq"], np.float32)),
            "wk": np.ascontiguousarray(np.asarray(d["L0_attn_Wk"], np.float32)),
            "wv": np.ascontiguousarray(np.asarray(d["L0_attn_Wv"], np.float32)),
            "wo": np.ascontiguousarray(np.asarray(d["L0_attn_Wo"], np.float32)),
            "maskd": maskd,
            "mv0": full if pr == 0 else zero,
            "mv1": zero if pr == 0 else full,
            "eye": np.eye(128, dtype=np.float32),
            "wt": np.ascontiguousarray(np.asarray(d["L1_ffn_weight"], np.float32).T),
            "ut": np.ascontiguousarray(
                np.asarray(d["L1_ffn_lora_U"], np.float32).reshape(128, D).T),
            "vs": np.ascontiguousarray(
                np.asarray(d["L1_ffn_lora_V"], np.float32).transpose(0, 2, 1)
                .reshape(128, D)),
            "gw": np.ascontiguousarray(np.asarray(d["L1_ffn_gate_w"], np.float32).T),
            "g16": np.ascontiguousarray(
                np.repeat(np.eye(8, dtype=np.float32), 16, axis=1)),
            "wqkvt": np.ascontiguousarray(
                np.asarray(d["L1_attn_W_qkv"], np.float32).T),
            "wot": np.ascontiguousarray(np.asarray(d["L1_attn_W_o"], np.float32).T),
            "cosi": rope_c, "sini": rope_s,
            "prot": prot,
            "bc16": bc16,
        })
    return maps


def gather_output(results):
    out = np.zeros((4, S, D), np.float32)
    for core in range(8):
        b = core // 2
        idx = core_idx(core)
        y = results[core]["y_out"].reshape(D, MYT).T
        out[b][idx] = y
    return out


from concourse.bass_utils import run_bass_kernel_spmd

_NC_CACHE = []


def kernel(**inputs):
    maps = stage_inputs(inputs)
    if not _NC_CACHE:
        _NC_CACHE.append(build_program(6))
    nc = _NC_CACHE[0]
    res = run_bass_kernel_spmd(nc, maps, core_ids=list(range(8)))
    return gather_output(res.results)


# version 08
# speedup vs baseline: 1.3847x; 1.0186x over previous
import numpy as np
from contextlib import ExitStack
import concourse.bass as bass
import concourse.mybir as mybir

F32 = mybir.dt.float32
BF16 = mybir.dt.bfloat16
AF = mybir.ActivationFunctionType
ALU = mybir.AluOpType

D = 1024
S = 4096
BLK = 256
NBLK = S // BLK
MYB = 8
MYT = MYB * BLK
DCH = 8
HCH = 32
TCH = MYT // BLK
M0 = 12.0
M1 = 26.0
NEG = -1.0e30

PAT = [[0, 3, 4, 7, 8, 11, 12, 15], [1, 2, 5, 6, 9, 10, 13, 14]]
C1 = [1, 1, 3, 3, 5, 5, 7, 7]
C0 = [0, 2, 2, 4, 4, 6, 6, 8]
L0 = [max(i, C1[i]) for i in range(8)]
L1 = [max(C0[i], i) for i in range(8)]
RING = 2

SEM_NAMES = ["ld", "wl", "castv", "casts", "sq", "stm", "rowe", "rowms",
             "rowmv", "xnr", "hmm", "hp", "ymm", "xev", "out", "wl2",
             "qmm", "qev", "stg", "stg2", "cc", "kv0", "kv1", "kvl", "gwl", "smm", "pxp", "avm",
             "fin", "ymt", "yv"]


def visits_of(band):
    vs = [("own", band)]
    vs += [("r0", j) for j in range(L0[band])]
    vs += [("r1", j) for j in range(L1[band])]
    return vs


class E:

    def __init__(self, nc, engname, sems):
        self.nc = nc
        self.engname = engname
        self.engines = {"vector": nc.vector, "scalar": nc.scalar,
                        "tensor": nc.tensor, "gpsimd": nc.gpsimd, "sync": nc.sync}
        self.semh = sems
        self.cnt = {k: 0 for k in sems}

    def _fin(self, ins, inc, inck):
        if inc is not None:
            if ins is not None:
                ins.then_inc(self.semh[inc], inck)
            self.cnt[inc] += inck
        return ins

    def emit(self, engine, fn, inc=None, inck=1):
        ins = fn(self.engines[engine]) if self.engname == engine else None
        return self._fin(ins, inc, inck)

    def dma(self, engine, out, in_, inc=None):
        ins = None
        if self.engname == engine:
            ins = self.engines[engine].dma_start(out=out, in_=in_)
        return self._fin(ins, inc, 16 if inc else 0)

    def wait(self, engine, sem, val=None):
        if val is None:
            val = self.cnt[sem]
        if val <= 0:
            return
        if self.engname == engine:
            self.engines[engine].wait_ge(self.semh[sem], val)

    def val(self, sem):
        return self.cnt[sem]


def build_program(upto=2):
    QH = HCH // 4
    nc = bass.Bass("TRN2", target_bir_lowering=False, debug=False)

    def din(name, shape, dtype=F32):
        return nc.dram_tensor(name, shape, dtype, kind="ExternalInput").ap()

    x_in = din("x_in", [DCH, 128, MYT])
    w1_in = din("w1", [D, 4 * D])
    w2_in = din("w2", [4 * D, D])
    b1r_in = din("b1r", [1, 4 * D])
    b2_in = din("b2", [128, DCH])
    wq_in = din("wq", [D, D])
    wk_in = din("wk", [D, D])
    wv_in = din("wv", [D, D])
    wo_in = din("wo", [D, D])
    maskd_in = din("maskd", [128, 2, BLK])
    mv0_in = din("mv0", [128, 2, BLK])
    mv1_in = din("mv1", [128, 2, BLK])
    eye_in = din("eye", [128, 128])
    wt_in = din("wt", [D, D])
    ut_in = din("ut", [D, 128])
    vs_in = din("vs", [128, D])
    gw_in = din("gw", [D, 8])
    g16_in = din("g16", [8, 128])
    wqkvt_in = din("wqkvt", [D, 3 * D])
    wot_in = din("wot", [D, D])
    cosi_in = din("cosi", [128, MYT])
    sini_in = din("sini", [128, MYT])
    prot_in = din("prot", [128, 128])
    bc16_in = din("bc16", [16, 8, 128])
    y_out = nc.dram_tensor("y_out", [DCH, 128, MYT], F32, kind="ExternalOutput").ap()

    KVW = DCH * BLK + 2 * 1040
    kvin = nc.dram_tensor("kvin", [MYB, 128, KVW], BF16)
    kvg = nc.dram_tensor("kvg", [2, MYB, 128, KVW], BF16)
    onorm_d = nc.dram_tensor("onorm_d", [8, 8, 2, 65, BLK], BF16)
    w1c_d = nc.dram_tensor("w1c", [4, DCH, 128, 1024], BF16)
    w2c_d = nc.dram_tensor("w2c", [4, QH, 128, 1024], BF16)
    b1c_d = nc.dram_tensor("b1c", [4, 1, 1024], BF16)
    wtc_d = nc.dram_tensor("wtc", [DCH, 128, 1024], BF16)
    smallc_d = nc.dram_tensor("smallc", [4, 128, 1024], BF16)

    BASE = 16512
    off = [BASE]
    def at(name, shape, dtype, offset=None):
        if offset is None:
            offset = off[0]
        per_part = int(np.prod(shape[1:])) * mybir.dt.size(dtype)
        h = nc.alloc_sbuf_tensor_at(name, shape, dtype, offset=offset)
        off[0] = max(off[0], offset + ((per_part + 63) // 64) * 64)
        return h

    x_sb = at("x_sb", [128, DCH, MYT], F32)
    xn_sb = at("xn_sb", [128, DCH, MYT], BF16)
    stat_sb = at("stat_sb", [128, 4, BLK], F32)
    const_sb = at("const_sb", [128, 16], F32)
    onesrep_sb = at("onesrep_sb", [128, 128], F32)
    onesbf_sb = at("onesbf_sb", [1, BLK], BF16)
    onescol_sb = at("onescol_sb", [128, 2], BF16)
    onesrow_sb = at("onesrow_sb", [1, 128], F32)
    b2h_sb = at("b2h_sb", [128, DCH], F32)
    b2stg_sb = at("b2stg_sb", [128, DCH], F32)
    PERS_END = off[0]

    FREG = PERS_END
    w1q_sb = at("w1q_sb", [128, 2, DCH, QH, 128], BF16, FREG)
    w2q_sb = at("w2q_sb", [128, 2, QH, DCH, 128], BF16, FREG + 32768)
    h_sb = at("h_sb", [128, 2, QH, BLK], BF16, FREG + 65536)
    scr_sb = at("scr_sb", [128, 1024], F32, FREG + 73728)
    b1q_sb = at("b1q_sb", [1, 2, QH * 128], BF16, FREG + 77824)
    b1qstg_sb = at("b1qstg_sb", [1, 2, QH * 128], F32, FREG + 81920)
    xsq_sb = at("xsq_sb", [128, 1024], F32, FREG + 90112)
    FFN_END = FREG + 94208

    wt_sb = at("wt_sb", [128, DCH, DCH, 128], BF16, FREG)
    ut_sb = at("ut_sb", [128, DCH, 128], BF16, FREG + 16384)
    vs_sb = at("vs_sb", [128, DCH, 128], BF16, FREG + 18432)
    gw_sb = at("gw_sb", [128, DCH, 8], BF16, FREG + 20480)
    g16_sb = at("g16_sb", [8, 128], BF16, FREG + 20608)
    gates_sb = at("gates_sb", [128, MYT], BF16, FREG + 20864)
    gte_sb = at("gte_sb", [128, 2, 8], F32, FREG + 24960)
    gtr_sb = at("gtr_sb", [128, 4], F32, FREG + 25024)
    gtn_sb = at("gtn_sb", [128, 2, 8], F32, FREG + 25088)
    gb_sb = at("gb_sb", [128, 2, 512], BF16, FREG + 25152)
    tg_sb = at("tg_sb", [128, 2, 512], BF16, FREG + 27200)
    g16stg_sb = at("g16stg_sb", [8, 128], F32, FREG + 29248)
    eyef_sb = at("eyef_sb", [128, 128], F32, FREG + 29760)

    q_sb = at("q_sb", [128, DCH, MYT], BF16, FREG)
    wqkv_sb = at("wqkv_sb", [128, 2, DCH, DCH, 128], BF16, FREG + 32768)
    kring_sb = at("kring_sb", [128, RING, DCH, BLK], BF16, FREG + 65536)
    vring_sb = at("vring_sb", [128, RING, 2, 1024], BF16, FREG + 73728)
    pp_sb = at("pp_sb", [128, 2, 512], BF16, FREG + 82048)
    maskd_sb = at("maskd_sb", [128, 2, BLK], BF16, FREG + 84096)
    mv0_sb = at("mv0_sb", [128, 2, BLK], BF16, FREG + 85120)
    mv1_sb = at("mv1_sb", [128, 2, BLK], BF16, FREG + 86144)
    eye_sb = at("eye_sb", [128, 128], BF16, FREG + 87168)
    wo_sb = at("wo_sb", [128, 4, DCH, 128], BF16, FREG + 87424)
    lrow_sb = at("lrow_sb", [128, BLK], F32, FREG + 95616)
    rl_sb = at("rl_sb", [128, BLK], F32, FREG + 96640)
    rlb_sb = at("rlb_sb", [128, BLK], F32, FREG + 97664)
    onorm_sb = at("onorm_sb", [128, 4, BLK], BF16, FREG + 98688)
    scrA_sb = at("scrA_sb", [128, 1024], F32, FREG + 100736)
    ATT_END = FREG + 104832

    XNOFF = BASE + 65536
    kband_sb = at("kband_sb", [128, 2, 16, BLK], BF16, XNOFF)
    vband_sb = at("vband_sb", [128, 2, 16, 2, 130], BF16, XNOFF + 16384)
    pp1_sb = at("pp1_sb", [128, 2, 1024], BF16, FREG + 100736)
    bc16_sb = at("bc16_sb", [16, DCH, 128], F32, FREG + 65536)
    lld_sb = at("lld_sb", [16, 2, BLK], F32, FREG + 91648)
    rlbb_sb = at("rlbb_sb", [128, BLK], F32, FREG + 93696)
    onf_sb = at("onf_sb", [128, DCH, BLK], BF16, FREG + 16384)
    cosi_sb = at("cosi_sb", [128, MYT], BF16, FREG + 73728)
    sini_sb = at("sini_sb", [128, MYT], BF16, FREG + 77824)
    vstg1_sb = at("vstg1_sb", [128, 2, 16, 65], BF16, FREG + 87424)
    prot_sb = at("prot_sb", [128, 128], BF16, FREG + 104832)
    ropet_sb = at("ropet_sb", [128, 512], BF16, FREG + 105088)
    wof_sb = at("wof_sb", [128, DCH, DCH, 128], BF16, FREG)
    onld_sb = at("onld_sb", [128, DCH, BLK], BF16, FREG + 32768)
    A1_END = FREG + 106112
    assert max(FFN_END, ATT_END, A1_END) + 0 <= 229344, (FFN_END, ATT_END, A1_END)

    est = ExitStack()
    psA = est.enter_context(nc.psum_tensor("psA", [128, 2048], F32))
    psB = est.enter_context(nc.psum_tensor("psB", [128, 1024], F32))
    psC = est.enter_context(nc.psum_tensor("psC", [128, 1024], F32))
    sems = {n: est.enter_context(nc.semaphore(n)) for n in SEM_NAMES}


    def emit_ln_chunk(em, t, xnr_base):
        V, SC, T = "vector", "scalar", "tensor"
        sl = slice(t * BLK, (t + 1) * BLK)
        em.wait(T, "rowe", em.val("rowe"))
        em.wait(T, "hp", em.val("hp"))
        em.wait(T, "pxp", em.val("pxp"))
        em.wait(T, "xnr", xnr_base)
        em.wait(SC, "xnr", xnr_base)
        em.wait(SC, "ymt", em.val("ymt"))
        for c in range(DCH):
            em.emit(T, lambda e, c=c, sl=sl: e.matmul(
                psB[:, 0:256], onesrep_sb[:], x_sb[:, c, sl],
                start=(c == 0), stop=(c == DCH - 1)),
                inc="stm" if c == DCH - 1 else None)
        for half in range(2):
            c0 = half * 4
            em.wait(SC, "stm", em.val("stm"))
            em.emit(SC, lambda e, c0=c0, sl=sl: e.activation(
                xsq_sb[:].rearrange("p (c m) -> p c m", c=4),
                x_sb[:, c0:c0 + 4, sl], AF.Square), inc="sq")
            em.wait(T, "sq")
            for j in range(4):
                em.emit(T, lambda e, j=j, half=half: e.matmul(
                    psB[:, 256:512], onesrep_sb[:], xsq_sb[:, j * 256:(j + 1) * 256],
                    start=(half == 0 and j == 0), stop=(half == 1 and j == 3)),
                    inc="stm" if j == 3 else None)
        em.wait(V, "stm")
        em.emit(V, lambda e: e.tensor_copy(stat_sb[:, 0, :], psB[:, 0:256]),
                inc="rowe")
        em.emit(V, lambda e: e.tensor_copy(stat_sb[:, 1, :], psB[:, 256:512]),
                inc="rowe")
        em.wait(SC, "rowe")
        em.emit(SC, lambda e: e.activation(stat_sb[:, 2, :], stat_sb[:, 0, :],
                                           AF.Square), inc="rowms")
        em.wait(V, "rowms")
        em.emit(V, lambda e: e.tensor_tensor(stat_sb[:, 1, :], stat_sb[:, 1, :],
                                             stat_sb[:, 2, :], op=ALU.subtract),
                inc="rowmv")
        em.wait(SC, "rowmv")
        em.emit(SC, lambda e: e.activation(stat_sb[:, 2, :], stat_sb[:, 1, :],
                                           AF.Sqrt, bias=const_sb[:, 1:2]),
                inc="rowms")
        em.wait(V, "rowms")
        em.emit(V, lambda e: e.reciprocal(stat_sb[:, 1, :], stat_sb[:, 2, :]),
                inc="rowmv")
        for c in range(DCH):
            em.emit(V, lambda e, c=c, sl=sl: e.tensor_tensor(
                xn_sb[:, c, sl], x_sb[:, c, sl], stat_sb[:, 0, :],
                op=ALU.subtract))
        em.emit(V, lambda e: e.drain())
        for c in range(DCH):
            em.emit(V, lambda e, c=c, sl=sl: e.tensor_tensor(
                xn_sb[:, c, sl], xn_sb[:, c, sl], stat_sb[:, 1, :],
                op=ALU.mult), inc="xnr" if c == DCH - 1 else None)
        em.emit(V, lambda e: e.drain())


    def emit_ffn0(em, first):
        V, SC, T, G, SY = "vector", "scalar", "tensor", "gpsimd", "sync"
        xnr_base = em.val("xnr")
        xnr_at = []
        for t in range(TCH):
            emit_ln_chunk(em, t, xnr_base)
            xnr_at.append(em.val("xnr"))
        XNR_ALL = em.val("xnr")
        for sem in ("smm", "avm", "ymt", "pxp", "qmm", "yv"):
            em.wait(SY, sem, em.val(sem))
        for q in range(4):
            sg = q % 2
            em.wait(SY, "xev", em.val("xev") - 2 * TCH)
            if first:
                qsem = "stg" if q % 2 == 0 else "stg2"
                em.wait(G, qsem, em.val(qsem))
                em.wait(G, "ymm", max(0, em.val("ymm") - 16))
                em.dma(G, w1q_sb[:, sg].rearrange("p c h m -> p c (h m)"),
                       w1_in.rearrange("(c p) m -> p c m", p=128)
                       [:, :, q * 1024:(q + 1) * 1024], inc="gwl")
                em.dma(G, w2q_sb[:, sg].rearrange("p h d m -> p h (d m)"),
                       w2_in.rearrange("(a p) m -> p a m", p=128)
                       [:, q * QH:(q + 1) * QH, :], inc="gwl")
                em.dma(G, b1q_sb[:, sg, :],
                       b1r_in[:, q * QH * 128:(q + 1) * QH * 128], inc="gwl")
                em.wait(G, "gwl", em.val("gwl"))
                em.dma(G, w1c_d[q].rearrange("c p m -> p c m"),
                       w1q_sb[:, sg].rearrange("p c h m -> p c (h m)"),
                       inc=qsem)
                em.dma(G, w2c_d[q].rearrange("h p m -> p h m"),
                       w2q_sb[:, sg].rearrange("p h d m -> p h (d m)"),
                       inc=qsem)
                em.dma(G, b1c_d[q], b1q_sb[:, sg, :].rearrange("p m -> p m"),
                       inc=qsem)
            else:
                em.wait(SY, "stg", em.val("stg"))
                em.wait(SY, "stg2", em.val("stg2"))
                em.wait(SY, "wl", em.val("wl"))
                em.dma(SY, w1q_sb[:, sg].rearrange("p c h m -> p c (h m)"),
                       w1c_d[q].rearrange("c p m -> p c m"), inc="wl")
                em.dma(SY, w2q_sb[:, sg].rearrange("p h d m -> p h (d m)"),
                       w2c_d[q].rearrange("h p m -> p h m"), inc="wl")
                em.dma(SY, b1q_sb[:, sg, :], b1c_d[q], inc="wl")
                em.wait(T, "wl", em.val("wl"))
            em.wait(T, "castv")
            em.wait(T, "casts")
            em.wait(T, "gwl", em.val("gwl"))
            if q == 0:
                em.wait(T, "rowe")
            for t in range(TCH):
                k = q * TCH + t
                hs = k % 2
                sl = slice(t * BLK, (t + 1) * BLK)
                if q == 0:
                    em.wait(T, "xnr", xnr_at[t])
                em.wait(T, "hp", em.val("hp"))
                for grp in range(2):
                    pg = psB if grp == 0 else psC
                    for j in range(4):
                        hh = grp * 4 + j
                        pa = pg[:, j * 256:(j + 1) * 256]
                        em.emit(T, lambda e, sg=sg, hh=hh, pa=pa: e.matmul(
                            pa, b1q_sb[:, sg, hh * 128:(hh + 1) * 128],
                            onesbf_sb[:], start=True, stop=False))
                        for c in range(DCH):
                            em.emit(T, lambda e, sg=sg, hh=hh, c=c, sl=sl, pa=pa:
                                    e.matmul(pa, w1q_sb[:, sg, c, hh, :],
                                             xn_sb[:, c, sl],
                                             start=False, stop=(c == DCH - 1)),
                                    inc="hmm" if c == DCH - 1 else None)
                em.wait(SC, "hmm")
                em.wait(SC, "ymm", em.val("ymm") - 1)
                em.emit(SC, lambda e, hs=hs: e.activation(
                    h_sb[:, hs, 0:4].rearrange("p h m -> p (h m)"), psB[:],
                    AF.Gelu))
                em.emit(SC, lambda e, hs=hs: e.activation(
                    h_sb[:, hs, 4:8].rearrange("p h m -> p (h m)"), psC[:],
                    AF.Gelu), inc="hp")
                em.wait(T, "hp")
                for half in range(2):
                    em.wait(T, "xev", em.val("xev") - 1)
                    for c in range(half * 4, half * 4 + 4):
                        for hh in range(QH):
                            em.emit(T, lambda e, sg=sg, c=c, hh=hh, hs=hs:
                                    e.matmul(
                                        psA[:, c * 256:(c + 1) * 256],
                                        w2q_sb[:, sg, hh, c, :],
                                        h_sb[:, hs, hh, :],
                                        start=(hh == 0), stop=(hh == QH - 1)),
                                    inc="ymm" if (c % 4 == 3 and hh == QH - 1)
                                    else None)
                    em.wait(V, "ymm")
                    em.emit(V, lambda e, sl=sl, half=half:
                            e.scalar_tensor_tensor(
                                x_sb[:, half * 4:half * 4 + 4, sl],
                                psA[:, half * 1024:(half + 1) * 1024]
                                .rearrange("p (c m) -> p c m", c=4),
                                0.5, x_sb[:, half * 4:half * 4 + 4, sl],
                                op0=ALU.mult, op1=ALU.add),
                        inc="xev")
        em.wait(V, "xev")
        em.wait(V, "casts", 1)
        for t in range(TCH):
            sl = slice(t * BLK, (t + 1) * BLK)
            for c in range(DCH):
                em.emit(V, lambda e, c=c, sl=sl: e.tensor_scalar_add(
                    x_sb[:, c, sl], x_sb[:, c, sl], b2h_sb[:, c:c + 1]),
                    inc="xnr")
        em.emit(V, lambda e: e.drain())


    def emit_attn0(em):
        xnr_base0 = None
        V, SC, T, G, SY = "vector", "scalar", "tensor", "gpsimd", "sync"
        xnr_base = em.val("xnr")
        for t in range(TCH):
            emit_ln_chunk(em, t, xnr_base)
        XNR_ALL = em.val("xnr")

        em.wait(SY, "xnr", XNR_ALL)
        em.wait(SY, "yv", em.val("yv"))
        for sem in ("ymm", "hmm", "hp", "xev"):
            em.wait(SY, sem, em.val(sem))
        em.dma(SY, scrA_sb[:, 0:512],
               maskd_in.rearrange("p a b -> p (a b)"), inc="wl")
        em.dma(SY, scrA_sb[:, 512:640], eye_in, inc="wl")
        em.wait(V, "wl")
        em.wait(V, "stg", em.val("stg"))
        em.wait(SC, "stg", em.val("stg"))
        em.wait(V, "stg2", em.val("stg2"))
        em.wait(SC, "stg2", em.val("stg2"))
        em.emit(V, lambda e: e.tensor_copy(
            maskd_sb[:].rearrange("p a b -> p (a b)"), scrA_sb[:, 0:512]),
            inc="castv")
        em.emit(V, lambda e: e.tensor_copy(eye_sb[:], scrA_sb[:, 512:640]),
                inc="castv")
        em.wait(SY, "castv", em.val("castv"))
        em.dma(SY, scrA_sb[:, 0:512],
               mv0_in.rearrange("p a b -> p (a b)"), inc="wl")
        em.dma(SY, scrA_sb[:, 512:1024],
               mv1_in.rearrange("p a b -> p (a b)"), inc="wl")
        em.wait(V, "wl")
        em.emit(V, lambda e: e.tensor_copy(
            mv0_sb[:].rearrange("p a b -> p (a b)"), scrA_sb[:, 0:512]),
            inc="castv")
        em.emit(V, lambda e: e.tensor_copy(
            mv1_sb[:].rearrange("p a b -> p (a b)"), scrA_sb[:, 512:1024]),
            inc="castv")
        em.emit(V, lambda e: e.memset(const_sb[:, 2:3], -M0), inc="castv")
        em.emit(V, lambda e: e.memset(const_sb[:, 3:4], 1.0 / 32.0), inc="castv")

        em.wait(T, "xnr", XNR_ALL)
        em.wait(T, "rowe")
        for pos, m in enumerate([1, 2, 0]):
            w_in = [wq_in, wk_in, wv_in][m]
            ws = pos % 2
            if pos == 0:
                em.wait(G, "ymm", em.val("ymm"))
                em.wait(G, "stg", em.val("stg"))
                em.wait(G, "stg2", em.val("stg2"))
            em.wait(G, "qmm", em.val("qmm") - 32)
            em.dma(G, wqkv_sb[:, ws].rearrange("p c a b -> p c (a b)"),
                   w_in.rearrange("(c p) m -> p c m", p=128), inc="gwl")
            em.wait(T, "gwl", em.val("gwl"))
            if m < 2:
                for tc in range(4):
                    ts = slice(tc * 512, (tc + 1) * 512)
                    if m == 1:
                        em.wait(V, "stg", em.val("stg"))
                    for c2 in range(DCH):
                        kq = em.val("qmm")
                        pa = psA[:, (kq % 4) * 512:(kq % 4 + 1) * 512]
                        em.wait(T, "qev", em.val("qev") - 3)
                        for cin in range(DCH):
                            em.emit(T, lambda e, ws=ws, c2=c2, cin=cin, ts=ts,
                                    pa=pa: e.matmul(
                                        pa, wqkv_sb[:, ws, cin, c2, :],
                                        xn_sb[:, cin, ts],
                                        start=(cin == 0), stop=(cin == DCH - 1)),
                                    inc="qmm" if cin == DCH - 1 else None)
                        em.wait(V, "qmm")
                        if m == 0:
                            em.emit(V, lambda e, c2=c2, ts=ts, pa=pa:
                                    e.tensor_copy(q_sb[:, c2, ts], pa),
                                    inc="qev")
                        else:
                            for jj in range(2):
                                em.emit(V, lambda e, c2=c2, jj=jj, pa=pa:
                                        e.tensor_copy(
                                            kring_sb[:, jj, c2, :],
                                            pa[:, jj * 256:(jj + 1) * 256]),
                                        inc="qev" if jj == 1 else None)
                    if m == 1:
                        em.wait(G, "qev")
                        for jj in range(2):
                            em.dma(G, kvin[2 * tc + jj][:, 0:2048],
                                   kring_sb[:, jj].rearrange("p a b -> p (a b)"),
                                   inc="stg")
            else:
                for sv in range(TCH):
                    vslot = sv % 2
                    em.wait(V, "stg", em.val("stg"))
                    for u in range(2):
                        for dvh in range(2):
                            pb = psB if dvh == 0 else psC
                            tok0 = sv * 256 + u * 128
                            em.wait(T, "qev", em.val("qev"))
                            for cin in range(DCH):
                                em.emit(T, lambda e, ws=ws, cin=cin, tok0=tok0,
                                        pb=pb, dvh=dvh: e.matmul(
                                            pb[:, 0:512],
                                            xn_sb[:, cin, tok0:tok0 + 128],
                                            wqkv_sb[:, ws, cin,
                                                    dvh * 4:(dvh + 1) * 4, :]
                                            .rearrange("p a b -> p (a b)"),
                                            start=(cin == 0),
                                            stop=(cin == DCH - 1)),
                                        inc="qmm" if cin == DCH - 1 else None)
                        em.wait(V, "qmm")
                        em.emit(V, lambda e, vslot=vslot, u=u: e.tensor_copy(
                            vring_sb[:, vslot, u, 0:512], psB[:, 0:512]))
                        em.emit(V, lambda e, vslot=vslot, u=u: e.tensor_copy(
                            vring_sb[:, vslot, u, 512:1024], psC[:, 0:512]),
                            inc="qev")
                    em.wait(G, "qev")
                    em.dma(G, kvin[sv][:, 2048:4096],
                           vring_sb[:, vslot].rearrange("p a b -> p (a b)"),
                           inc="stg")
                em.emit(V, lambda e: e.memset(pp_sb[:, 0, 0:32], 0.0),
                        inc="fin")
                em.wait(G, "fin", em.val("fin"))
                for sv in range(MYB):
                    em.dma(G, kvin[sv][:, 4096:KVW], pp_sb[:, 0, 0:32],
                           inc="stg")
                em.wait(G, "stg")
                em.emit(G, lambda e: e.collective_compute(
                    "AllGather", ALU.bypass,
                    replica_groups=[[0, 1], [2, 3], [4, 5], [6, 7]],
                    ins=[kvin.ap().opt()],
                    outs=[kvg.ap().opt()]), inc="cc", inck=1)

        em.wait(SY, "cc", 1)
        em.wait(T, "qev", em.val("qev"))
        em.wait(T, "castv")
        em.wait(T, "cc", 1)
        vno = 0
        for p in range(2):
            for cin in range(4):
                em.wait(SY, "casts", em.val("casts"))
                em.wait(SY, "castv", em.val("castv"))
                if p == 1:
                    em.wait(SY, "ymt", em.val("ymt"))
                em.dma(SY, scrA_sb[:],
                       wo_in[(4 * p + cin) * 128:(4 * p + cin + 1) * 128, :],
                       inc="wl")
                em.wait(SC, "wl")
                em.wait(SC, "stm", em.val("stm"))
                em.emit(SC, lambda e, cin=cin: e.copy(
                    wo_sb[:, cin].rearrange("p a b -> p (a b)"), scrA_sb[:]),
                    inc="casts")
            em.wait(T, "casts")
            for band in range(8):
                qs = slice(band * BLK, (band + 1) * BLK)
                vs = visits_of(band)
                for vi, (reg, j) in enumerate(vs):
                    r = vno % RING
                    sb_ = vno % 2
                    kvs = "kv0" if r == 0 else "kv1"
                    em.wait(SY, "avm", em.val("avm") - (RING - 1))
                    em.wait(SY, "pxp", em.val("pxp") - (RING - 1))
                    em.wait(SY, kvs, em.val(kvs))
                    slot = {"own": kvin[j], "r0": kvg[0, j],
                            "r1": kvg[1, j]}[reg]
                    em.dma(SY, kring_sb[:, r].rearrange("p a b -> p (a b)"),
                           slot[:, 0:2048], inc=kvs)
                    em.dma(SY, vring_sb[:, r].rearrange("p a b -> p (a b)"),
                           slot[:, 2048:4096], inc=kvs)
                    em.wait(T, kvs)
                    em.wait(T, "pxp", em.val("pxp") - 1)
                    mask = None
                    if reg == "own":
                        mask = maskd_sb
                    elif reg == "r0" and j == band:
                        mask = mv0_sb
                    elif reg == "r1" and j == band:
                        mask = mv1_sb
                    for ksub in range(2):
                        pt = psB[:, sb_ * 512 + ksub * 256:
                                 sb_ * 512 + (ksub + 1) * 256]
                        if mask is not None:
                            em.emit(T, lambda e, mask=mask, ksub=ksub, pt=pt:
                                    e.matmul(pt, eye_sb[:], mask[:, ksub, :],
                                             start=True, stop=False))
                        for cin in range(DCH):
                            first = (mask is None and cin == 0)
                            em.emit(T, lambda e, r=r, cin=cin, ksub=ksub, qs=qs,
                                    pt=pt, first=first:
                                    e.matmul(pt,
                                             kring_sb[:, r, cin,
                                                      ksub * 128:(ksub + 1) * 128],
                                             q_sb[:, cin, qs],
                                             start=first, stop=(cin == DCH - 1)),
                                    inc="smm" if (ksub == 1 and cin == DCH - 1)
                                    else None)
                    em.wait(SC, "smm")
                    em.wait(SC, "avm", em.val("avm") - 1)
                    em.emit(SC, lambda e, sb_=sb_: e.activation(
                        pp_sb[:, sb_, :], psB[:, sb_ * 512:(sb_ + 1) * 512],
                        AF.Exp, bias=const_sb[:, 2:3], scale=const_sb[:, 3:4]),
                        inc="pxp")
                    em.wait(T, "pxp")
                    if vi == 0:
                        em.wait(T, "fin", em.val("fin"))
                    for ksub in range(2):
                        for cdv in range(4):
                            h0 = 8 * p + 2 * cdv
                            em.emit(T, lambda e, r=r, ksub=ksub, cdv=cdv, h0=h0,
                                    sb_=sb_: e.matmul(
                                        psA[:, cdv * 512:cdv * 512 + 256],
                                        vring_sb[:, r, ksub,
                                                 (h0 // 2) * 128:(h0 // 2) * 128 + 128],
                                        pp_sb[:, sb_, ksub * 256:(ksub + 1) * 256],
                                        start=(vi == 0 and ksub == 0),
                                        stop=(vi == len(vs) - 1 and ksub == 1)))
                        em.emit(T, lambda e, r=r, ksub=ksub, sb_=sb_: e.matmul(
                            psC[0:1, 0:256], onescol_sb[:, 0:1],
                            pp_sb[:, sb_, ksub * 256:(ksub + 1) * 256],
                            start=(vi == 0 and ksub == 0),
                            stop=(vi == len(vs) - 1 and ksub == 1)),
                            inc="avm" if ksub == 1 else None)
                    vno += 1
                em.wait(V, "avm")
                em.emit(V, lambda e: e.tensor_copy(lrow_sb[0:1, :],
                                                   psC[0:1, 0:256]))
                em.emit(V, lambda e: e.drain())
                em.emit(V, lambda e: e.reciprocal(rl_sb[0:1, :], lrow_sb[0:1, :]),
                        inc="fin")
                em.wait(T, "fin")
                em.wait(T, "yv", em.val("yv"))
                em.emit(T, lambda e: e.matmul(psC[:, 768:1024], onesrow_sb[:],
                                              rl_sb[0:1, :], start=True, stop=True),
                        inc="ymt")
                em.wait(V, "ymt")
                em.emit(V, lambda e: e.tensor_copy(rlb_sb[:], psC[:, 768:1024]))
                em.emit(V, lambda e: e.drain())
                for cdv in range(4):
                    em.emit(V, lambda e, cdv=cdv: e.tensor_tensor(
                        onorm_sb[:, cdv, :], psA[:, cdv * 512:cdv * 512 + 256],
                        rlb_sb[:], op=ALU.mult),
                        inc="fin" if cdv == 3 else None)
                em.emit(V, lambda e: e.drain())
                em.wait(T, "fin")
                for c2 in range(DCH):
                    ys = psC[:, 512:768]
                    em.wait(T, "yv", em.val("yv"))
                    for cdv in range(4):
                        em.emit(T, lambda e, c2=c2, cdv=cdv, ys=ys: e.matmul(
                            ys, wo_sb[:, cdv, c2, :], onorm_sb[:, cdv, :],
                            start=(cdv == 0), stop=(cdv == 3)),
                            inc="ymt" if cdv == 3 else None)
                    em.wait(V, "ymt")
                    em.emit(V, lambda e, c2=c2, qs=qs, ys=ys: e.tensor_tensor(
                        x_sb[:, c2, qs], x_sb[:, c2, qs], ys, op=ALU.add),
                        inc="yv")
                em.emit(V, lambda e: e.drain())


    def emit_ffn1(em, first):
        V, SC, T, G, SY = "vector", "scalar", "tensor", "gpsimd", "sync"
        xnr_base = em.val("xnr")
        xnr_at = []
        for t in range(TCH):
            emit_ln_chunk(em, t, xnr_base)
            xnr_at.append(em.val("xnr"))
        XNR_ALL = em.val("xnr")
        for sem in ("smm", "avm", "ymt", "pxp", "qmm", "yv", "ymm", "hmm",
                    "hp", "xev", "fin", "qev"):
            em.wait(SY, sem, em.val(sem))
        em.wait(V, "stg", em.val("stg"))
        em.wait(SC, "stg", em.val("stg"))
        em.wait(V, "stg2", em.val("stg2"))
        em.wait(SC, "stg2", em.val("stg2"))
        if first:
            for cin in range(DCH):
                for sh in range(2):
                    wsem = "wl" if sh == 0 else "wl2"
                    em.wait(SY, "casts", em.val("casts") -
                            (0 if (cin == 0 and sh == 0) else 1))
                    em.dma(SY, scr_sb[:, sh * 512:(sh + 1) * 512],
                           wt_in[cin * 128:(cin + 1) * 128,
                                 sh * 512:(sh + 1) * 512], inc=wsem)
                    em.wait(SC, wsem)
                    em.emit(SC, lambda e, cin=cin, sh=sh: e.mul(
                        wt_sb[:, cin].rearrange("p a b -> p (a b)")
                        [:, sh * 512:(sh + 1) * 512],
                        scr_sb[:, sh * 512:(sh + 1) * 512], 0.5), inc="casts")
                em.wait(G, "casts")
                em.dma(G, wtc_d[cin],
                       wt_sb[:, cin].rearrange("p a b -> p (a b)"), inc="stg")
        else:
            em.wait(SY, "stg", em.val("stg"))
            em.wait(SY, "wl", em.val("wl"))
            em.dma(SY, wt_sb[:].rearrange("p c a b -> p c (a b)"),
                   wtc_d.rearrange("c p m -> p c m"), inc="wl")
            em.wait(T, "wl", em.val("wl"))
            em.wait(SC, "wl", em.val("wl"))
        em.wait(SY, "casts", em.val("casts"))
        em.wait(SY, "wl", em.val("wl"))
        em.dma(SY, scr_sb[:].rearrange("p (c m) -> p c m", c=DCH),
               ut_in.rearrange("(c p) m -> p c m", p=128), inc="wl")
        em.dma(SY, g16stg_sb[:], g16_in, inc="wl")
        em.dma(SY, eyef_sb[:], eye_in, inc="wl")
        em.wait(V, "wl")
        em.emit(V, lambda e: e.tensor_copy(
            ut_sb[:].rearrange("p c m -> p (c m)"), scr_sb[:]), inc="castv")
        em.emit(V, lambda e: e.tensor_copy(g16_sb[:], g16stg_sb[:]), inc="castv")
        em.wait(SY, "castv", em.val("castv"))
        em.dma(SY, scr_sb[:], vs_in, inc="wl")
        em.wait(SC, "wl")
        em.emit(SC, lambda e: e.mul(
            vs_sb[:].rearrange("p c m -> p (c m)"), scr_sb[:], 1.0 / 32.0),
            inc="casts")
        em.wait(SY, "casts", em.val("casts"))
        em.dma(SY, scr_sb[:, 0:64].rearrange("p (c m) -> p c m", c=DCH),
               gw_in.rearrange("(c p) m -> p c m", p=128), inc="wl")
        em.wait(V, "wl")
        em.emit(V, lambda e: e.tensor_copy(
            gw_sb[:].rearrange("p c m -> p (c m)"), scr_sb[:, 0:64]),
            inc="castv")
        em.emit(V, lambda e: e.memset(const_sb[:, 4:5], -8.0), inc="castv")
        em.wait(T, "wl", em.val("wl"))
        em.wait(T, "castv")
        em.wait(T, "casts")
        em.wait(T, "rowe")

        for u in range(MYT // 128):
            tok0 = u * 128
            ub = u % 2
            em.wait(T, "xnr", xnr_at[u // 2])
            em.wait(T, "pxp", em.val("pxp"))
            for cin in range(DCH):
                em.emit(T, lambda e, cin=cin, tok0=tok0: e.matmul(
                    psC[:, 0:8], xn_sb[:, cin, tok0:tok0 + 128],
                    gw_sb[:, cin, :], start=(cin == 0), stop=(cin == DCH - 1)),
                    inc="qmm" if cin == DCH - 1 else None)
            em.wait(SC, "qmm")
            em.wait(SC, "fin", em.val("fin"))
            em.emit(SC, lambda e, ub=ub: e.activation(
                gte_sb[:, ub, :], psC[:, 0:8], AF.Exp,
                bias=const_sb[:, 4:5]), inc="pxp")
            em.wait(V, "pxp")
            em.emit(V, lambda e, ub=ub: e.tensor_reduce(
                gtr_sb[:, ub:ub + 1], gte_sb[:, ub, :],
                axis=mybir.AxisListType.X, op=ALU.add))
            em.emit(V, lambda e: e.drain())
            em.emit(V, lambda e, ub=ub: e.reciprocal(
                gtr_sb[:, 2 + ub:3 + ub], gtr_sb[:, ub:ub + 1]))
            em.emit(V, lambda e: e.drain())
            em.emit(V, lambda e, ub=ub: e.tensor_scalar(
                gtn_sb[:, ub, :], gte_sb[:, ub, :], gtr_sb[:, 2 + ub:3 + ub],
                None, op0=ALU.mult), inc="fin")
            em.wait(T, "fin")
            em.wait(T, "qev", em.val("qev"))
            em.emit(T, lambda e, ub=ub: e.transpose(
                psC[0:8, 512:640], gtn_sb[:, ub, :], eyef_sb[:]), inc="ymt")
            em.wait(V, "ymt")
            em.emit(V, lambda e, tok0=tok0: e.tensor_copy(
                gates_sb[0:8, tok0:tok0 + 128], psC[0:8, 512:640]), inc="qev")

        for j in range(4):
            js = slice(j * 512, (j + 1) * 512)
            jb = j % 2
            em.wait(T, "qev", em.val("qev"))
            em.emit(T, lambda e, js=js: e.matmul(
                psB[:, 0:512], g16_sb[:], gates_sb[0:8, js],
                start=True, stop=True), inc="smm")
            for cin in range(DCH):
                em.emit(T, lambda e, cin=cin, js=js: e.matmul(
                    psB[:, 512:1024], ut_sb[:, cin, :], xn_sb[:, cin, js],
                    start=(cin == 0), stop=(cin == DCH - 1)),
                    inc="hmm" if cin == DCH - 1 else None)
            em.wait(V, "smm")
            em.emit(V, lambda e, jb=jb: e.tensor_copy(gb_sb[:, jb, :],
                                                      psB[:, 0:512]))
            em.emit(V, lambda e: e.drain())
            em.wait(V, "hmm")
            em.wait(V, "ymm", em.val("ymm") - 4)
            em.emit(V, lambda e, jb=jb: e.tensor_tensor(
                tg_sb[:, jb, :], psB[:, 512:1024], gb_sb[:, jb, :],
                op=ALU.mult), inc="fin")
            for u2 in range(2):
                t = j * 2 + u2
                sl = slice(t * BLK, (t + 1) * BLK)
                em.wait(T, "fin")
                for half in range(2):
                    em.wait(T, "xev", em.val("xev") - 1)
                    for c2 in range(half * 4, half * 4 + 4):
                        em.emit(T, lambda e, c2=c2, sl=sl: e.matmul(
                            psA[:, c2 * 256:(c2 + 1) * 256],
                            wt_sb[:, 0, c2, :], xn_sb[:, 0, sl],
                            start=True, stop=False))
                        for cin in range(1, DCH):
                            em.emit(T, lambda e, c2=c2, cin=cin, sl=sl: e.matmul(
                                psA[:, c2 * 256:(c2 + 1) * 256],
                                wt_sb[:, cin, c2, :], xn_sb[:, cin, sl],
                                start=False, stop=False))
                        em.emit(T, lambda e, c2=c2, jb=jb, u2=u2: e.matmul(
                            psA[:, c2 * 256:(c2 + 1) * 256],
                            vs_sb[:, c2, :],
                            tg_sb[:, jb, u2 * 256:(u2 + 1) * 256],
                            start=False, stop=True),
                            inc="ymm" if c2 % 4 == 3 else None)
                    em.wait(V, "ymm")
                    em.emit(V, lambda e, sl=sl, half=half: e.tensor_tensor(
                        x_sb[:, half * 4:half * 4 + 4, sl],
                        x_sb[:, half * 4:half * 4 + 4, sl],
                        psA[:, half * 1024:(half + 1) * 1024]
                        .rearrange("p (c m) -> p c m", c=4), op=ALU.add),
                        inc="xev")
        em.emit(V, lambda e: e.drain())


    def emit_attn1(em):
        V, SC, T, G, SY = "vector", "scalar", "tensor", "gpsimd", "sync"
        xnr_base = em.val("xnr")
        xnr_at = []
        for t in range(TCH):
            emit_ln_chunk(em, t, xnr_base)
            xnr_at.append(em.val("xnr"))
        XNR_ALL = em.val("xnr")

        em.wait(SY, "xnr", XNR_ALL)
        em.wait(SY, "yv", em.val("yv"))
        for sem in ("ymm", "hmm", "hp", "xev", "smm", "avm", "ymt", "pxp",
                    "qmm", "fin", "qev"):
            em.wait(SY, sem, em.val(sem))
        em.wait(V, "stg", em.val("stg"))
        em.wait(SC, "stg", em.val("stg"))
        em.wait(V, "stg2", em.val("stg2"))
        em.wait(SC, "stg2", em.val("stg2"))
        em.dma(SY, scrA_sb[:, 0:512],
               maskd_in.rearrange("p a b -> p (a b)"), inc="wl")
        em.dma(SY, scrA_sb[:, 512:640], eye_in, inc="wl")
        em.wait(V, "wl")
        em.emit(V, lambda e: e.tensor_copy(
            maskd_sb[:].rearrange("p a b -> p (a b)"), scrA_sb[:, 0:512]),
            inc="castv")
        em.emit(V, lambda e: e.tensor_copy(eye_sb[:], scrA_sb[:, 512:640]),
                inc="castv")
        em.wait(SY, "castv", em.val("castv"))
        em.dma(SY, scrA_sb[:, 0:512],
               mv0_in.rearrange("p a b -> p (a b)"), inc="wl")
        em.dma(SY, scrA_sb[:, 512:1024],
               mv1_in.rearrange("p a b -> p (a b)"), inc="wl")
        em.wait(V, "wl")
        em.emit(V, lambda e: e.tensor_copy(
            mv0_sb[:].rearrange("p a b -> p (a b)"), scrA_sb[:, 0:512]),
            inc="castv")
        em.emit(V, lambda e: e.tensor_copy(
            mv1_sb[:].rearrange("p a b -> p (a b)"), scrA_sb[:, 512:1024]),
            inc="castv")
        for piece in range(2):
            em.wait(SY, "castv", em.val("castv"))
            em.dma(SY, scrA_sb[:], cosi_in[:, piece * 1024:(piece + 1) * 1024],
                   inc="wl")
            em.wait(V, "wl")
            em.emit(V, lambda e, piece=piece: e.tensor_copy(
                cosi_sb[:, piece * 1024:(piece + 1) * 1024], scrA_sb[:]),
                inc="castv")
            em.wait(SY, "castv", em.val("castv"))
            em.dma(SY, scrA_sb[:], sini_in[:, piece * 1024:(piece + 1) * 1024],
                   inc="wl")
            em.wait(V, "wl")
            em.emit(V, lambda e, piece=piece: e.tensor_copy(
                sini_sb[:, piece * 1024:(piece + 1) * 1024], scrA_sb[:]),
                inc="castv")
        em.wait(SY, "castv", em.val("castv"))
        em.dma(SY, scrA_sb[:, 0:128], prot_in, inc="wl")
        em.wait(V, "wl")
        em.emit(V, lambda e: e.tensor_copy(prot_sb[:], scrA_sb[:, 0:128]),
                inc="castv")
        em.emit(V, lambda e: e.memset(const_sb[:, 5:6], -M1), inc="castv")
        em.emit(V, lambda e: e.memset(const_sb[:, 6:7], 0.125), inc="castv")
        em.emit(V, lambda e: e.memset(vstg1_sb[:, :, :, 64:65], 1.0), inc="castv")

        em.wait(T, "castv")
        em.wait(T, "rowe")
        for pos, m in enumerate([1, 2, 0]):
            ws = pos % 2
            if pos == 0:
                em.wait(G, "qmm", em.val("qmm"))
            else:
                em.wait(G, "qmm", em.val("qmm") - 32)
            em.dma(G, wqkv_sb[:, ws].rearrange("p c a b -> p c (a b)"),
                   wqkvt_in.rearrange("(c p) n -> p c n", p=128)
                   [:, :, m * D:(m + 1) * D], inc="gwl")
            em.wait(T, "gwl", em.val("gwl"))
            if m < 2:
                for tc in range(4):
                    ts = slice(tc * 512, (tc + 1) * 512)
                    em.wait(T, "xnr", xnr_at[2 * tc + 1])
                    if m == 1:
                        em.wait(V, "stg", em.val("stg"))
                    for c2 in range(DCH):
                        kq = em.val("qmm")
                        pa = psA[:, (kq % 2) * 512:(kq % 2) * 512 + 512]
                        em.wait(T, "qev", em.val("qev") - 3)
                        for cin in range(DCH):
                            em.emit(T, lambda e, ws=ws, c2=c2, cin=cin, ts=ts,
                                    pa=pa: e.matmul(
                                        pa, wqkv_sb[:, ws, cin, c2, :],
                                        xn_sb[:, cin, ts],
                                        start=(cin == 0), stop=(cin == DCH - 1)),
                                    inc="qmm" if cin == DCH - 1 else None)
                        dst = (lambda e, c2=c2, ts=ts: q_sb[:, c2, ts]) if m == 0                             else None
                        em.wait(V, "qmm")
                        em.emit(V, lambda e, pa=pa: e.tensor_copy(
                            ropet_sb[:], pa), inc="qev")
                        em.wait(T, "qev")
                        em.wait(T, "fin", em.val("fin") - 1)
                        em.emit(T, lambda e, kq=kq: e.matmul(
                            psB[:, (kq % 2) * 512:(kq % 2) * 512 + 512],
                            prot_sb[:], ropet_sb[:], start=True, stop=True),
                            inc="smm")
                        em.wait(V, "smm")
                        em.emit(V, lambda e, ts=ts: e.tensor_tensor(
                            ropet_sb[:], ropet_sb[:], cosi_sb[:, ts],
                            op=ALU.mult))
                        em.emit(V, lambda e: e.drain())
                        if m == 0:
                            tgt = lambda e, c2=c2, ts=ts: q_sb[:, c2, ts]
                        else:
                            tgt = lambda e, c2=c2: kring_sb[:, 0, c2, :]                                 .rearrange("p b -> p b") if False else None
                        if m == 0:
                            em.emit(V, lambda e, c2=c2, ts=ts, kq=kq:
                                    e.scalar_tensor_tensor(
                                        q_sb[:, c2, ts],
                                        psB[:, (kq % 2) * 512:(kq % 2) * 512 + 512],
                                        1.0, sini_sb[:, ts],
                                        op0=ALU.mult, op1=ALU.mult))
                            em.emit(V, lambda e: e.drain())
                            em.emit(V, lambda e, c2=c2, ts=ts: e.tensor_tensor(
                                q_sb[:, c2, ts], q_sb[:, c2, ts], ropet_sb[:],
                                op=ALU.add), inc="fin")
                        else:
                            for jj in range(2):
                                em.emit(V, lambda e, c2=c2, ts=ts, kq=kq, jj=jj:
                                        e.scalar_tensor_tensor(
                                            kring_sb[:, jj, c2, :],
                                            psB[:, (kq % 2) * 512 + jj * 256:
                                                (kq % 2) * 512 + (jj + 1) * 256],
                                            1.0,
                                            sini_sb[:, ts.start + jj * 256:
                                                    ts.start + (jj + 1) * 256],
                                            op0=ALU.mult, op1=ALU.mult))
                            em.emit(V, lambda e: e.drain())
                            for jj in range(2):
                                em.emit(V, lambda e, c2=c2, jj=jj: e.tensor_tensor(
                                    kring_sb[:, jj, c2, :], kring_sb[:, jj, c2, :],
                                    ropet_sb[:, jj * 256:(jj + 1) * 256],
                                    op=ALU.add), inc="fin" if jj == 1 else None)
                        em.emit(V, lambda e: e.drain())
                    if m == 1:
                        em.wait(G, "fin", em.val("fin"))
                        for jj in range(2):
                            em.dma(G, kvin[2 * tc + jj][:, 0:2048],
                                   kring_sb[:, jj].rearrange("p a b -> p (a b)"),
                                   inc="stg")
            else:
                for sv in range(TCH):
                    em.wait(T, "xnr", xnr_at[sv])
                    em.wait(V, "stg", em.val("stg"))
                    for u in range(2):
                        for dvh in range(2):
                            pb = psB if dvh == 0 else psC
                            tok0 = sv * 256 + u * 128
                            em.wait(T, "qev", em.val("qev"))
                            for cin in range(DCH):
                                em.emit(T, lambda e, ws=ws, cin=cin, tok0=tok0,
                                        pb=pb, dvh=dvh: e.matmul(
                                            pb[:, 0:512],
                                            xn_sb[:, cin, tok0:tok0 + 128],
                                            wqkv_sb[:, ws, cin,
                                                    dvh * 4:(dvh + 1) * 4, :]
                                            .rearrange("p a b -> p (a b)"),
                                            start=(cin == 0),
                                            stop=(cin == DCH - 1)),
                                        inc="qmm" if cin == DCH - 1 else None)
                        em.wait(V, "qmm")
                        em.emit(V, lambda e, u=u: e.tensor_copy(
                            vstg1_sb[:, u, 0:8, 0:64],
                            psB[:, 0:512].rearrange("p (a b) -> p a b", a=8)))
                        em.emit(V, lambda e, u=u: e.tensor_copy(
                            vstg1_sb[:, u, 8:16, 0:64],
                            psC[:, 0:512].rearrange("p (a b) -> p a b", a=8)),
                            inc="qev")
                    em.wait(G, "qev")
                    em.dma(G, kvin[sv][:, 2048:KVW],
                           vstg1_sb[:].rearrange("p a b c -> p (a b c)"),
                           inc="stg")
                em.wait(G, "stg")
                em.emit(G, lambda e: e.collective_compute(
                    "AllGather", ALU.bypass,
                    replica_groups=[[0, 1], [2, 3], [4, 5], [6, 7]],
                    ins=[kvin.ap().opt()],
                    outs=[kvg.ap().opt()]), inc="cc", inck=1)

        em.wait(SY, "cc", em.val("cc"))
        em.wait(SY, "qmm", em.val("qmm"))
        em.wait(T, "qev", em.val("qev"))
        em.wait(T, "fin", em.val("fin"))
        em.wait(T, "cc", em.val("cc"))
        vno = 0
        for pr in range(8):
            for band in range(8):
                qs = slice(band * BLK, (band + 1) * BLK)
                vs = visits_of(band)
                bb = band % 2
                nv = len(vs)
                kvs = "kv0" if bb == 0 else "kv1"
                guard = em.val("avm") - (len(visits_of(band - 1))
                                         if band > 0 else 0)
                em.wait(SY, "avm", guard)
                em.wait(SY, kvs, em.val(kvs))
                segs = [("own", band, 0, 1), ("r0", 0, 1, L0[band]),
                        ("r1", 0, 1 + L0[band], L1[band])]
                for reg, j0, at_, cnt in segs:
                    if cnt == 0:
                        continue
                    if reg == "own":
                        em.dma(SY, kband_sb[:, bb, at_],
                               kvin[j0][:, pr * 256:(pr + 1) * 256], inc=kvs)
                        em.dma(SY, vband_sb[:, bb, at_],
                               kvin[j0][:, 2048:KVW].rearrange(
                                   "p (a b) -> p a b", a=2)[:, :,
                                   pr * 130:(pr + 1) * 130], inc=kvs)
                    else:
                        rr = 0 if reg == "r0" else 1
                        em.dma(SY, kband_sb[:, bb, at_:at_ + cnt],
                               kvg[rr, j0:j0 + cnt][:, :,
                                                    pr * 256:(pr + 1) * 256]
                               .rearrange("s p b -> p s b"), inc=kvs)
                        for cc2 in range(cnt):
                            em.dma(SY, vband_sb[:, bb, at_ + cc2],
                                   kvg[rr, j0 + cc2][:, 2048:KVW]
                                   .rearrange("p (a b) -> p a b", a=2)
                                   [:, :, pr * 130:(pr + 1) * 130], inc=kvs)
                em.wait(T, kvs)
                base = vno

                def emit_av(vi, tail):
                    rr_ = (base + vi) % 2
                    if tail:
                        em.wait(T, "pxp")
                    else:
                        em.wait(T, "pxp", em.val("pxp") - 1)
                    if vi == 0:
                        em.wait(T, "fin", em.val("fin"))
                    for hh in range(2):
                        ob = psA[:, hh * 512: hh * 512 + 256]
                        for ksub in range(2):
                            em.emit(T, lambda e, bb=bb, vsl=vi, hh=hh,
                                    ksub=ksub, r=rr_, ob=ob: e.matmul(
                                        ob[0:65, :],
                                        vband_sb[:, bb, vsl, ksub,
                                                 hh * 65:(hh + 1) * 65],
                                        pp1_sb[:, r, hh * 512 + ksub * 256:
                                               hh * 512 + (ksub + 1) * 256],
                                        start=(vi == 0 and ksub == 0),
                                        stop=(vi == nv - 1 and ksub == 1)),
                                    inc="avm" if (hh == 1 and ksub == 1)
                                    else None)

                for vi, (reg, j) in enumerate(vs):
                    r = vno % 2
                    em.wait(T, "pxp", em.val("pxp") - 1)
                    mask = None
                    if reg == "own":
                        mask = maskd_sb
                    elif reg == "r0" and j == band:
                        mask = mv0_sb
                    elif reg == "r1" and j == band:
                        mask = mv1_sb
                    vsl = vi
                    sbk = psB if (vno % 2 == 0) else psC
                    for hh in range(2):
                        for ksub in range(2):
                            pt = sbk[:, hh * 512 + ksub * 256:
                                     hh * 512 + (ksub + 1) * 256]
                            if mask is not None:
                                em.emit(T, lambda e, mask=mask, ksub=ksub, pt=pt:
                                        e.matmul(pt, eye_sb[:], mask[:, ksub, :],
                                                 start=True, stop=False))
                            em.emit(T, lambda e, bb=bb, vsl=vsl, hh=hh,
                                    ksub=ksub, qs=qs, pr=pr, pt=pt,
                                    first=(mask is None):
                                    e.matmul(
                                        pt,
                                        kband_sb[0 + 64 * hh:64 + 64 * hh, bb,
                                                 vsl,
                                                 ksub * 128:(ksub + 1) * 128],
                                        q_sb[0 + 64 * hh:64 + 64 * hh, pr, qs],
                                        start=first, stop=True,
                                        tile_position=(64 * hh, 0)),
                                    inc="smm" if (hh == 1 and ksub == 1)
                                    else None)
                    em.wait(SC, "smm")
                    em.wait(SC, "avm", em.val("avm"))
                    em.emit(SC, lambda e, r=r, sbk=sbk: e.activation(
                        pp1_sb[:, r, :], sbk[:],
                        AF.Exp, bias=const_sb[:, 5:6], scale=const_sb[:, 6:7]),
                        inc="pxp")
                    if vi > 0:
                        emit_av(vi - 1, False)
                    vno += 1
                emit_av(nv - 1, True)
                em.wait(V, "avm")
                em.wait(V, "stg", em.val("stg"))
                em.emit(V, lambda e: e.tensor_copy(onorm_sb[0:65, 0, :],
                                                   psA[0:65, 0:256]))
                em.emit(V, lambda e: e.tensor_copy(onorm_sb[0:65, 1, :],
                                                   psA[0:65, 512:768]),
                        inc="fin")
                em.wait(G, "fin")
                em.dma(G, onorm_d[pr, band, 0], onorm_sb[0:65, 0, :],
                       inc="stg")
                em.dma(G, onorm_d[pr, band, 1], onorm_sb[0:65, 1, :],
                       inc="stg")

        em.wait(SY, "smm", em.val("smm"))
        em.wait(SY, "avm", em.val("avm"))
        em.wait(SY, "pxp", em.val("pxp"))
        em.wait(SY, "stg", em.val("stg"))
        em.dma(SY, bc16_sb[:], bc16_in, inc="wl")
        for cin in range(DCH):
            em.wait(SY, "casts", em.val("casts"))
            em.dma(SY, scrA_sb[:], wot_in[cin * 128:(cin + 1) * 128, :], inc="wl")
            em.wait(SC, "wl")
            em.emit(SC, lambda e, cin=cin: e.copy(
                wof_sb[:, cin].rearrange("p a b -> p (a b)"), scrA_sb[:]),
                inc="casts")
        em.wait(G, "stg", em.val("stg"))
        em.wait(T, "wl", em.val("wl"))
        em.wait(T, "casts")
        for band in range(8):
            qs = slice(band * BLK, (band + 1) * BLK)
            em.wait(SY, "ymm", em.val("ymm"))
            em.wait(SY, "kv0", em.val("kv0"))
            for cin in range(DCH):
                for h2 in range(2):
                    em.dma(SY, onld_sb[64 * h2:64 * h2 + 64, cin, :],
                           onorm_d[cin, band, h2][0:64, :], inc="kv0")
            em.wait(G, "kvl", em.val("kvl"))
            em.wait(G, "fin", em.val("fin"))
            for s8 in range(8):
                em.dma(G, lld_sb[2 * s8:2 * s8 + 2, 0, :],
                       onorm_d[s8, band][:, 64, :], inc="kvl")
            em.wait(V, "kv0")
            em.wait(V, "kvl")
            em.emit(V, lambda e: e.reciprocal(lld_sb[:, 1, :], lld_sb[:, 0, :]),
                    inc="fin")
            em.emit(V, lambda e: e.drain())
            for cin in range(DCH):
                em.wait(T, "fin")
                em.emit(T, lambda e, cin=cin: e.matmul(
                    psB[:, 0:256], bc16_sb[:, cin, :],
                    lld_sb[:, 1, :], start=True, stop=True), inc="ymt")
                em.wait(V, "ymt")
                em.emit(V, lambda e: e.tensor_copy(rlbb_sb[:], psB[:, 0:256]))
                em.emit(V, lambda e: e.drain())
                em.emit(V, lambda e, cin=cin: e.tensor_tensor(
                    onf_sb[:, cin, :], onld_sb[:, cin, :], rlbb_sb[:],
                    op=ALU.mult), inc="fin")
                em.emit(V, lambda e: e.drain())
            em.wait(T, "fin")
            em.wait(T, "xev", em.val("xev"))
            for c2 in range(DCH):
                for cin in range(DCH):
                    em.emit(T, lambda e, c2=c2, cin=cin: e.matmul(
                        psA[:, c2 * 256:(c2 + 1) * 256],
                        wof_sb[:, cin, c2, :], onf_sb[:, cin, :],
                        start=(cin == 0), stop=(cin == DCH - 1)),
                        inc="ymm" if (c2 == DCH - 1 and cin == DCH - 1)
                        else None)
            em.wait(V, "ymm")
            em.emit(V, lambda e, qs=qs: e.tensor_tensor(
                x_sb[:, :, qs], x_sb[:, :, qs],
                psA[:].rearrange("p (c m) -> p c m", c=DCH), op=ALU.add),
                inc="xev")
        em.emit(V, lambda e: e.drain())


    def body(em: E):
        V, SC, T, G, SY = "vector", "scalar", "tensor", "gpsimd", "sync"
        for c in range(DCH):
            em.dma(SY, x_sb[:, c, :], x_in[c], inc="ld")
        em.dma(SY, b2stg_sb[:], b2_in, inc="ld")
        em.emit(V, lambda e: e.memset(const_sb[:, 1:2], 1e-5), inc="castv")
        em.emit(V, lambda e: e.memset(onesrep_sb[:], 1.0 / D), inc="castv")
        em.emit(V, lambda e: e.memset(onesbf_sb[:], 1.0), inc="castv")
        em.emit(V, lambda e: e.memset(onesrow_sb[:], 1.0), inc="castv")
        em.emit(V, lambda e: e.memset(onescol_sb[:], 1.0), inc="castv")
        em.wait(SC, "ld", em.val("ld"))
        em.emit(SC, lambda e: e.mul(b2h_sb[:], b2stg_sb[:], 0.5), inc="casts")
        em.wait(T, "ld", em.val("ld"))
        em.wait(T, "castv", 5)
        em.wait(SC, "castv", 5)
        em.wait(V, "ld", em.val("ld"))

        emit_ffn0(em, True)
        if upto >= 2:
            emit_attn0(em)
        if upto >= 3:
            emit_ffn0(em, False)
        if upto >= 4:
            emit_ffn1(em, True)
        if upto >= 5:
            emit_attn1(em)
        if upto >= 6:
            emit_ffn1(em, False)

        em.wait(SY, "xnr")
        em.wait(SY, "yv", em.val("yv"))
        em.wait(SY, "xev", em.val("xev"))
        for c in range(DCH):
            em.dma(SY, y_out[c], x_sb[:, c, :], inc="out")
        em.wait(SY, "out")

    with nc.Block() as block:
        @block.vector
        def _(eng):
            body(E(nc, "vector", sems))

        @block.scalar
        def _(eng):
            body(E(nc, "scalar", sems))

        @block.tensor
        def _(eng):
            body(E(nc, "tensor", sems))

        @block.gpsimd
        def _(eng):
            body(E(nc, "gpsimd", sems))

        @block.sync
        def _(eng):
            body(E(nc, "sync", sems))

    est.close()
    return nc



def core_idx(core):
    pr = core % 2
    return np.concatenate([np.arange(blk * BLK, (blk + 1) * BLK)
                           for blk in PAT[pr]])


def make_masks():
    j = np.arange(BLK)[None, :]
    pp = np.arange(128)[:, None]
    d0 = np.where(pp <= j, 0.0, NEG).astype(np.float32)
    d1 = np.where(pp + 128 <= j, 0.0, NEG).astype(np.float32)
    maskd = np.ascontiguousarray(np.stack([d0, d1], axis=1))
    full = np.full((128, 2, BLK), NEG, np.float32)
    zero = np.zeros((128, 2, BLK), np.float32)
    return maskd, full, zero


def stage_inputs(d):
    x = np.asarray(d["L0_attn_x"], np.float32)
    w1 = np.ascontiguousarray(np.asarray(d["L0_ffn_w1"], np.float32))
    w2 = np.ascontiguousarray(np.asarray(d["L0_ffn_w2"], np.float32))
    b1 = np.asarray(d["L0_ffn_b1"], np.float32)
    b2 = np.asarray(d["L0_ffn_b2"], np.float32)
    maskd, full, zero = make_masks()
    prot = np.zeros((128, 128), np.float32)
    for blk0 in (0, 64):
        for i in range(32):
            prot[blk0 + 2 * i + 1, blk0 + 2 * i] = -1.0
            prot[blk0 + 2 * i, blk0 + 2 * i + 1] = 1.0
    bc16 = np.zeros((16, 8, 128), np.float32)
    for cin8 in range(8):
        bc16[2 * cin8, cin8, 0:64] = 1.0
        bc16[2 * cin8 + 1, cin8, 64:128] = 1.0
    maps = []
    for core in range(8):
        b = core // 2
        pr = core % 2
        idx = core_idx(core)
        inv = 10000.0 ** (-np.arange(0, 64, 2, dtype=np.float64) / 64.0)
        ang = idx[None, :].astype(np.float64) * inv[:, None]
        cos64 = np.repeat(np.cos(ang), 2, axis=0)
        sin64 = np.repeat(np.sin(ang), 2, axis=0)
        rope_c = np.ascontiguousarray(
            np.concatenate([cos64, cos64], axis=0).astype(np.float32))
        rope_s = np.ascontiguousarray(
            np.concatenate([sin64, sin64], axis=0).astype(np.float32))
        xs = np.ascontiguousarray(x[b][idx].T).reshape(DCH, 128, MYT)
        maps.append({
            "x_in": xs,
            "w1": w1,
            "w2": w2,
            "b1r": np.ascontiguousarray(b1.reshape(1, 4 * D)),
            "b2": np.ascontiguousarray(b2.reshape(DCH, 128).T),
            "wq": np.ascontiguousarray(np.asarray(d["L0_attn_Wq"], np.float32)),
            "wk": np.ascontiguousarray(np.asarray(d["L0_attn_Wk"], np.float32)),
            "wv": np.ascontiguousarray(np.asarray(d["L0_attn_Wv"], np.float32)),
            "wo": np.ascontiguousarray(np.asarray(d["L0_attn_Wo"], np.float32)),
            "maskd": maskd,
            "mv0": full if pr == 0 else zero,
            "mv1": zero if pr == 0 else full,
            "eye": np.eye(128, dtype=np.float32),
            "wt": np.ascontiguousarray(np.asarray(d["L1_ffn_weight"], np.float32).T),
            "ut": np.ascontiguousarray(
                np.asarray(d["L1_ffn_lora_U"], np.float32).reshape(128, D).T),
            "vs": np.ascontiguousarray(
                np.asarray(d["L1_ffn_lora_V"], np.float32).transpose(0, 2, 1)
                .reshape(128, D)),
            "gw": np.ascontiguousarray(np.asarray(d["L1_ffn_gate_w"], np.float32).T),
            "g16": np.ascontiguousarray(
                np.repeat(np.eye(8, dtype=np.float32), 16, axis=1)),
            "wqkvt": np.ascontiguousarray(
                np.asarray(d["L1_attn_W_qkv"], np.float32).T),
            "wot": np.ascontiguousarray(np.asarray(d["L1_attn_W_o"], np.float32).T),
            "cosi": rope_c, "sini": rope_s,
            "prot": prot,
            "bc16": bc16,
        })
    return maps


def gather_output(results):
    out = np.zeros((4, S, D), np.float32)
    for core in range(8):
        b = core // 2
        idx = core_idx(core)
        y = results[core]["y_out"].reshape(D, MYT).T
        out[b][idx] = y
    return out


from concourse.bass_utils import run_bass_kernel_spmd

_NC_CACHE = []


def kernel(**inputs):
    maps = stage_inputs(inputs)
    if not _NC_CACHE:
        _NC_CACHE.append(build_program(6))
    nc = _NC_CACHE[0]
    res = run_bass_kernel_spmd(nc, maps, core_ids=list(range(8)))
    return gather_output(res.results)
